# version 10 (final)
# baseline (speedup 1.0000x reference)
import contextlib
import numpy as np
import ml_dtypes
import concourse.bass as bass
import concourse.mybir as mybir
from concourse.bass_utils import run_bass_kernel_spmd

F32 = mybir.dt.float32
BF16 = mybir.dt.bfloat16
U8 = mybir.dt.uint8
AF = mybir.ActivationFunctionType
ALU = mybir.AluOpType

S = 4096
D = 1024
NB = 8
NQ = 512
EPS = 1e-5
GAMMA = 0.96875
NPT = S // 128
NDC = D // 128
BF = ml_dtypes.bfloat16


def host_constants():
    c = {}
    c["id_bf"] = np.eye(128, dtype=BF)
    c["id_f32"] = np.eye(128, dtype=np.float32)
    c["ones_1x128_bf"] = np.ones((1, 128), dtype=BF)
    c["ones_1x128_f32"] = np.ones((1, 128), dtype=np.float32)
    c["ones_64x1_bf"] = np.ones((64, 1), dtype=BF)
    c["ones_128x1_bf"] = np.ones((128, 1), dtype=BF)
    c["ones_1x512_bf"] = np.ones((1, 512), dtype=BF)
    c["ones_128x1_f32"] = np.ones((128, 1), dtype=np.float32)
    kk = np.arange(128)
    c["tri128"] = (kk[:, None] <= kk[None, :]).astype(np.float32)
    k32 = np.arange(32)
    c["lt32"] = (k32[:, None] < k32[None, :]).astype(np.float32)
    c["erow"] = (np.arange(8, dtype=np.float32) * 768.0 - 1.0).reshape(1, 8)
    c["lt8"] = (np.arange(8)[:, None] < np.arange(8)[None, :]).astype(np.float32)
    c["ones_32x1_f32"] = np.ones((32, 1), dtype=np.float32)
    c["ones_8x1_f32"] = np.ones((8, 1), dtype=np.float32)
    c["tilebases"] = np.tile((np.arange(16, dtype=np.float32) * 512.0)[None, :], (8, 1))
    c["iota128f"] = np.arange(128, dtype=np.float32).reshape(128, 1)
    c["iota_col"] = np.arange(4096, dtype=np.int32).reshape(32, 128).T.copy()
    k = np.arange(128)[:, None].astype(np.float64)
    u = np.arange(896)[None, :].astype(np.float64)
    c["m01"] = ((u - 384.0) >= k).astype(BF)
    e = u - 384.0 - k
    c["mrm"] = np.where(e >= 0, GAMMA ** np.maximum(e, 0), 0.0).astype(BF)
    u2 = np.arange(512)[None, :].astype(np.float64)
    c["mrp"] = (GAMMA ** (u2 - k + 127.0)).astype(BF)
    sel = np.zeros((128, 8, 4), dtype=np.float32)
    for j in range(8):
        sel[:, j, j // 2] = 1.0 / 16.0
    c["sel"] = sel.reshape(128, 32).astype(BF)
    return c


INPUT_SHAPES = {
    "x_mine": ([S, D], F32), "x_full": ([NB * S, D], F32),
    "id_bf": ([128, 128], BF16), "id_f32": ([128, 128], F32),
    "ones_1x128_bf": ([1, 128], BF16), "ones_1x128_f32": ([1, 128], F32),
    "ones_64x1_bf": ([64, 1], BF16), "ones_128x1_bf": ([128, 1], BF16),
    "ones_1x512_bf": ([1, 512], BF16),
    "m01": ([128, 896], BF16), "mrm": ([128, 896], BF16), "mrp": ([128, 512], BF16),
    "sel": ([128, 32], BF16), "bqkv_vrow": ([1, 2 * D], BF16),
    "ones_128x1_f32": ([128, 1], F32), "tri128": ([128, 128], F32),
    "lt32": ([32, 32], F32), "erow": ([1, 8], F32), "iota_col": ([128, 32], mybir.dt.int32),
    "lt8": ([8, 8], F32), "ones_32x1_f32": ([32, 1], F32), "ones_8x1_f32": ([8, 1], F32),
    "tilebases": ([8, 16], F32), "iota128f": ([128, 1], F32),
    "WqkvT": ([D, 3 * D], F32), "bqkv": ([128, 24], F32),
    "WoT": ([D, D], F32), "bo": ([128, 8], F32),
    "g1": ([128, 8], F32), "be1": ([128, 8], F32),
    "g2": ([128, 8], F32), "be2": ([128, 8], F32),
    "WgT": ([D, 8], F32),
    "W1": ([8, D, 4 * D], F32), "b1": ([128, 256], F32), "b1t": ([8, 128, 32], F32), "b2t": ([8, 128, 8], F32),
    "W2": ([8, 4 * D, D], F32), "b2": ([128, 64], F32),
    "Wq1": ([D, D], F32), "Wk1": ([D, D], F32), "Wv1": ([D, D], F32),
    "theta": ([128, 8], F32),
    "k0u": ([32, 128, 2 * D], U8), "k1u": ([32, 128, 4 * D], U8), "k2u": ([8, 128, 4 * D], U8),
    "s012": ([1, 3], F32),
}

CONST_SBUF = {
    "id_bf": ([128, 128], BF16),
    "ones_1x128_bf": ([1, 128], BF16), "ones_1x128_f32": ([1, 128], F32),
    "ones_128x1_bf": ([128, 1], BF16),
    "ones_1x512_bf": ([1, 512], BF16),
    "m01": ([128, 896], BF16), "mrm": ([128, 896], BF16), "mrp": ([128, 512], BF16),
    "sel": ([128, 32], BF16),
    "ones_128x1_f32": ([128, 1], F32), "tri128": ([128, 128], F32),
    "lt32": ([32, 32], F32), "erow": ([1, 8], F32), "iota_col": ([128, 32], mybir.dt.int32),
    "lt8": ([8, 8], F32), "ones_32x1_f32": ([32, 1], F32), "ones_8x1_f32": ([8, 1], F32),
    "tilebases": ([8, 16], F32), "iota128f": ([128, 1], F32),
    "bqkv": ([128, 24], F32), "bo": ([128, 8], F32),
    "g1": ([128, 8], F32), "be1": ([128, 8], F32),
    "g2": ([128, 8], F32), "be2": ([128, 8], F32),
    "theta": ([128, 8], F32), "s012": ([1, 3], F32),
}


def host_prep(inputs):
    c = host_constants()
    x = np.ascontiguousarray(np.asarray(inputs["L0_attn_x"], np.float32))
    sh = dict(c)
    sh["x_full"] = x.reshape(NB * S, D)
    sh["WqkvT"] = np.ascontiguousarray(np.asarray(inputs["L0_ffn_Wqkv"], np.float32).T)
    sh["bqkv"] = np.ascontiguousarray(np.asarray(inputs["L0_ffn_bqkv"], np.float32).reshape(24, 128).T)
    sh["WoT"] = np.ascontiguousarray(np.asarray(inputs["L0_ffn_Wo"], np.float32).T)
    sh["bqkv_vrow"] = np.ascontiguousarray(np.asarray(inputs["L0_ffn_bqkv"], np.float32)[1024:3072].reshape(1, 2 * D).astype(BF))
    sh["bo"] = np.ascontiguousarray(np.asarray(inputs["L0_ffn_bo"], np.float32).reshape(8, 128).T)
    for n, src in (("g1", "L0_ffn_g1"), ("be1", "L0_ffn_be1"), ("g2", "L0_ffn_g2"), ("be2", "L0_ffn_be2")):
        sh[n] = np.ascontiguousarray(np.asarray(inputs[src], np.float32).reshape(8, 128).T)
    sh["WgT"] = np.ascontiguousarray(np.asarray(inputs["L0_ffn_Wg"], np.float32).T)
    sh["W1"] = np.ascontiguousarray(np.asarray(inputs["L0_ffn_W1"], np.float32))
    sh["b1"] = np.ascontiguousarray(np.asarray(inputs["L0_ffn_b1"], np.float32).reshape(256, 128).T)
    sh["b1t"] = np.ascontiguousarray(np.asarray(inputs["L0_ffn_b1"], np.float32).reshape(8, 32, 128).transpose(0, 2, 1))
    sh["b2t"] = np.ascontiguousarray(np.asarray(inputs["L0_ffn_b2"], np.float32).reshape(8, 8, 128).transpose(0, 2, 1))
    sh["W2"] = np.ascontiguousarray(np.asarray(inputs["L0_ffn_W2"], np.float32))
    sh["b2"] = np.ascontiguousarray(np.asarray(inputs["L0_ffn_b2"], np.float32).reshape(64, 128).T)
    sh["Wq1"] = np.ascontiguousarray(np.asarray(inputs["L1_attn_weight_q"], np.float32))
    sh["Wk1"] = np.ascontiguousarray(np.asarray(inputs["L1_attn_weight_k"], np.float32))
    sh["Wv1"] = np.ascontiguousarray(np.asarray(inputs["L1_attn_weight_v"], np.float32))
    sh["theta"] = np.ascontiguousarray(np.asarray(inputs["L1_attn_theta"], np.float32).reshape(8, 128).T)
    k0 = np.asarray(inputs["L1_ffn_k0"]).astype(np.uint8)
    k1 = np.asarray(inputs["L1_ffn_k1"]).astype(np.uint8)
    k2 = np.asarray(inputs["L1_ffn_k2"]).astype(np.uint8)
    sh["k0u"] = np.ascontiguousarray(k0.reshape(16, 128, 32, 128).transpose(2, 1, 0, 3).reshape(32, 128, 2048))
    sh["k1u"] = np.ascontiguousarray(k1.reshape(32, 128, 32, 128).transpose(2, 1, 0, 3).reshape(32, 128, 4096))
    sh["k2u"] = np.ascontiguousarray(k2.reshape(32, 128, 8, 128).transpose(2, 1, 0, 3).reshape(8, 128, 4096))
    sh["s012"] = np.ascontiguousarray(np.stack([
        np.asarray(inputs["L1_ffn_s0"], np.float32),
        np.asarray(inputs["L1_ffn_s1"], np.float32),
        np.asarray(inputs["L1_ffn_s2"], np.float32)]).reshape(1, 3))
    in_maps = []
    for core in range(NB):
        m = dict(sh)
        m["x_mine"] = np.ascontiguousarray(x[core])
        in_maps.append(m)
    return in_maps


class Ctr:

    def __init__(self, P, name, persistent=False):
        if persistent:
            self.h = P.es.enter_context(P.nc.semaphore(name))
        else:
            self.h = P.nc.alloc_semaphore(name=name)
            P._allsems.append(self.h)
        self.n = 0

    def inc(self, instr, k=1):
        instr.then_inc(self.h, k)
        self.n += k
        return self.n


class Slots:

    def __init__(self, P, name, n):
        self.cs = [Ctr(P, f"{name}_{i}") for i in range(n)]
        self.n = n

    def c(self, i):
        return self.cs[i % self.n]


class Prog:
    def __init__(self):
        self.nc = bass.Bass("TRN2", target_bir_lowering=False)
        self.es = contextlib.ExitStack()
        self.t = {}
        self._bar = None
        self._barn = 0
        self._uid = 0
        self._allsems = []

    def sem_mark(self):
        return len(self._allsems)

    def sem_release(self, mark):
        hs = self._allsems[mark:]
        if not hs:
            return
        del self._allsems[mark:]
        with self.block() as blk:
            @blk.gpsimd
            def _(gp):
                self.nc.clear_and_free_semaphores(hs)
        self.barrier()

    def uid(self):
        self._uid += 1
        return self._uid

    def dram(self, name, shape, dtype, kind=None):
        t = self.nc.dram_tensor(name, list(shape), dtype, kind=kind) if kind \
            else self.nc.dram_tensor(name, list(shape), dtype)
        self.t[name] = t
        return t

    def sbuf(self, name, shape, dtype, es=None):
        t = (es or self.es).enter_context(self.nc.sbuf_tensor(name, list(shape), dtype))
        self.t[name] = t
        return t

    def psum(self, name, shape, dtype=F32, es=None):
        t = (es or self.es).enter_context(self.nc.psum_tensor(name, list(shape), dtype))
        self.t[name] = t
        return t

    def block(self):
        return self.nc.Block()

    def barrier(self, engines=("tensor", "vector", "scalar", "gpsimd", "sync")):
        if self._bar is None:
            self._bar = self.es.enter_context(self.nc.semaphore("bar"))
        tot = self._barn + len(engines)
        b = self._bar
        with self.block() as blk:
            for e in engines:
                def mk():
                    def f(eng):
                        eng.sem_inc(b, 1)
                        eng.wait_ge(b, tot)
                    return f
                getattr(blk, e)(mk())
        self._barn = tot

    def declare_inputs(self, names):
        for n in names:
            shp, dt = INPUT_SHAPES[n]
            self.dram(n, shp, dt, kind="ExternalInput")

    def load_consts(self, names):
        ld = Ctr(self, "constld")
        with self.block() as blk:
            @blk.sync
            def _(sy):
                for n in names:
                    shp, dt = CONST_SBUF[n]
                    sb = self.sbuf("c_" + n, shp, dt)
                    ld.inc(sy.dma_start(out=sb[:, :], in_=self.t[n][:, :]), 16)
        with self.block() as blk:
            @blk.sync
            def _(sy):
                sy.wait_ge(ld.h, ld.n)
        self.barrier()

    def c(self, name):
        return self.t["c_" + name]


def emit_ln0(P, src, xnT=None, xnF=None, tag=""):
    nc = P.nc
    u = P.uid()
    _smark = P.sem_mark()
    pes = contextlib.ExitStack()
    xb = [P.sbuf(f"lnx{u}_{i}", [128, D], F32, es=pes) for i in range(2)]
    sqb = [P.sbuf(f"lnq{u}_{i}", [128, D], F32, es=pes) for i in range(2)]
    stat = [P.sbuf(f"lns{u}_{i}", [128, 8], F32, es=pes) for i in range(2)]
    xnt_tmp = None
    if xnT is None:
        xnt_tmp = [P.sbuf(f"lnt{u}_{i}", [128, D], BF16, es=pes) for i in range(2)]
    ptp = [P.psum(f"lnp{u}_{i}", [128, 128], BF16, es=pes) for i in range(2)]
    ld = Slots(P, f"ln_ld{u}", 2)
    asq = Ctr(P, f"ln_asq{u}")
    dst = Ctr(P, f"ln_dst{u}")
    art = Ctr(P, f"ln_art{u}")
    nrm = Ctr(P, f"ln_nrm{u}")
    ttp = Ctr(P, f"ln_ttp{u}")
    aev = Ctr(P, f"ln_aev{u}")
    id_bf = P.c("id_bf")

    with P.block() as blk:
        @blk.sync
        def _(sy):
            for t in range(NPT):
                if t >= 2:
                    sy.wait_ge(nrm.h, t - 1)
                ld.c(t).inc(sy.dma_start(out=xb[t % 2][:, :], in_=src(t)), 16)

        @blk.scalar
        def _(sc):
            def evacs(t):
                if xnF is None:
                    return
                for j in range(NDC):
                    n = 8 * t + j
                    sc.wait_ge(ttp.h, n + 1)
                    aev.inc(sc.copy(xnF[:, j * S + t * 128: j * S + (t + 1) * 128], ptp[n % 2][:, :]))
            for t in range(NPT):
                sc.wait_ge(ld.c(t).h, 16 * (t // 2 + 1))
                if t >= 2:
                    sc.wait_ge(dst.h, t - 1)
                asq.inc(sc.activation(sqb[t % 2][:, :], xb[t % 2][:, :], AF.Square))
                sc.wait_ge(dst.h, t + 1)
                art.inc(sc.activation(stat[t % 2][:, 7:8], stat[t % 2][:, 6:7], AF.Sqrt))
                if t >= 1:
                    evacs(t - 1)
            evacs(NPT - 1)

        @blk.vector
        def _(v):
            vc = Ctr(P, f"ln_vc{u}")
            for t in range(NPT):
                v.wait_ge(ld.c(t).h, 16 * (t // 2 + 1))
                if t >= 2:
                    v.wait_ge(nrm.h, t - 1)
                vc.inc(v.reduce_sum(stat[t % 2][:, 0:1], xb[t % 2][:, :], axis=mybir.AxisListType.X))
                v.wait_ge(asq.h, t + 1)
                vc.inc(v.reduce_sum(stat[t % 2][:, 1:2], sqb[t % 2][:, :], axis=mybir.AxisListType.X))
                v.wait_ge(vc.h, vc.n)
                vc.inc(v.tensor_scalar(stat[t % 2][:, 2:3], stat[t % 2][:, 0:1], 1.0 / D, None, ALU.mult))
                vc.inc(v.tensor_scalar(stat[t % 2][:, 3:4], stat[t % 2][:, 1:2], 1.0 / D, None, ALU.mult))
                v.wait_ge(vc.h, vc.n)
                vc.inc(v.tensor_mul(stat[t % 2][:, 4:5], stat[t % 2][:, 2:3], stat[t % 2][:, 2:3]))
                v.wait_ge(vc.h, vc.n)
                vc.inc(v.tensor_sub(stat[t % 2][:, 5:6], stat[t % 2][:, 3:4], stat[t % 2][:, 4:5]))
                v.wait_ge(vc.h, vc.n)
                dst.inc(v.tensor_scalar(stat[t % 2][:, 6:7], stat[t % 2][:, 5:6], EPS, None, ALU.add))
                v.wait_ge(art.h, t + 1)
                vc.inc(v.reciprocal(stat[t % 2][:, 6:7], stat[t % 2][:, 7:8]))
                v.wait_ge(vc.h, vc.n)
                dstt = xnT[:, t * D:(t + 1) * D] if xnT is not None else xnt_tmp[t % 2][:, :]
                if xnT is None and t >= 2:
                    v.wait_ge(ttp.h, 8 * (t - 1))
                nrm.inc(v.tensor_scalar(dstt, xb[t % 2][:, :], stat[t % 2][:, 2:3],
                                        stat[t % 2][:, 6:7], ALU.subtract, ALU.mult))

        if xnF is not None:
            @blk.tensor
            def _(te):
                for t in range(NPT):
                    te.wait_ge(nrm.h, t + 1)
                    srct = xnT[:, t * D:(t + 1) * D] if xnT is not None else xnt_tmp[t % 2][:, :]
                    for j in range(NDC):
                        n = 8 * t + j
                        if n >= 2:
                            te.wait_ge(aev.h, n - 1)
                        ttp.inc(te.transpose(ptp[n % 2][:, :], srct[:, j * 128:(j + 1) * 128], id_bf[:, :]))
    P.barrier()
    P.sem_release(_smark)
    pes.close()


def emit_l0_attn(P, xnF, xnT, sc_attn, extra=None):
    nc = P.nc
    u = P.uid()
    _smark = P.sem_mark()
    pes = contextlib.ExitStack()
    vstrip = [P.sbuf(f"at_vs{u}_{i}", [128, 65 * NPT], BF16, es=pes) for i in range(2)]
    negc = P.sbuf(f"at_nc{u}", [128, 1], F32, es=pes)
    ksl = [P.sbuf(f"at_ks{u}_{i}", [128, S], BF16, es=pes) for i in range(2)]
    pbuf = [P.sbuf(f"at_pb{u}_{i}", [128, NQ], BF16, es=pes) for i in range(4)]
    pdtmp = [P.sbuf(f"at_pd{u}_{i}", [128, NQ], BF16, es=pes) for i in range(2)]
    srec = [P.sbuf(f"at_rc{u}_{i}", [1, NQ], F32, es=pes) for i in range(2)]
    srb = [P.sbuf(f"at_rb{u}_{i}", [64, NQ], F32, es=pes) for i in range(2)]
    aout = [P.sbuf(f"at_ao{u}_{i}", [64, NQ], BF16, es=pes) for i in range(2)]
    ps_s = [P.psum(f"at_ps{u}_{i}", [128, NQ], F32, es=pes) for i in range(4)]
    ps_pv = [P.psum(f"at_pv{u}_{i}", [128, NQ], F32, es=pes) for i in range(2)]
    ps_rb = P.psum(f"at_prb{u}", [64, NQ], F32, es=pes)

    c_ic = Ctr(P, f"at_ic{u}")
    c_ksl = Ctr(P, f"at_kl{u}")
    c_mm = Ctr(P, f"at_mm{u}")
    c_exp = Ctr(P, f"at_exp{u}")
    c_msk = Ctr(P, f"at_msk{u}")
    c_pv = Ctr(P, f"at_pv{u}")
    c_vst = Ctr(P, f"at_vst{u}")
    c_rc = Ctr(P, f"at_rc{u}")
    c_rb = Ctr(P, f"at_rb{u}")
    c_arb = Ctr(P, f"at_arb{u}")
    c_do = Ctr(P, f"at_do{u}")
    c_st = Slots(P, f"at_st{u}", 2)

    ones1f = P.c("ones_1x128_f32")
    m01 = P.c("m01")

    groups = [(h, qc) for h in range(16) for qc in range(8)]
    ntile_g = {qc: 4 * (qc + 1) for qc in range(8)}

    def tiles_of(qc):
        return list(range(4 * (qc + 1)))

    def schedule():
        mm = exp = msk = pv = 0
        nd = 0
        sched = []
        for gi, (h, qc) in enumerate(groups):
            tl = []
            for kc in tiles_of(qc):
                diag = kc >= 4 * qc
                mm += 1
                exp += 1
                d = dict(kc=kc, diag=diag, mm=mm, exp=exp, msk=msk, pv=pv)
                if diag:
                    msk += 1
                    d["msk"] = msk
                    d["nd"] = nd
                    nd += 1
                pv += 1
                d["pv"] = pv
                tl.append(d)
            sched.append(dict(h=h, qc=qc, gi=gi, tiles=tl))
        return sched

    SCH = schedule()

    def xnF_head(h, c0, cn):
        j = h // 2
        r = (h % 2) * 64
        return xnF[r:r + 64, j * S + c0: j * S + c0 + cn]

    with P.block() as blk:
        @blk.vector
        def _(v):
            for i in range(2):
                c_vst.inc(v.memset(vstrip[i][:, :].rearrange("p (a b) -> p a b", b=65)[:, :, 64:65], 1.0))
            c_ic.inc(v.memset(negc[:, :], -20.0))
            c_ksl.inc(v.memset(ksl[0][64:128, :], 0.0))
            c_ksl.inc(v.memset(ksl[1][0:64, :], 0.0))
            mme = {hh: SCH[hh * 8 + 7]["tiles"][-1]["mm"] for hh in range(16)}
            pvend = {hh: sum(len(gg["tiles"]) for gg in SCH[: (hh + 1) * 8]) for hh in range(16)}

            def head_copies(h):
                if h >= 2:
                    v.wait_ge(c_mm.h, mme[h - 2])
                    v.wait_ge(c_pv.h, pvend[h - 2])
                r0 = (h % 2) * 64
                c_ksl.inc(v.tensor_copy(ksl[h % 2][r0:r0 + 64, :],
                                        xnF[r0:r0 + 64, (h // 2) * S:(h // 2 + 1) * S]))
                for kc in range(NPT):
                    c_vst.inc(v.tensor_copy(
                        vstrip[h % 2][:, 65 * kc: 65 * kc + 64],
                        xnT[:, kc * D + 64 * h: kc * D + 64 * h + 64]))
            head_copies(0)
            head_copies(1)
            for g in SCH:
                h, qc, gi = g["h"], g["qc"], g["gi"]
                if qc == 6 and h < 15:
                    head_copies(h + 1)
                for tl in g["tiles"]:
                    if tl["diag"]:
                        cofs = qc * NQ - tl["kc"] * 128 + 384
                        v.wait_ge(c_exp.h, tl["exp"])
                        if tl["pv"] > 4:
                            v.wait_ge(c_pv.h, tl["pv"] - 4)
                        c_msk.inc(v.tensor_mul(pbuf[(tl["pv"] - 1) % 4][:, :],
                                               pdtmp[tl["nd"] % 2][:, :],
                                               m01[:, cofs:cofs + NQ]))
                v.wait_ge(c_pv.h, g["tiles"][-1]["pv"])
                c_rc.inc(v.reciprocal(srec[gi % 2][:, :], ps_pv[gi % 2][64:65, :]))
                v.wait_ge(c_arb.h, gi + 1)
                if gi >= 2:
                    v.wait_ge(c_st.c(gi).h, 16 * ((gi - 2) // 2 + 1))
                c_do.inc(v.tensor_mul(aout[gi % 2][:, :], ps_pv[gi % 2][0:64, :], srb[gi % 2][:, :]))

        @blk.scalar
        def _(sc):
            sc.wait_ge(c_ic.h, 1)

            def store_group(gp_):
                hp, qp = gp_["h"], gp_["qc"]
                sc.wait_ge(c_do.h, gp_["gi"] + 1)
                c_st.c(gp_["gi"]).inc(sc.dma_start(
                    out=sc_attn[64 * hp:64 * hp + 64, qp * NQ:(qp + 1) * NQ],
                    in_=aout[gp_["gi"] % 2][:, :]), 16)

            for g in SCH:
                h, qc, gi = g["h"], g["qc"], g["gi"]
                if gi >= 1:
                    store_group(SCH[gi - 1])
                for tl in g["tiles"]:
                    sc.wait_ge(c_mm.h, tl["mm"])
                    if tl["diag"]:
                        if tl["nd"] > 1:
                            sc.wait_ge(c_msk.h, tl["nd"] - 1)
                        dst = pdtmp[tl["nd"] % 2][:, :]
                    else:
                        dst = pbuf[(tl["pv"] - 1) % 4][:, :]
                        if tl["pv"] > 4:
                            sc.wait_ge(c_pv.h, tl["pv"] - 4)
                    c_exp.inc(sc.activation(dst, ps_s[tl["mm"] % 4][:, :], AF.Exp,
                                            scale=0.125, bias=negc[:, 0:1]))
                sc.wait_ge(c_rb.h, gi + 1)
                c_arb.inc(sc.copy(srb[gi % 2][:, :], ps_rb[0:64, :]))
            store_group(SCH[-1])
            sc.wait_ge(c_st.c(0).h, c_st.c(0).n)
            sc.wait_ge(c_st.c(1).h, c_st.c(1).n)

        @blk.tensor
        def _(te):
            RUN = 3
            vst_seen = 0
            rb_pend = None

            def emit_rb(g_):
                te.wait_ge(c_rc.h, g_["gi"] + 1)
                c_rb.inc(te.matmul(ps_rb[0:64, :], ones1f[0:1, 0:64],
                                   srec[g_["gi"] % 2][:, :], start=True, stop=True))
            for g in SCH:
                h, qc, gi = g["h"], g["qc"], g["gi"]
                if qc == 0:
                    te.wait_ge(c_ksl.h, h + 3)
                def emit_s(tl):
                    if tl["mm"] > 4:
                        te.wait_ge(c_exp.h, tl["mm"] - 4)
                    c_mm.inc(te.matmul(ps_s[tl["mm"] % 4][:, :],
                                       ksl[h % 2][:, tl["kc"] * 128:tl["kc"] * 128 + 128],
                                       xnF[0:128, (h // 2) * S + qc * NQ: (h // 2) * S + (qc + 1) * NQ],
                                       start=True, stop=True))

                def emit_pv(tl):
                    nonlocal vst_seen
                    need = 2 + 32 * h + min(tl["kc"] + 1, NPT)
                    if need > vst_seen:
                        te.wait_ge(c_vst.h, need)
                        vst_seen = need
                    if tl["diag"]:
                        te.wait_ge(c_msk.h, tl["msk"])
                    else:
                        te.wait_ge(c_exp.h, tl["exp"])
                    if gi >= 2:
                        te.wait_ge(c_do.h, gi - 1)
                    c_pv.inc(te.matmul(ps_pv[gi % 2][0:65, :],
                                       vstrip[h % 2][:, 65 * tl["kc"]:65 * tl["kc"] + 65],
                                       pbuf[(tl["pv"] - 1) % 4][:, :],
                                       start=(tl["kc"] == 0), stop=(tl["kc"] == g["tiles"][-1]["kc"])))
                pending = []
                tl_all = g["tiles"]
                first = True
                for j0 in range(0, len(tl_all), RUN):
                    for tl in tl_all[j0:j0 + RUN]:
                        emit_s(tl)
                    if first and rb_pend is not None:
                        emit_rb(rb_pend)
                        rb_pend = None
                    first = False
                    for tl in pending:
                        emit_pv(tl)
                    pending = tl_all[j0:j0 + RUN]
                for tl in pending:
                    emit_pv(tl)
                rb_pend = g
            emit_rb(SCH[-1])

        if extra is not None:
            extra(blk)
    P.barrier()
    P.sem_release(_smark)
    pes.close()


if __name__ == "__main__":
    print("library")


def proj_phase(P, tag, tiles, nk, lhs_ap, rhs_ap, evac_engine, evac_fn,
               psum_shape=(128, 512), extra_mm=None, store_fn=None):
    u = P.uid()
    _smark = P.sem_mark()
    pes = contextlib.ExitStack()
    NPS = 4
    ps = [P.psum(f"pj{u}_{i}", list(psum_shape), F32, es=pes) for i in range(NPS)]
    c_mm = Ctr(P, f"pj_mm{u}")
    c_ev = Ctr(P, f"pj_ev{u}")
    with P.block() as blk:
        @blk.tensor
        def _(te):
            for ti, tile in enumerate(tiles):
                if ti >= NPS:
                    te.wait_ge(c_ev.h, ti - NPS + 1)
                for k in range(nk):
                    last = (k == nk - 1) and extra_mm is None
                    r = te.matmul(ps[ti % NPS][:, :], lhs_ap(tile, k), rhs_ap(tile, k),
                                  start=(k == 0), stop=last)
                if extra_mm is not None:
                    r = extra_mm(te, tile, ps[ti % NPS])
                c_mm.inc(r)

        def mkevac(eng):
            for ti, tile in enumerate(tiles):
                eng.wait_ge(c_mm.h, ti + 1)
                r = evac_fn(eng, ti, tile, ps[ti % NPS])
                c_ev.inc(r)
        getattr(blk, evac_engine)(mkevac)

        if store_fn is not None:
            @blk.gpsimd
            def _(gp):
                for ti, tile in enumerate(tiles):
                    gp.wait_ge(c_ev.h, ti + 1)
                    store_fn(gp, ti, tile)
    P.barrier()
    P.sem_release(_smark)
    pes.close()
    return c_ev


def emit_tiny_qproj(P, xnF, wqkv_q, sc_q):
    bq = P.c("bqkv")
    tiles = [dict(do=do, tc=tc) for do in range(8) for tc in range(8)]
    u = P.uid()
    _smark = P.sem_mark()
    pes = contextlib.ExitStack()
    ob = [P.sbuf(f"qp{u}_{i}", [128, NQ], BF16, es=pes) for i in range(2)]
    stc = Slots(P, f"qpst{u}", 2)

    def lhs(tile, k):
        return wqkv_q[:, k * D + tile["do"] * 128: k * D + tile["do"] * 128 + 128]

    def rhs(tile, k):
        return xnF[:, k * S + tile["tc"] * NQ: k * S + (tile["tc"] + 1) * NQ]

    def evac(v, ti, tile, psum):
        if ti >= 2:
            v.wait_ge(stc.c(ti).h, 16 * ((ti - 2) // 2 + 1))
        return v.tensor_scalar(ob[ti % 2][:, :], psum[:, :],
                               bq[:, tile["do"]:tile["do"] + 1], None, ALU.add)

    def store(gp, ti, tile):
        stc.c(ti).inc(gp.dma_start(
            out=sc_q[tile["do"] * 128:(tile["do"] + 1) * 128, tile["tc"] * NQ:(tile["tc"] + 1) * NQ],
            in_=ob[ti % 2][:, :]), 16)

    proj_phase(P, "qproj", tiles, 8, lhs, rhs, "vector", evac, store_fn=store)
    with P.block() as blk:
        @blk.gpsimd
        def _(gp):
            gp.wait_ge(stc.c(0).h, stc.c(0).n)
            gp.wait_ge(stc.c(1).h, stc.c(1).n)
    P.barrier()
    P.sem_release(_smark)
    pes.close()


def emit_tiny_scores(P, xnF, x_full, wqkv_kv, vrow, sc_q, sc_vall, scores):
    sel = P.c("sel")
    bq = P.c("bqkv")
    ones1 = P.c("ones_1x128_bf")
    for bp in range(NB):
        u = P.uid()
        _smark = P.sem_mark()
        pes = contextlib.ExitStack()
        xnbF = P.sbuf(f"tsx{u}", [128, 8 * S], BF16, es=pes)
        emit_ln0(P, lambda t: x_full[bp * S + t * 128: bp * S + (t + 1) * 128, :],
                 xnT=None, xnF=xnbF)
        kb = [P.sbuf(f"tsk{u}_{i}", [128, NQ], BF16, es=pes) for i in range(2)]
        qb = [P.sbuf(f"tsq{u}_{i}", [128, NQ], BF16, es=pes) for i in range(2)]
        eb = P.sbuf(f"tse{u}", [128, 8 * NQ], BF16, es=pes)
        vb = [P.sbuf(f"tsv{u}_{i}", [128, NQ], BF16, es=pes) for i in range(2)]
        psk = [P.psum(f"tspk{u}_{i}", [128, NQ], F32, es=pes) for i in range(3)]
        pss = [P.psum(f"tsps{u}_{i}", [128, 4], F32, es=pes) for i in range(2)]
        psv = [P.psum(f"tspv{u}_{i}", [128, NQ], F32, es=pes) for i in range(2)]
        qld = Slots(P, f"tsqld{u}", 2)
        vst = Slots(P, f"tsvs{u}", 2)
        c_kmm = Ctr(P, f"tskm{u}")
        c_kev = Ctr(P, f"tske{u}")
        c_e = Ctr(P, f"tsee{u}")
        c_smm = Ctr(P, f"tssm{u}")
        c_sev = Ctr(P, f"tssv{u}")
        c_vmm = Ctr(P, f"tsvm{u}")
        c_vev = Ctr(P, f"tsvv{u}")

        with P.block() as blk:
            @blk.sync
            def _(sy):
                for tc in range(8):
                    for j in range(NDC):
                        n = tc * 8 + j
                        if n >= 2:
                            sy.wait_ge(c_e.h, n - 1)
                        qld.c(n).inc(sy.dma_start(
                            out=qb[n % 2][:, :],
                            in_=sc_q[j * 128:(j + 1) * 128, tc * NQ:(tc + 1) * NQ]), 16)

            @blk.tensor
            def _(te):
                for tc in range(8):
                    for j in range(NDC):
                        n = tc * 8 + j
                        if n >= 3:
                            te.wait_ge(c_kev.h, n - 2)
                        for k in range(NDC):
                            te.matmul(psk[n % 3][:, :],
                                      wqkv_kv[:, k * 2048 + j * 128: k * 2048 + j * 128 + 128],
                                      xnbF[:, k * S + tc * NQ: k * S + (tc + 1) * NQ],
                                      start=(k == 0), stop=False)
                        c_kmm.inc(te.matmul(psk[n % 3][:, :],
                                            vrow[0:1, j * 128:(j + 1) * 128],
                                            P.c("ones_1x512_bf")[0:1, :],
                                            start=False, stop=True))
                    for sc_ in range(4):
                        g = tc * 4 + sc_
                        if g >= 2:
                            te.wait_ge(c_sev.h, g - 1)
                        te.wait_ge(c_e.h, tc * 8 + 8)
                        for j in range(NDC):
                            r = te.matmul(pss[g % 2][:, :],
                                          eb[:, j * NQ + sc_ * 128: j * NQ + sc_ * 128 + 128],
                                          sel[:, j * 4:(j + 1) * 4],
                                          start=(j == 0), stop=(j == NDC - 1))
                        c_smm.inc(r)
                    for pt in range(4):
                        for dh in range(2):
                            m = tc * 8 + pt * 2 + dh
                            if m >= 2:
                                te.wait_ge(c_vev.h, m - 1)
                            for k in range(NDC):
                                te.matmul(psv[m % 2][:, :],
                                          xnbF[:, k * S + tc * NQ + pt * 128: k * S + tc * NQ + pt * 128 + 128],
                                          wqkv_kv[:, k * 2048 + 1024 + dh * NQ: k * 2048 + 1024 + (dh + 1) * NQ],
                                          start=(k == 0), stop=False)
                            c_vmm.inc(te.matmul(psv[m % 2][:, :], ones1[:, :],
                                                vrow[0:1, 1024 + dh * NQ: 1024 + (dh + 1) * NQ],
                                                start=False, stop=True))

            @blk.scalar
            def _(sc):
                for tc in range(8):
                    for j in range(NDC):
                        n = tc * 8 + j
                        sc.wait_ge(c_kmm.h, n + 1)
                        if n >= 2:
                            sc.wait_ge(c_e.h, n - 1)
                        c_kev.inc(sc.copy(kb[n % 2][:, :], psk[n % 3][:, :]))
                    for pt in range(4):
                        for dh in range(2):
                            m = tc * 8 + pt * 2 + dh
                            sc.wait_ge(c_vmm.h, m + 1)
                            if m >= 2:
                                sc.wait_ge(vst.c(m).h, 16 * ((m - 2) // 2 + 1))
                            c_vev.inc(sc.copy(vb[m % 2][:, :], psv[m % 2][:, :]))
                            sc.wait_ge(c_vev.h, m + 1)
                            vst.c(m).inc(sc.dma_start(
                                out=sc_vall[bp * S + tc * NQ + pt * 128: bp * S + tc * NQ + pt * 128 + 128,
                                            dh * NQ:(dh + 1) * NQ],
                                in_=vb[m % 2][:, :]), 16)

            @blk.vector
            def _(v):
                for tc in range(8):
                    for j in range(NDC):
                        n = tc * 8 + j
                        v.wait_ge(c_kev.h, n + 1)
                        v.wait_ge(qld.c(n).h, 16 * (n // 2 + 1))
                        if tc >= 1 and j == 0:
                            v.wait_ge(c_smm.h, tc * 4)
                        c_e.inc(v.tensor_mul(eb[:, j * NQ:(j + 1) * NQ],
                                             qb[n % 2][:, :], kb[n % 2][:, :]))
                    for sc_ in range(4):
                        g = tc * 4 + sc_
                        v.wait_ge(c_smm.h, g + 1)
                        sg = g
                        dst = scores.rearrange("p (s h b) -> p s h b", h=4, b=8)[:, sg, :, bp:bp + 1]
                        srcp = pss[g % 2].rearrange("p (h o) -> p h o", o=1)
                        c_sev.inc(v.tensor_copy(dst, srcp))

        with P.block() as blk:
            @blk.scalar
            def _(sc):
                sc.wait_ge(vst.c(0).h, vst.c(0).n)
                sc.wait_ge(vst.c(1).h, vst.c(1).n)
        P.barrier()
        P.sem_release(_smark)
        pes.close()


def emit_tiny_softmax_pv(P, scores, sc_vall, attoutF):
    u = P.uid()
    _smark = P.sem_mark()
    pes = contextlib.ExitStack()
    att = P.sbuf(f"tpa{u}", [128, 32 * 32], F32, es=pes)
    mx = P.sbuf(f"tpm{u}", [128, 4], F32, es=pes)
    ex = [P.sbuf(f"tpe{u}_{i}", [128, 32], F32, es=pes) for i in range(2)]
    exe = [P.sbuf(f"tpx{u}_{i}", [128, 32], F32, es=pes) for i in range(2)]
    sm = P.sbuf(f"tpsm{u}", [128, 4], F32, es=pes)
    rc = P.sbuf(f"tprc{u}", [128, 4], F32, es=pes)
    vtl = [P.sbuf(f"tpv{u}_{i}", [128, D], BF16, es=pes) for i in range(2)]
    acc = [P.sbuf(f"tpc{u}_{i}", [128, D], BF16, es=pes) for i in range(2)]
    ptp = [P.psum(f"tpp{u}_{i}", [128, 128], BF16, es=pes) for i in range(2)]
    vld = Slots(P, f"tpl{u}", 2)
    c_sub = Ctr(P, f"tpsb{u}")
    c_exp = Ctr(P, f"tpxp{u}")
    c_att = Ctr(P, f"tpat{u}")
    c_vdn = Ctr(P, f"tpvd{u}")
    c_pvd = Ctr(P, f"tppd{u}")
    c_tp = Ctr(P, f"tptp{u}")
    c_ev = Ctr(P, f"tpev{u}")
    id_bf = P.c("id_bf")

    vc1 = Ctr(P, f"tpvc{u}")
    with P.block() as blk:
        @blk.vector
        def _(v):
            def tail(sg):
                v.wait_ge(c_exp.h, sg + 1)
                e3 = exe[sg % 2].rearrange("p (h b) -> p h b", b=8)
                v.wait_ge(vc1.h, vc1.n)
                vc1.inc(v.reduce_sum(sm[:, :], e3, axis=mybir.AxisListType.X))
                v.wait_ge(vc1.h, vc1.n)
                vc1.inc(v.reciprocal(rc[:, :], sm[:, :]))
                v.wait_ge(vc1.h, vc1.n)
                rcb = bass.AP(rc, 0, [[4, 128], [1, 4], [0, 8]])
                vc1.inc(v.tensor_tensor(att[:, sg * 32:(sg + 1) * 32].rearrange("p (h b) -> p h b", b=8),
                                        e3, rcb, ALU.mult))
            for sg in range(32):
                b3 = scores[:, sg * 32:(sg + 1) * 32].rearrange("p (h b) -> p h b", b=8)
                v.wait_ge(vc1.h, vc1.n)
                vc1.inc(v.reduce_max(mx[:, :], b3, axis=mybir.AxisListType.X))
                v.wait_ge(vc1.h, vc1.n)
                vc1.inc(v.tensor_tensor(ex[sg % 2].rearrange("p (h b) -> p h b", b=8), b3,
                                        bass.AP(mx, 0, [[4, 128], [1, 4], [0, 8]]), ALU.subtract))
                if sg >= 1:
                    tail(sg - 1)
            tail(31)

        @blk.scalar
        def _(sc):
            for sg in range(32):
                cnt = 5 * sg + 2
                sc.wait_ge(vc1.h, cnt)
                c_exp.inc(sc.activation(exe[sg % 2][:, :], ex[sg % 2][:, :], AF.Exp))

    P.barrier()
    with P.block() as blk:
        vc3 = Ctr(P, f"tpvc3{u}")

        @blk.sync
        def _(sy):
            for sg in range(32):
                for bp in range(NB):
                    n = sg * 8 + bp
                    if n >= 2:
                        sy.wait_ge(vc3.h, (n - 1) * 4)
                    vld.c(n).inc(sy.dma_start(
                        out=vtl[n % 2][:, :],
                        in_=sc_vall[bp * S + sg * 128: bp * S + (sg + 1) * 128, :]), 16)

        @blk.vector
        def _(v):
            for sg in range(32):
                for bp in range(NB):
                    n = sg * 8 + bp
                    v.wait_ge(vld.c(n).h, 16 * (n // 2 + 1))
                    if bp == 1 and sg >= 1:
                        v.wait_ge(c_tp.h, 8 * sg)
                    for h in range(4):
                        cidx = sg * 32 + h * 8 + bp
                        if vc3.n >= 4:
                            v.wait_ge(vc3.h, vc3.n - 3)
                        if bp == 0:
                            r = v.tensor_scalar(acc[0][:, 256 * h:256 * (h + 1)],
                                                vtl[n % 2][:, 256 * h:256 * (h + 1)],
                                                att[:, cidx:cidx + 1], None, ALU.mult)
                        else:
                            r = v.scalar_tensor_tensor(acc[bp % 2][:, 256 * h:256 * (h + 1)],
                                                       vtl[n % 2][:, 256 * h:256 * (h + 1)],
                                                       att[:, cidx:cidx + 1],
                                                       acc[(bp + 1) % 2][:, 256 * h:256 * (h + 1)],
                                                       ALU.mult, ALU.add)
                        vc3.inc(r)

        @blk.tensor
        def _(te):
            for sg in range(32):
                te.wait_ge(vc3.h, (sg + 1) * 32)
                for j in range(NDC):
                    n = sg * 8 + j
                    if n >= 2:
                        te.wait_ge(c_ev.h, n - 1)
                    c_tp.inc(te.transpose(ptp[n % 2][:, :], acc[1][:, j * 128:(j + 1) * 128], id_bf[:, :]))

        @blk.scalar
        def _(sc):
            for sg in range(32):
                for j in range(NDC):
                    n = sg * 8 + j
                    sc.wait_ge(c_tp.h, n + 1)
                    c_ev.inc(sc.copy(attoutF[:, j * S + sg * 128: j * S + (sg + 1) * 128], ptp[n % 2][:, :]))
    P.barrier()
    P.sem_release(_smark)
    pes.close()


def emit_wo_s(P, attoutF, xnF, woT_sb, sc_s):
    bo = P.c("bo")
    u = P.uid()
    _smark = P.sem_mark()
    pes = contextlib.ExitStack()
    ob = [P.sbuf(f"wo{u}_{i}", [128, NQ], BF16, es=pes) for i in range(2)]
    stc = Slots(P, f"wos{u}", 2)
    tiles = [dict(do=do, tc=tc) for tc in range(8) for do in range(8)]

    def lhs(tile, k):
        return woT_sb[:, k * D + tile["do"] * 128: k * D + tile["do"] * 128 + 128]

    def rhs(tile, k):
        return attoutF[:, k * S + tile["tc"] * NQ: k * S + (tile["tc"] + 1) * NQ]

    def evac(v, ti, tile, psum):
        if ti >= 2:
            v.wait_ge(stc.c(ti).h, 16 * ((ti - 2) // 2 + 1))
        return v.scalar_tensor_tensor(
            ob[ti % 2][:, :], psum[:, :], bo[:, tile["do"]:tile["do"] + 1],
            xnF[:, tile["do"] * S + tile["tc"] * NQ: tile["do"] * S + (tile["tc"] + 1) * NQ],
            ALU.add, ALU.add)

    def store(gp, ti, tile):
        stc.c(ti).inc(gp.dma_start(
            out=sc_s[tile["do"] * 128:(tile["do"] + 1) * 128, tile["tc"] * NQ:(tile["tc"] + 1) * NQ],
            in_=ob[ti % 2][:, :]), 16)

    proj_phase(P, "wos", tiles, 8, lhs, rhs, "vector", evac, store_fn=store)
    with P.block() as blk:
        @blk.gpsimd
        def _(gp):
            gp.wait_ge(stc.c(0).h, stc.c(0).n)
            gp.wait_ge(stc.c(1).h, stc.c(1).n)
    P.barrier()
    P.sem_release(_smark)
    pes.close()


def emit_fln(P, src_dram, gname, bname, dst_dram, dst32=None):
    g_ = P.c(gname)
    be_ = P.c(bname)
    ones128 = P.c("ones_128x1_bf")
    ones1f = P.c("ones_1x128_f32")
    u = P.uid()
    _smark = P.sem_mark()
    pes = contextlib.ExitStack()
    sb = [P.sbuf(f"fl{u}_{i}", [128, NQ], BF16, es=pes) for i in range(3)]
    sb2 = [P.sbuf(f"flb{u}_{i}", [128, NQ], BF16, es=pes) for i in range(2)]
    sqb = [P.sbuf(f"flq{u}_{i}", [128, NQ], BF16, es=pes) for i in range(2)]
    murow = P.sbuf(f"flmu{u}", [1, S], F32, es=pes)
    rsrow = P.sbuf(f"flrs{u}", [1, S], F32, es=pes)
    tmpr = P.sbuf(f"fltm{u}", [1, NQ], F32, es=pes)
    ob = [P.sbuf(f"flo{u}_{i}", [128, NQ], BF16, es=pes) for i in range(2)]
    ob32 = [P.sbuf(f"flo32{u}_{i}", [128, NQ], F32, es=pes) for i in range(2)] if dst32 is not None else None
    t1b = [P.sbuf(f"flt{u}_{i}", [128, NQ], F32, es=pes) for i in range(2)]
    ps1 = P.psum(f"flp1{u}", [1, NQ], F32, es=pes)
    ps2 = P.psum(f"flp2{u}", [1, NQ], F32, es=pes)
    psb = [P.psum(f"flpb{u}_{i}", [128, NQ], F32, es=pes) for i in range(4)]
    ld = Slots(P, f"flld{u}", 3)
    ld2 = Slots(P, f"flld2{u}", 2)
    stc = Slots(P, f"flst2{u}", 2)
    stc32 = Slots(P, f"flst3{u}", 2) if dst32 is not None else None
    c_sq = Ctr(P, f"flsq{u}")
    c_s1m = Ctr(P, f"fls1{u}")
    c_sqm = Ctr(P, f"flsqm{u}")
    c_srt = Ctr(P, f"flsr{u}")
    c_bc = Ctr(P, f"flbc{u}")
    vcs = Ctr(P, f"flvc{u}")

    NOPS = 4 if dst32 is not None else 3

    def vcs_smalls_done(tc):
        return 6 * (tc + 1) + 8 * NOPS * tc

    def vcs_norm_done(tc, j):
        return 6 * (tc + 1) + 8 * NOPS * tc + NOPS * (j + 1)

    with P.block() as blk:
        @blk.sync
        def _(sy):
            def load2(tc):
                for j in range(NDC):
                    n = tc * 8 + j
                    if n >= 2:
                        sy.wait_ge(vcs.h, vcs_norm_done((n - 2) // 8, (n - 2) % 8))
                    ld2.c(n).inc(sy.dma_start(
                        out=sb2[n % 2][:, :],
                        in_=src_dram[j * 128:(j + 1) * 128, tc * NQ:(tc + 1) * NQ]), 16)
            for tc in range(8):
                for j in range(NDC):
                    n = tc * 8 + j
                    if n >= 3:
                        sy.wait_ge(c_sqm.h, n - 2)
                    ld.c(n).inc(sy.dma_start(
                        out=sb[n % 3][:, :],
                        in_=src_dram[j * 128:(j + 1) * 128, tc * NQ:(tc + 1) * NQ]), 16)
                if tc >= 1:
                    load2(tc - 1)
            load2(7)

        @blk.scalar
        def _(sc):
            for tc in range(8):
                for j in range(NDC):
                    n = tc * 8 + j
                    sc.wait_ge(ld.c(n).h, 16 * (n // 3 + 1))
                    if n >= 2:
                        sc.wait_ge(c_sqm.h, n - 1)
                    c_sq.inc(sc.activation(sqb[n % 2][:, :], sb[n % 3][:, :], AF.Square))
                sc.wait_ge(vcs.h, vcs_smalls_done(tc) - 1)
                c_srt.inc(sc.activation(tmpr[0:1, :], rsrow[0:1, tc * NQ:(tc + 1) * NQ], AF.Sqrt))

        @blk.tensor
        def _(te):
            for tc in range(8):
                if tc >= 1:
                    te.wait_ge(vcs.h, vcs_smalls_done(tc - 1) - 4)
                for j in range(NDC):
                    n = tc * 8 + j
                    te.wait_ge(ld.c(n).h, 16 * (n // 3 + 1))
                    c_s1m.inc(te.matmul(ps1[0:1, :], ones128[:, :], sb[n % 3][:, :],
                                        start=(j == 0), stop=(j == NDC - 1)))
                    te.wait_ge(c_sq.h, n + 1)
                    c_sqm.inc(te.matmul(ps2[0:1, :], ones128[:, :], sqb[n % 2][:, :],
                                        start=(j == 0), stop=(j == NDC - 1)))
                te.wait_ge(vcs.h, vcs_smalls_done(tc))
                if tc >= 2:
                    te.wait_ge(vcs.h, vcs_norm_done(tc - 2, 7))
                te.matmul(psb[2 * (tc % 2)][:, :], ones1f[:, :], murow[0:1, tc * NQ:(tc + 1) * NQ],
                          start=True, stop=True)
                c_bc.inc(te.matmul(psb[2 * (tc % 2) + 1][:, :], ones1f[:, :], rsrow[0:1, tc * NQ:(tc + 1) * NQ],
                                   start=True, stop=True), 2)

        @blk.vector
        def _(v):
            def normalize(tc):
                v.wait_ge(c_bc.h, 2 * (tc + 1))
                for j in range(NDC):
                    n = tc * 8 + j
                    v.wait_ge(ld2.c(n).h, 16 * (n // 2 + 1))
                    v.wait_ge(vcs.h, vcs.n)
                    vcs.inc(v.tensor_sub(t1b[n % 2][:, :], sb2[n % 2][:, :], psb[2 * (tc % 2)][:, :]))
                    v.wait_ge(vcs.h, vcs.n)
                    vcs.inc(v.tensor_mul(t1b[n % 2][:, :], t1b[n % 2][:, :], psb[2 * (tc % 2) + 1][:, :]))
                    v.wait_ge(vcs.h, vcs.n)
                    if n >= 2:
                        v.wait_ge(stc.c(n).h, 16 * ((n - 2) // 2 + 1))
                    vcs.inc(v.tensor_scalar(ob[n % 2][:, :], t1b[n % 2][:, :],
                                            g_[:, j:j + 1], be_[:, j:j + 1], ALU.mult, ALU.add))
                    if dst32 is not None:
                        v.wait_ge(vcs.h, vcs.n)
                        if n >= 2:
                            v.wait_ge(stc32.c(n).h, 16 * ((n - 2) // 2 + 1))
                        vcs.inc(v.tensor_scalar(ob32[n % 2][:, :], t1b[n % 2][:, :],
                                                g_[:, j:j + 1], be_[:, j:j + 1], ALU.mult, ALU.add))
            for tc in range(8):
                v.wait_ge(c_s1m.h, tc * 8 + 8)
                v.wait_ge(c_sqm.h, tc * 8 + 8)
                v.wait_ge(vcs.h, vcs.n)
                vcs.inc(v.tensor_scalar(murow[0:1, tc * NQ:(tc + 1) * NQ], ps1[0:1, :], 1.0 / D, None, ALU.mult))
                vcs.inc(v.tensor_scalar(rsrow[0:1, tc * NQ:(tc + 1) * NQ], ps2[0:1, :], 1.0 / D, None, ALU.mult))
                v.wait_ge(vcs.h, vcs.n)
                vcs.inc(v.tensor_tensor(tmpr[0:1, :], murow[0:1, tc * NQ:(tc + 1) * NQ],
                                        murow[0:1, tc * NQ:(tc + 1) * NQ], ALU.mult))
                v.wait_ge(vcs.h, vcs.n)
                vcs.inc(v.tensor_sub(rsrow[0:1, tc * NQ:(tc + 1) * NQ], rsrow[0:1, tc * NQ:(tc + 1) * NQ], tmpr[0:1, :]))
                v.wait_ge(vcs.h, vcs.n)
                vcs.inc(v.tensor_scalar(rsrow[0:1, tc * NQ:(tc + 1) * NQ], rsrow[0:1, tc * NQ:(tc + 1) * NQ], EPS, None, ALU.add))
                v.wait_ge(c_srt.h, tc + 1)
                vcs.inc(v.reciprocal(rsrow[0:1, tc * NQ:(tc + 1) * NQ], tmpr[0:1, :]))
                assert vcs.n == vcs_smalls_done(tc), (vcs.n, tc)
                normalize(tc)
                assert vcs.n == vcs_norm_done(tc, 7)

        @blk.gpsimd
        def _(gp):
            for tc in range(8):
                for j in range(NDC):
                    n = tc * 8 + j
                    gp.wait_ge(vcs.h, vcs_norm_done(tc, j))
                    stc.c(n).inc(gp.dma_start(
                        out=dst_dram[j * 128:(j + 1) * 128, tc * NQ:(tc + 1) * NQ],
                        in_=ob[n % 2][:, :]), 16)
                    if dst32 is not None:
                        stc32.c(n).inc(gp.dma_start(
                            out=dst32[j * 128:(j + 1) * 128, tc * NQ:(tc + 1) * NQ],
                            in_=ob32[n % 2][:, :]), 16)
            gp.wait_ge(stc.c(0).h, stc.c(0).n)
            gp.wait_ge(stc.c(1).h, stc.c(1).n)
            if dst32 is not None:
                gp.wait_ge(stc32.c(0).h, stc32.c(0).n)
                gp.wait_ge(stc32.c(1).h, stc32.c(1).n)
    P.barrier()
    P.sem_release(_smark)
    pes.close()


def emit_load_wslab(P, dst_sb, srcs, tag):
    u = P.uid()
    _smark = P.sem_mark()
    c = Ctr(P, f"wl{u}")
    with P.block() as blk:
        @blk.gpsimd
        def _(gp):
            for col0, ap in srcs:
                ncols = ap.shape[-1]
                c.inc(gp.dma_start(out=dst_sb[:, col0:col0 + ncols], in_=ap), 16)
            gp.wait_ge(c.h, c.n)
    P.barrier()
    P.sem_release(_smark)


def emit_moe_masks(P, sc_y, wg_sb, mrow):
    u = P.uid()
    _smark = P.sem_mark()
    pes = contextlib.ExitStack()
    yb = [P.sbuf(f"mk{u}_{i}", [128, 8 * NQ], F32, es=pes) for i in range(2)]
    lgT = [P.sbuf(f"mkl{u}_{i}", [128, 8], F32, es=pes) for i in range(2)]
    mxc = P.sbuf(f"mkm{u}", [128, 1], F32, es=pes)
    mskT = P.sbuf(f"mks{u}", [128, 8], BF16, es=pes)
    maskF = P.sbuf(f"mkf{u}", [8, S], BF16, es=pes)
    pslg = [P.psum(f"mkp{u}_{i}", [128, 8], F32, es=pes) for i in range(2)]
    pstp = P.psum(f"mkt{u}", [8, 128], BF16, es=pes)
    ld = Slots(P, f"mkld{u}", 2)
    c_mm = Ctr(P, f"mkmm{u}")
    c_ev = Ctr(P, f"mkev{u}")
    c_tp = Ctr(P, f"mktp{u}")
    c_ae = Ctr(P, f"mkae{u}")
    id_bf = P.c("id_bf")

    with P.block() as blk:
        @blk.sync
        def _(sy):
            for tc in range(8):
                if tc >= 2:
                    sy.wait_ge(c_mm.h, (tc - 1) * 4)
                ld.c(tc).inc(sy.dma_start(
                    out=yb[tc % 2][:, :].rearrange("p (a c) -> p a c", a=8),
                    in_=sc_y[:, tc * NQ:(tc + 1) * NQ].rearrange("(a p) c -> p a c", p=128)), 16)

        @blk.tensor
        def _(te):
            def tpose(pt):
                te.wait_ge(c_ev.h, 3 * (pt + 1))
                if pt >= 1:
                    te.wait_ge(c_ae.h, pt)
                c_tp.inc(te.transpose(pstp[:, :], mskT[:, :], id_bf[:, :]))
            for tc in range(8):
                te.wait_ge(ld.c(tc).h, 16 * (tc // 2 + 1))
                for ps_ in range(4):
                    pt = tc * 4 + ps_
                    if pt >= 2:
                        te.wait_ge(c_ev.h, 3 * (pt - 1))
                    for j in range(NDC):
                        r = te.matmul(pslg[pt % 2][:, :],
                                      yb[tc % 2][:, j * NQ + ps_ * 128: j * NQ + ps_ * 128 + 128],
                                      wg_sb[:, j * 8:(j + 1) * 8],
                                      start=(j == 0), stop=(j == NDC - 1))
                    c_mm.inc(r)
                    if pt >= 1:
                        tpose(pt - 1)
            tpose(31)

        @blk.vector
        def _(v):
            for pt in range(32):
                v.wait_ge(c_mm.h, pt + 1)
                if pt >= 1:
                    v.wait_ge(c_tp.h, pt)
                v.wait_ge(c_ev.h, c_ev.n)
                c_ev.inc(v.tensor_copy(lgT[pt % 2][:, :], pslg[pt % 2][:, :]))
                v.wait_ge(c_ev.h, c_ev.n)
                c_ev.inc(v.reduce_max(mxc[:, :], lgT[pt % 2][:, :], axis=mybir.AxisListType.X))
                v.wait_ge(c_ev.h, c_ev.n)
                c_ev.inc(v.tensor_scalar(mskT[:, :], lgT[pt % 2][:, :], mxc[:, :], None, ALU.is_ge))

        @blk.scalar
        def _(sc):
            for pt in range(32):
                sc.wait_ge(c_tp.h, pt + 1)
                c_ae.inc(sc.copy(maskF[:, pt * 128:(pt + 1) * 128], pstp[:, :]))

        @blk.gpsimd
        def _(gp):
            gp.wait_ge(c_ae.h, 32)
            cst = Ctr(P, f"mkst{u}")
            cst.inc(gp.dma_start(out=mrow[:, :], in_=maskF[:, :]), 16)
            gp.wait_ge(cst.h, cst.n)
    P.barrier()
    P.sem_release(_smark)
    pes.close()


def emit_moe_experts(P, sc_y, mrow, sc_moe):
    W1 = P.t["W1"]
    W2 = P.t["W2"]
    b1 = P.c("b1")
    b2 = P.c("b2")
    ones1 = P.c("ones_1x128_bf")
    u0 = P.uid()
    pes0 = contextlib.ExitStack()
    w1sb = P.sbuf(f"mew1{u0}", [128, 8 * 4096], BF16, es=pes0)
    w2sb = P.sbuf(f"mew2{u0}", [128, 32 * 1024], BF16, es=pes0)
    for e in range(NB):
        u = P.uid()
        _smark = P.sem_mark()
        pes = contextlib.ExitStack()
        yb = [P.sbuf(f"mey{u}_{i}", [128, 8 * NQ], BF16, es=pes) for i in range(2)]
        mrow_sb = P.sbuf(f"memr{u}", [1, S], BF16, es=pes)
        hsb = P.sbuf(f"meh{u}", [128, 32 * NQ], BF16, es=pes)
        mbb = P.sbuf(f"memb{u}", [128, 8 * NQ], BF16, es=pes)
        oeb = [P.sbuf(f"meo{u}_{i}", [128, NQ], BF16, es=pes) for i in range(2)]
        psh = [P.psum(f"meph{u}_{i}", [128, NQ], F32, es=pes) for i in range(3)]
        pso = [P.psum(f"mepo{u}_{i}", [128, NQ], F32, es=pes) for i in range(4)]
        psm = P.psum(f"mepm{u}", [128, NQ], F32, es=pes)
        wld = Ctr(P, f"mewl{u}")
        yld = Slots(P, f"meyl{u}", 2)
        ost = Slots(P, f"meos{u}", 2)
        c_mbm = Ctr(P, f"memb_m{u}")
        c_mbe = Ctr(P, f"memb_e{u}")
        c_hm = Ctr(P, f"mehm{u}")
        c_he = Ctr(P, f"mehe{u}")
        c_om = Ctr(P, f"meom{u}")
        c_oe = Ctr(P, f"meoe{u}")

        with P.block() as blk:
            @blk.gpsimd
            def _(gp):
                wld.inc(gp.dma_start(out=mrow_sb[0:1, :], in_=mrow[e:e + 1, :]), 16)
                for k in range(NDC):
                    wld.inc(gp.dma_start(out=w1sb[:, k * 4096:(k + 1) * 4096],
                                         in_=W1[e, k * 128:(k + 1) * 128, :]), 16)
                for dfi in range(32):
                    wld.inc(gp.dma_start(out=w2sb[:, dfi * 1024:(dfi + 1) * 1024],
                                         in_=W2[e, dfi * 128:(dfi + 1) * 128, :]), 16)
                for tc in range(8):
                    for do in range(8):
                        n = tc * 8 + do
                        gp.wait_ge(c_oe.h, n + 1)
                        ost.c(n).inc(gp.dma_start(
                            out=sc_moe[do * 128:(do + 1) * 128, tc * NQ:(tc + 1) * NQ],
                            in_=oeb[n % 2][:, :],
                            accum_op=(ALU.bypass if e == 0 else ALU.add)), 16)
                gp.wait_ge(ost.c(0).h, ost.c(0).n)
                gp.wait_ge(ost.c(1).h, ost.c(1).n)

            @blk.sync
            def _(sy):
                for tc in range(8):
                    if tc >= 2:
                        sy.wait_ge(c_hm.h, 32 * (tc - 1))
                    yld.c(tc).inc(sy.dma_start(
                        out=yb[tc % 2][:, :].rearrange("p (a c) -> p a c", a=8),
                        in_=sc_y[:, tc * NQ:(tc + 1) * NQ].rearrange("(a p) c -> p a c", p=128)), 16)

            @blk.tensor
            def _(te):
                te.wait_ge(wld.h, wld.n)
                for tc in range(8):
                    if tc >= 1:
                        te.wait_ge(c_mbe.h, tc)
                    c_mbm.inc(te.matmul(psm[:, :], ones1[:, :], mrow_sb[0:1, tc * NQ:(tc + 1) * NQ],
                                        start=True, stop=True))
                for tc in range(8):
                    te.wait_ge(yld.c(tc).h, 16 * (tc // 2 + 1))
                    for df in range(32):
                        n3 = tc * 32 + df
                        if n3 >= 3:
                            te.wait_ge(c_he.h, n3 - 2)
                        for k in range(NDC):
                            r = te.matmul(psh[n3 % 3][:, :],
                                          wsb[:, k * 4096 + df * 128: k * 4096 + df * 128 + 128],
                                          yb[tc % 2][:, k * NQ:(k + 1) * NQ],
                                          start=(k == 0), stop=(k == NDC - 1))
                        c_hm.inc(r)
                    te.wait_ge(c_he.h, 32 * (tc + 1))
                    for dog in range(2):
                        for do4 in range(4):
                            do = dog * 4 + do4
                            n = tc * 8 + do
                            if n >= 4:
                                te.wait_ge(c_oe.h, n - 3)
                            for df in range(32):
                                r = te.matmul(pso[do4][:, :],
                                              w2sb[:, df * 1024 + do * 128: df * 1024 + do * 128 + 128],
                                              hsb[:, df * NQ:(df + 1) * NQ],
                                              start=(df == 0), stop=(df == 31))
                            c_om.inc(r)

            @blk.scalar
            def _(sc):
                for tc in range(8):
                    sc.wait_ge(c_mbm.h, tc + 1)
                    c_mbe.inc(sc.copy(mbb[:, tc * NQ:(tc + 1) * NQ], psm[:, :]))
                for tc in range(8):
                    for df in range(32):
                        n3 = tc * 32 + df
                        sc.wait_ge(c_hm.h, n3 + 1)
                        if df == 0 and tc >= 1:
                            sc.wait_ge(c_om.h, 8 * tc)
                        c_he.inc(sc.activation(hsb[:, df * NQ:(df + 1) * NQ], psh[n3 % 3][:, :],
                                               AF.Relu, bias=b1[:, e * 32 + df: e * 32 + df + 1]))

            @blk.vector
            def _(v):
                for tc in range(8):
                    for do in range(8):
                        n = tc * 8 + do
                        v.wait_ge(c_om.h, n + 1)
                        v.wait_ge(c_mbe.h, min(tc + 1, 8))
                        if n >= 2:
                            v.wait_ge(ost.c(n).h, 16 * ((n - 2) // 2 + 1))
                        v.wait_ge(c_oe.h, c_oe.n)
                        c_oe.inc(v.scalar_tensor_tensor(oeb[n % 2][:, :], pso[do % 4][:, :],
                                                        b2[:, e * 8 + do: e * 8 + do + 1],
                                                        mbb[:, tc * NQ:(tc + 1) * NQ],
                                                        ALU.add, ALU.mult))
        P.barrier()
        P.sem_release(_smark)
        pes.close()
    pes0.close()


def emit_binop(P, a_dram, b_dram, dst_dram, op=None, out_dt=BF16):
    op = op if op is not None else ALU.add
    u = P.uid()
    _smark = P.sem_mark()
    pes = contextlib.ExitStack()
    ab = [P.sbuf(f"bo{u}a{i}", [128, NQ], BF16, es=pes) for i in range(2)]
    bb = [P.sbuf(f"bo{u}b{i}", [128, NQ], BF16, es=pes) for i in range(2)]
    obf = [P.sbuf(f"bo{u}o{i}", [128, NQ], out_dt, es=pes) for i in range(2)]
    lda = Slots(P, f"bo{u}la", 2)
    ldb = Slots(P, f"bo{u}lb", 2)
    stc = Slots(P, f"bo{u}st", 2)
    c_op = Ctr(P, f"bo{u}op")
    tiles = [(j, tc) for tc in range(8) for j in range(NDC)]
    with P.block() as blk:
        @blk.sync
        def _(sy):
            for n, (j, tc) in enumerate(tiles):
                if n >= 2:
                    sy.wait_ge(c_op.h, n - 1)
                lda.c(n).inc(sy.dma_start(out=ab[n % 2][:, :],
                                          in_=a_dram[j * 128:(j + 1) * 128, tc * NQ:(tc + 1) * NQ]), 16)
                ldb.c(n).inc(sy.dma_start(out=bb[n % 2][:, :],
                                          in_=b_dram[j * 128:(j + 1) * 128, tc * NQ:(tc + 1) * NQ]), 16)

        @blk.vector
        def _(v):
            for n, (j, tc) in enumerate(tiles):
                v.wait_ge(lda.c(n).h, 16 * (n // 2 + 1))
                v.wait_ge(ldb.c(n).h, 16 * (n // 2 + 1))
                if n >= 2:
                    v.wait_ge(stc.c(n).h, 16 * ((n - 2) // 2 + 1))
                c_op.inc(v.tensor_tensor(obf[n % 2][:, :], ab[n % 2][:, :], bb[n % 2][:, :], op))

        @blk.gpsimd
        def _(gp):
            for n, (j, tc) in enumerate(tiles):
                gp.wait_ge(c_op.h, n + 1)
                stc.c(n).inc(gp.dma_start(
                    out=dst_dram[j * 128:(j + 1) * 128, tc * NQ:(tc + 1) * NQ],
                    in_=obf[n % 2][:, :]), 16)
            gp.wait_ge(stc.c(0).h, stc.c(0).n)
            gp.wait_ge(stc.c(1).h, stc.c(1).n)
    P.barrier()
    P.sem_release(_smark)
    pes.close()


def emit_x1(P, sc_s3, x_dram, sc_x1T):
    u = P.uid()
    _smark = P.sem_mark()
    pes = contextlib.ExitStack()
    s3b = [P.sbuf(f"xo{u}s{i}", [128, 128], BF16, es=pes) for i in range(3)]
    xb = [P.sbuf(f"xo{u}x{i}", [128, D], F32, es=pes) for i in range(2)]
    s3t = [P.sbuf(f"xo{u}t{i}", [128, D], BF16, es=pes) for i in range(2)]
    ob = [P.sbuf(f"xo{u}o{i}", [128, D], F32, es=pes) for i in range(2)]
    ptp = [P.psum(f"xo{u}p{i}", [128, 128], BF16, es=pes) for i in range(2)]
    lds = Slots(P, f"xo{u}l", 3)
    ldx = Slots(P, f"xo{u}m", 2)
    stc = Slots(P, f"xo{u}st", 2)
    c_tp = Ctr(P, f"xo{u}tp")
    c_ae = Ctr(P, f"xo{u}ae")
    c_ad = Ctr(P, f"xo{u}ad")
    id_bf = P.c("id_bf")
    with P.block() as blk:
        @blk.sync
        def _(sy):
            for t in range(NPT):
                if t >= 2:
                    sy.wait_ge(c_ad.h, t - 1)
                ldx.c(t).inc(sy.dma_start(out=xb[t % 2][:, :],
                                          in_=x_dram[t * 128:(t + 1) * 128, :]), 16)
                for j in range(NDC):
                    n = t * 8 + j
                    if n >= 3:
                        sy.wait_ge(c_tp.h, n - 2)
                    lds.c(n).inc(sy.dma_start(
                        out=s3b[n % 3][:, :],
                        in_=sc_s3[j * 128:(j + 1) * 128, t * 128:(t + 1) * 128]), 16)

        @blk.tensor
        def _(te):
            for t in range(NPT):
                for j in range(NDC):
                    n = t * 8 + j
                    te.wait_ge(lds.c(n).h, 16 * (n // 3 + 1))
                    if n >= 2:
                        te.wait_ge(c_ae.h, n - 1)
                    c_tp.inc(te.transpose(ptp[n % 2][:, :], s3b[n % 3][:, :], id_bf[:, :]))

        @blk.scalar
        def _(sc):
            for t in range(NPT):
                for j in range(NDC):
                    n = t * 8 + j
                    sc.wait_ge(c_tp.h, n + 1)
                    if j == 0 and t >= 2:
                        sc.wait_ge(c_ad.h, t - 1)
                    c_ae.inc(sc.copy(s3t[t % 2][:, j * 128:(j + 1) * 128], ptp[n % 2][:, :]))

        @blk.vector
        def _(v):
            for t in range(NPT):
                v.wait_ge(c_ae.h, 8 * (t + 1))
                v.wait_ge(ldx.c(t).h, 16 * (t // 2 + 1))
                if t >= 2:
                    v.wait_ge(stc.c(t).h, 16 * ((t - 2) // 2 + 1))
                c_ad.inc(v.tensor_add(ob[t % 2][:, :], xb[t % 2][:, :], s3t[t % 2][:, :]))

        @blk.gpsimd
        def _(gp):
            for t in range(NPT):
                gp.wait_ge(c_ad.h, t + 1)
                stc.c(t).inc(gp.dma_start(out=sc_x1T[t * 128:(t + 1) * 128, :],
                                          in_=ob[t % 2][:, :]), 16)
            gp.wait_ge(stc.c(0).h, stc.c(0).n)
            gp.wait_ge(stc.c(1).h, stc.c(1).n)
    P.barrier()
    P.sem_release(_smark)
    pes.close()


def emit_sfactor_cols(P, scols):
    u = P.uid()
    _smark = P.sem_mark()
    pes = contextlib.ExitStack()
    psc = P.psum(f"sf{u}", [128, 3], F32, es=pes)
    c_m = Ctr(P, f"sfm{u}")
    c_v = Ctr(P, f"sfv{u}")
    with P.block() as blk:
        @blk.tensor
        def _(te):
            c_m.inc(te.matmul(psc[:, :], P.c("ones_1x128_f32")[0:1, :], P.c("s012")[0:1, :],
                              start=True, stop=True))

        @blk.vector
        def _(v):
            v.wait_ge(c_m.h, 1)
            c_v.inc(v.tensor_scalar(scols[:, :], psc[:, :], 2.0, None, ALU.mult))
    P.barrier()
    P.sem_release(_smark)
    pes.close()


def emit_ret_proj(P, xn1F, sc_q1, sc_k1, sc_v1):
    u0 = P.uid()
    pes0 = contextlib.ExitStack()
    tcol = P.sbuf(f"rpt{u0}", [128, 8], F32, es=pes0)
    wsl = P.sbuf(f"rpw{u0}", [128, 8 * 1024], BF16, es=pes0)
    _smark = P.sem_mark()
    c_t = Ctr(P, f"rptc{u0}")
    with P.block() as blk:
        @blk.scalar
        def _(sc):
            c_t.inc(sc.activation(tcol[:, :], P.c("theta")[:, :], AF.Exp, scale=0.0))
    P.barrier()
    P.sem_release(_smark)

    for wname, dst, theta_mult, tlay in (("Wq1", sc_q1, True, False),
                                         ("Wk1", sc_k1, True, False),
                                         ("Wv1", sc_v1, False, True)):
        W = P.t[wname]
        emit_load_wslab(P, wsl, [(k * 1024, W[k * 128:(k + 1) * 128, :]) for k in range(8)], wname)
        u = P.uid()
        _smark = P.sem_mark()
        pes = contextlib.ExitStack()
        ob = [P.sbuf(f"rp{u}_{i}", [128, NQ], BF16, es=pes) for i in range(2)]
        stc = Slots(P, f"rps{u}", 2)
        if not tlay:
            tiles = [dict(do=do, tc=tc) for tc in range(8) for do in range(8)]

            def lhs(tile, k):
                return wsl[:, k * 1024 + tile["do"] * 128: k * 1024 + tile["do"] * 128 + 128]

            def rhs(tile, k):
                return xn1F[:, k * S + tile["tc"] * NQ: k * S + (tile["tc"] + 1) * NQ]

            def evac(v, ti, tile, psum):
                if ti >= 2:
                    v.wait_ge(stc.c(ti).h, 16 * ((ti - 2) // 2 + 1))
                if theta_mult:
                    return v.tensor_scalar(ob[ti % 2][:, :], psum[:, :],
                                           tcol[:, tile["do"]:tile["do"] + 1], None, ALU.mult)
                return v.tensor_copy(ob[ti % 2][:, :], psum[:, :])

            def store(gp, ti, tile):
                stc.c(ti).inc(gp.dma_start(
                    out=dst[tile["do"] * 128:(tile["do"] + 1) * 128, tile["tc"] * NQ:(tile["tc"] + 1) * NQ],
                    in_=ob[ti % 2][:, :]), 16)
        else:
            tiles = [dict(pt=pt, dh=dh) for pt in range(NPT) for dh in range(2)]

            def lhs(tile, k):
                return xn1F[:, k * S + tile["pt"] * 128: k * S + (tile["pt"] + 1) * 128]

            def rhs(tile, k):
                return wsl[:, k * 1024 + tile["dh"] * NQ: k * 1024 + (tile["dh"] + 1) * NQ]

            def evac(v, ti, tile, psum):
                if ti >= 2:
                    v.wait_ge(stc.c(ti).h, 16 * ((ti - 2) // 2 + 1))
                return v.tensor_copy(ob[ti % 2][:, :], psum[:, :])

            def store(gp, ti, tile):
                stc.c(ti).inc(gp.dma_start(
                    out=dst[tile["pt"] * 128:(tile["pt"] + 1) * 128, tile["dh"] * NQ:(tile["dh"] + 1) * NQ],
                    in_=ob[ti % 2][:, :]), 16)

        proj_phase(P, "rp" + wname, tiles, 8, lhs, rhs, "vector", evac, store_fn=store)
        with P.block() as blk:
            @blk.gpsimd
            def _(gp):
                gp.wait_ge(stc.c(0).h, stc.c(0).n)
                gp.wait_ge(stc.c(1).h, stc.c(1).n)
        P.barrier()
        P.sem_release(_smark)
        pes.close()
    pes0.close()


def emit_retention(P, sc_q1, sc_k1, sc_v1, sc_ret):
    mrm = P.c("mrm")
    mrp = P.c("mrp")
    u = P.uid()
    _smark = P.sem_mark()
    pes = contextlib.ExitStack()
    qsb = P.sbuf(f"rt{u}q", [128, 8 * NQ], BF16, es=pes)
    kbuf = [P.sbuf(f"rt{u}k{i}", [128, 8 * 128], BF16, es=pes) for i in range(4)]
    vbuf = [P.sbuf(f"rt{u}v{i}", [128, D], BF16, es=pes) for i in range(4)]
    pT = P.sbuf(f"rt{u}p", [128, 32 * NQ], BF16, es=pes)
    ob = [P.sbuf(f"rt{u}o{i}", [128, NQ], BF16, es=pes) for i in range(2)]
    pss = [P.psum(f"rt{u}s{i}", [128, NQ], F32, es=pes) for i in range(3)]
    psr = [P.psum(f"rt{u}r{i}", [128, NQ], F32, es=pes) for i in range(4)]
    qld = Ctr(P, f"rt{u}ql")
    kld = Slots(P, f"rt{u}kl", 4)
    vld = Slots(P, f"rt{u}vl", 4)
    stc = Slots(P, f"rt{u}st", 2)
    c_smm = Ctr(P, f"rt{u}sm")
    c_dec = Ctr(P, f"rt{u}dc")
    c_pv = Ctr(P, f"rt{u}pv")
    c_ev = Ctr(P, f"rt{u}ev")
    nkc = {qc: 4 * qc + 4 for qc in range(8)}
    ksum = [0]
    for qc in range(8):
        ksum.append(ksum[-1] + nkc[qc])

    with P.block() as blk:
        @blk.sync
        def _(sy):
            nk_ = 0
            nv_ = 0
            for qc in range(8):
                if qc >= 1:
                    sy.wait_ge(c_smm.h, ksum[qc])
                qld.inc(sy.dma_start(
                    out=qsb[:, :].rearrange("p (a c) -> p a c", a=8),
                    in_=sc_q1[:, qc * NQ:(qc + 1) * NQ].rearrange("(a p) c -> p a c", p=128)), 16)
                for kc in range(nkc[qc]):
                    n = nk_ + kc
                    if n >= 4:
                        sy.wait_ge(c_smm.h, n - 3)
                    kld.c(n).inc(sy.dma_start(
                        out=kbuf[n % 4][:, :].rearrange("p (a c) -> p a c", a=8),
                        in_=sc_k1[:, kc * 128:(kc + 1) * 128].rearrange("(a p) c -> p a c", p=128)), 16)
                nk_ += nkc[qc]
                for dh in range(2):
                    for kc in range(nkc[qc]):
                        m = nv_ + dh * nkc[qc] + kc
                        if m >= 4:
                            sy.wait_ge(c_pv.h, 4 * (m - 3))
                        vld.c(m).inc(sy.dma_start(
                            out=vbuf[m % 4][:, :],
                            in_=sc_v1[kc * 128:(kc + 1) * 128, :]), 16)
                nv_ += 2 * nkc[qc]

        @blk.tensor
        def _(te):
            nk_ = 0
            nv_ = 0
            for qc in range(8):
                te.wait_ge(qld.h, 16 * (qc + 1))
                for kc in range(nkc[qc]):
                    n = nk_ + kc
                    te.wait_ge(kld.c(n).h, 16 * (n // 4 + 1))
                    if n >= 3:
                        te.wait_ge(c_dec.h, n - 2)
                    for j in range(NDC):
                        r = te.matmul(pss[n % 3][:, :],
                                      kbuf[n % 4][:, j * 128:(j + 1) * 128],
                                      qsb[:, j * NQ:(j + 1) * NQ],
                                      start=(j == 0), stop=(j == NDC - 1))
                    c_smm.inc(r)
                nk_ += nkc[qc]
                te.wait_ge(c_dec.h, nk_)
                for dh in range(2):
                    for kc in range(nkc[qc]):
                        m = nv_ + dh * nkc[qc] + kc
                        te.wait_ge(vld.c(m).h, 16 * (m // 4 + 1))
                        for dj in range(4):
                            if kc == 0 and (qc * 8 + dh * 4 + dj) >= 4:
                                te.wait_ge(c_ev.h, qc * 8 + dh * 4 + dj - 3)
                            r = te.matmul(psr[dj][:, :],
                                          vbuf[m % 4][:, dh * NQ + dj * 128: dh * NQ + (dj + 1) * 128],
                                          pT[:, kc * NQ:(kc + 1) * NQ],
                                          start=(kc == 0), stop=(kc == nkc[qc] - 1))
                            if dj == 3:
                                c_pv.inc(r, 4)
                nv_ += 2 * nkc[qc]

        @blk.vector
        def _(v):
            nk_ = 0
            for qc in range(8):
                for kc in range(nkc[qc]):
                    n = nk_ + kc
                    v.wait_ge(c_smm.h, n + 1)
                    if qc >= 1 and kc == 0:
                        v.wait_ge(c_pv.h, 8 * ksum[qc])
                    delta = qc * NQ - kc * 128
                    if delta >= 128:
                        scl = float(GAMMA ** (delta - 127))
                        c_dec.inc(v.scalar_tensor_tensor(pT[:, kc * NQ:(kc + 1) * NQ],
                                                         pss[n % 3][:, :], scl, mrp[:, :],
                                                         ALU.mult, ALU.mult))
                    else:
                        cofs = delta + 384
                        c_dec.inc(v.scalar_tensor_tensor(pT[:, kc * NQ:(kc + 1) * NQ],
                                                         pss[n % 3][:, :], 1.0, mrm[:, cofs:cofs + NQ],
                                                         ALU.mult, ALU.mult))
                nk_ += nkc[qc]

        @blk.scalar
        def _(sc):
            for qc in range(8):
                for dh in range(2):
                    for dj in range(4):
                        g = qc * 8 + dh * 4 + dj
                        sc.wait_ge(c_pv.h, 4 * (2 * ksum[qc] + (dh + 1) * nkc[qc]))
                        if g >= 2:
                            sc.wait_ge(stc.c(g).h, 16 * ((g - 2) // 2 + 1))
                        c_ev.inc(sc.copy(ob[g % 2][:, :], psr[dj][:, :]))

        @blk.gpsimd
        def _(gp):
            for qc in range(8):
                for dh in range(2):
                    for dj in range(4):
                        g = qc * 8 + dh * 4 + dj
                        gp.wait_ge(c_ev.h, g + 1)
                        stc.c(g).inc(gp.dma_start(
                            out=sc_ret[(dh * 4 + dj) * 128:(dh * 4 + dj + 1) * 128, qc * NQ:(qc + 1) * NQ],
                            in_=ob[g % 2][:, :]), 16)
            gp.wait_ge(stc.c(0).h, stc.c(0).n)
            gp.wait_ge(stc.c(1).h, stc.c(1).n)
    P.barrier()
    P.sem_release(_smark)
    pes.close()


def emit_smlp_l0(P, xn1F, scols, sc_h1):
    k0 = P.t["k0u"]
    u = P.uid()
    _smark = P.sem_mark()
    pes = contextlib.ExitStack()
    wu = [P.sbuf(f"m0{u}u{i}", [128, 2048], BF16, es=pes) for i in range(2)]
    wd = [P.sbuf(f"m0{u}d{i}", [128, 1024], BF16, es=pes) for i in range(2)]
    ob = [P.sbuf(f"m0{u}o{i}", [128, NQ], BF16, es=pes) for i in range(2)]
    ps = [P.psum(f"m0{u}p{i}", [128, NQ], F32, es=pes) for i in range(4)]
    wld = Slots(P, f"m0{u}wl", 2)
    stc = Slots(P, f"m0{u}st", 2)
    c_sub = Ctr(P, f"m0{u}sb")
    c_mm = Ctr(P, f"m0{u}mm")
    c_ev = Ctr(P, f"m0{u}ev")
    with P.block() as blk:
        @blk.gpsimd
        def _(gp):
            for df in range(32):
                if df >= 2:
                    gp.wait_ge(c_sub.h, 8 * (df - 1))
                wld.c(df).inc(gp.dma_start(out=wu[df % 2][:, :], in_=k0[df, :, :]), 16)

        @blk.vector
        def _(v):
            for df in range(32):
                v.wait_ge(wld.c(df).h, 16 * (df // 2 + 1))
                if df >= 2:
                    v.wait_ge(c_mm.h, 8 * (df - 1))
                for j in range(NDC):
                    v.wait_ge(c_sub.h, c_sub.n)
                    c_sub.inc(v.tensor_sub(wd[df % 2][:, j * 128:(j + 1) * 128],
                                           wu[df % 2][:, j * 128:(j + 1) * 128],
                                           wu[df % 2][:, 1024 + j * 128: 1024 + (j + 1) * 128]))

        @blk.tensor
        def _(te):
            for df in range(32):
                te.wait_ge(c_sub.h, 8 * (df + 1))
                for tc in range(8):
                    n = df * 8 + tc
                    if n >= 4:
                        te.wait_ge(c_ev.h, n - 3)
                    for j in range(NDC):
                        r = te.matmul(ps[n % 4][:, :],
                                      wd[df % 2][:, j * 128:(j + 1) * 128],
                                      xn1F[:, j * S + tc * NQ: j * S + (tc + 1) * NQ],
                                      start=(j == 0), stop=(j == NDC - 1))
                    c_mm.inc(r)

        @blk.scalar
        def _(sc):
            for df in range(32):
                for tc in range(8):
                    n = df * 8 + tc
                    sc.wait_ge(c_mm.h, n + 1)
                    if n >= 2:
                        sc.wait_ge(stc.c(n).h, 16 * ((n - 2) // 2 + 1))
                    c_ev.inc(sc.activation(ob[n % 2][:, :], ps[n % 4][:, :], AF.Relu,
                                           scale=scols[:, 0:1]))
                    sc.wait_ge(c_ev.h, n + 1)
                    stc.c(n).inc(sc.dma_start(
                        out=sc_h1[df * 128:(df + 1) * 128, tc * NQ:(tc + 1) * NQ],
                        in_=ob[n % 2][:, :]), 16)
            sc.wait_ge(stc.c(0).h, stc.c(0).n)
            sc.wait_ge(stc.c(1).h, stc.c(1).n)
    P.barrier()
    P.sem_release(_smark)
    pes.close()


def emit_smlp_mid(P, kt, src_h, scol_ap, dst, nin_chunks, nout_blk, relu):
    u = P.uid()
    _smark = P.sem_mark()
    pes = contextlib.ExitStack()
    hsb = P.sbuf(f"mm{u}h", [128, nin_chunks * 1024], BF16, es=pes)
    wu = [P.sbuf(f"mm{u}u{i}", [128, nin_chunks * 128], BF16, es=pes) for i in range(2)]
    wpm = [P.sbuf(f"mm{u}w{i}", [128, nin_chunks * 128], BF16, es=pes) for i in range(2)]
    ob = [P.sbuf(f"mm{u}o{i}", [128, NQ], BF16, es=pes) for i in range(2)]
    ps = [P.psum(f"mm{u}p{i}", [128, NQ], F32, es=pes) for i in range(4)]
    hld = Ctr(P, f"mm{u}hl")
    wld = Slots(P, f"mm{u}wl", 2)
    stc = Slots(P, f"mm{u}st", 2)
    c_pm = Ctr(P, f"mm{u}pm")
    c_mm = Ctr(P, f"mm{u}mm")
    c_ev = Ctr(P, f"mm{u}ev")
    with P.block() as blk:
        @blk.sync
        def _(sy):
            for tcp in range(4):
                if tcp >= 1:
                    sy.wait_ge(c_mm.h, 2 * nout_blk * tcp)
                hld.inc(sy.dma_start(
                    out=hsb[:, :].rearrange("p (a c) -> p a c", c=1024),
                    in_=src_h[:, tcp * 1024:(tcp + 1) * 1024].rearrange("(a p) c -> p a c", p=128)), 16)

        @blk.gpsimd
        def _(gp):
            for tcp in range(4):
                for do in range(nout_blk):
                    n = tcp * nout_blk + do
                    if n >= 2:
                        gp.wait_ge(c_pm.h, n - 1)
                    wld.c(n).inc(gp.dma_start(out=wu[n % 2][:, :], in_=kt[do, :, :]), 16)

        @blk.vector
        def _(v):
            for tcp in range(4):
                for do in range(nout_blk):
                    n = tcp * nout_blk + do
                    v.wait_ge(wld.c(n).h, 16 * (n // 2 + 1))
                    if n >= 2:
                        v.wait_ge(c_mm.h, 2 * (n - 1))
                    v.wait_ge(c_pm.h, c_pm.n)
                    c_pm.inc(v.tensor_scalar(wpm[n % 2][:, :], wu[n % 2][:, :],
                                             2.0, -1.0, ALU.mult, ALU.add))

        @blk.tensor
        def _(te):
            for tcp in range(4):
                te.wait_ge(hld.h, 16 * (tcp + 1))
                for do in range(nout_blk):
                    n = tcp * nout_blk + do
                    te.wait_ge(c_pm.h, n + 1)
                    for half in range(2):
                        g = n * 2 + half
                        if g >= 4:
                            te.wait_ge(c_ev.h, g - 3)
                        for dfi in range(nin_chunks):
                            r = te.matmul(ps[g % 4][:, :],
                                          wpm[n % 2][:, dfi * 128:(dfi + 1) * 128],
                                          hsb[:, dfi * 1024 + half * NQ: dfi * 1024 + half * NQ + NQ],
                                          start=(dfi == 0), stop=(dfi == nin_chunks - 1))
                        c_mm.inc(r)

        @blk.scalar
        def _(sc):
            for tcp in range(4):
                for do in range(nout_blk):
                    for half in range(2):
                        g = (tcp * nout_blk + do) * 2 + half
                        sc.wait_ge(c_mm.h, g + 1)
                        if g >= 2:
                            sc.wait_ge(stc.c(g).h, 16 * ((g - 2) // 2 + 1))
                        if relu:
                            c_ev.inc(sc.activation(ob[g % 2][:, :], ps[g % 4][:, :], AF.Relu,
                                                   scale=scol_ap))
                        else:
                            c_ev.inc(sc.activation(ob[g % 2][:, :], ps[g % 4][:, :], AF.Copy,
                                                   scale=scol_ap))
                        sc.wait_ge(c_ev.h, g + 1)
                        stc.c(g).inc(sc.dma_start(
                            out=dst[do * 128:(do + 1) * 128,
                                    tcp * 1024 + half * NQ: tcp * 1024 + half * NQ + NQ],
                            in_=ob[g % 2][:, :]), 16)
            sc.wait_ge(stc.c(0).h, stc.c(0).n)
            sc.wait_ge(stc.c(1).h, stc.c(1).n)
    P.barrier()
    P.sem_release(_smark)
    pes.close()


def build_full():
    P = Prog()
    cn = list(CONST_SBUF.keys())
    P.declare_inputs(list(INPUT_SHAPES.keys()))
    out = P.dram("out", [S, D], F32, kind="ExternalOutput")
    sc_attn = P.dram("sc_attn", [D, S], BF16)
    sc_q = P.dram("sc_q", [D, S], BF16)
    sc_vall = P.dram("sc_vall", [NB * S, D], BF16)
    sc_s = P.dram("sc_s", [D, S], BF16)
    sc_y = P.dram("sc_y", [D, S], BF16)
    sc_y32 = P.dram("sc_y32", [D, S], F32)
    mrow = P.dram("mrow", [8, S], BF16)
    sc_moe = P.dram("sc_moe", [D, S], BF16)
    sc_yT = P.dram("sc_yT", [S, D], BF16)
    sc_moeT = P.dram("sc_moeT", [S, D], BF16)
    gidx = P.dram("gidx", [NTILE * 512, 1], mybir.dt.int32)
    sidx = P.dram("sidx", [NTILE * 512, 1], mybir.dt.int32)
    sc_bs = P.dram("sc_bs", [1, 256], F32)
    sc_bas = P.dram("sc_bas", [1, 256], F32)
    sc_pb = P.dram("sc_pb", [1, 256], F32)
    sc_w1b = P.dram("sc_w1b", [8 * D, 4 * D], BF16)
    sc_w2b = P.dram("sc_w2b", [8 * 4 * D, D], BF16)
    sc_b1b = P.dram("sc_b1b", [8 * 128, 1024], BF16)
    sc_b2b = P.dram("sc_b2b", [8 * 128, 1024], BF16)
    sc_s2 = P.dram("sc_s2", [D, S], BF16)
    sc_z = P.dram("sc_z", [D, S], BF16)
    sc_s3 = P.dram("sc_s3", [D, S], BF16)
    sc_x1T = P.dram("sc_x1T", [S, D], F32)
    sc_q1 = P.dram("sc_q1", [D, S], BF16)
    sc_k1 = P.dram("sc_k1", [D, S], BF16)
    sc_v1 = P.dram("sc_v1", [S, D], BF16)
    sc_ret = P.dram("sc_ret", [D, S], BF16)
    sc_h1 = P.dram("sc_h1", [4 * D, S], BF16)
    sc_h2 = P.dram("sc_h2", [4 * D, S], BF16)
    sc_mlp = P.dram("sc_mlp", [D, S], BF16)
    sc_s4 = P.dram("sc_s4", [D, S], BF16)

    P.load_consts(cn)
    idxw = P.sbuf("idxw", [128, 48], mybir.dt.int32)
    eidi = P.sbuf("eidi", [1, 16], mybir.dt.int32)
    wc = make_wconv(P)
    xm = P.t["x_mine"]
    Wt = P.t["WqkvT"]
    Wo = P.t["WoT"]
    Wg = P.t["WgT"]

    with contextlib.ExitStack() as es0:
        xnF = P.sbuf("xnF", [128, 8 * S], BF16, es=es0)
        with contextlib.ExitStack() as es1:
            xnT = P.sbuf("xnT", [128, NPT * D], BF16, es=es1)
            emit_ln0(P, lambda t: xm[t * 128:(t + 1) * 128, :], xnT=xnT, xnF=xnF)
            emit_l0_attn(P, xnF, xnT, sc_attn,
                         extra=wconv_sections(P, wc, sc_w1b, sc_w2b, sc_b1b, sc_b2b))
        with contextlib.ExitStack() as es1:
            wq = P.sbuf("wq", [128, 8 * 1024], BF16, es=es1)
            emit_load_wslab(P, wq, [(k * 1024, Wt[k * 128:(k + 1) * 128, 0:1024]) for k in range(8)], "wq")
            emit_tiny_qproj(P, xnF, wq, sc_q)
        with contextlib.ExitStack() as es1:
            wkv = P.sbuf("wkv", [128, 16 * 1024], BF16, es=es1)
            scores = P.sbuf("scores", [128, 1024], F32, es=es1)
            vrowb = P.sbuf("vrowb", [1, 2 * D], BF16, es=es1)
            emit_load_wslab(P, wkv,
                            [(k * 2048, Wt[k * 128:(k + 1) * 128, 1024:2048]) for k in range(8)] +
                            [(k * 2048 + 1024, Wt[k * 128:(k + 1) * 128, 2048:3072]) for k in range(8)], "wkv")
            emit_load_wslab(P, vrowb, [(0, P.t["bqkv_vrow"][:, :])], "vrow")
            emit_tiny_scores(P, xnF, P.t["x_full"], wkv, vrowb, sc_q, sc_vall, scores)
            with contextlib.ExitStack() as es2:
                attoutF = P.sbuf("attoutF", [128, 8 * S], BF16, es=es2)
                emit_tiny_softmax_pv(P, scores, sc_vall, attoutF)
                with contextlib.ExitStack() as es3:
                    wo = P.sbuf("wo", [128, 8 * 1024], BF16, es=es3)
                    emit_load_wslab(P, wo, [(k * 1024, Wo[k * 128:(k + 1) * 128, :]) for k in range(8)], "wo")
                    emit_wo_s(P, attoutF, xnF, wo, sc_s)
    emit_fln(P, sc_s, "g1", "be1", sc_y, dst32=sc_y32)
    with contextlib.ExitStack() as es1:
        wg = P.sbuf("wg", [128, 64], F32, es=es1)
        emit_load_wslab(P, wg, [(j * 8, Wg[j * 128:(j + 1) * 128, :]) for j in range(8)], "wg")
        emit_moe_route2(P, sc_y32, wg, gidx, sidx, sc_bs, sc_bas, sc_pb, idxw, eidi)
    emit_transpose(P, sc_y, sc_yT, 8, 32)
    emit_wconv_drain(P, wc)
    emit_moe_routed2(P, sc_yT, sc_w1b, sc_w2b, sc_b1b, sc_b2b, gidx, sidx, idxw, eidi, sc_moeT)
    emit_transpose(P, sc_moeT, sc_moe, 32, 8)
    emit_binop(P, sc_y, sc_moe, sc_s2)
    emit_fln(P, sc_s2, "g2", "be2", sc_z)
    emit_binop(P, sc_attn, sc_z, sc_s3)
    emit_x1(P, sc_s3, xm, sc_x1T)
    scols = P.sbuf("scols", [128, 3], F32)
    with contextlib.ExitStack() as es0:
        xn1F = P.sbuf("xn1F", [128, 8 * S], BF16, es=es0)
        emit_ln0(P, lambda t: sc_x1T[t * 128:(t + 1) * 128, :], xnT=None, xnF=xn1F)
        emit_sfactor_cols(P, scols)
        emit_ret_proj(P, xn1F, sc_q1, sc_k1, sc_v1)
        emit_smlp_l0(P, xn1F, scols, sc_h1)
    emit_retention(P, sc_q1, sc_k1, sc_v1, sc_ret)
    emit_smlp_mid(P, P.t["k1u"], sc_h1, scols[:, 1:2], sc_h2, 32, 32, relu=True)
    emit_smlp_mid(P, P.t["k2u"], sc_h2, scols[:, 2:3], sc_mlp, 32, 8, relu=False)
    emit_binop(P, sc_ret, sc_mlp, sc_s4)
    emit_x1(P, sc_s4, sc_x1T, out)
    P.nc.finalize()
    return P


_BUILT = None


def kernel(**inputs):
    global _BUILT
    in_maps = host_prep(inputs)
    if _BUILT is None:
        _BUILT = build_full()
    P = _BUILT
    res = run_bass_kernel_spmd(P.nc, in_maps, core_ids=list(range(NB)))
    outs = [np.asarray(res.results[c]["out"], dtype=np.float32) for c in range(NB)]
    return np.stack(outs, axis=0)


NSLOT = 768


def emit_transpose(P, src_dram, dst_dram, nrow_t, ncol_t, dt=BF16):
    u = P.uid()
    _smark = P.sem_mark()
    pes = contextlib.ExitStack()
    sb = [P.sbuf(f"tp{u}s{i}", [128, 128], dt, es=pes) for i in range(3)]
    rowb = [P.sbuf(f"tp{u}r{i}", [128, nrow_t * 128], dt, es=pes) for i in range(2)]
    ptp = [P.psum(f"tp{u}p{i}", [128, 128], dt, es=pes) for i in range(2)]
    lds = Slots(P, f"tp{u}l", 3)
    stc = Slots(P, f"tp{u}st", 2)
    c_tp = Ctr(P, f"tp{u}tp")
    c_ae = Ctr(P, f"tp{u}ae")
    assert dt == BF16
    id_ = P.c("id_bf")
    with P.block() as blk:
        @blk.sync
        def _(sy):
            for j in range(ncol_t):
                for i in range(nrow_t):
                    n = j * nrow_t + i
                    if n >= 3:
                        sy.wait_ge(c_tp.h, n - 2)
                    lds.c(n).inc(sy.dma_start(
                        out=sb[n % 3][:, :],
                        in_=src_dram[i * 128:(i + 1) * 128, j * 128:(j + 1) * 128]), 16)

        @blk.tensor
        def _(te):
            for j in range(ncol_t):
                for i in range(nrow_t):
                    n = j * nrow_t + i
                    te.wait_ge(lds.c(n).h, 16 * (n // 3 + 1))
                    if n >= 2:
                        te.wait_ge(c_ae.h, n - 1)
                    c_tp.inc(te.transpose(ptp[n % 2][:, :], sb[n % 3][:, :], id_[:, :]))

        @blk.scalar
        def _(sc):
            for j in range(ncol_t):
                for i in range(nrow_t):
                    n = j * nrow_t + i
                    sc.wait_ge(c_tp.h, n + 1)
                    if i == 0 and j >= 2:
                        sc.wait_ge(stc.c(j).h, 16 * ((j - 2) // 2 + 1))
                    c_ae.inc(sc.copy(rowb[j % 2][:, i * 128:(i + 1) * 128], ptp[n % 2][:, :]))

        @blk.gpsimd
        def _(gp):
            for j in range(ncol_t):
                gp.wait_ge(c_ae.h, (j + 1) * nrow_t)
                stc.c(j).inc(gp.dma_start(
                    out=dst_dram[j * 128:(j + 1) * 128, :],
                    in_=rowb[j % 2][:, :]), 16)
            gp.wait_ge(stc.c(0).h, stc.c(0).n)
            gp.wait_ge(stc.c(1).h, stc.c(1).n)
    P.barrier()
    P.sem_release(_smark)
    pes.close()


def emit_moe_route(P, sc_y32, wg_sb, gidx, sidx, sc_bs, sc_bas):
    u = P.uid()
    _smark = P.sem_mark()
    pes = contextlib.ExitStack()
    I32 = mybir.dt.int32
    yb = [P.sbuf(f"rr{u}y{i}", [128, 8 * NQ], F32, es=pes) for i in range(2)]
    lgT = [P.sbuf(f"rr{u}l{i}", [128, 8], F32, es=pes) for i in range(2)]
    mxc = P.sbuf(f"rr{u}m", [128, 1], F32, es=pes)
    mskT = P.sbuf(f"rr{u}s", [128, 8 * 32], F32, es=pes)
    bsrow = P.sbuf(f"rr{u}b", [1, 256], F32, es=pes)
    bsT = P.sbuf(f"rr{u}bt", [32, 8], F32, es=pes)
    basT = P.sbuf(f"rr{u}bb", [32, 8], F32, es=pes)
    basrow = P.sbuf(f"rr{u}br", [1, 256], F32, es=pes)
    slotf = P.sbuf(f"rr{u}sf", [128, 8], F32, es=pes)
    slotc = P.sbuf(f"rr{u}sc", [128, 1], F32, es=pes)
    sloti = P.sbuf(f"rr{u}si", [128, 1], I32, es=pes)
    init0 = P.sbuf(f"rr{u}i0", [128, 48], I32, es=pes)
    initb = P.sbuf(f"rr{u}ib", [128, 48], I32, es=pes)
    pslg = [P.psum(f"rr{u}p{i}", [128, 8], F32, es=pes) for i in range(2)]
    psbs = P.psum(f"rr{u}pb", [1, 8], F32, es=pes)
    psba = P.psum(f"rr{u}pa", [32, 8], F32, es=pes)
    pssl = [P.psum(f"rr{u}ps{i}", [128, 8], F32, es=pes) for i in range(2)]
    ld = Slots(P, f"rr{u}ld", 2)
    c_mm = Ctr(P, f"rr{u}mm")
    c_ev = Ctr(P, f"rr{u}ev")
    c_bs = Ctr(P, f"rr{u}bs")
    c_bse = Ctr(P, f"rr{u}be")
    c_dmab = Ctr(P, f"rr{u}db")
    c_ltm = Ctr(P, f"rr{u}lt")
    c_bae = Ctr(P, f"rr{u}ba")
    c_slm = Ctr(P, f"rr{u}sl")
    c_sl = Ctr(P, f"rr{u}s2")
    c_scg = Ctr(P, f"rr{u}scg")
    c_scs = Ctr(P, f"rr{u}scs")
    ones1f = P.c("ones_1x128_f32")
    tri = P.c("tri128")
    lt = P.c("lt32")
    erow = P.c("erow")
    iota = P.c("iota_col")
    o128f = P.c("ones_128x1_f32")

    with P.block() as blk:
        @blk.sync
        def _(sy):
            for tc in range(8):
                if tc >= 2:
                    sy.wait_ge(c_mm.h, (tc - 1) * 4)
                ld.c(tc).inc(sy.dma_start(
                    out=yb[tc % 2][:, :].rearrange("p (a c) -> p a c", a=8),
                    in_=sc_y32[:, tc * NQ:(tc + 1) * NQ].rearrange("(a p) c -> p a c", p=128)), 16)

        @blk.tensor
        def _(te):
            for tc in range(8):
                te.wait_ge(ld.c(tc).h, 16 * (tc // 2 + 1))
                for ps_ in range(4):
                    pt = tc * 4 + ps_
                    if pt >= 2:
                        te.wait_ge(c_ev.h, 3 * (pt - 1))
                    for j in range(NDC):
                        r = te.matmul(pslg[pt % 2][:, :],
                                      yb[tc % 2][:, j * NQ + ps_ * 128: j * NQ + ps_ * 128 + 128],
                                      wg_sb[:, j * 8:(j + 1) * 8],
                                      start=(j == 0), stop=(j == NDC - 1))
                    c_mm.inc(r)
                    te.wait_ge(c_ev.h, 3 * (pt + 1))
                    if pt >= 1:
                        te.wait_ge(c_bse.h, pt)
                    c_bs.inc(te.matmul(psbs, o128f[:, :],
                                       mskT[:, pt * 8:(pt + 1) * 8], start=True, stop=True))
            te.wait_ge(c_dmab.h, 64)
            c_ltm.inc(te.matmul(psba[:, :], lt[:, :], bsT[:, :], start=True, stop=True))
            te.wait_ge(c_dmab.h, 96)
            for pt in range(32):
                if pt >= 2:
                    te.wait_ge(c_sl.h, 3 * (pt - 1))
                te.matmul(pssl[pt % 2][:, :], tri[:, :], mskT[:, pt * 8:(pt + 1) * 8],
                          start=True, stop=False)
                te.matmul(pssl[pt % 2][:, :], ones1f[:, :], basrow[0:1, pt * 8:(pt + 1) * 8],
                          start=False, stop=False)
                c_slm.inc(te.matmul(pssl[pt % 2][:, :], ones1f[:, :], erow[0:1, :],
                                    start=False, stop=True))

        cinit = Ctr(P, f"rr{u}in")

        @blk.vector
        def _(v):
            cinit.inc(v.memset(init0[:, :], 0))
            cinit.inc(v.memset(initb[:, :], 100000))
            for pt in range(32):
                v.wait_ge(c_mm.h, pt + 1)
                v.wait_ge(c_ev.h, c_ev.n)
                c_ev.inc(v.tensor_copy(lgT[pt % 2][:, :], pslg[pt % 2][:, :]))
                v.wait_ge(c_ev.h, c_ev.n)
                c_ev.inc(v.reduce_max(mxc[:, :], lgT[pt % 2][:, :], axis=mybir.AxisListType.X))
                v.wait_ge(c_ev.h, c_ev.n)
                c_ev.inc(v.tensor_scalar(mskT[:, pt * 8:(pt + 1) * 8], lgT[pt % 2][:, :],
                                         mxc[:, :], None, ALU.is_ge))
                v.wait_ge(c_bs.h, pt + 1)
                c_bse.inc(v.tensor_copy(bsrow[0:1, pt * 8:(pt + 1) * 8], psbs[0:1, :]))
            v.wait_ge(c_ltm.h, 1)
            c_bae.inc(v.tensor_copy(basT[:, :], psba[:, :]))
            for pt in range(32):
                v.wait_ge(c_slm.h, pt + 1)
                v.wait_ge(c_sl.h, c_sl.n)
                if pt >= 1:
                    v.wait_ge(c_scg.h, 16 * pt)
                    v.wait_ge(c_scs.h, 16 * pt)
                c_sl.inc(v.tensor_mul(slotf[:, :], mskT[:, pt * 8:(pt + 1) * 8], pssl[pt % 2][:, :]))
                v.wait_ge(c_sl.h, c_sl.n)
                c_sl.inc(v.reduce_sum(slotc[:, :], slotf[:, :], axis=mybir.AxisListType.X))
                v.wait_ge(c_sl.h, c_sl.n)
                c_sl.inc(v.tensor_copy(sloti[:, :], slotc[:, :]))

        @blk.gpsimd
        def _(gp):
            gp.wait_ge(cinit.h, 2)
            c_dmab.inc(gp.dma_start(out=gidx[:, :].rearrange("(a b) o -> a (b o)", a=128),
                                    in_=init0[:, :]), 16)
            c_dmab.inc(gp.dma_start(out=sidx[:, :].rearrange("(a b) o -> a (b o)", a=128),
                                    in_=initb[:, :]), 16)
            gp.wait_ge(c_bse.h, 32)
            c_dmab.inc(gp.dma_start(out=sc_bs[0:1, :], in_=bsrow[0:1, :]), 16)
            gp.wait_ge(c_dmab.h, 48)
            c_dmab.inc(gp.dma_start(out=bsT[:, :],
                                    in_=sc_bs[0:1, :].rearrange("o (a b) -> (o a) b", a=32)), 16)
            gp.wait_ge(c_bae.h, 1)
            c_dmab.inc(gp.dma_start(out=sc_bas[0:1, :].rearrange("o (a b) -> (o a) b", a=32),
                                    in_=basT[:, :]), 16)
            gp.wait_ge(c_dmab.h, 80)
            c_dmab.inc(gp.dma_start(out=basrow[0:1, :], in_=sc_bas[0:1, :]), 16)
            with gp.register(f"rrbc{u}") as rb:
                gp.reg_mov(rb, 8 * NSLOT - 1)
                for pt in range(32):
                    gp.wait_ge(c_sl.h, 3 * (pt + 1))
                    c_scg.inc(gp.indirect_dma_start(
                        gidx[:, :], bass.IndirectOffsetOnAxis(sloti[:, 0:1], 0),
                        iota[:, pt:pt + 1], None,
                        bounds_check=rb, oob_is_err=False), 16)
                    c_scs.inc(gp.indirect_dma_start(
                        sidx[:, :], bass.IndirectOffsetOnAxis(sloti[:, 0:1], 0),
                        iota[:, pt:pt + 1], None,
                        bounds_check=rb, oob_is_err=False), 16)
            gp.wait_ge(c_scg.h, c_scg.n)
            gp.wait_ge(c_scs.h, c_scs.n)
    P.barrier()
    P.sem_release(_smark)
    pes.close()


def emit_moe_routed(P, sc_yT, gidx, sidx, sc_moeT):
    W1 = P.t["W1"]
    W2 = P.t["W2"]
    b1 = P.c("b1")
    b2 = P.c("b2")
    I32 = mybir.dt.int32
    NG = NSLOT // 128
    HN = NSLOT // 2
    u0 = P.uid()
    pes0 = contextlib.ExitStack()
    wsb = P.sbuf(f"rx{u0}w", [128, 8 * 4096], BF16, es=pes0)
    id_bf = P.c("id_bf")
    for e in range(NB):
        u = P.uid()
        _smark = P.sem_mark()
        pes = contextlib.ExitStack()
        gcol = [P.sbuf(f"rx{u}g{i}", [128, 1], I32, es=pes) for i in range(2)]
        scol = [P.sbuf(f"rx{u}s{i}", [128, 1], I32, es=pes) for i in range(2)]
        gth = [P.sbuf(f"rx{u}t{i}", [128, D], BF16, es=pes) for i in range(2)]
        xe = P.sbuf(f"rx{u}x", [128, 8 * NSLOT], BF16, es=pes)
        he = P.sbuf(f"rx{u}h", [128, 32 * NSLOT], BF16, es=pes)
        oe = P.sbuf(f"rx{u}o", [128, 8 * NSLOT], BF16, es=pes)
        oT = [P.sbuf(f"rx{u}ot{i}", [128, D], BF16, es=pes) for i in range(2)]
        psw = [P.psum(f"rx{u}p{i}", [128, HN], F32, es=pes) for i in range(3)]
        ptp = [P.psum(f"rx{u}q{i}", [128, 128], BF16, es=pes) for i in range(2)]
        wld = Ctr(P, f"rx{u}wl")
        ild = Slots(P, f"rx{u}il", 2)
        gld = Slots(P, f"rx{u}gl", 2)
        c_ti = Ctr(P, f"rx{u}ti")
        c_te = Ctr(P, f"rx{u}te")
        c_h = Ctr(P, f"rx{u}hm")
        c_he = Ctr(P, f"rx{u}he")
        c_o = Ctr(P, f"rx{u}om")
        c_oe = Ctr(P, f"rx{u}oe")
        c_to = Ctr(P, f"rx{u}to")
        c_toe = Ctr(P, f"rx{u}tv")
        scsl = Slots(P, f"rx{u}sx", 2)

        with P.block() as blk:
            @blk.gpsimd
            def _(gp):
                for k in range(NDC):
                    wld.inc(gp.dma_start(out=wsb[:, k * 4096:(k + 1) * 4096],
                                         in_=W1[e, k * 128:(k + 1) * 128, :]), 16)
                for g in range(NG):
                    if g >= 2:
                        gp.wait_ge(gld.c(g).h, 16 * ((g - 2) // 2 + 1))
                    ild.c(g).inc(gp.dma_start(
                        out=gcol[g % 2][:, :],
                        in_=gidx[e * NSLOT + g * 128: e * NSLOT + (g + 1) * 128, 0:1]), 16)
                    gp.wait_ge(ild.c(g).h, 16 * (g // 2 + 1))
                    if g >= 2:
                        gp.wait_ge(c_ti.h, 8 * (g - 1))
                    gld.c(g).inc(gp.indirect_dma_start(
                        gth[g % 2][:, :], None,
                        sc_yT[:, :], bass.IndirectOffsetOnAxis(gcol[g % 2][:, 0:1], 0)), 16)
                gp.wait_ge(c_h.h, 64)
                for dfi in range(32):
                    wld.inc(gp.dma_start(out=wsb[:, dfi * 1024:(dfi + 1) * 1024],
                                         in_=W2[e, dfi * 128:(dfi + 1) * 128, :]), 16)
                with gp.register(f"rxbc{u}") as rb:
                    gp.reg_mov(rb, S - 1)
                    for g in range(NG):
                        n = NG + g
                        if g >= 2:
                            gp.wait_ge(scsl.c(g).h, 16 * ((g - 2) // 2 + 1))
                        ild.c(n).inc(gp.dma_start(
                            out=scol[g % 2][:, :],
                            in_=sidx[e * NSLOT + g * 128: e * NSLOT + (g + 1) * 128, 0:1]), 16)
                        gp.wait_ge(ild.c(n).h, 16 * (n // 2 + 1))
                        gp.wait_ge(c_toe.h, 8 * (g + 1))
                        scsl.c(g).inc(gp.indirect_dma_start(
                            sc_moeT[:, :], bass.IndirectOffsetOnAxis(scol[g % 2][:, 0:1], 0),
                            oT[g % 2][:, :], None,
                            bounds_check=rb, oob_is_err=False), 16)
                gp.wait_ge(scsl.c(0).h, scsl.c(0).n)
                gp.wait_ge(scsl.c(1).h, scsl.c(1).n)

            @blk.tensor
            def _(te):
                for g in range(NG):
                    te.wait_ge(gld.c(g).h, 16 * (g // 2 + 1))
                    for j in range(NDC):
                        n = g * 8 + j
                        if n >= 2:
                            te.wait_ge(c_te.h, n - 1)
                        c_ti.inc(te.transpose(ptp[n % 2][:, :], gth[g % 2][:, j * 128:(j + 1) * 128],
                                              id_bf[:, :]))
                te.wait_ge(wld.h, 16 * NDC)
                for df in range(32):
                    for hf in range(2):
                        n = df * 2 + hf
                        if n >= 3:
                            te.wait_ge(c_he.h, n - 2)
                        if n == 0:
                            te.wait_ge(c_te.h, 8 * NG)
                        for k in range(NDC):
                            r = te.matmul(psw[n % 3][:, :],
                                          wsb[:, k * 4096 + df * 128: k * 4096 + df * 128 + 128],
                                          xe[:, k * NSLOT + hf * HN: k * NSLOT + hf * HN + HN],
                                          start=(k == 0), stop=(k == NDC - 1))
                        c_h.inc(r)
                te.wait_ge(wld.h, 16 * (NDC + 32))
                for do in range(8):
                    for hf in range(2):
                        n = do * 2 + hf
                        if n >= 3:
                            te.wait_ge(c_oe.h, n - 2)
                        if n == 0:
                            te.wait_ge(c_he.h, 64)
                        for df in range(32):
                            r = te.matmul(psw[n % 3][:, :],
                                          wsb[:, df * 1024 + do * 128: df * 1024 + do * 128 + 128],
                                          he[:, df * NSLOT + hf * HN: df * NSLOT + hf * HN + HN],
                                          start=(df == 0), stop=(df == 31))
                        c_o.inc(r)
                for g in range(NG):
                    te.wait_ge(c_oe.h, 16)
                    for j in range(NDC):
                        n = g * 8 + j
                        if g >= 2 and j == 0:
                            te.wait_ge(scsl.c(g).h, 16 * ((g - 2) // 2 + 1))
                        if n >= 2:
                            te.wait_ge(c_toe.h, n - 1)
                        c_to.inc(te.transpose(ptp[n % 2][:, :], oe[:, j * NSLOT + g * 128: j * NSLOT + (g + 1) * 128],
                                              id_bf[:, :]))

            @blk.scalar
            def _(sc):
                for g in range(NG):
                    for j in range(NDC):
                        n = g * 8 + j
                        sc.wait_ge(c_ti.h, n + 1)
                        c_te.inc(sc.copy(xe[:, j * NSLOT + g * 128: j * NSLOT + (g + 1) * 128],
                                         ptp[n % 2][:, :]))
                for df in range(32):
                    for hf in range(2):
                        n = df * 2 + hf
                        sc.wait_ge(c_h.h, n + 1)
                        c_he.inc(sc.activation(he[:, df * NSLOT + hf * HN: df * NSLOT + hf * HN + HN],
                                               psw[n % 3][:, :], AF.Relu,
                                               bias=b1[:, e * 32 + df: e * 32 + df + 1]))
                for g in range(NG):
                    for j in range(NDC):
                        n = g * 8 + j
                        sc.wait_ge(c_to.h, n + 1)
                        c_toe.inc(sc.copy(oT[g % 2][:, j * 128:(j + 1) * 128], ptp[n % 2][:, :]))

            @blk.vector
            def _(v):
                for do in range(8):
                    for hf in range(2):
                        n = do * 2 + hf
                        v.wait_ge(c_o.h, n + 1)
                        v.wait_ge(c_oe.h, c_oe.n)
                        c_oe.inc(v.tensor_scalar(oe[:, do * NSLOT + hf * HN: do * NSLOT + hf * HN + HN],
                                                 psw[n % 3][:, :],
                                                 b2[:, e * 8 + do: e * 8 + do + 1], None, ALU.add))
        P.barrier()
        P.sem_release(_smark)
        pes.close()
    pes0.close()


NTILE = 16


def make_wconv(P):
    u = P.uid()
    wc = {"mark": P.sem_mark(), "pes": contextlib.ExitStack()}
    wc["buf"] = [P.sbuf(f"wc{u}_{i}", [128, 2048], BF16, es=wc["pes"]) for i in range(2)]
    wc["ld"] = Slots(P, f"wc{u}l", 2)
    wc["st"] = Slots(P, f"wc{u}s", 2)
    return wc


def wconv_sections(P, wc, sc_w1b, sc_w2b, sc_b1b, sc_b2b):
    W1f = P.t["W1"][:, :, :].rearrange("e r c -> (e r c)").rearrange("(j p c) -> j p c", p=128, c=2048)
    W2f = P.t["W2"][:, :, :].rearrange("e r c -> (e r c)").rearrange("(j p c) -> j p c", p=128, c=2048)
    o1 = sc_w1b[:, :].rearrange("r c -> (r c)").rearrange("(j p c) -> j p c", p=128, c=2048)
    o2 = sc_w2b[:, :].rearrange("r c -> (r c)").rearrange("(j p c) -> j p c", p=128, c=2048)
    buf, ld, st = wc["buf"], wc["ld"], wc["st"]
    jobs = [(W1f[j, :, :], o1[j, :, :], 2048, 2048) for j in range(128)] + \
           [(W2f[j, :, :], o2[j, :, :], 2048, 2048) for j in range(128)]
    for e in range(8):
        jobs.append((P.t["b1t"][e, :, :], sc_b1b[e * 128:(e + 1) * 128, :], 32, 1024))
        jobs.append((P.t["b2t"][e, :, :], sc_b2b[e * 128:(e + 1) * 128, :], 8, 1024))

    def attach(blk):
        @blk.gpsimd
        def _(gp):
            for n, (iap, oap, wl, ws) in enumerate(jobs):
                if n >= 2:
                    gp.wait_ge(st.c(n).h, 16 * ((n - 2) // 2 + 1))
                ld.c(n).inc(gp.dma_start(out=buf[n % 2][:, 0:wl], in_=iap), 16)

        @blk.sync
        def _(sy):
            for n, (iap, oap, wl, ws) in enumerate(jobs):
                sy.wait_ge(ld.c(n).h, 16 * (n // 2 + 1))
                st.c(n).inc(sy.dma_start(out=oap, in_=buf[n % 2][:, 0:ws]), 16)
    return attach


def emit_wconv_drain(P, wc):
    st = wc["st"]
    with P.block() as blk:
        @blk.gpsimd
        def _(gp):
            gp.wait_ge(st.c(0).h, st.c(0).n)
            gp.wait_ge(st.c(1).h, st.c(1).n)
    P.barrier()
    P.sem_release(wc["mark"])
    wc["pes"].close()


def emit_moe_route2(P, sc_y32, wg_sb, gidx, sidx, sc_bs, sc_bas, sc_pb, idxw, eidi):
    u = P.uid()
    _smark = P.sem_mark()
    pes = contextlib.ExitStack()
    I32 = mybir.dt.int32
    yb = [P.sbuf(f"r2{u}y{i}", [128, 8 * NQ], F32, es=pes) for i in range(2)]
    lgT = [P.sbuf(f"r2{u}l{i}", [128, 8], F32, es=pes) for i in range(2)]
    mxc = P.sbuf(f"r2{u}m", [128, 1], F32, es=pes)
    mskT = P.sbuf(f"r2{u}s", [128, 8 * 32], F32, es=pes)
    bsrow = P.sbuf(f"r2{u}b", [1, 256], F32, es=pes)
    bsT = P.sbuf(f"r2{u}bt", [32, 8], F32, es=pes)
    basT = P.sbuf(f"r2{u}bb", [32, 8], F32, es=pes)
    basrow = P.sbuf(f"r2{u}br", [1, 256], F32, es=pes)
    ccol = P.sbuf(f"r2{u}cc", [8, 4], F32, es=pes)
    ccoli = P.sbuf(f"r2{u}ci", [8, 1], I32, es=pes)
    pbcol = P.sbuf(f"r2{u}pb", [8, 2], F32, es=pes)
    pbrow = P.sbuf(f"r2{u}pr", [1, 16], F32, es=pes)
    cmpe = P.sbuf(f"r2{u}ce", [8, 16], F32, es=pes)
    erowf = P.sbuf(f"r2{u}er", [1, 16], F32, es=pes)
    eidf = P.sbuf(f"r2{u}ef", [1, 16], F32, es=pes)
    idxf = P.sbuf(f"r2{u}if", [128, 48], F32, es=pes)
    slotf = P.sbuf(f"r2{u}sf", [128, 8], F32, es=pes)
    slotc = P.sbuf(f"r2{u}sc", [128, 1], F32, es=pes)
    sloti = P.sbuf(f"r2{u}si", [128, 1], I32, es=pes)
    init0 = P.sbuf(f"r2{u}i0", [128, NTILE * 4], I32, es=pes)
    initb = P.sbuf(f"r2{u}ib", [128, NTILE * 4], I32, es=pes)
    pslg = [P.psum(f"r2{u}p{i}", [128, 8], F32, es=pes) for i in range(2)]
    psm_ = P.psum(f"r2{u}pm", [128, 512], F32, es=pes)
    psbs = psm_[0:1, 0:8]
    psba = psm_[0:32, 8:16]
    psct = psm_[0:8, 16:17]
    pspb = psm_[0:8, 17:18]
    pser = psm_[0:1, 32:48]
    psix = psm_[:, 64:80]
    pssl = [P.psum(f"r2{u}ps{i}", [128, 8], F32, es=pes) for i in range(2)]
    ld = Slots(P, f"r2{u}ld", 2)
    c_mm = Ctr(P, f"r2{u}mm")
    c_ev = Ctr(P, f"r2{u}ev")
    c_bs = Ctr(P, f"r2{u}bs")
    c_bse = Ctr(P, f"r2{u}be")
    c_dmab = Ctr(P, f"r2{u}db")
    c_tmm = Ctr(P, f"r2{u}tm")
    c_dsm = Ctr(P, f"r2{u}ds")
    c_slm = Ctr(P, f"r2{u}sl")
    c_sl = Ctr(P, f"r2{u}s2")
    c_scg = Ctr(P, f"r2{u}sg")
    c_scs = Ctr(P, f"r2{u}ss")
    cinit = Ctr(P, f"r2{u}in")
    ones1f = P.c("ones_1x128_f32")
    tri = P.c("tri128")
    lt = P.c("lt32")
    lt8 = P.c("lt8")
    iota = P.c("iota_col")
    o128f = P.c("ones_128x1_f32")
    o32f = P.c("ones_32x1_f32")
    o8f = P.c("ones_8x1_f32")
    tbase = P.c("tilebases")
    io128f = P.c("iota128f")

    with P.block() as blk:
        @blk.sync
        def _(sy):
            for tc in range(8):
                if tc >= 2:
                    sy.wait_ge(c_mm.h, (tc - 1) * 4)
                ld.c(tc).inc(sy.dma_start(
                    out=yb[tc % 2][:, :].rearrange("p (a c) -> p a c", a=8),
                    in_=sc_y32[:, tc * NQ:(tc + 1) * NQ].rearrange("(a p) c -> p a c", p=128)), 16)

        @blk.tensor
        def _(te):
            for tc in range(8):
                te.wait_ge(ld.c(tc).h, 16 * (tc // 2 + 1))
                for ps_ in range(4):
                    pt = tc * 4 + ps_
                    if pt >= 2:
                        te.wait_ge(c_ev.h, 3 * (pt - 1))
                    for j in range(NDC):
                        r = te.matmul(pslg[pt % 2][:, :],
                                      yb[tc % 2][:, j * NQ + ps_ * 128: j * NQ + ps_ * 128 + 128],
                                      wg_sb[:, j * 8:(j + 1) * 8],
                                      start=(j == 0), stop=(j == NDC - 1))
                    c_mm.inc(r)
                    te.wait_ge(c_ev.h, 3 * (pt + 1))
                    if pt >= 1:
                        te.wait_ge(c_bse.h, pt)
                    c_bs.inc(te.matmul(psbs, o128f[:, :],
                                       mskT[:, pt * 8:(pt + 1) * 8], start=True, stop=True))
            te.wait_ge(c_dmab.h, 64)
            c_tmm.inc(te.matmul(psba, lt[:, :], bsT[:, :], start=True, stop=True))
            c_tmm.inc(te.matmul(psct, bsT[:, :], o32f[:, :], start=True, stop=True))
            te.wait_ge(c_dsm.h, 5)
            c_tmm.inc(te.matmul(pspb, lt8[:, :], ccol[:, 1:2], start=True, stop=True))
            te.wait_ge(c_dsm.h, 8)
            c_tmm.inc(te.matmul(pser, o8f[:, :], cmpe[:, :], start=True, stop=True))
            te.wait_ge(c_dsm.h, 10)
            c_tmm.inc(te.matmul(psix, ones1f[:, :], erowf[0:1, :], start=True, stop=True))
            te.wait_ge(c_dmab.h, 128)
            for pt in range(32):
                if pt >= 2:
                    te.wait_ge(c_sl.h, 3 * (pt - 1))
                te.matmul(pssl[pt % 2][:, :], tri[:, :], mskT[:, pt * 8:(pt + 1) * 8],
                          start=True, stop=False)
                te.matmul(pssl[pt % 2][:, :], ones1f[:, :], basrow[0:1, pt * 8:(pt + 1) * 8],
                          start=False, stop=False)
                c_slm.inc(te.matmul(pssl[pt % 2][:, :], ones1f[:, :], pbrow[0:1, 8:16],
                                    start=False, stop=True))

        @blk.vector
        def _(v):
            cinit.inc(v.memset(init0[:, :], 0))
            cinit.inc(v.memset(initb[:, :], 100000))
            for pt in range(32):
                v.wait_ge(c_mm.h, pt + 1)
                v.wait_ge(c_ev.h, c_ev.n)
                c_ev.inc(v.tensor_copy(lgT[pt % 2][:, :], pslg[pt % 2][:, :]))
                v.wait_ge(c_ev.h, c_ev.n)
                c_ev.inc(v.reduce_max(mxc[:, :], lgT[pt % 2][:, :], axis=mybir.AxisListType.X))
                v.wait_ge(c_ev.h, c_ev.n)
                c_ev.inc(v.tensor_scalar(mskT[:, pt * 8:(pt + 1) * 8], lgT[pt % 2][:, :],
                                         mxc[:, :], None, ALU.is_ge))
                v.wait_ge(c_bs.h, pt + 1)
                c_bse.inc(v.tensor_copy(bsrow[0:1, pt * 8:(pt + 1) * 8], psbs))
            v.wait_ge(c_tmm.h, 2)
            c_dsm.inc(v.tensor_copy(basT[:, :], psba))
            c_dsm.inc(v.tensor_copy(ccol[:, 0:1], psct))
            v.wait_ge(c_dsm.h, c_dsm.n)
            c_dsm.inc(v.tensor_scalar(ccol[:, 2:3], ccol[:, 0:1], 511.0, 1.0 / 512.0,
                                      ALU.add, ALU.mult))
            v.wait_ge(c_dsm.h, c_dsm.n)
            c_dsm.inc(v.tensor_copy(ccoli[:, 0:1], ccol[:, 2:3]))
            v.wait_ge(c_dsm.h, c_dsm.n)
            c_dsm.inc(v.tensor_scalar(ccol[:, 1:2], ccoli[:, 0:1], 512.0, None, ALU.mult))
            v.wait_ge(c_tmm.h, 3)
            c_dsm.inc(v.tensor_copy(pbcol[:, 0:1], pspb))
            v.wait_ge(c_dsm.h, c_dsm.n)
            c_dsm.inc(v.tensor_scalar(pbcol[:, 1:2], pbcol[:, 0:1], -1.0, None, ALU.add))
            c_dsm.inc(v.tensor_scalar(cmpe[:, :], P.c("tilebases")[:, :], pbcol[:, 0:1], None, ALU.is_ge))
            v.wait_ge(c_tmm.h, 4)
            v.wait_ge(c_dsm.h, c_dsm.n)
            c_dsm.inc(v.tensor_scalar(erowf[0:1, :], pser, -1.0, None, ALU.add))
            v.wait_ge(c_dsm.h, c_dsm.n)
            c_dsm.inc(v.tensor_copy(eidi[0:1, :], erowf[0:1, :]))
            v.wait_ge(c_tmm.h, 5)
            bci = bass.AP(io128f, 0, [[1, 128], [0, 16]])
            v.wait_ge(c_dsm.h, c_dsm.n)
            c_dsm.inc(v.scalar_tensor_tensor(idxf[:, 0:16], psix, 1024.0, bci,
                                             ALU.mult, ALU.add))
            v.wait_ge(c_dsm.h, c_dsm.n)
            c_dsm.inc(v.scalar_tensor_tensor(idxf[:, 16:32], psix, 4096.0, bci,
                                             ALU.mult, ALU.add))
            v.wait_ge(c_dsm.h, c_dsm.n)
            c_dsm.inc(v.scalar_tensor_tensor(idxf[:, 32:48], psix, 128.0, bci,
                                             ALU.mult, ALU.add))
            v.wait_ge(c_dsm.h, c_dsm.n)
            c_dsm.inc(v.tensor_copy(idxw[:, :], idxf[:, :]))
            for pt in range(32):
                v.wait_ge(c_slm.h, pt + 1)
                v.wait_ge(c_sl.h, c_sl.n)
                if pt >= 1:
                    v.wait_ge(c_scg.h, 16 * pt)
                    v.wait_ge(c_scs.h, 16 * pt)
                c_sl.inc(v.tensor_mul(slotf[:, :], mskT[:, pt * 8:(pt + 1) * 8], pssl[pt % 2][:, :]))
                v.wait_ge(c_sl.h, c_sl.n)
                c_sl.inc(v.reduce_sum(slotc[:, :], slotf[:, :], axis=mybir.AxisListType.X))
                v.wait_ge(c_sl.h, c_sl.n)
                c_sl.inc(v.tensor_copy(sloti[:, :], slotc[:, :]))

        @blk.gpsimd
        def _(gp):
            gp.wait_ge(cinit.h, 2)
            c_dmab.inc(gp.dma_start(out=gidx[:, :].rearrange("(a b) o -> a (b o)", a=128),
                                    in_=init0[:, :]), 16)
            c_dmab.inc(gp.dma_start(out=sidx[:, :].rearrange("(a b) o -> a (b o)", a=128),
                                    in_=initb[:, :]), 16)
            gp.wait_ge(c_bse.h, 32)
            c_dmab.inc(gp.dma_start(out=sc_bs[0:1, :], in_=bsrow[0:1, :]), 16)
            gp.wait_ge(c_dmab.h, 48)
            c_dmab.inc(gp.dma_start(out=bsT[:, :],
                                    in_=sc_bs[0:1, :].rearrange("o (a b) -> (o a) b", a=32)), 16)
            gp.wait_ge(c_dsm.h, 1)
            c_dmab.inc(gp.dma_start(out=sc_bas[0:1, :].rearrange("o (a b) -> (o a) b", a=32),
                                    in_=basT[:, :]), 16)
            gp.wait_ge(c_dmab.h, 80)
            c_dmab.inc(gp.dma_start(out=basrow[0:1, :], in_=sc_bas[0:1, :]), 16)
            gp.wait_ge(c_dsm.h, 7)
            c_dmab.inc(gp.dma_start(out=sc_pb[0:1, 0:16].rearrange("o (b a) -> (o a) b", a=8),
                                    in_=pbcol[:, :], allow_slow_non_contiguous=True), 16)
            gp.wait_ge(c_dmab.h, 112)
            c_dmab.inc(gp.dma_start(out=pbrow[0:1, :], in_=sc_pb[0:1, 0:16]), 16)
            with gp.register(f"r2bc{u}") as rb:
                gp.reg_mov(rb, NTILE * 512 - 1)
                for pt in range(32):
                    gp.wait_ge(c_sl.h, 3 * (pt + 1))
                    c_scg.inc(gp.indirect_dma_start(
                        gidx[:, :], bass.IndirectOffsetOnAxis(sloti[:, 0:1], 0),
                        iota[:, pt:pt + 1], None,
                        bounds_check=rb, oob_is_err=False), 16)
                    c_scs.inc(gp.indirect_dma_start(
                        sidx[:, :], bass.IndirectOffsetOnAxis(sloti[:, 0:1], 0),
                        iota[:, pt:pt + 1], None,
                        bounds_check=rb, oob_is_err=False), 16)
            gp.wait_ge(c_scg.h, c_scg.n)
            gp.wait_ge(c_scs.h, c_scs.n)
    P.barrier()
    P.sem_release(_smark)
    pes.close()


def emit_moe_routed2(P, sc_yT, sc_w1b, sc_w2b, sc_b1b, sc_b2b, gidx, sidx, idxw, eidi, sc_moeT):
    I32 = mybir.dt.int32
    IOOA = bass.IndirectOffsetOnAxis
    id_bf = P.c("id_bf")
    NG = 4
    u = P.uid()
    _smark = P.sem_mark()
    pes = contextlib.ExitStack()
    w1sb = P.sbuf(f"m2{u}w1", [128, 8 * 4096], BF16, es=pes)
    w2sb = P.sbuf(f"m2{u}w2", [128, 32 * 1024], BF16, es=pes)
    b1c = [P.sbuf(f"m2{u}c1{i}", [128, 32], BF16, es=pes) for i in range(2)]
    b2cb = [P.sbuf(f"m2{u}cb{i}", [128, 8], BF16, es=pes) for i in range(2)]
    b2c = [P.sbuf(f"m2{u}c2{i}", [128, 8], F32, es=pes) for i in range(2)]
    gcol = [P.sbuf(f"m2{u}g{i}", [128, 1], I32, es=pes) for i in range(2)]
    scol = [P.sbuf(f"m2{u}s{i}", [128, 1], I32, es=pes) for i in range(2)]
    gth = [P.sbuf(f"m2{u}t{i}", [128, D], BF16, es=pes) for i in range(2)]
    oT = [P.sbuf(f"m2{u}u{i}", [128, D], BF16, es=pes) for i in range(4)]
    xe = [P.sbuf(f"m2{u}x{i}", [128, 8 * 512], BF16, es=pes) for i in range(2)]
    he = P.sbuf(f"m2{u}h", [128, 32 * 512], BF16, es=pes)
    oe = P.sbuf(f"m2{u}o", [128, 8 * 512], BF16, es=pes)
    psw = [P.psum(f"m2{u}p{i}", [128, 512], F32, es=pes) for i in range(3)]
    ptp = [P.psum(f"m2{u}q{i}", [128, 128], BF16, es=pes) for i in range(2)]

    wld = Ctr(P, f"m2{u}wl")
    wl2 = Ctr(P, f"m2{u}w2")
    bl1 = Ctr(P, f"m2{u}b1")
    bl2 = Ctr(P, f"m2{u}b2")
    cbc = Ctr(P, f"m2{u}cc")
    il = Slots(P, f"m2{u}il", 2)
    i2 = Slots(P, f"m2{u}i2", 2)
    gld = Slots(P, f"m2{u}gl", 2)
    scs = Slots(P, f"m2{u}sc", 4)
    c_tr = Ctr(P, f"m2{u}tr")
    c_ev = Ctr(P, f"m2{u}ev")
    c_h = Ctr(P, f"m2{u}hm")
    c_he = Ctr(P, f"m2{u}he")
    c_o = Ctr(P, f"m2{u}om")
    c_oe = Ctr(P, f"m2{u}oe")

    def ps_wait(te, m):
        pm = m - 3
        if pm < 0:
            return
        tp, r = divmod(pm, 40)
        if r < 32:
            te.wait_ge(c_he.h, 32 * tp + r + 1)
        else:
            te.wait_ge(c_oe.h, 8 * tp + (r - 32) + 1)

    with P.block() as blk:
        @blk.gpsimd
        def _(gp):
            def token_gather(t, g):
                GG = 4 * t + g
                if GG >= 2:
                    gp.wait_ge(gld.c(GG).h, 16 * (GG // 2))
                il.c(GG).inc(gp.dma_start(
                    out=gcol[GG % 2][:, :],
                    in_=gidx[t * 512 + g * 128: t * 512 + (g + 1) * 128, 0:1]), 16)
                gp.wait_ge(il.c(GG).h, 16 * (GG // 2 + 1))
                if GG >= 2:
                    pv, gv = divmod(GG - 2, 4)
                    gp.wait_ge(c_tr.h, 64 * pv + 8 * (gv + 1))
                gld.c(GG).inc(gp.indirect_dma_start(
                    gth[GG % 2][:, :], None,
                    sc_yT[:, :], IOOA(gcol[GG % 2][:, 0:1], 0)), 16)

            def scatters(tt):
                for g in range(NG):
                    GG = 4 * tt + g
                    if GG >= 2:
                        pt, pg = divmod(GG - 2, 4)
                        gp.wait_ge(scs.c(pg).h, 16 * (pt + 1))
                    i2.c(GG).inc(gp.dma_start(
                        out=scol[GG % 2][:, :],
                        in_=sidx[tt * 512 + g * 128: tt * 512 + (g + 1) * 128, 0:1]), 16)
                    gp.wait_ge(i2.c(GG).h, 16 * (GG // 2 + 1))
                    gp.wait_ge(c_ev.h, 64 * tt + 32 + 8 * (g + 1))
                    scs.c(g).inc(gp.indirect_dma_start(
                        sc_moeT[:, :], IOOA(scol[GG % 2][:, 0:1], 0),
                        oT[g][:, :], None,
                        bounds_check=rb, oob_is_err=False), 16)

            with gp.register(f"m2bc{u}") as rb:
                gp.reg_mov(rb, S - 1)
                for t in range(NTILE):
                    if t >= 1:
                        gp.wait_ge(c_h.h, 32 * t)
                    for k in range(NDC):
                        wld.inc(gp.indirect_dma_start(
                            w1sb[:, k * 4096:(k + 1) * 4096], None,
                            sc_w1b[:, :], IOOA(idxw[:, t:t + 1], 0),
                            element_offset=k * 128 * 4096), 16)
                    token_gather(t, 0)
                    token_gather(t, 1)
                    if t >= 1:
                        gp.wait_ge(c_o.h, 8 * t)
                    for dfi in range(32):
                        wl2.inc(gp.indirect_dma_start(
                            w2sb[:, dfi * 1024:(dfi + 1) * 1024], None,
                            sc_w2b[:, :], IOOA(idxw[:, 16 + t:17 + t], 0),
                            element_offset=dfi * 128 * 1024), 16)
                    if t >= 2:
                        gp.wait_ge(c_he.h, 32 * (t - 1))
                    bl1.inc(gp.indirect_dma_start(
                        b1c[t % 2][:, :], None,
                        sc_b1b[:, :], IOOA(idxw[:, 32 + t:33 + t], 0)), 16)
                    if t >= 2:
                        gp.wait_ge(cbc.h, t - 1)
                    bl2.inc(gp.indirect_dma_start(
                        b2cb[t % 2][:, :], None,
                        sc_b2b[:, :], IOOA(idxw[:, 32 + t:33 + t], 0)), 16)
                    token_gather(t, 2)
                    token_gather(t, 3)
                    if t >= 1:
                        scatters(t - 1)
                scatters(NTILE - 1)
                for i in range(4):
                    gp.wait_ge(scs.c(i).h, scs.c(i).n)

        @blk.tensor
        def _(te):
            for t in range(NTILE):
                for g in range(NG):
                    GG = 4 * t + g
                    te.wait_ge(gld.c(GG).h, 16 * (GG // 2 + 1))
                    for j in range(NDC):
                        k = 64 * t + 8 * g + j
                        if k >= 2:
                            te.wait_ge(c_ev.h, k - 1)
                        c_tr.inc(te.transpose(ptp[k % 2][:, :],
                                              gth[g % 2][:, j * 128:(j + 1) * 128],
                                              id_bf[:, :]))
                te.wait_ge(wld.h, 16 * 8 * (t + 1))
                te.wait_ge(c_ev.h, 64 * t + 32)
                for df in range(32):
                    m = 40 * t + df
                    ps_wait(te, m)
                    for k in range(NDC):
                        r = te.matmul(psw[m % 3][:, :],
                                      w1sb[:, k * 4096 + df * 128: k * 4096 + df * 128 + 128],
                                      xe[t % 2][:, k * 512:(k + 1) * 512],
                                      start=(k == 0), stop=(k == NDC - 1))
                    c_h.inc(r)
                te.wait_ge(wl2.h, 16 * 32 * (t + 1))
                te.wait_ge(c_he.h, 32 * (t + 1))
                for do in range(8):
                    m = 40 * t + 32 + do
                    ps_wait(te, m)
                    for df in range(32):
                        r = te.matmul(psw[m % 3][:, :],
                                      w2sb[:, df * 1024 + do * 128: df * 1024 + do * 128 + 128],
                                      he[:, df * 512:(df + 1) * 512],
                                      start=(df == 0), stop=(df == 31))
                    c_o.inc(r)
                te.wait_ge(c_oe.h, 8 * (t + 1))
                for g in range(NG):
                    for j in range(NDC):
                        k = 64 * t + 32 + 8 * g + j
                        if k >= 2:
                            te.wait_ge(c_ev.h, k - 1)
                        c_tr.inc(te.transpose(ptp[k % 2][:, :],
                                              oe[:, j * 512 + g * 128: j * 512 + (g + 1) * 128],
                                              id_bf[:, :]))

        @blk.scalar
        def _(sc):
            for t in range(NTILE):
                if t >= 2:
                    sc.wait_ge(c_h.h, 32 * (t - 1))
                for g in range(NG):
                    for j in range(NDC):
                        k = 64 * t + 8 * g + j
                        sc.wait_ge(c_tr.h, k + 1)
                        c_ev.inc(sc.copy(xe[t % 2][:, j * 512 + g * 128: j * 512 + (g + 1) * 128],
                                         ptp[k % 2][:, :]))
                sc.wait_ge(bl1.h, 16 * (t + 1))
                for df in range(32):
                    sc.wait_ge(c_h.h, 32 * t + df + 1)
                    if t >= 1 and df == 0:
                        sc.wait_ge(c_o.h, 8 * t)
                    c_he.inc(sc.activation(he[:, df * 512:(df + 1) * 512],
                                           psw[(40 * t + df) % 3][:, :],
                                           AF.Relu, bias=b1c[t % 2][:, df:df + 1]))
                for g in range(NG):
                    GG = 4 * t + g
                    for j in range(NDC):
                        k = 64 * t + 32 + 8 * g + j
                        sc.wait_ge(c_tr.h, k + 1)
                        if j == 0 and t >= 1:
                            sc.wait_ge(scs.c(g).h, 16 * t)
                        c_ev.inc(sc.copy(oT[g][:, j * 128:(j + 1) * 128],
                                         ptp[k % 2][:, :]))

        @blk.vector
        def _(v):
            for t in range(NTILE):
                v.wait_ge(bl2.h, 16 * (t + 1))
                if t >= 2:
                    v.wait_ge(c_oe.h, 8 * (t - 1))
                cbc.inc(v.tensor_copy(b2c[t % 2][:, :], b2cb[t % 2][:, :]))
                for do in range(8):
                    v.wait_ge(c_o.h, 8 * t + do + 1)
                    if do == 0:
                        v.wait_ge(cbc.h, t + 1)
                        if t >= 1:
                            v.wait_ge(c_tr.h, 64 * t)
                    c_oe.inc(v.tensor_scalar(oe[:, do * 512:(do + 1) * 512],
                                             psw[(40 * t + 32 + do) % 3][:, :],
                                             b2c[t % 2][:, do:do + 1], None, ALU.add))
    P.barrier()
    P.sem_release(_smark)
    pes.close()


# version 11
# speedup vs baseline: 1.1091x; 1.1091x over previous
import contextlib
import numpy as np
import ml_dtypes
import concourse.bass as bass
import concourse.mybir as mybir
from concourse.bass_utils import run_bass_kernel_spmd

F32 = mybir.dt.float32
BF16 = mybir.dt.bfloat16
U8 = mybir.dt.uint8
AF = mybir.ActivationFunctionType
ALU = mybir.AluOpType

S = 4096
D = 1024
NB = 8
NQ = 512
EPS = 1e-5
GAMMA = 0.96875
NPT = S // 128
NDC = D // 128
BF = ml_dtypes.bfloat16


def host_constants():
    c = {}
    c["id_bf"] = np.eye(128, dtype=BF)
    c["id_f32"] = np.eye(128, dtype=np.float32)
    c["ones_1x128_bf"] = np.ones((1, 128), dtype=BF)
    c["ones_1x128_f32"] = np.ones((1, 128), dtype=np.float32)
    c["ones_64x1_bf"] = np.ones((64, 1), dtype=BF)
    c["ones_128x1_bf"] = np.ones((128, 1), dtype=BF)
    c["ones_1x512_bf"] = np.ones((1, 512), dtype=BF)
    c["ones_128x1_f32"] = np.ones((128, 1), dtype=np.float32)
    kk = np.arange(128)
    c["tri128"] = (kk[:, None] <= kk[None, :]).astype(np.float32)
    k32 = np.arange(32)
    c["lt32"] = (k32[:, None] < k32[None, :]).astype(np.float32)
    c["erow"] = (np.arange(8, dtype=np.float32) * 768.0 - 1.0).reshape(1, 8)
    c["lt8"] = (np.arange(8)[:, None] < np.arange(8)[None, :]).astype(np.float32)
    c["ones_32x1_f32"] = np.ones((32, 1), dtype=np.float32)
    c["ones_8x1_f32"] = np.ones((8, 1), dtype=np.float32)
    c["tilebases"] = np.tile((np.arange(16, dtype=np.float32) * 512.0)[None, :], (8, 1))
    c["iota128f"] = np.arange(128, dtype=np.float32).reshape(128, 1)
    c["iota_col"] = np.arange(4096, dtype=np.int32).reshape(32, 128).T.copy()
    k = np.arange(128)[:, None].astype(np.float64)
    u = np.arange(896)[None, :].astype(np.float64)
    c["m01"] = ((u - 384.0) >= k).astype(BF)
    e = u - 384.0 - k
    c["mrm"] = np.where(e >= 0, GAMMA ** np.maximum(e, 0), 0.0).astype(BF)
    u2 = np.arange(512)[None, :].astype(np.float64)
    c["mrp"] = (GAMMA ** (u2 - k + 127.0)).astype(BF)
    sel = np.zeros((128, 8, 4), dtype=np.float32)
    for j in range(8):
        sel[:, j, j // 2] = 1.0 / 16.0
    c["sel"] = sel.reshape(128, 32).astype(BF)
    return c


INPUT_SHAPES = {
    "x_mine": ([S, D], F32), "x_full": ([NB * S, D], F32),
    "id_bf": ([128, 128], BF16), "id_f32": ([128, 128], F32),
    "ones_1x128_bf": ([1, 128], BF16), "ones_1x128_f32": ([1, 128], F32),
    "ones_64x1_bf": ([64, 1], BF16), "ones_128x1_bf": ([128, 1], BF16),
    "ones_1x512_bf": ([1, 512], BF16),
    "m01": ([128, 896], BF16), "mrm": ([128, 896], BF16), "mrp": ([128, 512], BF16),
    "sel": ([128, 32], BF16), "bqkv_vrow": ([1, 2 * D], BF16),
    "ones_128x1_f32": ([128, 1], F32), "tri128": ([128, 128], F32),
    "lt32": ([32, 32], F32), "erow": ([1, 8], F32), "iota_col": ([128, 32], mybir.dt.int32),
    "lt8": ([8, 8], F32), "ones_32x1_f32": ([32, 1], F32), "ones_8x1_f32": ([8, 1], F32),
    "tilebases": ([8, 16], F32), "iota128f": ([128, 1], F32),
    "WqkvT": ([D, 3 * D], F32), "bqkv": ([128, 24], F32),
    "WoT": ([D, D], F32), "bo": ([128, 8], F32),
    "g1": ([128, 8], F32), "be1": ([128, 8], F32),
    "g2": ([128, 8], F32), "be2": ([128, 8], F32),
    "WgT": ([D, 8], F32),
    "W1": ([8, D, 4 * D], F32), "b1": ([128, 256], F32), "b1t": ([8, 128, 32], F32), "b2t": ([8, 128, 8], F32),
    "W2": ([8, 4 * D, D], F32), "b2": ([128, 64], F32),
    "Wq1": ([D, D], F32), "Wk1": ([D, D], F32), "Wv1": ([D, D], F32),
    "theta": ([128, 8], F32),
    "k0u": ([32, 128, 2 * D], U8), "k1u": ([32, 128, 4 * D], U8), "k2u": ([8, 128, 4 * D], U8),
    "s012": ([1, 3], F32),
}

CONST_SBUF = {
    "id_bf": ([128, 128], BF16),
    "ones_1x128_bf": ([1, 128], BF16), "ones_1x128_f32": ([1, 128], F32),
    "ones_128x1_bf": ([128, 1], BF16),
    "ones_1x512_bf": ([1, 512], BF16),
    "m01": ([128, 896], BF16), "mrm": ([128, 896], BF16), "mrp": ([128, 512], BF16),
    "sel": ([128, 32], BF16),
    "ones_128x1_f32": ([128, 1], F32), "tri128": ([128, 128], F32),
    "lt32": ([32, 32], F32), "erow": ([1, 8], F32), "iota_col": ([128, 32], mybir.dt.int32),
    "lt8": ([8, 8], F32), "ones_32x1_f32": ([32, 1], F32), "ones_8x1_f32": ([8, 1], F32),
    "tilebases": ([8, 16], F32), "iota128f": ([128, 1], F32),
    "bqkv": ([128, 24], F32), "bo": ([128, 8], F32),
    "g1": ([128, 8], F32), "be1": ([128, 8], F32),
    "g2": ([128, 8], F32), "be2": ([128, 8], F32),
    "theta": ([128, 8], F32), "s012": ([1, 3], F32),
}


def host_prep(inputs):
    c = host_constants()
    x = np.ascontiguousarray(np.asarray(inputs["L0_attn_x"], np.float32))
    sh = dict(c)
    sh["x_full"] = x.reshape(NB * S, D)
    sh["WqkvT"] = np.ascontiguousarray(np.asarray(inputs["L0_ffn_Wqkv"], np.float32).T)
    sh["bqkv"] = np.ascontiguousarray(np.asarray(inputs["L0_ffn_bqkv"], np.float32).reshape(24, 128).T)
    sh["WoT"] = np.ascontiguousarray(np.asarray(inputs["L0_ffn_Wo"], np.float32).T)
    sh["bqkv_vrow"] = np.ascontiguousarray(np.asarray(inputs["L0_ffn_bqkv"], np.float32)[1024:3072].reshape(1, 2 * D).astype(BF))
    sh["bo"] = np.ascontiguousarray(np.asarray(inputs["L0_ffn_bo"], np.float32).reshape(8, 128).T)
    for n, src in (("g1", "L0_ffn_g1"), ("be1", "L0_ffn_be1"), ("g2", "L0_ffn_g2"), ("be2", "L0_ffn_be2")):
        sh[n] = np.ascontiguousarray(np.asarray(inputs[src], np.float32).reshape(8, 128).T)
    sh["WgT"] = np.ascontiguousarray(np.asarray(inputs["L0_ffn_Wg"], np.float32).T)
    sh["W1"] = np.ascontiguousarray(np.asarray(inputs["L0_ffn_W1"], np.float32))
    sh["b1"] = np.ascontiguousarray(np.asarray(inputs["L0_ffn_b1"], np.float32).reshape(256, 128).T)
    sh["b1t"] = np.ascontiguousarray(np.asarray(inputs["L0_ffn_b1"], np.float32).reshape(8, 32, 128).transpose(0, 2, 1))
    sh["b2t"] = np.ascontiguousarray(np.asarray(inputs["L0_ffn_b2"], np.float32).reshape(8, 8, 128).transpose(0, 2, 1))
    sh["W2"] = np.ascontiguousarray(np.asarray(inputs["L0_ffn_W2"], np.float32))
    sh["b2"] = np.ascontiguousarray(np.asarray(inputs["L0_ffn_b2"], np.float32).reshape(64, 128).T)
    sh["Wq1"] = np.ascontiguousarray(np.asarray(inputs["L1_attn_weight_q"], np.float32))
    sh["Wk1"] = np.ascontiguousarray(np.asarray(inputs["L1_attn_weight_k"], np.float32))
    sh["Wv1"] = np.ascontiguousarray(np.asarray(inputs["L1_attn_weight_v"], np.float32))
    sh["theta"] = np.ascontiguousarray(np.asarray(inputs["L1_attn_theta"], np.float32).reshape(8, 128).T)
    k0 = np.asarray(inputs["L1_ffn_k0"]).astype(np.uint8)
    k1 = np.asarray(inputs["L1_ffn_k1"]).astype(np.uint8)
    k2 = np.asarray(inputs["L1_ffn_k2"]).astype(np.uint8)
    sh["k0u"] = np.ascontiguousarray(k0.reshape(16, 128, 32, 128).transpose(2, 1, 0, 3).reshape(32, 128, 2048))
    sh["k1u"] = np.ascontiguousarray(k1.reshape(32, 128, 32, 128).transpose(2, 1, 0, 3).reshape(32, 128, 4096))
    sh["k2u"] = np.ascontiguousarray(k2.reshape(32, 128, 8, 128).transpose(2, 1, 0, 3).reshape(8, 128, 4096))
    sh["s012"] = np.ascontiguousarray(np.stack([
        np.asarray(inputs["L1_ffn_s0"], np.float32),
        np.asarray(inputs["L1_ffn_s1"], np.float32),
        np.asarray(inputs["L1_ffn_s2"], np.float32)]).reshape(1, 3))
    in_maps = []
    for core in range(NB):
        m = dict(sh)
        m["x_mine"] = np.ascontiguousarray(x[core])
        in_maps.append(m)
    return in_maps


class Ctr:

    def __init__(self, P, name, persistent=False):
        if persistent:
            self.h = P.es.enter_context(P.nc.semaphore(name))
        else:
            self.h = P.nc.alloc_semaphore(name=name)
            P._allsems.append(self.h)
        self.n = 0

    def inc(self, instr, k=1):
        instr.then_inc(self.h, k)
        self.n += k
        return self.n


class Slots:

    def __init__(self, P, name, n):
        self.cs = [Ctr(P, f"{name}_{i}") for i in range(n)]
        self.n = n

    def c(self, i):
        return self.cs[i % self.n]


class Prog:
    def __init__(self):
        self.nc = bass.Bass("TRN2", target_bir_lowering=False)
        self.es = contextlib.ExitStack()
        self.t = {}
        self._bar = None
        self._barn = 0
        self._uid = 0
        self._allsems = []

    def sem_mark(self):
        return len(self._allsems)

    def sem_release(self, mark):
        hs = self._allsems[mark:]
        if not hs:
            return
        del self._allsems[mark:]
        with self.block() as blk:
            @blk.gpsimd
            def _(gp):
                self.nc.clear_and_free_semaphores(hs)
        self.barrier()

    def uid(self):
        self._uid += 1
        return self._uid

    def dram(self, name, shape, dtype, kind=None):
        t = self.nc.dram_tensor(name, list(shape), dtype, kind=kind) if kind \
            else self.nc.dram_tensor(name, list(shape), dtype)
        self.t[name] = t
        return t

    def sbuf(self, name, shape, dtype, es=None):
        t = (es or self.es).enter_context(self.nc.sbuf_tensor(name, list(shape), dtype))
        self.t[name] = t
        return t

    def psum(self, name, shape, dtype=F32, es=None):
        t = (es or self.es).enter_context(self.nc.psum_tensor(name, list(shape), dtype))
        self.t[name] = t
        return t

    def block(self):
        return self.nc.Block()

    def barrier(self, engines=("tensor", "vector", "scalar", "gpsimd", "sync")):
        if self._bar is None:
            self._bar = self.es.enter_context(self.nc.semaphore("bar"))
        tot = self._barn + len(engines)
        b = self._bar
        with self.block() as blk:
            for e in engines:
                def mk():
                    def f(eng):
                        eng.sem_inc(b, 1)
                        eng.wait_ge(b, tot)
                    return f
                getattr(blk, e)(mk())
        self._barn = tot

    def declare_inputs(self, names):
        for n in names:
            shp, dt = INPUT_SHAPES[n]
            self.dram(n, shp, dt, kind="ExternalInput")

    def load_consts(self, names):
        ld = Ctr(self, "constld")
        with self.block() as blk:
            @blk.sync
            def _(sy):
                for n in names:
                    shp, dt = CONST_SBUF[n]
                    sb = self.sbuf("c_" + n, shp, dt)
                    ld.inc(sy.dma_start(out=sb[:, :], in_=self.t[n][:, :]), 16)
        with self.block() as blk:
            @blk.sync
            def _(sy):
                sy.wait_ge(ld.h, ld.n)
        self.barrier()

    def c(self, name):
        return self.t["c_" + name]


def emit_ln0(P, src, xnT=None, xnF=None, tag=""):
    nc = P.nc
    u = P.uid()
    _smark = P.sem_mark()
    pes = contextlib.ExitStack()
    xb = [P.sbuf(f"lnx{u}_{i}", [128, D], F32, es=pes) for i in range(2)]
    sqb = [P.sbuf(f"lnq{u}_{i}", [128, D], F32, es=pes) for i in range(2)]
    stat = [P.sbuf(f"lns{u}_{i}", [128, 8], F32, es=pes) for i in range(2)]
    xnt_tmp = None
    if xnT is None:
        xnt_tmp = [P.sbuf(f"lnt{u}_{i}", [128, D], BF16, es=pes) for i in range(2)]
    ptp = [P.psum(f"lnp{u}_{i}", [128, 128], BF16, es=pes) for i in range(2)]
    ld = Slots(P, f"ln_ld{u}", 2)
    asq = Ctr(P, f"ln_asq{u}")
    dst = Ctr(P, f"ln_dst{u}")
    art = Ctr(P, f"ln_art{u}")
    nrm = Ctr(P, f"ln_nrm{u}")
    ttp = Ctr(P, f"ln_ttp{u}")
    aev = Ctr(P, f"ln_aev{u}")
    id_bf = P.c("id_bf")

    with P.block() as blk:
        @blk.sync
        def _(sy):
            for t in range(NPT):
                if t >= 2:
                    sy.wait_ge(nrm.h, t - 1)
                ld.c(t).inc(sy.dma_start(out=xb[t % 2][:, :], in_=src(t)), 16)

        @blk.scalar
        def _(sc):
            def evacs(t):
                if xnF is None:
                    return
                for j in range(NDC):
                    n = 8 * t + j
                    sc.wait_ge(ttp.h, n + 1)
                    aev.inc(sc.copy(xnF[:, j * S + t * 128: j * S + (t + 1) * 128], ptp[n % 2][:, :]))
            for t in range(NPT):
                sc.wait_ge(ld.c(t).h, 16 * (t // 2 + 1))
                if t >= 2:
                    sc.wait_ge(dst.h, t - 1)
                asq.inc(sc.activation(sqb[t % 2][:, :], xb[t % 2][:, :], AF.Square))
                sc.wait_ge(dst.h, t + 1)
                art.inc(sc.activation(stat[t % 2][:, 7:8], stat[t % 2][:, 6:7], AF.Sqrt))
                if t >= 1:
                    evacs(t - 1)
            evacs(NPT - 1)

        @blk.vector
        def _(v):
            vc = Ctr(P, f"ln_vc{u}")
            for t in range(NPT):
                v.wait_ge(ld.c(t).h, 16 * (t // 2 + 1))
                if t >= 2:
                    v.wait_ge(nrm.h, t - 1)
                vc.inc(v.reduce_sum(stat[t % 2][:, 0:1], xb[t % 2][:, :], axis=mybir.AxisListType.X))
                v.wait_ge(asq.h, t + 1)
                vc.inc(v.reduce_sum(stat[t % 2][:, 1:2], sqb[t % 2][:, :], axis=mybir.AxisListType.X))
                v.wait_ge(vc.h, vc.n)
                vc.inc(v.tensor_scalar(stat[t % 2][:, 2:3], stat[t % 2][:, 0:1], 1.0 / D, None, ALU.mult))
                vc.inc(v.tensor_scalar(stat[t % 2][:, 3:4], stat[t % 2][:, 1:2], 1.0 / D, None, ALU.mult))
                v.wait_ge(vc.h, vc.n)
                vc.inc(v.tensor_mul(stat[t % 2][:, 4:5], stat[t % 2][:, 2:3], stat[t % 2][:, 2:3]))
                v.wait_ge(vc.h, vc.n)
                vc.inc(v.tensor_sub(stat[t % 2][:, 5:6], stat[t % 2][:, 3:4], stat[t % 2][:, 4:5]))
                v.wait_ge(vc.h, vc.n)
                dst.inc(v.tensor_scalar(stat[t % 2][:, 6:7], stat[t % 2][:, 5:6], EPS, None, ALU.add))
                v.wait_ge(art.h, t + 1)
                vc.inc(v.reciprocal(stat[t % 2][:, 6:7], stat[t % 2][:, 7:8]))
                v.wait_ge(vc.h, vc.n)
                dstt = xnT[:, t * D:(t + 1) * D] if xnT is not None else xnt_tmp[t % 2][:, :]
                if xnT is None and t >= 2:
                    v.wait_ge(ttp.h, 8 * (t - 1))
                nrm.inc(v.tensor_scalar(dstt, xb[t % 2][:, :], stat[t % 2][:, 2:3],
                                        stat[t % 2][:, 6:7], ALU.subtract, ALU.mult))

        if xnF is not None:
            @blk.tensor
            def _(te):
                for t in range(NPT):
                    te.wait_ge(nrm.h, t + 1)
                    srct = xnT[:, t * D:(t + 1) * D] if xnT is not None else xnt_tmp[t % 2][:, :]
                    for j in range(NDC):
                        n = 8 * t + j
                        if n >= 2:
                            te.wait_ge(aev.h, n - 1)
                        ttp.inc(te.transpose(ptp[n % 2][:, :], srct[:, j * 128:(j + 1) * 128], id_bf[:, :]))
    P.barrier()
    P.sem_release(_smark)
    pes.close()


def emit_l0_attn(P, xnF, xnT, sc_attn, extra=None):
    nc = P.nc
    u = P.uid()
    _smark = P.sem_mark()
    pes = contextlib.ExitStack()
    vstrip = [P.sbuf(f"at_vs{u}_{i}", [128, 65 * NPT], BF16, es=pes) for i in range(2)]
    negc = P.sbuf(f"at_nc{u}", [128, 1], F32, es=pes)
    ksl = [P.sbuf(f"at_ks{u}_{i}", [128, S], BF16, es=pes) for i in range(2)]
    pbuf = [P.sbuf(f"at_pb{u}_{i}", [128, NQ], BF16, es=pes) for i in range(4)]
    pdtmp = [P.sbuf(f"at_pd{u}_{i}", [128, NQ], BF16, es=pes) for i in range(2)]
    srec = [P.sbuf(f"at_rc{u}_{i}", [1, NQ], F32, es=pes) for i in range(2)]
    srb = [P.sbuf(f"at_rb{u}_{i}", [64, NQ], F32, es=pes) for i in range(2)]
    aout = [P.sbuf(f"at_ao{u}_{i}", [64, NQ], BF16, es=pes) for i in range(2)]
    ps_s = [P.psum(f"at_ps{u}_{i}", [128, NQ], F32, es=pes) for i in range(4)]
    ps_pv = [P.psum(f"at_pv{u}_{i}", [128, NQ], F32, es=pes) for i in range(2)]
    ps_rb = P.psum(f"at_prb{u}", [64, NQ], F32, es=pes)

    c_ic = Ctr(P, f"at_ic{u}")
    c_ksl = Ctr(P, f"at_kl{u}")
    c_mm = Ctr(P, f"at_mm{u}")
    c_exp = Ctr(P, f"at_exp{u}")
    c_msk = Ctr(P, f"at_msk{u}")
    c_pv = Ctr(P, f"at_pv{u}")
    c_vst = Ctr(P, f"at_vst{u}")
    c_rc = Ctr(P, f"at_rc{u}")
    c_rb = Ctr(P, f"at_rb{u}")
    c_arb = Ctr(P, f"at_arb{u}")
    c_do = Ctr(P, f"at_do{u}")
    c_st = Slots(P, f"at_st{u}", 2)

    ones1f = P.c("ones_1x128_f32")
    m01 = P.c("m01")

    groups = [(h, qc) for h in range(16) for qc in range(8)]
    ntile_g = {qc: 4 * (qc + 1) for qc in range(8)}

    def tiles_of(qc):
        return list(range(4 * (qc + 1)))

    def schedule():
        mm = exp = msk = pv = 0
        nd = 0
        sched = []
        for gi, (h, qc) in enumerate(groups):
            tl = []
            for kc in tiles_of(qc):
                diag = kc >= 4 * qc
                mm += 1
                exp += 1
                d = dict(kc=kc, diag=diag, mm=mm, exp=exp, msk=msk, pv=pv)
                if diag:
                    msk += 1
                    d["msk"] = msk
                    d["nd"] = nd
                    nd += 1
                pv += 1
                d["pv"] = pv
                tl.append(d)
            sched.append(dict(h=h, qc=qc, gi=gi, tiles=tl))
        return sched

    SCH = schedule()

    def xnF_head(h, c0, cn):
        j = h // 2
        r = (h % 2) * 64
        return xnF[r:r + 64, j * S + c0: j * S + c0 + cn]

    with P.block() as blk:
        @blk.vector
        def _(v):
            for i in range(2):
                c_vst.inc(v.memset(vstrip[i][:, :].rearrange("p (a b) -> p a b", b=65)[:, :, 64:65], 1.0))
            c_ic.inc(v.memset(negc[:, :], -20.0))
            c_ksl.inc(v.memset(ksl[0][64:128, :], 0.0))
            c_ksl.inc(v.memset(ksl[1][0:64, :], 0.0))
            mme = {hh: SCH[hh * 8 + 7]["tiles"][-1]["mm"] for hh in range(16)}
            pvend = {hh: sum(len(gg["tiles"]) for gg in SCH[: (hh + 1) * 8]) for hh in range(16)}

            def head_copies(h):
                if h >= 2:
                    v.wait_ge(c_mm.h, mme[h - 2])
                    v.wait_ge(c_pv.h, pvend[h - 2])
                r0 = (h % 2) * 64
                c_ksl.inc(v.tensor_copy(ksl[h % 2][r0:r0 + 64, :],
                                        xnF[r0:r0 + 64, (h // 2) * S:(h // 2 + 1) * S]))
                for kc in range(NPT):
                    c_vst.inc(v.tensor_copy(
                        vstrip[h % 2][:, 65 * kc: 65 * kc + 64],
                        xnT[:, kc * D + 64 * h: kc * D + 64 * h + 64]))
            head_copies(0)
            head_copies(1)
            for g in SCH:
                h, qc, gi = g["h"], g["qc"], g["gi"]
                if qc == 6 and h < 15:
                    head_copies(h + 1)
                for tl in g["tiles"]:
                    if tl["diag"]:
                        cofs = qc * NQ - tl["kc"] * 128 + 384
                        v.wait_ge(c_exp.h, tl["exp"])
                        if tl["pv"] > 4:
                            v.wait_ge(c_pv.h, tl["pv"] - 4)
                        c_msk.inc(v.tensor_mul(pbuf[(tl["pv"] - 1) % 4][:, :],
                                               pdtmp[tl["nd"] % 2][:, :],
                                               m01[:, cofs:cofs + NQ]))
                v.wait_ge(c_pv.h, g["tiles"][-1]["pv"])
                c_rc.inc(v.reciprocal(srec[gi % 2][:, :], ps_pv[gi % 2][64:65, :]))
                v.wait_ge(c_arb.h, gi + 1)
                if gi >= 2:
                    v.wait_ge(c_st.c(gi).h, 16 * ((gi - 2) // 2 + 1))
                c_do.inc(v.tensor_mul(aout[gi % 2][:, :], ps_pv[gi % 2][0:64, :], srb[gi % 2][:, :]))

        @blk.scalar
        def _(sc):
            sc.wait_ge(c_ic.h, 1)

            def store_group(gp_):
                hp, qp = gp_["h"], gp_["qc"]
                sc.wait_ge(c_do.h, gp_["gi"] + 1)
                c_st.c(gp_["gi"]).inc(sc.dma_start(
                    out=sc_attn[64 * hp:64 * hp + 64, qp * NQ:(qp + 1) * NQ],
                    in_=aout[gp_["gi"] % 2][:, :]), 16)

            for g in SCH:
                h, qc, gi = g["h"], g["qc"], g["gi"]
                if gi >= 1:
                    store_group(SCH[gi - 1])
                for tl in g["tiles"]:
                    sc.wait_ge(c_mm.h, tl["mm"])
                    if tl["diag"]:
                        if tl["nd"] > 1:
                            sc.wait_ge(c_msk.h, tl["nd"] - 1)
                        dst = pdtmp[tl["nd"] % 2][:, :]
                    else:
                        dst = pbuf[(tl["pv"] - 1) % 4][:, :]
                        if tl["pv"] > 4:
                            sc.wait_ge(c_pv.h, tl["pv"] - 4)
                    c_exp.inc(sc.activation(dst, ps_s[tl["mm"] % 4][:, :], AF.Exp,
                                            scale=0.125, bias=negc[:, 0:1]))
                sc.wait_ge(c_rb.h, gi + 1)
                c_arb.inc(sc.copy(srb[gi % 2][:, :], ps_rb[0:64, :]))
            store_group(SCH[-1])
            sc.wait_ge(c_st.c(0).h, c_st.c(0).n)
            sc.wait_ge(c_st.c(1).h, c_st.c(1).n)

        @blk.tensor
        def _(te):
            RUN = 3
            vst_seen = 0
            rb_pend = None

            def emit_rb(g_):
                te.wait_ge(c_rc.h, g_["gi"] + 1)
                c_rb.inc(te.matmul(ps_rb[0:64, :], ones1f[0:1, 0:64],
                                   srec[g_["gi"] % 2][:, :], start=True, stop=True))
            for g in SCH:
                h, qc, gi = g["h"], g["qc"], g["gi"]
                if qc == 0:
                    te.wait_ge(c_ksl.h, h + 3)
                def emit_s(tl):
                    if tl["mm"] > 4:
                        te.wait_ge(c_exp.h, tl["mm"] - 4)
                    c_mm.inc(te.matmul(ps_s[tl["mm"] % 4][:, :],
                                       ksl[h % 2][:, tl["kc"] * 128:tl["kc"] * 128 + 128],
                                       xnF[0:128, (h // 2) * S + qc * NQ: (h // 2) * S + (qc + 1) * NQ],
                                       start=True, stop=True))

                def emit_pv(tl):
                    nonlocal vst_seen
                    need = 2 + 32 * h + min(tl["kc"] + 1, NPT)
                    if need > vst_seen:
                        te.wait_ge(c_vst.h, need)
                        vst_seen = need
                    if tl["diag"]:
                        te.wait_ge(c_msk.h, tl["msk"])
                    else:
                        te.wait_ge(c_exp.h, tl["exp"])
                    if gi >= 2:
                        te.wait_ge(c_do.h, gi - 1)
                    c_pv.inc(te.matmul(ps_pv[gi % 2][0:65, :],
                                       vstrip[h % 2][:, 65 * tl["kc"]:65 * tl["kc"] + 65],
                                       pbuf[(tl["pv"] - 1) % 4][:, :],
                                       start=(tl["kc"] == 0), stop=(tl["kc"] == g["tiles"][-1]["kc"])))
                pending = []
                tl_all = g["tiles"]
                first = True
                for j0 in range(0, len(tl_all), RUN):
                    for tl in tl_all[j0:j0 + RUN]:
                        emit_s(tl)
                    if first and rb_pend is not None:
                        emit_rb(rb_pend)
                        rb_pend = None
                    first = False
                    for tl in pending:
                        emit_pv(tl)
                    pending = tl_all[j0:j0 + RUN]
                for tl in pending:
                    emit_pv(tl)
                rb_pend = g
            emit_rb(SCH[-1])

        if extra is not None:
            extra(blk)
    P.barrier()
    P.sem_release(_smark)
    pes.close()


if __name__ == "__main__":
    print("library")


def proj_phase(P, tag, tiles, nk, lhs_ap, rhs_ap, evac_engine, evac_fn,
               psum_shape=(128, 512), extra_mm=None, store_fn=None):
    u = P.uid()
    _smark = P.sem_mark()
    pes = contextlib.ExitStack()
    NPS = 4
    ps = [P.psum(f"pj{u}_{i}", list(psum_shape), F32, es=pes) for i in range(NPS)]
    c_mm = Ctr(P, f"pj_mm{u}")
    c_ev = Ctr(P, f"pj_ev{u}")
    with P.block() as blk:
        @blk.tensor
        def _(te):
            for ti, tile in enumerate(tiles):
                if ti >= NPS:
                    te.wait_ge(c_ev.h, ti - NPS + 1)
                for k in range(nk):
                    last = (k == nk - 1) and extra_mm is None
                    r = te.matmul(ps[ti % NPS][:, :], lhs_ap(tile, k), rhs_ap(tile, k),
                                  start=(k == 0), stop=last)
                if extra_mm is not None:
                    r = extra_mm(te, tile, ps[ti % NPS])
                c_mm.inc(r)

        def mkevac(eng):
            for ti, tile in enumerate(tiles):
                eng.wait_ge(c_mm.h, ti + 1)
                r = evac_fn(eng, ti, tile, ps[ti % NPS])
                c_ev.inc(r)
        getattr(blk, evac_engine)(mkevac)

        if store_fn is not None:
            @blk.gpsimd
            def _(gp):
                for ti, tile in enumerate(tiles):
                    gp.wait_ge(c_ev.h, ti + 1)
                    store_fn(gp, ti, tile)
    P.barrier()
    P.sem_release(_smark)
    pes.close()
    return c_ev


def emit_tiny_qproj(P, xnF, wqkv_q, sc_q):
    bq = P.c("bqkv")
    tiles = [dict(do=do, tc=tc) for do in range(8) for tc in range(8)]
    u = P.uid()
    _smark = P.sem_mark()
    pes = contextlib.ExitStack()
    ob = [P.sbuf(f"qp{u}_{i}", [128, NQ], BF16, es=pes) for i in range(2)]
    stc = Slots(P, f"qpst{u}", 2)

    def lhs(tile, k):
        return wqkv_q[:, k * D + tile["do"] * 128: k * D + tile["do"] * 128 + 128]

    def rhs(tile, k):
        return xnF[:, k * S + tile["tc"] * NQ: k * S + (tile["tc"] + 1) * NQ]

    def evac(v, ti, tile, psum):
        if ti >= 2:
            v.wait_ge(stc.c(ti).h, 16 * ((ti - 2) // 2 + 1))
        return v.tensor_scalar(ob[ti % 2][:, :], psum[:, :],
                               bq[:, tile["do"]:tile["do"] + 1], None, ALU.add)

    def store(gp, ti, tile):
        stc.c(ti).inc(gp.dma_start(
            out=sc_q[tile["do"] * 128:(tile["do"] + 1) * 128, tile["tc"] * NQ:(tile["tc"] + 1) * NQ],
            in_=ob[ti % 2][:, :]), 16)

    proj_phase(P, "qproj", tiles, 8, lhs, rhs, "vector", evac, store_fn=store)
    with P.block() as blk:
        @blk.gpsimd
        def _(gp):
            gp.wait_ge(stc.c(0).h, stc.c(0).n)
            gp.wait_ge(stc.c(1).h, stc.c(1).n)
    P.barrier()
    P.sem_release(_smark)
    pes.close()


def emit_tiny_scores(P, xnF, x_full, wqkv_kv, vrow, sc_q, sc_vall, scores):
    sel = P.c("sel")
    bq = P.c("bqkv")
    ones1 = P.c("ones_1x128_bf")
    for bp in range(NB):
        u = P.uid()
        _smark = P.sem_mark()
        pes = contextlib.ExitStack()
        xnbF = P.sbuf(f"tsx{u}", [128, 8 * S], BF16, es=pes)
        emit_ln0(P, lambda t: x_full[bp * S + t * 128: bp * S + (t + 1) * 128, :],
                 xnT=None, xnF=xnbF)
        kb = [P.sbuf(f"tsk{u}_{i}", [128, NQ], BF16, es=pes) for i in range(2)]
        qb = [P.sbuf(f"tsq{u}_{i}", [128, NQ], BF16, es=pes) for i in range(2)]
        eb = P.sbuf(f"tse{u}", [128, 8 * NQ], BF16, es=pes)
        vb = [P.sbuf(f"tsv{u}_{i}", [128, NQ], BF16, es=pes) for i in range(2)]
        psk = [P.psum(f"tspk{u}_{i}", [128, NQ], F32, es=pes) for i in range(3)]
        pss = [P.psum(f"tsps{u}_{i}", [128, 4], F32, es=pes) for i in range(2)]
        psv = [P.psum(f"tspv{u}_{i}", [128, NQ], F32, es=pes) for i in range(2)]
        qld = Slots(P, f"tsqld{u}", 2)
        vst = Slots(P, f"tsvs{u}", 2)
        c_kmm = Ctr(P, f"tskm{u}")
        c_kev = Ctr(P, f"tske{u}")
        c_e = Ctr(P, f"tsee{u}")
        c_smm = Ctr(P, f"tssm{u}")
        c_sev = Ctr(P, f"tssv{u}")
        c_vmm = Ctr(P, f"tsvm{u}")
        c_vev = Ctr(P, f"tsvv{u}")

        with P.block() as blk:
            @blk.sync
            def _(sy):
                for tc in range(8):
                    for j in range(NDC):
                        n = tc * 8 + j
                        if n >= 2:
                            sy.wait_ge(c_e.h, n - 1)
                        qld.c(n).inc(sy.dma_start(
                            out=qb[n % 2][:, :],
                            in_=sc_q[j * 128:(j + 1) * 128, tc * NQ:(tc + 1) * NQ]), 16)

            @blk.tensor
            def _(te):
                for tc in range(8):
                    for j in range(NDC):
                        n = tc * 8 + j
                        if n >= 3:
                            te.wait_ge(c_kev.h, n - 2)
                        for k in range(NDC):
                            te.matmul(psk[n % 3][:, :],
                                      wqkv_kv[:, k * 2048 + j * 128: k * 2048 + j * 128 + 128],
                                      xnbF[:, k * S + tc * NQ: k * S + (tc + 1) * NQ],
                                      start=(k == 0), stop=False)
                        c_kmm.inc(te.matmul(psk[n % 3][:, :],
                                            vrow[0:1, j * 128:(j + 1) * 128],
                                            P.c("ones_1x512_bf")[0:1, :],
                                            start=False, stop=True))
                    for sc_ in range(4):
                        g = tc * 4 + sc_
                        if g >= 2:
                            te.wait_ge(c_sev.h, g - 1)
                        te.wait_ge(c_e.h, tc * 8 + 8)
                        for j in range(NDC):
                            r = te.matmul(pss[g % 2][:, :],
                                          eb[:, j * NQ + sc_ * 128: j * NQ + sc_ * 128 + 128],
                                          sel[:, j * 4:(j + 1) * 4],
                                          start=(j == 0), stop=(j == NDC - 1))
                        c_smm.inc(r)
                    for pt in range(4):
                        for dh in range(2):
                            m = tc * 8 + pt * 2 + dh
                            if m >= 2:
                                te.wait_ge(c_vev.h, m - 1)
                            for k in range(NDC):
                                te.matmul(psv[m % 2][:, :],
                                          xnbF[:, k * S + tc * NQ + pt * 128: k * S + tc * NQ + pt * 128 + 128],
                                          wqkv_kv[:, k * 2048 + 1024 + dh * NQ: k * 2048 + 1024 + (dh + 1) * NQ],
                                          start=(k == 0), stop=False)
                            c_vmm.inc(te.matmul(psv[m % 2][:, :], ones1[:, :],
                                                vrow[0:1, 1024 + dh * NQ: 1024 + (dh + 1) * NQ],
                                                start=False, stop=True))

            @blk.scalar
            def _(sc):
                for tc in range(8):
                    for j in range(NDC):
                        n = tc * 8 + j
                        sc.wait_ge(c_kmm.h, n + 1)
                        if n >= 2:
                            sc.wait_ge(c_e.h, n - 1)
                        c_kev.inc(sc.copy(kb[n % 2][:, :], psk[n % 3][:, :]))
                    for pt in range(4):
                        for dh in range(2):
                            m = tc * 8 + pt * 2 + dh
                            sc.wait_ge(c_vmm.h, m + 1)
                            if m >= 2:
                                sc.wait_ge(vst.c(m).h, 16 * ((m - 2) // 2 + 1))
                            c_vev.inc(sc.copy(vb[m % 2][:, :], psv[m % 2][:, :]))
                            sc.wait_ge(c_vev.h, m + 1)
                            vst.c(m).inc(sc.dma_start(
                                out=sc_vall[bp * S + tc * NQ + pt * 128: bp * S + tc * NQ + pt * 128 + 128,
                                            dh * NQ:(dh + 1) * NQ],
                                in_=vb[m % 2][:, :]), 16)

            @blk.vector
            def _(v):
                for tc in range(8):
                    for j in range(NDC):
                        n = tc * 8 + j
                        v.wait_ge(c_kev.h, n + 1)
                        v.wait_ge(qld.c(n).h, 16 * (n // 2 + 1))
                        if tc >= 1 and j == 0:
                            v.wait_ge(c_smm.h, tc * 4)
                        c_e.inc(v.tensor_mul(eb[:, j * NQ:(j + 1) * NQ],
                                             qb[n % 2][:, :], kb[n % 2][:, :]))
                    for sc_ in range(4):
                        g = tc * 4 + sc_
                        v.wait_ge(c_smm.h, g + 1)
                        sg = g
                        dst = scores.rearrange("p (s h b) -> p s h b", h=4, b=8)[:, sg, :, bp:bp + 1]
                        srcp = pss[g % 2].rearrange("p (h o) -> p h o", o=1)
                        c_sev.inc(v.tensor_copy(dst, srcp))

        with P.block() as blk:
            @blk.scalar
            def _(sc):
                sc.wait_ge(vst.c(0).h, vst.c(0).n)
                sc.wait_ge(vst.c(1).h, vst.c(1).n)
        P.barrier()
        P.sem_release(_smark)
        pes.close()


def emit_tiny_softmax_pv(P, scores, sc_vall, attoutF):
    u = P.uid()
    _smark = P.sem_mark()
    pes = contextlib.ExitStack()
    att = P.sbuf(f"tpa{u}", [128, 32 * 32], F32, es=pes)
    mx = P.sbuf(f"tpm{u}", [128, 4], F32, es=pes)
    ex = [P.sbuf(f"tpe{u}_{i}", [128, 32], F32, es=pes) for i in range(2)]
    exe = [P.sbuf(f"tpx{u}_{i}", [128, 32], F32, es=pes) for i in range(2)]
    sm = P.sbuf(f"tpsm{u}", [128, 4], F32, es=pes)
    rc = P.sbuf(f"tprc{u}", [128, 4], F32, es=pes)
    vtl = [P.sbuf(f"tpv{u}_{i}", [128, D], BF16, es=pes) for i in range(2)]
    acc = [P.sbuf(f"tpc{u}_{i}", [128, D], BF16, es=pes) for i in range(2)]
    ptp = [P.psum(f"tpp{u}_{i}", [128, 128], BF16, es=pes) for i in range(2)]
    vld = Slots(P, f"tpl{u}", 2)
    c_sub = Ctr(P, f"tpsb{u}")
    c_exp = Ctr(P, f"tpxp{u}")
    c_att = Ctr(P, f"tpat{u}")
    c_vdn = Ctr(P, f"tpvd{u}")
    c_pvd = Ctr(P, f"tppd{u}")
    c_tp = Ctr(P, f"tptp{u}")
    c_ev = Ctr(P, f"tpev{u}")
    id_bf = P.c("id_bf")

    vc1 = Ctr(P, f"tpvc{u}")
    with P.block() as blk:
        @blk.vector
        def _(v):
            def tail(sg):
                v.wait_ge(c_exp.h, sg + 1)
                e3 = exe[sg % 2].rearrange("p (h b) -> p h b", b=8)
                v.wait_ge(vc1.h, vc1.n)
                vc1.inc(v.reduce_sum(sm[:, :], e3, axis=mybir.AxisListType.X))
                v.wait_ge(vc1.h, vc1.n)
                vc1.inc(v.reciprocal(rc[:, :], sm[:, :]))
                v.wait_ge(vc1.h, vc1.n)
                rcb = bass.AP(rc, 0, [[4, 128], [1, 4], [0, 8]])
                vc1.inc(v.tensor_tensor(att[:, sg * 32:(sg + 1) * 32].rearrange("p (h b) -> p h b", b=8),
                                        e3, rcb, ALU.mult))
            for sg in range(32):
                b3 = scores[:, sg * 32:(sg + 1) * 32].rearrange("p (h b) -> p h b", b=8)
                v.wait_ge(vc1.h, vc1.n)
                vc1.inc(v.reduce_max(mx[:, :], b3, axis=mybir.AxisListType.X))
                v.wait_ge(vc1.h, vc1.n)
                vc1.inc(v.tensor_tensor(ex[sg % 2].rearrange("p (h b) -> p h b", b=8), b3,
                                        bass.AP(mx, 0, [[4, 128], [1, 4], [0, 8]]), ALU.subtract))
                if sg >= 1:
                    tail(sg - 1)
            tail(31)

        @blk.scalar
        def _(sc):
            for sg in range(32):
                cnt = 5 * sg + 2
                sc.wait_ge(vc1.h, cnt)
                c_exp.inc(sc.activation(exe[sg % 2][:, :], ex[sg % 2][:, :], AF.Exp))

    P.barrier()
    with P.block() as blk:
        vc3 = Ctr(P, f"tpvc3{u}")

        @blk.sync
        def _(sy):
            for sg in range(32):
                for bp in range(NB):
                    n = sg * 8 + bp
                    if n >= 2:
                        sy.wait_ge(vc3.h, (n - 1) * 4)
                    vld.c(n).inc(sy.dma_start(
                        out=vtl[n % 2][:, :],
                        in_=sc_vall[bp * S + sg * 128: bp * S + (sg + 1) * 128, :]), 16)

        @blk.vector
        def _(v):
            for sg in range(32):
                for bp in range(NB):
                    n = sg * 8 + bp
                    v.wait_ge(vld.c(n).h, 16 * (n // 2 + 1))
                    if bp == 1 and sg >= 1:
                        v.wait_ge(c_tp.h, 8 * sg)
                    for h in range(4):
                        cidx = sg * 32 + h * 8 + bp
                        if vc3.n >= 4:
                            v.wait_ge(vc3.h, vc3.n - 3)
                        if bp == 0:
                            r = v.tensor_scalar(acc[0][:, 256 * h:256 * (h + 1)],
                                                vtl[n % 2][:, 256 * h:256 * (h + 1)],
                                                att[:, cidx:cidx + 1], None, ALU.mult)
                        else:
                            r = v.scalar_tensor_tensor(acc[bp % 2][:, 256 * h:256 * (h + 1)],
                                                       vtl[n % 2][:, 256 * h:256 * (h + 1)],
                                                       att[:, cidx:cidx + 1],
                                                       acc[(bp + 1) % 2][:, 256 * h:256 * (h + 1)],
                                                       ALU.mult, ALU.add)
                        vc3.inc(r)

        @blk.tensor
        def _(te):
            for sg in range(32):
                te.wait_ge(vc3.h, (sg + 1) * 32)
                for j in range(NDC):
                    n = sg * 8 + j
                    if n >= 2:
                        te.wait_ge(c_ev.h, n - 1)
                    c_tp.inc(te.transpose(ptp[n % 2][:, :], acc[1][:, j * 128:(j + 1) * 128], id_bf[:, :]))

        @blk.scalar
        def _(sc):
            for sg in range(32):
                for j in range(NDC):
                    n = sg * 8 + j
                    sc.wait_ge(c_tp.h, n + 1)
                    c_ev.inc(sc.copy(attoutF[:, j * S + sg * 128: j * S + (sg + 1) * 128], ptp[n % 2][:, :]))
    P.barrier()
    P.sem_release(_smark)
    pes.close()


def emit_wo_s(P, attoutF, xnF, woT_sb, sc_s):
    bo = P.c("bo")
    u = P.uid()
    _smark = P.sem_mark()
    pes = contextlib.ExitStack()
    ob = [P.sbuf(f"wo{u}_{i}", [128, NQ], BF16, es=pes) for i in range(2)]
    stc = Slots(P, f"wos{u}", 2)
    tiles = [dict(do=do, tc=tc) for tc in range(8) for do in range(8)]

    def lhs(tile, k):
        return woT_sb[:, k * D + tile["do"] * 128: k * D + tile["do"] * 128 + 128]

    def rhs(tile, k):
        return attoutF[:, k * S + tile["tc"] * NQ: k * S + (tile["tc"] + 1) * NQ]

    def evac(v, ti, tile, psum):
        if ti >= 2:
            v.wait_ge(stc.c(ti).h, 16 * ((ti - 2) // 2 + 1))
        return v.scalar_tensor_tensor(
            ob[ti % 2][:, :], psum[:, :], bo[:, tile["do"]:tile["do"] + 1],
            xnF[:, tile["do"] * S + tile["tc"] * NQ: tile["do"] * S + (tile["tc"] + 1) * NQ],
            ALU.add, ALU.add)

    def store(gp, ti, tile):
        stc.c(ti).inc(gp.dma_start(
            out=sc_s[tile["do"] * 128:(tile["do"] + 1) * 128, tile["tc"] * NQ:(tile["tc"] + 1) * NQ],
            in_=ob[ti % 2][:, :]), 16)

    proj_phase(P, "wos", tiles, 8, lhs, rhs, "vector", evac, store_fn=store)
    with P.block() as blk:
        @blk.gpsimd
        def _(gp):
            gp.wait_ge(stc.c(0).h, stc.c(0).n)
            gp.wait_ge(stc.c(1).h, stc.c(1).n)
    P.barrier()
    P.sem_release(_smark)
    pes.close()


def emit_fln(P, src_dram, gname, bname, dst_dram, dst32=None):
    g_ = P.c(gname)
    be_ = P.c(bname)
    ones128 = P.c("ones_128x1_bf")
    ones1f = P.c("ones_1x128_f32")
    u = P.uid()
    _smark = P.sem_mark()
    pes = contextlib.ExitStack()
    sb = [P.sbuf(f"fl{u}_{i}", [128, NQ], BF16, es=pes) for i in range(3)]
    sb2 = [P.sbuf(f"flb{u}_{i}", [128, NQ], BF16, es=pes) for i in range(2)]
    sqb = [P.sbuf(f"flq{u}_{i}", [128, NQ], BF16, es=pes) for i in range(2)]
    murow = P.sbuf(f"flmu{u}", [1, S], F32, es=pes)
    rsrow = P.sbuf(f"flrs{u}", [1, S], F32, es=pes)
    tmpr = P.sbuf(f"fltm{u}", [1, NQ], F32, es=pes)
    ob = [P.sbuf(f"flo{u}_{i}", [128, NQ], BF16, es=pes) for i in range(2)]
    ob32 = [P.sbuf(f"flo32{u}_{i}", [128, NQ], F32, es=pes) for i in range(2)] if dst32 is not None else None
    t1b = [P.sbuf(f"flt{u}_{i}", [128, NQ], F32, es=pes) for i in range(2)]
    ps1 = P.psum(f"flp1{u}", [1, NQ], F32, es=pes)
    ps2 = P.psum(f"flp2{u}", [1, NQ], F32, es=pes)
    psb = [P.psum(f"flpb{u}_{i}", [128, NQ], F32, es=pes) for i in range(4)]
    ld = Slots(P, f"flld{u}", 3)
    ld2 = Slots(P, f"flld2{u}", 2)
    stc = Slots(P, f"flst2{u}", 2)
    stc32 = Slots(P, f"flst3{u}", 2) if dst32 is not None else None
    c_sq = Ctr(P, f"flsq{u}")
    c_s1m = Ctr(P, f"fls1{u}")
    c_sqm = Ctr(P, f"flsqm{u}")
    c_srt = Ctr(P, f"flsr{u}")
    c_bc = Ctr(P, f"flbc{u}")
    vcs = Ctr(P, f"flvc{u}")

    NOPS = 4 if dst32 is not None else 3

    def vcs_smalls_done(tc):
        return 6 * (tc + 1) + 8 * NOPS * tc

    def vcs_norm_done(tc, j):
        return 6 * (tc + 1) + 8 * NOPS * tc + NOPS * (j + 1)

    with P.block() as blk:
        @blk.sync
        def _(sy):
            def load2(tc):
                for j in range(NDC):
                    n = tc * 8 + j
                    if n >= 2:
                        sy.wait_ge(vcs.h, vcs_norm_done((n - 2) // 8, (n - 2) % 8))
                    ld2.c(n).inc(sy.dma_start(
                        out=sb2[n % 2][:, :],
                        in_=src_dram[j * 128:(j + 1) * 128, tc * NQ:(tc + 1) * NQ]), 16)
            for tc in range(8):
                for j in range(NDC):
                    n = tc * 8 + j
                    if n >= 3:
                        sy.wait_ge(c_sqm.h, n - 2)
                    ld.c(n).inc(sy.dma_start(
                        out=sb[n % 3][:, :],
                        in_=src_dram[j * 128:(j + 1) * 128, tc * NQ:(tc + 1) * NQ]), 16)
                if tc >= 1:
                    load2(tc - 1)
            load2(7)

        @blk.scalar
        def _(sc):
            for tc in range(8):
                for j in range(NDC):
                    n = tc * 8 + j
                    sc.wait_ge(ld.c(n).h, 16 * (n // 3 + 1))
                    if n >= 2:
                        sc.wait_ge(c_sqm.h, n - 1)
                    c_sq.inc(sc.activation(sqb[n % 2][:, :], sb[n % 3][:, :], AF.Square))
                sc.wait_ge(vcs.h, vcs_smalls_done(tc) - 1)
                c_srt.inc(sc.activation(tmpr[0:1, :], rsrow[0:1, tc * NQ:(tc + 1) * NQ], AF.Sqrt))

        @blk.tensor
        def _(te):
            for tc in range(8):
                if tc >= 1:
                    te.wait_ge(vcs.h, vcs_smalls_done(tc - 1) - 4)
                for j in range(NDC):
                    n = tc * 8 + j
                    te.wait_ge(ld.c(n).h, 16 * (n // 3 + 1))
                    c_s1m.inc(te.matmul(ps1[0:1, :], ones128[:, :], sb[n % 3][:, :],
                                        start=(j == 0), stop=(j == NDC - 1)))
                    te.wait_ge(c_sq.h, n + 1)
                    c_sqm.inc(te.matmul(ps2[0:1, :], ones128[:, :], sqb[n % 2][:, :],
                                        start=(j == 0), stop=(j == NDC - 1)))
                te.wait_ge(vcs.h, vcs_smalls_done(tc))
                if tc >= 2:
                    te.wait_ge(vcs.h, vcs_norm_done(tc - 2, 7))
                te.matmul(psb[2 * (tc % 2)][:, :], ones1f[:, :], murow[0:1, tc * NQ:(tc + 1) * NQ],
                          start=True, stop=True)
                c_bc.inc(te.matmul(psb[2 * (tc % 2) + 1][:, :], ones1f[:, :], rsrow[0:1, tc * NQ:(tc + 1) * NQ],
                                   start=True, stop=True), 2)

        @blk.vector
        def _(v):
            def normalize(tc):
                v.wait_ge(c_bc.h, 2 * (tc + 1))
                for j in range(NDC):
                    n = tc * 8 + j
                    v.wait_ge(ld2.c(n).h, 16 * (n // 2 + 1))
                    v.wait_ge(vcs.h, vcs.n)
                    vcs.inc(v.tensor_sub(t1b[n % 2][:, :], sb2[n % 2][:, :], psb[2 * (tc % 2)][:, :]))
                    v.wait_ge(vcs.h, vcs.n)
                    vcs.inc(v.tensor_mul(t1b[n % 2][:, :], t1b[n % 2][:, :], psb[2 * (tc % 2) + 1][:, :]))
                    v.wait_ge(vcs.h, vcs.n)
                    if n >= 2:
                        v.wait_ge(stc.c(n).h, 16 * ((n - 2) // 2 + 1))
                    vcs.inc(v.tensor_scalar(ob[n % 2][:, :], t1b[n % 2][:, :],
                                            g_[:, j:j + 1], be_[:, j:j + 1], ALU.mult, ALU.add))
                    if dst32 is not None:
                        v.wait_ge(vcs.h, vcs.n)
                        if n >= 2:
                            v.wait_ge(stc32.c(n).h, 16 * ((n - 2) // 2 + 1))
                        vcs.inc(v.tensor_scalar(ob32[n % 2][:, :], t1b[n % 2][:, :],
                                                g_[:, j:j + 1], be_[:, j:j + 1], ALU.mult, ALU.add))
            for tc in range(8):
                v.wait_ge(c_s1m.h, tc * 8 + 8)
                v.wait_ge(c_sqm.h, tc * 8 + 8)
                v.wait_ge(vcs.h, vcs.n)
                vcs.inc(v.tensor_scalar(murow[0:1, tc * NQ:(tc + 1) * NQ], ps1[0:1, :], 1.0 / D, None, ALU.mult))
                vcs.inc(v.tensor_scalar(rsrow[0:1, tc * NQ:(tc + 1) * NQ], ps2[0:1, :], 1.0 / D, None, ALU.mult))
                v.wait_ge(vcs.h, vcs.n)
                vcs.inc(v.tensor_tensor(tmpr[0:1, :], murow[0:1, tc * NQ:(tc + 1) * NQ],
                                        murow[0:1, tc * NQ:(tc + 1) * NQ], ALU.mult))
                v.wait_ge(vcs.h, vcs.n)
                vcs.inc(v.tensor_sub(rsrow[0:1, tc * NQ:(tc + 1) * NQ], rsrow[0:1, tc * NQ:(tc + 1) * NQ], tmpr[0:1, :]))
                v.wait_ge(vcs.h, vcs.n)
                vcs.inc(v.tensor_scalar(rsrow[0:1, tc * NQ:(tc + 1) * NQ], rsrow[0:1, tc * NQ:(tc + 1) * NQ], EPS, None, ALU.add))
                v.wait_ge(c_srt.h, tc + 1)
                vcs.inc(v.reciprocal(rsrow[0:1, tc * NQ:(tc + 1) * NQ], tmpr[0:1, :]))
                assert vcs.n == vcs_smalls_done(tc), (vcs.n, tc)
                normalize(tc)
                assert vcs.n == vcs_norm_done(tc, 7)

        @blk.gpsimd
        def _(gp):
            for tc in range(8):
                for j in range(NDC):
                    n = tc * 8 + j
                    gp.wait_ge(vcs.h, vcs_norm_done(tc, j))
                    stc.c(n).inc(gp.dma_start(
                        out=dst_dram[j * 128:(j + 1) * 128, tc * NQ:(tc + 1) * NQ],
                        in_=ob[n % 2][:, :]), 16)
                    if dst32 is not None:
                        stc32.c(n).inc(gp.dma_start(
                            out=dst32[j * 128:(j + 1) * 128, tc * NQ:(tc + 1) * NQ],
                            in_=ob32[n % 2][:, :]), 16)
            gp.wait_ge(stc.c(0).h, stc.c(0).n)
            gp.wait_ge(stc.c(1).h, stc.c(1).n)
            if dst32 is not None:
                gp.wait_ge(stc32.c(0).h, stc32.c(0).n)
                gp.wait_ge(stc32.c(1).h, stc32.c(1).n)
    P.barrier()
    P.sem_release(_smark)
    pes.close()


def emit_load_wslab(P, dst_sb, srcs, tag):
    u = P.uid()
    _smark = P.sem_mark()
    c = Ctr(P, f"wl{u}")
    with P.block() as blk:
        @blk.gpsimd
        def _(gp):
            for col0, ap in srcs:
                ncols = ap.shape[-1]
                c.inc(gp.dma_start(out=dst_sb[:, col0:col0 + ncols], in_=ap), 16)
            gp.wait_ge(c.h, c.n)
    P.barrier()
    P.sem_release(_smark)


def emit_moe_masks(P, sc_y, wg_sb, mrow):
    u = P.uid()
    _smark = P.sem_mark()
    pes = contextlib.ExitStack()
    yb = [P.sbuf(f"mk{u}_{i}", [128, 8 * NQ], F32, es=pes) for i in range(2)]
    lgT = [P.sbuf(f"mkl{u}_{i}", [128, 8], F32, es=pes) for i in range(2)]
    mxc = P.sbuf(f"mkm{u}", [128, 1], F32, es=pes)
    mskT = P.sbuf(f"mks{u}", [128, 8], BF16, es=pes)
    maskF = P.sbuf(f"mkf{u}", [8, S], BF16, es=pes)
    pslg = [P.psum(f"mkp{u}_{i}", [128, 8], F32, es=pes) for i in range(2)]
    pstp = P.psum(f"mkt{u}", [8, 128], BF16, es=pes)
    ld = Slots(P, f"mkld{u}", 2)
    c_mm = Ctr(P, f"mkmm{u}")
    c_ev = Ctr(P, f"mkev{u}")
    c_tp = Ctr(P, f"mktp{u}")
    c_ae = Ctr(P, f"mkae{u}")
    id_bf = P.c("id_bf")

    with P.block() as blk:
        @blk.sync
        def _(sy):
            for tc in range(8):
                if tc >= 2:
                    sy.wait_ge(c_mm.h, (tc - 1) * 4)
                ld.c(tc).inc(sy.dma_start(
                    out=yb[tc % 2][:, :].rearrange("p (a c) -> p a c", a=8),
                    in_=sc_y[:, tc * NQ:(tc + 1) * NQ].rearrange("(a p) c -> p a c", p=128)), 16)

        @blk.tensor
        def _(te):
            def tpose(pt):
                te.wait_ge(c_ev.h, 3 * (pt + 1))
                if pt >= 1:
                    te.wait_ge(c_ae.h, pt)
                c_tp.inc(te.transpose(pstp[:, :], mskT[:, :], id_bf[:, :]))
            for tc in range(8):
                te.wait_ge(ld.c(tc).h, 16 * (tc // 2 + 1))
                for ps_ in range(4):
                    pt = tc * 4 + ps_
                    if pt >= 2:
                        te.wait_ge(c_ev.h, 3 * (pt - 1))
                    for j in range(NDC):
                        r = te.matmul(pslg[pt % 2][:, :],
                                      yb[tc % 2][:, j * NQ + ps_ * 128: j * NQ + ps_ * 128 + 128],
                                      wg_sb[:, j * 8:(j + 1) * 8],
                                      start=(j == 0), stop=(j == NDC - 1))
                    c_mm.inc(r)
                    if pt >= 1:
                        tpose(pt - 1)
            tpose(31)

        @blk.vector
        def _(v):
            for pt in range(32):
                v.wait_ge(c_mm.h, pt + 1)
                if pt >= 1:
                    v.wait_ge(c_tp.h, pt)
                v.wait_ge(c_ev.h, c_ev.n)
                c_ev.inc(v.tensor_copy(lgT[pt % 2][:, :], pslg[pt % 2][:, :]))
                v.wait_ge(c_ev.h, c_ev.n)
                c_ev.inc(v.reduce_max(mxc[:, :], lgT[pt % 2][:, :], axis=mybir.AxisListType.X))
                v.wait_ge(c_ev.h, c_ev.n)
                c_ev.inc(v.tensor_scalar(mskT[:, :], lgT[pt % 2][:, :], mxc[:, :], None, ALU.is_ge))

        @blk.scalar
        def _(sc):
            for pt in range(32):
                sc.wait_ge(c_tp.h, pt + 1)
                c_ae.inc(sc.copy(maskF[:, pt * 128:(pt + 1) * 128], pstp[:, :]))

        @blk.gpsimd
        def _(gp):
            gp.wait_ge(c_ae.h, 32)
            cst = Ctr(P, f"mkst{u}")
            cst.inc(gp.dma_start(out=mrow[:, :], in_=maskF[:, :]), 16)
            gp.wait_ge(cst.h, cst.n)
    P.barrier()
    P.sem_release(_smark)
    pes.close()


def emit_moe_experts(P, sc_y, mrow, sc_moe):
    W1 = P.t["W1"]
    W2 = P.t["W2"]
    b1 = P.c("b1")
    b2 = P.c("b2")
    ones1 = P.c("ones_1x128_bf")
    u0 = P.uid()
    pes0 = contextlib.ExitStack()
    w1sb = P.sbuf(f"mew1{u0}", [128, 8 * 4096], BF16, es=pes0)
    w2sb = P.sbuf(f"mew2{u0}", [128, 32 * 1024], BF16, es=pes0)
    for e in range(NB):
        u = P.uid()
        _smark = P.sem_mark()
        pes = contextlib.ExitStack()
        yb = [P.sbuf(f"mey{u}_{i}", [128, 8 * NQ], BF16, es=pes) for i in range(2)]
        mrow_sb = P.sbuf(f"memr{u}", [1, S], BF16, es=pes)
        hsb = P.sbuf(f"meh{u}", [128, 32 * NQ], BF16, es=pes)
        mbb = P.sbuf(f"memb{u}", [128, 8 * NQ], BF16, es=pes)
        oeb = [P.sbuf(f"meo{u}_{i}", [128, NQ], BF16, es=pes) for i in range(2)]
        psh = [P.psum(f"meph{u}_{i}", [128, NQ], F32, es=pes) for i in range(3)]
        pso = [P.psum(f"mepo{u}_{i}", [128, NQ], F32, es=pes) for i in range(4)]
        psm = P.psum(f"mepm{u}", [128, NQ], F32, es=pes)
        wld = Ctr(P, f"mewl{u}")
        yld = Slots(P, f"meyl{u}", 2)
        ost = Slots(P, f"meos{u}", 2)
        c_mbm = Ctr(P, f"memb_m{u}")
        c_mbe = Ctr(P, f"memb_e{u}")
        c_hm = Ctr(P, f"mehm{u}")
        c_he = Ctr(P, f"mehe{u}")
        c_om = Ctr(P, f"meom{u}")
        c_oe = Ctr(P, f"meoe{u}")

        with P.block() as blk:
            @blk.gpsimd
            def _(gp):
                wld.inc(gp.dma_start(out=mrow_sb[0:1, :], in_=mrow[e:e + 1, :]), 16)
                for k in range(NDC):
                    wld.inc(gp.dma_start(out=w1sb[:, k * 4096:(k + 1) * 4096],
                                         in_=W1[e, k * 128:(k + 1) * 128, :]), 16)
                for dfi in range(32):
                    wld.inc(gp.dma_start(out=w2sb[:, dfi * 1024:(dfi + 1) * 1024],
                                         in_=W2[e, dfi * 128:(dfi + 1) * 128, :]), 16)
                for tc in range(8):
                    for do in range(8):
                        n = tc * 8 + do
                        gp.wait_ge(c_oe.h, n + 1)
                        ost.c(n).inc(gp.dma_start(
                            out=sc_moe[do * 128:(do + 1) * 128, tc * NQ:(tc + 1) * NQ],
                            in_=oeb[n % 2][:, :],
                            accum_op=(ALU.bypass if e == 0 else ALU.add)), 16)
                gp.wait_ge(ost.c(0).h, ost.c(0).n)
                gp.wait_ge(ost.c(1).h, ost.c(1).n)

            @blk.sync
            def _(sy):
                for tc in range(8):
                    if tc >= 2:
                        sy.wait_ge(c_hm.h, 32 * (tc - 1))
                    yld.c(tc).inc(sy.dma_start(
                        out=yb[tc % 2][:, :].rearrange("p (a c) -> p a c", a=8),
                        in_=sc_y[:, tc * NQ:(tc + 1) * NQ].rearrange("(a p) c -> p a c", p=128)), 16)

            @blk.tensor
            def _(te):
                te.wait_ge(wld.h, wld.n)
                for tc in range(8):
                    if tc >= 1:
                        te.wait_ge(c_mbe.h, tc)
                    c_mbm.inc(te.matmul(psm[:, :], ones1[:, :], mrow_sb[0:1, tc * NQ:(tc + 1) * NQ],
                                        start=True, stop=True))
                for tc in range(8):
                    te.wait_ge(yld.c(tc).h, 16 * (tc // 2 + 1))
                    for df in range(32):
                        n3 = tc * 32 + df
                        if n3 >= 3:
                            te.wait_ge(c_he.h, n3 - 2)
                        for k in range(NDC):
                            r = te.matmul(psh[n3 % 3][:, :],
                                          wsb[:, k * 4096 + df * 128: k * 4096 + df * 128 + 128],
                                          yb[tc % 2][:, k * NQ:(k + 1) * NQ],
                                          start=(k == 0), stop=(k == NDC - 1))
                        c_hm.inc(r)
                    te.wait_ge(c_he.h, 32 * (tc + 1))
                    for dog in range(2):
                        for do4 in range(4):
                            do = dog * 4 + do4
                            n = tc * 8 + do
                            if n >= 4:
                                te.wait_ge(c_oe.h, n - 3)
                            for df in range(32):
                                r = te.matmul(pso[do4][:, :],
                                              w2sb[:, df * 1024 + do * 128: df * 1024 + do * 128 + 128],
                                              hsb[:, df * NQ:(df + 1) * NQ],
                                              start=(df == 0), stop=(df == 31))
                            c_om.inc(r)

            @blk.scalar
            def _(sc):
                for tc in range(8):
                    sc.wait_ge(c_mbm.h, tc + 1)
                    c_mbe.inc(sc.copy(mbb[:, tc * NQ:(tc + 1) * NQ], psm[:, :]))
                for tc in range(8):
                    for df in range(32):
                        n3 = tc * 32 + df
                        sc.wait_ge(c_hm.h, n3 + 1)
                        if df == 0 and tc >= 1:
                            sc.wait_ge(c_om.h, 8 * tc)
                        c_he.inc(sc.activation(hsb[:, df * NQ:(df + 1) * NQ], psh[n3 % 3][:, :],
                                               AF.Relu, bias=b1[:, e * 32 + df: e * 32 + df + 1]))

            @blk.vector
            def _(v):
                for tc in range(8):
                    for do in range(8):
                        n = tc * 8 + do
                        v.wait_ge(c_om.h, n + 1)
                        v.wait_ge(c_mbe.h, min(tc + 1, 8))
                        if n >= 2:
                            v.wait_ge(ost.c(n).h, 16 * ((n - 2) // 2 + 1))
                        v.wait_ge(c_oe.h, c_oe.n)
                        c_oe.inc(v.scalar_tensor_tensor(oeb[n % 2][:, :], pso[do % 4][:, :],
                                                        b2[:, e * 8 + do: e * 8 + do + 1],
                                                        mbb[:, tc * NQ:(tc + 1) * NQ],
                                                        ALU.add, ALU.mult))
        P.barrier()
        P.sem_release(_smark)
        pes.close()
    pes0.close()


def emit_binop(P, a_dram, b_dram, dst_dram, op=None, out_dt=BF16):
    op = op if op is not None else ALU.add
    u = P.uid()
    _smark = P.sem_mark()
    pes = contextlib.ExitStack()
    ab = [P.sbuf(f"bo{u}a{i}", [128, NQ], BF16, es=pes) for i in range(2)]
    bb = [P.sbuf(f"bo{u}b{i}", [128, NQ], BF16, es=pes) for i in range(2)]
    obf = [P.sbuf(f"bo{u}o{i}", [128, NQ], out_dt, es=pes) for i in range(2)]
    lda = Slots(P, f"bo{u}la", 2)
    ldb = Slots(P, f"bo{u}lb", 2)
    stc = Slots(P, f"bo{u}st", 2)
    c_op = Ctr(P, f"bo{u}op")
    tiles = [(j, tc) for tc in range(8) for j in range(NDC)]
    with P.block() as blk:
        @blk.sync
        def _(sy):
            for n, (j, tc) in enumerate(tiles):
                if n >= 2:
                    sy.wait_ge(c_op.h, n - 1)
                lda.c(n).inc(sy.dma_start(out=ab[n % 2][:, :],
                                          in_=a_dram[j * 128:(j + 1) * 128, tc * NQ:(tc + 1) * NQ]), 16)
                ldb.c(n).inc(sy.dma_start(out=bb[n % 2][:, :],
                                          in_=b_dram[j * 128:(j + 1) * 128, tc * NQ:(tc + 1) * NQ]), 16)

        @blk.vector
        def _(v):
            for n, (j, tc) in enumerate(tiles):
                v.wait_ge(lda.c(n).h, 16 * (n // 2 + 1))
                v.wait_ge(ldb.c(n).h, 16 * (n // 2 + 1))
                if n >= 2:
                    v.wait_ge(stc.c(n).h, 16 * ((n - 2) // 2 + 1))
                c_op.inc(v.tensor_tensor(obf[n % 2][:, :], ab[n % 2][:, :], bb[n % 2][:, :], op))

        @blk.gpsimd
        def _(gp):
            for n, (j, tc) in enumerate(tiles):
                gp.wait_ge(c_op.h, n + 1)
                stc.c(n).inc(gp.dma_start(
                    out=dst_dram[j * 128:(j + 1) * 128, tc * NQ:(tc + 1) * NQ],
                    in_=obf[n % 2][:, :]), 16)
            gp.wait_ge(stc.c(0).h, stc.c(0).n)
            gp.wait_ge(stc.c(1).h, stc.c(1).n)
    P.barrier()
    P.sem_release(_smark)
    pes.close()


def emit_x1(P, sc_s3, x_dram, sc_x1T, b_dram=None):
    u = P.uid()
    _smark = P.sem_mark()
    pes = contextlib.ExitStack()
    s3b = [P.sbuf(f"xo{u}s{i}", [128, 128], BF16, es=pes) for i in range(3)]
    if b_dram is not None:
        s3c = [P.sbuf(f"xo{u}c{i}", [128, 128], BF16, es=pes) for i in range(3)]
        ssum = [P.sbuf(f"xo{u}u{i}", [128, 128], BF16, es=pes) for i in range(3)]
    xb = [P.sbuf(f"xo{u}x{i}", [128, D], F32, es=pes) for i in range(2)]
    s3t = [P.sbuf(f"xo{u}t{i}", [128, D], BF16, es=pes) for i in range(2)]
    ob = [P.sbuf(f"xo{u}o{i}", [128, D], F32, es=pes) for i in range(2)]
    ptp = [P.psum(f"xo{u}p{i}", [128, 128], BF16, es=pes) for i in range(2)]
    lds = Slots(P, f"xo{u}l", 3)
    ldb2 = Slots(P, f"xo{u}lb", 3)
    ldx = Slots(P, f"xo{u}m", 2)
    stc = Slots(P, f"xo{u}st", 2)
    c_tp = Ctr(P, f"xo{u}tp")
    c_ae = Ctr(P, f"xo{u}ae")
    c_ad = Ctr(P, f"xo{u}ad")
    c_pre = Ctr(P, f"xo{u}pr")
    id_bf = P.c("id_bf")
    rdsem = c_pre if b_dram is not None else c_tp
    with P.block() as blk:
        @blk.sync
        def _(sy):
            for t in range(NPT):
                if t >= 2:
                    sy.wait_ge(c_ad.h, t - 1)
                ldx.c(t).inc(sy.dma_start(out=xb[t % 2][:, :],
                                          in_=x_dram[t * 128:(t + 1) * 128, :]), 16)
                for j in range(NDC):
                    n = t * 8 + j
                    if n >= 3:
                        sy.wait_ge(rdsem.h, n - 2)
                    lds.c(n).inc(sy.dma_start(
                        out=s3b[n % 3][:, :],
                        in_=sc_s3[j * 128:(j + 1) * 128, t * 128:(t + 1) * 128]), 16)
                    if b_dram is not None:
                        ldb2.c(n).inc(sy.dma_start(
                            out=s3c[n % 3][:, :],
                            in_=b_dram[j * 128:(j + 1) * 128, t * 128:(t + 1) * 128]), 16)

        @blk.tensor
        def _(te):
            for t in range(NPT):
                for j in range(NDC):
                    n = t * 8 + j
                    if b_dram is not None:
                        te.wait_ge(c_pre.h, n + 1)
                        src = ssum[n % 3]
                    else:
                        te.wait_ge(lds.c(n).h, 16 * (n // 3 + 1))
                        src = s3b[n % 3]
                    if n >= 2:
                        te.wait_ge(c_ae.h, n - 1)
                    c_tp.inc(te.transpose(ptp[n % 2][:, :], src[:, :], id_bf[:, :]))

        @blk.scalar
        def _(sc):
            for t in range(NPT):
                for j in range(NDC):
                    n = t * 8 + j
                    sc.wait_ge(c_tp.h, n + 1)
                    if j == 0 and t >= 2:
                        sc.wait_ge(c_ad.h, t - 1)
                    c_ae.inc(sc.copy(s3t[t % 2][:, j * 128:(j + 1) * 128], ptp[n % 2][:, :]))

        @blk.vector
        def _(v):
            def preadds(t):
                for j in range(NDC):
                    n = t * 8 + j
                    v.wait_ge(lds.c(n).h, 16 * (n // 3 + 1))
                    v.wait_ge(ldb2.c(n).h, 16 * (n // 3 + 1))
                    if n >= 3:
                        v.wait_ge(c_tp.h, n - 2)
                    c_pre.inc(v.tensor_add(ssum[n % 3][:, :], s3b[n % 3][:, :],
                                           s3c[n % 3][:, :]))

            def bigadd(t):
                v.wait_ge(c_ae.h, 8 * (t + 1))
                v.wait_ge(ldx.c(t).h, 16 * (t // 2 + 1))
                if t >= 2:
                    v.wait_ge(stc.c(t).h, 16 * ((t - 2) // 2 + 1))
                c_ad.inc(v.tensor_add(ob[t % 2][:, :], xb[t % 2][:, :], s3t[t % 2][:, :]))
            if b_dram is not None:
                for t in range(NPT):
                    preadds(t)
                    if t > 0:
                        bigadd(t - 1)
                bigadd(NPT - 1)
            else:
                for t in range(NPT):
                    bigadd(t)

        @blk.gpsimd
        def _(gp):
            for t in range(NPT):
                gp.wait_ge(c_ad.h, t + 1)
                stc.c(t).inc(gp.dma_start(out=sc_x1T[t * 128:(t + 1) * 128, :],
                                          in_=ob[t % 2][:, :]), 16)
            gp.wait_ge(stc.c(0).h, stc.c(0).n)
            gp.wait_ge(stc.c(1).h, stc.c(1).n)
    P.barrier()
    P.sem_release(_smark)
    pes.close()


def emit_sfactor_cols(P, scols):
    u = P.uid()
    _smark = P.sem_mark()
    pes = contextlib.ExitStack()
    psc = P.psum(f"sf{u}", [128, 3], F32, es=pes)
    c_m = Ctr(P, f"sfm{u}")
    c_v = Ctr(P, f"sfv{u}")
    with P.block() as blk:
        @blk.tensor
        def _(te):
            c_m.inc(te.matmul(psc[:, :], P.c("ones_1x128_f32")[0:1, :], P.c("s012")[0:1, :],
                              start=True, stop=True))

        @blk.vector
        def _(v):
            v.wait_ge(c_m.h, 1)
            c_v.inc(v.tensor_scalar(scols[:, :], psc[:, :], 2.0, None, ALU.mult))
    P.barrier()
    P.sem_release(_smark)
    pes.close()


def emit_ret_proj(P, xn1F, sc_q1, sc_k1, sc_v1):
    u0 = P.uid()
    pes0 = contextlib.ExitStack()
    tcol = P.sbuf(f"rpt{u0}", [128, 8], F32, es=pes0)
    wsl = P.sbuf(f"rpw{u0}", [128, 8 * 1024], BF16, es=pes0)
    _smark = P.sem_mark()
    c_t = Ctr(P, f"rptc{u0}")
    with P.block() as blk:
        @blk.scalar
        def _(sc):
            c_t.inc(sc.activation(tcol[:, :], P.c("theta")[:, :], AF.Exp, scale=0.0))
    P.barrier()
    P.sem_release(_smark)

    for wname, dst, theta_mult, tlay in (("Wq1", sc_q1, True, False),
                                         ("Wk1", sc_k1, True, False),
                                         ("Wv1", sc_v1, False, True)):
        W = P.t[wname]
        emit_load_wslab(P, wsl, [(k * 1024, W[k * 128:(k + 1) * 128, :]) for k in range(8)], wname)
        u = P.uid()
        _smark = P.sem_mark()
        pes = contextlib.ExitStack()
        ob = [P.sbuf(f"rp{u}_{i}", [128, NQ], BF16, es=pes) for i in range(2)]
        stc = Slots(P, f"rps{u}", 2)
        if not tlay:
            tiles = [dict(do=do, tc=tc) for tc in range(8) for do in range(8)]

            def lhs(tile, k):
                return wsl[:, k * 1024 + tile["do"] * 128: k * 1024 + tile["do"] * 128 + 128]

            def rhs(tile, k):
                return xn1F[:, k * S + tile["tc"] * NQ: k * S + (tile["tc"] + 1) * NQ]

            def evac(v, ti, tile, psum):
                if ti >= 2:
                    v.wait_ge(stc.c(ti).h, 16 * ((ti - 2) // 2 + 1))
                if theta_mult:
                    return v.tensor_scalar(ob[ti % 2][:, :], psum[:, :],
                                           tcol[:, tile["do"]:tile["do"] + 1], None, ALU.mult)
                return v.tensor_copy(ob[ti % 2][:, :], psum[:, :])

            def store(gp, ti, tile):
                stc.c(ti).inc(gp.dma_start(
                    out=dst[tile["do"] * 128:(tile["do"] + 1) * 128, tile["tc"] * NQ:(tile["tc"] + 1) * NQ],
                    in_=ob[ti % 2][:, :]), 16)
        else:
            tiles = [dict(pt=pt, dh=dh) for pt in range(NPT) for dh in range(2)]

            def lhs(tile, k):
                return xn1F[:, k * S + tile["pt"] * 128: k * S + (tile["pt"] + 1) * 128]

            def rhs(tile, k):
                return wsl[:, k * 1024 + tile["dh"] * NQ: k * 1024 + (tile["dh"] + 1) * NQ]

            def evac(v, ti, tile, psum):
                if ti >= 2:
                    v.wait_ge(stc.c(ti).h, 16 * ((ti - 2) // 2 + 1))
                return v.tensor_copy(ob[ti % 2][:, :], psum[:, :])

            def store(gp, ti, tile):
                stc.c(ti).inc(gp.dma_start(
                    out=dst[tile["pt"] * 128:(tile["pt"] + 1) * 128, tile["dh"] * NQ:(tile["dh"] + 1) * NQ],
                    in_=ob[ti % 2][:, :]), 16)

        proj_phase(P, "rp" + wname, tiles, 8, lhs, rhs, "vector", evac, store_fn=store)
        with P.block() as blk:
            @blk.gpsimd
            def _(gp):
                gp.wait_ge(stc.c(0).h, stc.c(0).n)
                gp.wait_ge(stc.c(1).h, stc.c(1).n)
        P.barrier()
        P.sem_release(_smark)
        pes.close()
    pes0.close()


def emit_retention(P, sc_q1, sc_k1, sc_v1, sc_ret):
    mrm = P.c("mrm")
    mrp = P.c("mrp")
    u = P.uid()
    _smark = P.sem_mark()
    pes = contextlib.ExitStack()
    qsb = P.sbuf(f"rt{u}q", [128, 8 * NQ], BF16, es=pes)
    kbuf = [P.sbuf(f"rt{u}k{i}", [128, 8 * 128], BF16, es=pes) for i in range(4)]
    vbuf = [P.sbuf(f"rt{u}v{i}", [128, D], BF16, es=pes) for i in range(4)]
    pT = P.sbuf(f"rt{u}p", [128, 32 * NQ], BF16, es=pes)
    ob = [P.sbuf(f"rt{u}o{i}", [128, NQ], BF16, es=pes) for i in range(2)]
    pss = [P.psum(f"rt{u}s{i}", [128, NQ], F32, es=pes) for i in range(3)]
    psr = [P.psum(f"rt{u}r{i}", [128, NQ], F32, es=pes) for i in range(4)]
    qld = Ctr(P, f"rt{u}ql")
    kld = Slots(P, f"rt{u}kl", 4)
    vld = Slots(P, f"rt{u}vl", 4)
    stc = Slots(P, f"rt{u}st", 2)
    c_smm = Ctr(P, f"rt{u}sm")
    c_dec = Ctr(P, f"rt{u}dc")
    c_pv = Ctr(P, f"rt{u}pv")
    c_ev = Ctr(P, f"rt{u}ev")
    nkc = {qc: 4 * qc + 4 for qc in range(8)}
    ksum = [0]
    for qc in range(8):
        ksum.append(ksum[-1] + nkc[qc])

    with P.block() as blk:
        @blk.sync
        def _(sy):
            nk_ = 0
            nv_ = 0
            for qc in range(8):
                if qc >= 1:
                    sy.wait_ge(c_smm.h, ksum[qc])
                qld.inc(sy.dma_start(
                    out=qsb[:, :].rearrange("p (a c) -> p a c", a=8),
                    in_=sc_q1[:, qc * NQ:(qc + 1) * NQ].rearrange("(a p) c -> p a c", p=128)), 16)
                for kc in range(nkc[qc]):
                    n = nk_ + kc
                    if n >= 4:
                        sy.wait_ge(c_smm.h, n - 3)
                    kld.c(n).inc(sy.dma_start(
                        out=kbuf[n % 4][:, :].rearrange("p (a c) -> p a c", a=8),
                        in_=sc_k1[:, kc * 128:(kc + 1) * 128].rearrange("(a p) c -> p a c", p=128)), 16)
                nk_ += nkc[qc]
                for dh in range(2):
                    for kc in range(nkc[qc]):
                        m = nv_ + dh * nkc[qc] + kc
                        if m >= 4:
                            sy.wait_ge(c_pv.h, 4 * (m - 3))
                        vld.c(m).inc(sy.dma_start(
                            out=vbuf[m % 4][:, :],
                            in_=sc_v1[kc * 128:(kc + 1) * 128, :]), 16)
                nv_ += 2 * nkc[qc]

        @blk.tensor
        def _(te):
            nk_ = 0
            nv_ = 0
            for qc in range(8):
                te.wait_ge(qld.h, 16 * (qc + 1))
                for kc in range(nkc[qc]):
                    n = nk_ + kc
                    te.wait_ge(kld.c(n).h, 16 * (n // 4 + 1))
                    if n >= 3:
                        te.wait_ge(c_dec.h, n - 2)
                    for j in range(NDC):
                        r = te.matmul(pss[n % 3][:, :],
                                      kbuf[n % 4][:, j * 128:(j + 1) * 128],
                                      qsb[:, j * NQ:(j + 1) * NQ],
                                      start=(j == 0), stop=(j == NDC - 1))
                    c_smm.inc(r)
                nk_ += nkc[qc]
                te.wait_ge(c_dec.h, nk_)
                for dh in range(2):
                    for kc in range(nkc[qc]):
                        m = nv_ + dh * nkc[qc] + kc
                        te.wait_ge(vld.c(m).h, 16 * (m // 4 + 1))
                        for dj in range(4):
                            if kc == 0 and (qc * 8 + dh * 4 + dj) >= 4:
                                te.wait_ge(c_ev.h, qc * 8 + dh * 4 + dj - 3)
                            r = te.matmul(psr[dj][:, :],
                                          vbuf[m % 4][:, dh * NQ + dj * 128: dh * NQ + (dj + 1) * 128],
                                          pT[:, kc * NQ:(kc + 1) * NQ],
                                          start=(kc == 0), stop=(kc == nkc[qc] - 1))
                            if dj == 3:
                                c_pv.inc(r, 4)
                nv_ += 2 * nkc[qc]

        @blk.vector
        def _(v):
            nk_ = 0
            for qc in range(8):
                for kc in range(nkc[qc]):
                    n = nk_ + kc
                    v.wait_ge(c_smm.h, n + 1)
                    if qc >= 1 and kc == 0:
                        v.wait_ge(c_pv.h, 8 * ksum[qc])
                    delta = qc * NQ - kc * 128
                    if delta >= 128:
                        scl = float(GAMMA ** (delta - 127))
                        c_dec.inc(v.scalar_tensor_tensor(pT[:, kc * NQ:(kc + 1) * NQ],
                                                         pss[n % 3][:, :], scl, mrp[:, :],
                                                         ALU.mult, ALU.mult))
                    else:
                        cofs = delta + 384
                        c_dec.inc(v.scalar_tensor_tensor(pT[:, kc * NQ:(kc + 1) * NQ],
                                                         pss[n % 3][:, :], 1.0, mrm[:, cofs:cofs + NQ],
                                                         ALU.mult, ALU.mult))
                nk_ += nkc[qc]

        @blk.scalar
        def _(sc):
            for qc in range(8):
                for dh in range(2):
                    for dj in range(4):
                        g = qc * 8 + dh * 4 + dj
                        sc.wait_ge(c_pv.h, 4 * (2 * ksum[qc] + (dh + 1) * nkc[qc]))
                        if g >= 2:
                            sc.wait_ge(stc.c(g).h, 16 * ((g - 2) // 2 + 1))
                        c_ev.inc(sc.copy(ob[g % 2][:, :], psr[dj][:, :]))

        @blk.gpsimd
        def _(gp):
            for qc in range(8):
                for dh in range(2):
                    for dj in range(4):
                        g = qc * 8 + dh * 4 + dj
                        gp.wait_ge(c_ev.h, g + 1)
                        stc.c(g).inc(gp.dma_start(
                            out=sc_ret[(dh * 4 + dj) * 128:(dh * 4 + dj + 1) * 128, qc * NQ:(qc + 1) * NQ],
                            in_=ob[g % 2][:, :]), 16)
            gp.wait_ge(stc.c(0).h, stc.c(0).n)
            gp.wait_ge(stc.c(1).h, stc.c(1).n)
    P.barrier()
    P.sem_release(_smark)
    pes.close()


def emit_smlp_l0(P, xn1F, scols, sc_h1):
    k0 = P.t["k0u"]
    u = P.uid()
    _smark = P.sem_mark()
    pes = contextlib.ExitStack()
    wu = [P.sbuf(f"m0{u}u{i}", [128, 2048], BF16, es=pes) for i in range(2)]
    wd = [P.sbuf(f"m0{u}d{i}", [128, 1024], BF16, es=pes) for i in range(2)]
    ob = [P.sbuf(f"m0{u}o{i}", [128, NQ], BF16, es=pes) for i in range(2)]
    ps = [P.psum(f"m0{u}p{i}", [128, NQ], F32, es=pes) for i in range(4)]
    wld = Slots(P, f"m0{u}wl", 2)
    stc = Slots(P, f"m0{u}st", 2)
    c_sub = Ctr(P, f"m0{u}sb")
    c_mm = Ctr(P, f"m0{u}mm")
    c_ev = Ctr(P, f"m0{u}ev")
    with P.block() as blk:
        @blk.gpsimd
        def _(gp):
            for df in range(32):
                if df >= 2:
                    gp.wait_ge(c_sub.h, 8 * (df - 1))
                wld.c(df).inc(gp.dma_start(out=wu[df % 2][:, :], in_=k0[df, :, :]), 16)

        @blk.vector
        def _(v):
            for df in range(32):
                v.wait_ge(wld.c(df).h, 16 * (df // 2 + 1))
                if df >= 2:
                    v.wait_ge(c_mm.h, 8 * (df - 1))
                for j in range(NDC):
                    v.wait_ge(c_sub.h, c_sub.n)
                    c_sub.inc(v.tensor_sub(wd[df % 2][:, j * 128:(j + 1) * 128],
                                           wu[df % 2][:, j * 128:(j + 1) * 128],
                                           wu[df % 2][:, 1024 + j * 128: 1024 + (j + 1) * 128]))

        @blk.tensor
        def _(te):
            for df in range(32):
                te.wait_ge(c_sub.h, 8 * (df + 1))
                for tc in range(8):
                    n = df * 8 + tc
                    if n >= 4:
                        te.wait_ge(c_ev.h, n - 3)
                    for j in range(NDC):
                        r = te.matmul(ps[n % 4][:, :],
                                      wd[df % 2][:, j * 128:(j + 1) * 128],
                                      xn1F[:, j * S + tc * NQ: j * S + (tc + 1) * NQ],
                                      start=(j == 0), stop=(j == NDC - 1))
                    c_mm.inc(r)

        @blk.scalar
        def _(sc):
            for df in range(32):
                for tc in range(8):
                    n = df * 8 + tc
                    sc.wait_ge(c_mm.h, n + 1)
                    if n >= 2:
                        sc.wait_ge(stc.c(n).h, 16 * ((n - 2) // 2 + 1))
                    c_ev.inc(sc.activation(ob[n % 2][:, :], ps[n % 4][:, :], AF.Relu,
                                           scale=scols[:, 0:1]))
                    sc.wait_ge(c_ev.h, n + 1)
                    stc.c(n).inc(sc.dma_start(
                        out=sc_h1[df * 128:(df + 1) * 128, tc * NQ:(tc + 1) * NQ],
                        in_=ob[n % 2][:, :]), 16)
            sc.wait_ge(stc.c(0).h, stc.c(0).n)
            sc.wait_ge(stc.c(1).h, stc.c(1).n)
    P.barrier()
    P.sem_release(_smark)
    pes.close()


def emit_smlp_mid(P, kt, src_h, scol_ap, dst, nin_chunks, nout_blk, relu):
    u = P.uid()
    _smark = P.sem_mark()
    pes = contextlib.ExitStack()
    hsb = P.sbuf(f"mm{u}h", [128, nin_chunks * 1024], BF16, es=pes)
    wu = [P.sbuf(f"mm{u}u{i}", [128, nin_chunks * 128], BF16, es=pes) for i in range(2)]
    wpm = [P.sbuf(f"mm{u}w{i}", [128, nin_chunks * 128], BF16, es=pes) for i in range(2)]
    ob = [P.sbuf(f"mm{u}o{i}", [128, NQ], BF16, es=pes) for i in range(2)]
    ps = [P.psum(f"mm{u}p{i}", [128, NQ], F32, es=pes) for i in range(4)]
    hld = Ctr(P, f"mm{u}hl")
    wld = Slots(P, f"mm{u}wl", 2)
    stc = Slots(P, f"mm{u}st", 2)
    c_pm = Ctr(P, f"mm{u}pm")
    c_mm = Ctr(P, f"mm{u}mm")
    c_ev = Ctr(P, f"mm{u}ev")
    with P.block() as blk:
        @blk.sync
        def _(sy):
            for tcp in range(4):
                if tcp >= 1:
                    sy.wait_ge(c_mm.h, 2 * nout_blk * tcp)
                hld.inc(sy.dma_start(
                    out=hsb[:, :].rearrange("p (a c) -> p a c", c=1024),
                    in_=src_h[:, tcp * 1024:(tcp + 1) * 1024].rearrange("(a p) c -> p a c", p=128)), 16)

        @blk.gpsimd
        def _(gp):
            for tcp in range(4):
                for do in range(nout_blk):
                    n = tcp * nout_blk + do
                    if n >= 2:
                        gp.wait_ge(c_pm.h, n - 1)
                    wld.c(n).inc(gp.dma_start(out=wu[n % 2][:, :], in_=kt[do, :, :]), 16)

        @blk.vector
        def _(v):
            for tcp in range(4):
                for do in range(nout_blk):
                    n = tcp * nout_blk + do
                    v.wait_ge(wld.c(n).h, 16 * (n // 2 + 1))
                    if n >= 2:
                        v.wait_ge(c_mm.h, 2 * (n - 1))
                    v.wait_ge(c_pm.h, c_pm.n)
                    c_pm.inc(v.tensor_scalar(wpm[n % 2][:, :], wu[n % 2][:, :],
                                             2.0, -1.0, ALU.mult, ALU.add))

        @blk.tensor
        def _(te):
            for tcp in range(4):
                te.wait_ge(hld.h, 16 * (tcp + 1))
                for do in range(nout_blk):
                    n = tcp * nout_blk + do
                    te.wait_ge(c_pm.h, n + 1)
                    for half in range(2):
                        g = n * 2 + half
                        if g >= 4:
                            te.wait_ge(c_ev.h, g - 3)
                        for dfi in range(nin_chunks):
                            r = te.matmul(ps[g % 4][:, :],
                                          wpm[n % 2][:, dfi * 128:(dfi + 1) * 128],
                                          hsb[:, dfi * 1024 + half * NQ: dfi * 1024 + half * NQ + NQ],
                                          start=(dfi == 0), stop=(dfi == nin_chunks - 1))
                        c_mm.inc(r)

        @blk.scalar
        def _(sc):
            for tcp in range(4):
                for do in range(nout_blk):
                    for half in range(2):
                        g = (tcp * nout_blk + do) * 2 + half
                        sc.wait_ge(c_mm.h, g + 1)
                        if g >= 2:
                            sc.wait_ge(stc.c(g).h, 16 * ((g - 2) // 2 + 1))
                        if relu:
                            c_ev.inc(sc.activation(ob[g % 2][:, :], ps[g % 4][:, :], AF.Relu,
                                                   scale=scol_ap))
                        else:
                            c_ev.inc(sc.activation(ob[g % 2][:, :], ps[g % 4][:, :], AF.Copy,
                                                   scale=scol_ap))
                        sc.wait_ge(c_ev.h, g + 1)
                        stc.c(g).inc(sc.dma_start(
                            out=dst[do * 128:(do + 1) * 128,
                                    tcp * 1024 + half * NQ: tcp * 1024 + half * NQ + NQ],
                            in_=ob[g % 2][:, :]), 16)
            sc.wait_ge(stc.c(0).h, stc.c(0).n)
            sc.wait_ge(stc.c(1).h, stc.c(1).n)
    P.barrier()
    P.sem_release(_smark)
    pes.close()


def build_full():
    P = Prog()
    cn = list(CONST_SBUF.keys())
    P.declare_inputs(list(INPUT_SHAPES.keys()))
    out = P.dram("out", [S, D], F32, kind="ExternalOutput")
    sc_attn = P.dram("sc_attn", [D, S], BF16)
    sc_q = P.dram("sc_q", [D, S], BF16)
    sc_vall = P.dram("sc_vall", [NB * S, D], BF16)
    sc_s = P.dram("sc_s", [D, S], BF16)
    sc_y = P.dram("sc_y", [D, S], BF16)
    sc_y32 = P.dram("sc_y32", [D, S], F32)
    mrow = P.dram("mrow", [8, S], BF16)
    sc_moe = P.dram("sc_moe", [D, S], BF16)
    sc_yT = P.dram("sc_yT", [S, D], BF16)
    sc_moeT = P.dram("sc_moeT", [S, D], BF16)
    gidx = P.dram("gidx", [NTILE * 512, 1], mybir.dt.int32)
    sidx = P.dram("sidx", [NTILE * 512, 1], mybir.dt.int32)
    sc_bs = P.dram("sc_bs", [1, 256], F32)
    sc_bas = P.dram("sc_bas", [1, 256], F32)
    sc_pb = P.dram("sc_pb", [1, 256], F32)
    sc_w1b = P.dram("sc_w1b", [8 * D, 4 * D], BF16)
    sc_w2b = P.dram("sc_w2b", [8 * 4 * D, D], BF16)
    sc_b1b = P.dram("sc_b1b", [8 * 128, 1024], BF16)
    sc_b2b = P.dram("sc_b2b", [8 * 128, 1024], BF16)
    sc_s2 = P.dram("sc_s2", [D, S], BF16)
    sc_z = P.dram("sc_z", [D, S], BF16)
    sc_s3 = P.dram("sc_s3", [D, S], BF16)
    sc_x1T = P.dram("sc_x1T", [S, D], F32)
    sc_q1 = P.dram("sc_q1", [D, S], BF16)
    sc_k1 = P.dram("sc_k1", [D, S], BF16)
    sc_v1 = P.dram("sc_v1", [S, D], BF16)
    sc_ret = P.dram("sc_ret", [D, S], BF16)
    sc_h1 = P.dram("sc_h1", [4 * D, S], BF16)
    sc_h2 = P.dram("sc_h2", [4 * D, S], BF16)
    sc_mlp = P.dram("sc_mlp", [D, S], BF16)
    sc_s4 = P.dram("sc_s4", [D, S], BF16)

    P.load_consts(cn)
    idxw = P.sbuf("idxw", [128, 48], mybir.dt.int32)
    eidi = P.sbuf("eidi", [1, 16], mybir.dt.int32)
    wc = make_wconv(P)
    xm = P.t["x_mine"]
    Wt = P.t["WqkvT"]
    Wo = P.t["WoT"]
    Wg = P.t["WgT"]

    with contextlib.ExitStack() as es0:
        xnF = P.sbuf("xnF", [128, 8 * S], BF16, es=es0)
        with contextlib.ExitStack() as es1:
            xnT = P.sbuf("xnT", [128, NPT * D], BF16, es=es1)
            emit_ln0(P, lambda t: xm[t * 128:(t + 1) * 128, :], xnT=xnT, xnF=xnF)
            emit_l0_attn(P, xnF, xnT, sc_attn,
                         extra=wconv_sections(P, wc, sc_w1b, sc_w2b, sc_b1b, sc_b2b))
        with contextlib.ExitStack() as es1:
            wq = P.sbuf("wq", [128, 8 * 1024], BF16, es=es1)
            emit_load_wslab(P, wq, [(k * 1024, Wt[k * 128:(k + 1) * 128, 0:1024]) for k in range(8)], "wq")
            emit_tiny_qproj(P, xnF, wq, sc_q)
        with contextlib.ExitStack() as es1:
            wkv = P.sbuf("wkv", [128, 16 * 1024], BF16, es=es1)
            scores = P.sbuf("scores", [128, 1024], F32, es=es1)
            vrowb = P.sbuf("vrowb", [1, 2 * D], BF16, es=es1)
            emit_load_wslab(P, wkv,
                            [(k * 2048, Wt[k * 128:(k + 1) * 128, 1024:2048]) for k in range(8)] +
                            [(k * 2048 + 1024, Wt[k * 128:(k + 1) * 128, 2048:3072]) for k in range(8)], "wkv")
            emit_load_wslab(P, vrowb, [(0, P.t["bqkv_vrow"][:, :])], "vrow")
            emit_tiny_scores(P, xnF, P.t["x_full"], wkv, vrowb, sc_q, sc_vall, scores)
            with contextlib.ExitStack() as es2:
                attoutF = P.sbuf("attoutF", [128, 8 * S], BF16, es=es2)
                emit_tiny_softmax_pv(P, scores, sc_vall, attoutF)
                with contextlib.ExitStack() as es3:
                    wo = P.sbuf("wo", [128, 8 * 1024], BF16, es=es3)
                    emit_load_wslab(P, wo, [(k * 1024, Wo[k * 128:(k + 1) * 128, :]) for k in range(8)], "wo")
                    emit_wo_s(P, attoutF, xnF, wo, sc_s)
    emit_fln(P, sc_s, "g1", "be1", sc_y, dst32=sc_y32)
    with contextlib.ExitStack() as es1:
        wg = P.sbuf("wg", [128, 64], F32, es=es1)
        emit_load_wslab(P, wg, [(j * 8, Wg[j * 128:(j + 1) * 128, :]) for j in range(8)], "wg")
        emit_moe_route2(P, sc_y32, wg, gidx, sidx, sc_bs, sc_bas, sc_pb, idxw, eidi)
    emit_transpose(P, sc_y, sc_yT, 8, 32)
    emit_wconv_drain(P, wc)
    emit_moe_routed2(P, sc_yT, sc_w1b, sc_w2b, sc_b1b, sc_b2b, gidx, sidx, idxw, eidi, sc_moeT)
    emit_transpose(P, sc_moeT, sc_moe, 32, 8)
    emit_binop(P, sc_y, sc_moe, sc_s2)
    emit_fln(P, sc_s2, "g2", "be2", sc_z)
    emit_x1(P, sc_attn, xm, sc_x1T, b_dram=sc_z)
    scols = P.sbuf("scols", [128, 3], F32)
    with contextlib.ExitStack() as es0:
        xn1F = P.sbuf("xn1F", [128, 8 * S], BF16, es=es0)
        emit_ln0(P, lambda t: sc_x1T[t * 128:(t + 1) * 128, :], xnT=None, xnF=xn1F)
        emit_sfactor_cols(P, scols)
        emit_ret_proj(P, xn1F, sc_q1, sc_k1, sc_v1)
        emit_smlp_l0(P, xn1F, scols, sc_h1)
    emit_retention(P, sc_q1, sc_k1, sc_v1, sc_ret)
    emit_smlp_mid(P, P.t["k1u"], sc_h1, scols[:, 1:2], sc_h2, 32, 32, relu=True)
    emit_smlp_mid(P, P.t["k2u"], sc_h2, scols[:, 2:3], sc_mlp, 32, 8, relu=False)
    emit_x1(P, sc_ret, sc_x1T, out, b_dram=sc_mlp)
    P.nc.finalize()
    return P


_BUILT = None


def kernel(**inputs):
    global _BUILT
    in_maps = host_prep(inputs)
    if _BUILT is None:
        _BUILT = build_full()
    P = _BUILT
    res = run_bass_kernel_spmd(P.nc, in_maps, core_ids=list(range(NB)))
    outs = [np.asarray(res.results[c]["out"], dtype=np.float32) for c in range(NB)]
    return np.stack(outs, axis=0)


NSLOT = 768


def emit_transpose(P, src_dram, dst_dram, nrow_t, ncol_t, dt=BF16):
    u = P.uid()
    _smark = P.sem_mark()
    pes = contextlib.ExitStack()
    sb = [P.sbuf(f"tp{u}s{i}", [128, 128], dt, es=pes) for i in range(3)]
    rowb = [P.sbuf(f"tp{u}r{i}", [128, nrow_t * 128], dt, es=pes) for i in range(2)]
    ptp = [P.psum(f"tp{u}p{i}", [128, 128], dt, es=pes) for i in range(2)]
    lds = Slots(P, f"tp{u}l", 3)
    stc = Slots(P, f"tp{u}st", 2)
    c_tp = Ctr(P, f"tp{u}tp")
    c_ae = Ctr(P, f"tp{u}ae")
    assert dt == BF16
    id_ = P.c("id_bf")
    with P.block() as blk:
        @blk.sync
        def _(sy):
            for j in range(ncol_t):
                for i in range(nrow_t):
                    n = j * nrow_t + i
                    if n >= 3:
                        sy.wait_ge(c_tp.h, n - 2)
                    lds.c(n).inc(sy.dma_start(
                        out=sb[n % 3][:, :],
                        in_=src_dram[i * 128:(i + 1) * 128, j * 128:(j + 1) * 128]), 16)

        @blk.tensor
        def _(te):
            for j in range(ncol_t):
                for i in range(nrow_t):
                    n = j * nrow_t + i
                    te.wait_ge(lds.c(n).h, 16 * (n // 3 + 1))
                    if n >= 2:
                        te.wait_ge(c_ae.h, n - 1)
                    c_tp.inc(te.transpose(ptp[n % 2][:, :], sb[n % 3][:, :], id_[:, :]))

        @blk.scalar
        def _(sc):
            for j in range(ncol_t):
                for i in range(nrow_t):
                    n = j * nrow_t + i
                    sc.wait_ge(c_tp.h, n + 1)
                    if i == 0 and j >= 2:
                        sc.wait_ge(stc.c(j).h, 16 * ((j - 2) // 2 + 1))
                    c_ae.inc(sc.copy(rowb[j % 2][:, i * 128:(i + 1) * 128], ptp[n % 2][:, :]))

        @blk.gpsimd
        def _(gp):
            for j in range(ncol_t):
                gp.wait_ge(c_ae.h, (j + 1) * nrow_t)
                stc.c(j).inc(gp.dma_start(
                    out=dst_dram[j * 128:(j + 1) * 128, :],
                    in_=rowb[j % 2][:, :]), 16)
            gp.wait_ge(stc.c(0).h, stc.c(0).n)
            gp.wait_ge(stc.c(1).h, stc.c(1).n)
    P.barrier()
    P.sem_release(_smark)
    pes.close()


def emit_moe_route(P, sc_y32, wg_sb, gidx, sidx, sc_bs, sc_bas):
    u = P.uid()
    _smark = P.sem_mark()
    pes = contextlib.ExitStack()
    I32 = mybir.dt.int32
    yb = [P.sbuf(f"rr{u}y{i}", [128, 8 * NQ], F32, es=pes) for i in range(2)]
    lgT = [P.sbuf(f"rr{u}l{i}", [128, 8], F32, es=pes) for i in range(2)]
    mxc = P.sbuf(f"rr{u}m", [128, 1], F32, es=pes)
    mskT = P.sbuf(f"rr{u}s", [128, 8 * 32], F32, es=pes)
    bsrow = P.sbuf(f"rr{u}b", [1, 256], F32, es=pes)
    bsT = P.sbuf(f"rr{u}bt", [32, 8], F32, es=pes)
    basT = P.sbuf(f"rr{u}bb", [32, 8], F32, es=pes)
    basrow = P.sbuf(f"rr{u}br", [1, 256], F32, es=pes)
    slotf = P.sbuf(f"rr{u}sf", [128, 8], F32, es=pes)
    slotc = P.sbuf(f"rr{u}sc", [128, 1], F32, es=pes)
    sloti = P.sbuf(f"rr{u}si", [128, 1], I32, es=pes)
    init0 = P.sbuf(f"rr{u}i0", [128, 48], I32, es=pes)
    initb = P.sbuf(f"rr{u}ib", [128, 48], I32, es=pes)
    pslg = [P.psum(f"rr{u}p{i}", [128, 8], F32, es=pes) for i in range(2)]
    psbs = P.psum(f"rr{u}pb", [1, 8], F32, es=pes)
    psba = P.psum(f"rr{u}pa", [32, 8], F32, es=pes)
    pssl = [P.psum(f"rr{u}ps{i}", [128, 8], F32, es=pes) for i in range(2)]
    ld = Slots(P, f"rr{u}ld", 2)
    c_mm = Ctr(P, f"rr{u}mm")
    c_ev = Ctr(P, f"rr{u}ev")
    c_bs = Ctr(P, f"rr{u}bs")
    c_bse = Ctr(P, f"rr{u}be")
    c_dmab = Ctr(P, f"rr{u}db")
    c_ltm = Ctr(P, f"rr{u}lt")
    c_bae = Ctr(P, f"rr{u}ba")
    c_slm = Ctr(P, f"rr{u}sl")
    c_sl = Ctr(P, f"rr{u}s2")
    c_scg = Ctr(P, f"rr{u}scg")
    c_scs = Ctr(P, f"rr{u}scs")
    ones1f = P.c("ones_1x128_f32")
    tri = P.c("tri128")
    lt = P.c("lt32")
    erow = P.c("erow")
    iota = P.c("iota_col")
    o128f = P.c("ones_128x1_f32")

    with P.block() as blk:
        @blk.sync
        def _(sy):
            for tc in range(8):
                if tc >= 2:
                    sy.wait_ge(c_mm.h, (tc - 1) * 4)
                ld.c(tc).inc(sy.dma_start(
                    out=yb[tc % 2][:, :].rearrange("p (a c) -> p a c", a=8),
                    in_=sc_y32[:, tc * NQ:(tc + 1) * NQ].rearrange("(a p) c -> p a c", p=128)), 16)

        @blk.tensor
        def _(te):
            for tc in range(8):
                te.wait_ge(ld.c(tc).h, 16 * (tc // 2 + 1))
                for ps_ in range(4):
                    pt = tc * 4 + ps_
                    if pt >= 2:
                        te.wait_ge(c_ev.h, 3 * (pt - 1))
                    for j in range(NDC):
                        r = te.matmul(pslg[pt % 2][:, :],
                                      yb[tc % 2][:, j * NQ + ps_ * 128: j * NQ + ps_ * 128 + 128],
                                      wg_sb[:, j * 8:(j + 1) * 8],
                                      start=(j == 0), stop=(j == NDC - 1))
                    c_mm.inc(r)
                    te.wait_ge(c_ev.h, 3 * (pt + 1))
                    if pt >= 1:
                        te.wait_ge(c_bse.h, pt)
                    c_bs.inc(te.matmul(psbs, o128f[:, :],
                                       mskT[:, pt * 8:(pt + 1) * 8], start=True, stop=True))
            te.wait_ge(c_dmab.h, 64)
            c_ltm.inc(te.matmul(psba[:, :], lt[:, :], bsT[:, :], start=True, stop=True))
            te.wait_ge(c_dmab.h, 96)
            for pt in range(32):
                if pt >= 2:
                    te.wait_ge(c_sl.h, 3 * (pt - 1))
                te.matmul(pssl[pt % 2][:, :], tri[:, :], mskT[:, pt * 8:(pt + 1) * 8],
                          start=True, stop=False)
                te.matmul(pssl[pt % 2][:, :], ones1f[:, :], basrow[0:1, pt * 8:(pt + 1) * 8],
                          start=False, stop=False)
                c_slm.inc(te.matmul(pssl[pt % 2][:, :], ones1f[:, :], erow[0:1, :],
                                    start=False, stop=True))

        cinit = Ctr(P, f"rr{u}in")

        @blk.vector
        def _(v):
            cinit.inc(v.memset(init0[:, :], 0))
            cinit.inc(v.memset(initb[:, :], 100000))
            for pt in range(32):
                v.wait_ge(c_mm.h, pt + 1)
                v.wait_ge(c_ev.h, c_ev.n)
                c_ev.inc(v.tensor_copy(lgT[pt % 2][:, :], pslg[pt % 2][:, :]))
                v.wait_ge(c_ev.h, c_ev.n)
                c_ev.inc(v.reduce_max(mxc[:, :], lgT[pt % 2][:, :], axis=mybir.AxisListType.X))
                v.wait_ge(c_ev.h, c_ev.n)
                c_ev.inc(v.tensor_scalar(mskT[:, pt * 8:(pt + 1) * 8], lgT[pt % 2][:, :],
                                         mxc[:, :], None, ALU.is_ge))
                v.wait_ge(c_bs.h, pt + 1)
                c_bse.inc(v.tensor_copy(bsrow[0:1, pt * 8:(pt + 1) * 8], psbs[0:1, :]))
            v.wait_ge(c_ltm.h, 1)
            c_bae.inc(v.tensor_copy(basT[:, :], psba[:, :]))
            for pt in range(32):
                v.wait_ge(c_slm.h, pt + 1)
                v.wait_ge(c_sl.h, c_sl.n)
                if pt >= 1:
                    v.wait_ge(c_scg.h, 16 * pt)
                    v.wait_ge(c_scs.h, 16 * pt)
                c_sl.inc(v.tensor_mul(slotf[:, :], mskT[:, pt * 8:(pt + 1) * 8], pssl[pt % 2][:, :]))
                v.wait_ge(c_sl.h, c_sl.n)
                c_sl.inc(v.reduce_sum(slotc[:, :], slotf[:, :], axis=mybir.AxisListType.X))
                v.wait_ge(c_sl.h, c_sl.n)
                c_sl.inc(v.tensor_copy(sloti[:, :], slotc[:, :]))

        @blk.gpsimd
        def _(gp):
            gp.wait_ge(cinit.h, 2)
            c_dmab.inc(gp.dma_start(out=gidx[:, :].rearrange("(a b) o -> a (b o)", a=128),
                                    in_=init0[:, :]), 16)
            c_dmab.inc(gp.dma_start(out=sidx[:, :].rearrange("(a b) o -> a (b o)", a=128),
                                    in_=initb[:, :]), 16)
            gp.wait_ge(c_bse.h, 32)
            c_dmab.inc(gp.dma_start(out=sc_bs[0:1, :], in_=bsrow[0:1, :]), 16)
            gp.wait_ge(c_dmab.h, 48)
            c_dmab.inc(gp.dma_start(out=bsT[:, :],
                                    in_=sc_bs[0:1, :].rearrange("o (a b) -> (o a) b", a=32)), 16)
            gp.wait_ge(c_bae.h, 1)
            c_dmab.inc(gp.dma_start(out=sc_bas[0:1, :].rearrange("o (a b) -> (o a) b", a=32),
                                    in_=basT[:, :]), 16)
            gp.wait_ge(c_dmab.h, 80)
            c_dmab.inc(gp.dma_start(out=basrow[0:1, :], in_=sc_bas[0:1, :]), 16)
            with gp.register(f"rrbc{u}") as rb:
                gp.reg_mov(rb, 8 * NSLOT - 1)
                for pt in range(32):
                    gp.wait_ge(c_sl.h, 3 * (pt + 1))
                    c_scg.inc(gp.indirect_dma_start(
                        gidx[:, :], bass.IndirectOffsetOnAxis(sloti[:, 0:1], 0),
                        iota[:, pt:pt + 1], None,
                        bounds_check=rb, oob_is_err=False), 16)
                    c_scs.inc(gp.indirect_dma_start(
                        sidx[:, :], bass.IndirectOffsetOnAxis(sloti[:, 0:1], 0),
                        iota[:, pt:pt + 1], None,
                        bounds_check=rb, oob_is_err=False), 16)
            gp.wait_ge(c_scg.h, c_scg.n)
            gp.wait_ge(c_scs.h, c_scs.n)
    P.barrier()
    P.sem_release(_smark)
    pes.close()


def emit_moe_routed(P, sc_yT, gidx, sidx, sc_moeT):
    W1 = P.t["W1"]
    W2 = P.t["W2"]
    b1 = P.c("b1")
    b2 = P.c("b2")
    I32 = mybir.dt.int32
    NG = NSLOT // 128
    HN = NSLOT // 2
    u0 = P.uid()
    pes0 = contextlib.ExitStack()
    wsb = P.sbuf(f"rx{u0}w", [128, 8 * 4096], BF16, es=pes0)
    id_bf = P.c("id_bf")
    for e in range(NB):
        u = P.uid()
        _smark = P.sem_mark()
        pes = contextlib.ExitStack()
        gcol = [P.sbuf(f"rx{u}g{i}", [128, 1], I32, es=pes) for i in range(2)]
        scol = [P.sbuf(f"rx{u}s{i}", [128, 1], I32, es=pes) for i in range(2)]
        gth = [P.sbuf(f"rx{u}t{i}", [128, D], BF16, es=pes) for i in range(2)]
        xe = P.sbuf(f"rx{u}x", [128, 8 * NSLOT], BF16, es=pes)
        he = P.sbuf(f"rx{u}h", [128, 32 * NSLOT], BF16, es=pes)
        oe = P.sbuf(f"rx{u}o", [128, 8 * NSLOT], BF16, es=pes)
        oT = [P.sbuf(f"rx{u}ot{i}", [128, D], BF16, es=pes) for i in range(2)]
        psw = [P.psum(f"rx{u}p{i}", [128, HN], F32, es=pes) for i in range(3)]
        ptp = [P.psum(f"rx{u}q{i}", [128, 128], BF16, es=pes) for i in range(2)]
        wld = Ctr(P, f"rx{u}wl")
        ild = Slots(P, f"rx{u}il", 2)
        gld = Slots(P, f"rx{u}gl", 2)
        c_ti = Ctr(P, f"rx{u}ti")
        c_te = Ctr(P, f"rx{u}te")
        c_h = Ctr(P, f"rx{u}hm")
        c_he = Ctr(P, f"rx{u}he")
        c_o = Ctr(P, f"rx{u}om")
        c_oe = Ctr(P, f"rx{u}oe")
        c_to = Ctr(P, f"rx{u}to")
        c_toe = Ctr(P, f"rx{u}tv")
        scsl = Slots(P, f"rx{u}sx", 2)

        with P.block() as blk:
            @blk.gpsimd
            def _(gp):
                for k in range(NDC):
                    wld.inc(gp.dma_start(out=wsb[:, k * 4096:(k + 1) * 4096],
                                         in_=W1[e, k * 128:(k + 1) * 128, :]), 16)
                for g in range(NG):
                    if g >= 2:
                        gp.wait_ge(gld.c(g).h, 16 * ((g - 2) // 2 + 1))
                    ild.c(g).inc(gp.dma_start(
                        out=gcol[g % 2][:, :],
                        in_=gidx[e * NSLOT + g * 128: e * NSLOT + (g + 1) * 128, 0:1]), 16)
                    gp.wait_ge(ild.c(g).h, 16 * (g // 2 + 1))
                    if g >= 2:
                        gp.wait_ge(c_ti.h, 8 * (g - 1))
                    gld.c(g).inc(gp.indirect_dma_start(
                        gth[g % 2][:, :], None,
                        sc_yT[:, :], bass.IndirectOffsetOnAxis(gcol[g % 2][:, 0:1], 0)), 16)
                gp.wait_ge(c_h.h, 64)
                for dfi in range(32):
                    wld.inc(gp.dma_start(out=wsb[:, dfi * 1024:(dfi + 1) * 1024],
                                         in_=W2[e, dfi * 128:(dfi + 1) * 128, :]), 16)
                with gp.register(f"rxbc{u}") as rb:
                    gp.reg_mov(rb, S - 1)
                    for g in range(NG):
                        n = NG + g
                        if g >= 2:
                            gp.wait_ge(scsl.c(g).h, 16 * ((g - 2) // 2 + 1))
                        ild.c(n).inc(gp.dma_start(
                            out=scol[g % 2][:, :],
                            in_=sidx[e * NSLOT + g * 128: e * NSLOT + (g + 1) * 128, 0:1]), 16)
                        gp.wait_ge(ild.c(n).h, 16 * (n // 2 + 1))
                        gp.wait_ge(c_toe.h, 8 * (g + 1))
                        scsl.c(g).inc(gp.indirect_dma_start(
                            sc_moeT[:, :], bass.IndirectOffsetOnAxis(scol[g % 2][:, 0:1], 0),
                            oT[g % 2][:, :], None,
                            bounds_check=rb, oob_is_err=False), 16)
                gp.wait_ge(scsl.c(0).h, scsl.c(0).n)
                gp.wait_ge(scsl.c(1).h, scsl.c(1).n)

            @blk.tensor
            def _(te):
                for g in range(NG):
                    te.wait_ge(gld.c(g).h, 16 * (g // 2 + 1))
                    for j in range(NDC):
                        n = g * 8 + j
                        if n >= 2:
                            te.wait_ge(c_te.h, n - 1)
                        c_ti.inc(te.transpose(ptp[n % 2][:, :], gth[g % 2][:, j * 128:(j + 1) * 128],
                                              id_bf[:, :]))
                te.wait_ge(wld.h, 16 * NDC)
                for df in range(32):
                    for hf in range(2):
                        n = df * 2 + hf
                        if n >= 3:
                            te.wait_ge(c_he.h, n - 2)
                        if n == 0:
                            te.wait_ge(c_te.h, 8 * NG)
                        for k in range(NDC):
                            r = te.matmul(psw[n % 3][:, :],
                                          wsb[:, k * 4096 + df * 128: k * 4096 + df * 128 + 128],
                                          xe[:, k * NSLOT + hf * HN: k * NSLOT + hf * HN + HN],
                                          start=(k == 0), stop=(k == NDC - 1))
                        c_h.inc(r)
                te.wait_ge(wld.h, 16 * (NDC + 32))
                for do in range(8):
                    for hf in range(2):
                        n = do * 2 + hf
                        if n >= 3:
                            te.wait_ge(c_oe.h, n - 2)
                        if n == 0:
                            te.wait_ge(c_he.h, 64)
                        for df in range(32):
                            r = te.matmul(psw[n % 3][:, :],
                                          wsb[:, df * 1024 + do * 128: df * 1024 + do * 128 + 128],
                                          he[:, df * NSLOT + hf * HN: df * NSLOT + hf * HN + HN],
                                          start=(df == 0), stop=(df == 31))
                        c_o.inc(r)
                for g in range(NG):
                    te.wait_ge(c_oe.h, 16)
                    for j in range(NDC):
                        n = g * 8 + j
                        if g >= 2 and j == 0:
                            te.wait_ge(scsl.c(g).h, 16 * ((g - 2) // 2 + 1))
                        if n >= 2:
                            te.wait_ge(c_toe.h, n - 1)
                        c_to.inc(te.transpose(ptp[n % 2][:, :], oe[:, j * NSLOT + g * 128: j * NSLOT + (g + 1) * 128],
                                              id_bf[:, :]))

            @blk.scalar
            def _(sc):
                for g in range(NG):
                    for j in range(NDC):
                        n = g * 8 + j
                        sc.wait_ge(c_ti.h, n + 1)
                        c_te.inc(sc.copy(xe[:, j * NSLOT + g * 128: j * NSLOT + (g + 1) * 128],
                                         ptp[n % 2][:, :]))
                for df in range(32):
                    for hf in range(2):
                        n = df * 2 + hf
                        sc.wait_ge(c_h.h, n + 1)
                        c_he.inc(sc.activation(he[:, df * NSLOT + hf * HN: df * NSLOT + hf * HN + HN],
                                               psw[n % 3][:, :], AF.Relu,
                                               bias=b1[:, e * 32 + df: e * 32 + df + 1]))
                for g in range(NG):
                    for j in range(NDC):
                        n = g * 8 + j
                        sc.wait_ge(c_to.h, n + 1)
                        c_toe.inc(sc.copy(oT[g % 2][:, j * 128:(j + 1) * 128], ptp[n % 2][:, :]))

            @blk.vector
            def _(v):
                for do in range(8):
                    for hf in range(2):
                        n = do * 2 + hf
                        v.wait_ge(c_o.h, n + 1)
                        v.wait_ge(c_oe.h, c_oe.n)
                        c_oe.inc(v.tensor_scalar(oe[:, do * NSLOT + hf * HN: do * NSLOT + hf * HN + HN],
                                                 psw[n % 3][:, :],
                                                 b2[:, e * 8 + do: e * 8 + do + 1], None, ALU.add))
        P.barrier()
        P.sem_release(_smark)
        pes.close()
    pes0.close()


NTILE = 16


def make_wconv(P):
    u = P.uid()
    wc = {"mark": P.sem_mark(), "pes": contextlib.ExitStack()}
    wc["buf"] = [P.sbuf(f"wc{u}_{i}", [128, 2048], BF16, es=wc["pes"]) for i in range(2)]
    wc["ld"] = Slots(P, f"wc{u}l", 2)
    wc["st"] = Slots(P, f"wc{u}s", 2)
    return wc


def wconv_sections(P, wc, sc_w1b, sc_w2b, sc_b1b, sc_b2b):
    W1f = P.t["W1"][:, :, :].rearrange("e r c -> (e r c)").rearrange("(j p c) -> j p c", p=128, c=2048)
    W2f = P.t["W2"][:, :, :].rearrange("e r c -> (e r c)").rearrange("(j p c) -> j p c", p=128, c=2048)
    o1 = sc_w1b[:, :].rearrange("r c -> (r c)").rearrange("(j p c) -> j p c", p=128, c=2048)
    o2 = sc_w2b[:, :].rearrange("r c -> (r c)").rearrange("(j p c) -> j p c", p=128, c=2048)
    buf, ld, st = wc["buf"], wc["ld"], wc["st"]
    jobs = [(W1f[j, :, :], o1[j, :, :], 2048, 2048) for j in range(128)] + \
           [(W2f[j, :, :], o2[j, :, :], 2048, 2048) for j in range(128)]
    for e in range(8):
        jobs.append((P.t["b1t"][e, :, :], sc_b1b[e * 128:(e + 1) * 128, :], 32, 1024))
        jobs.append((P.t["b2t"][e, :, :], sc_b2b[e * 128:(e + 1) * 128, :], 8, 1024))

    def attach(blk):
        @blk.gpsimd
        def _(gp):
            for n, (iap, oap, wl, ws) in enumerate(jobs):
                if n >= 2:
                    gp.wait_ge(st.c(n).h, 16 * ((n - 2) // 2 + 1))
                ld.c(n).inc(gp.dma_start(out=buf[n % 2][:, 0:wl], in_=iap), 16)

        @blk.sync
        def _(sy):
            for n, (iap, oap, wl, ws) in enumerate(jobs):
                sy.wait_ge(ld.c(n).h, 16 * (n // 2 + 1))
                st.c(n).inc(sy.dma_start(out=oap, in_=buf[n % 2][:, 0:ws]), 16)
    return attach


def emit_wconv_drain(P, wc):
    st = wc["st"]
    with P.block() as blk:
        @blk.gpsimd
        def _(gp):
            gp.wait_ge(st.c(0).h, st.c(0).n)
            gp.wait_ge(st.c(1).h, st.c(1).n)
    P.barrier()
    P.sem_release(wc["mark"])
    wc["pes"].close()


def emit_moe_route2(P, sc_y32, wg_sb, gidx, sidx, sc_bs, sc_bas, sc_pb, idxw, eidi):
    u = P.uid()
    _smark = P.sem_mark()
    pes = contextlib.ExitStack()
    I32 = mybir.dt.int32
    yb = [P.sbuf(f"r2{u}y{i}", [128, 8 * NQ], F32, es=pes) for i in range(2)]
    lgT = [P.sbuf(f"r2{u}l{i}", [128, 8], F32, es=pes) for i in range(2)]
    mxc = P.sbuf(f"r2{u}m", [128, 1], F32, es=pes)
    mskT = P.sbuf(f"r2{u}s", [128, 8 * 32], F32, es=pes)
    bsrow = P.sbuf(f"r2{u}b", [1, 256], F32, es=pes)
    bsT = P.sbuf(f"r2{u}bt", [32, 8], F32, es=pes)
    basT = P.sbuf(f"r2{u}bb", [32, 8], F32, es=pes)
    basrow = P.sbuf(f"r2{u}br", [1, 256], F32, es=pes)
    ccol = P.sbuf(f"r2{u}cc", [8, 4], F32, es=pes)
    ccoli = P.sbuf(f"r2{u}ci", [8, 1], I32, es=pes)
    pbcol = P.sbuf(f"r2{u}pb", [8, 2], F32, es=pes)
    pbrow = P.sbuf(f"r2{u}pr", [1, 16], F32, es=pes)
    cmpe = P.sbuf(f"r2{u}ce", [8, 16], F32, es=pes)
    erowf = P.sbuf(f"r2{u}er", [1, 16], F32, es=pes)
    eidf = P.sbuf(f"r2{u}ef", [1, 16], F32, es=pes)
    idxf = P.sbuf(f"r2{u}if", [128, 48], F32, es=pes)
    slotf = P.sbuf(f"r2{u}sf", [128, 8], F32, es=pes)
    slotc = P.sbuf(f"r2{u}sc", [128, 1], F32, es=pes)
    sloti = P.sbuf(f"r2{u}si", [128, 1], I32, es=pes)
    init0 = P.sbuf(f"r2{u}i0", [128, NTILE * 4], I32, es=pes)
    initb = P.sbuf(f"r2{u}ib", [128, NTILE * 4], I32, es=pes)
    pslg = [P.psum(f"r2{u}p{i}", [128, 8], F32, es=pes) for i in range(2)]
    psm_ = P.psum(f"r2{u}pm", [128, 512], F32, es=pes)
    psbs = psm_[0:1, 0:8]
    psba = psm_[0:32, 8:16]
    psct = psm_[0:8, 16:17]
    pspb = psm_[0:8, 17:18]
    pser = psm_[0:1, 32:48]
    psix = psm_[:, 64:80]
    pssl = [P.psum(f"r2{u}ps{i}", [128, 8], F32, es=pes) for i in range(2)]
    ld = Slots(P, f"r2{u}ld", 2)
    c_mm = Ctr(P, f"r2{u}mm")
    c_ev = Ctr(P, f"r2{u}ev")
    c_bs = Ctr(P, f"r2{u}bs")
    c_bse = Ctr(P, f"r2{u}be")
    c_dmab = Ctr(P, f"r2{u}db")
    c_tmm = Ctr(P, f"r2{u}tm")
    c_dsm = Ctr(P, f"r2{u}ds")
    c_slm = Ctr(P, f"r2{u}sl")
    c_sl = Ctr(P, f"r2{u}s2")
    c_scg = Ctr(P, f"r2{u}sg")
    c_scs = Ctr(P, f"r2{u}ss")
    cinit = Ctr(P, f"r2{u}in")
    ones1f = P.c("ones_1x128_f32")
    tri = P.c("tri128")
    lt = P.c("lt32")
    lt8 = P.c("lt8")
    iota = P.c("iota_col")
    o128f = P.c("ones_128x1_f32")
    o32f = P.c("ones_32x1_f32")
    o8f = P.c("ones_8x1_f32")
    tbase = P.c("tilebases")
    io128f = P.c("iota128f")

    with P.block() as blk:
        @blk.sync
        def _(sy):
            for tc in range(8):
                if tc >= 2:
                    sy.wait_ge(c_mm.h, (tc - 1) * 4)
                ld.c(tc).inc(sy.dma_start(
                    out=yb[tc % 2][:, :].rearrange("p (a c) -> p a c", a=8),
                    in_=sc_y32[:, tc * NQ:(tc + 1) * NQ].rearrange("(a p) c -> p a c", p=128)), 16)

        @blk.tensor
        def _(te):
            for tc in range(8):
                te.wait_ge(ld.c(tc).h, 16 * (tc // 2 + 1))
                for ps_ in range(4):
                    pt = tc * 4 + ps_
                    if pt >= 2:
                        te.wait_ge(c_ev.h, 3 * (pt - 1))
                    for j in range(NDC):
                        r = te.matmul(pslg[pt % 2][:, :],
                                      yb[tc % 2][:, j * NQ + ps_ * 128: j * NQ + ps_ * 128 + 128],
                                      wg_sb[:, j * 8:(j + 1) * 8],
                                      start=(j == 0), stop=(j == NDC - 1))
                    c_mm.inc(r)
                    te.wait_ge(c_ev.h, 3 * (pt + 1))
                    if pt >= 1:
                        te.wait_ge(c_bse.h, pt)
                    c_bs.inc(te.matmul(psbs, o128f[:, :],
                                       mskT[:, pt * 8:(pt + 1) * 8], start=True, stop=True))
            te.wait_ge(c_dmab.h, 64)
            c_tmm.inc(te.matmul(psba, lt[:, :], bsT[:, :], start=True, stop=True))
            c_tmm.inc(te.matmul(psct, bsT[:, :], o32f[:, :], start=True, stop=True))
            te.wait_ge(c_dsm.h, 5)
            c_tmm.inc(te.matmul(pspb, lt8[:, :], ccol[:, 1:2], start=True, stop=True))
            te.wait_ge(c_dsm.h, 8)
            c_tmm.inc(te.matmul(pser, o8f[:, :], cmpe[:, :], start=True, stop=True))
            te.wait_ge(c_dsm.h, 10)
            c_tmm.inc(te.matmul(psix, ones1f[:, :], erowf[0:1, :], start=True, stop=True))
            te.wait_ge(c_dmab.h, 128)
            for pt in range(32):
                if pt >= 2:
                    te.wait_ge(c_sl.h, 3 * (pt - 1))
                te.matmul(pssl[pt % 2][:, :], tri[:, :], mskT[:, pt * 8:(pt + 1) * 8],
                          start=True, stop=False)
                te.matmul(pssl[pt % 2][:, :], ones1f[:, :], basrow[0:1, pt * 8:(pt + 1) * 8],
                          start=False, stop=False)
                c_slm.inc(te.matmul(pssl[pt % 2][:, :], ones1f[:, :], pbrow[0:1, 8:16],
                                    start=False, stop=True))

        @blk.vector
        def _(v):
            cinit.inc(v.memset(init0[:, :], 0))
            cinit.inc(v.memset(initb[:, :], 100000))
            for pt in range(32):
                v.wait_ge(c_mm.h, pt + 1)
                v.wait_ge(c_ev.h, c_ev.n)
                c_ev.inc(v.tensor_copy(lgT[pt % 2][:, :], pslg[pt % 2][:, :]))
                v.wait_ge(c_ev.h, c_ev.n)
                c_ev.inc(v.reduce_max(mxc[:, :], lgT[pt % 2][:, :], axis=mybir.AxisListType.X))
                v.wait_ge(c_ev.h, c_ev.n)
                c_ev.inc(v.tensor_scalar(mskT[:, pt * 8:(pt + 1) * 8], lgT[pt % 2][:, :],
                                         mxc[:, :], None, ALU.is_ge))
                v.wait_ge(c_bs.h, pt + 1)
                c_bse.inc(v.tensor_copy(bsrow[0:1, pt * 8:(pt + 1) * 8], psbs))
            v.wait_ge(c_tmm.h, 2)
            c_dsm.inc(v.tensor_copy(basT[:, :], psba))
            c_dsm.inc(v.tensor_copy(ccol[:, 0:1], psct))
            v.wait_ge(c_dsm.h, c_dsm.n)
            c_dsm.inc(v.tensor_scalar(ccol[:, 2:3], ccol[:, 0:1], 511.0, 1.0 / 512.0,
                                      ALU.add, ALU.mult))
            v.wait_ge(c_dsm.h, c_dsm.n)
            c_dsm.inc(v.tensor_copy(ccoli[:, 0:1], ccol[:, 2:3]))
            v.wait_ge(c_dsm.h, c_dsm.n)
            c_dsm.inc(v.tensor_scalar(ccol[:, 1:2], ccoli[:, 0:1], 512.0, None, ALU.mult))
            v.wait_ge(c_tmm.h, 3)
            c_dsm.inc(v.tensor_copy(pbcol[:, 0:1], pspb))
            v.wait_ge(c_dsm.h, c_dsm.n)
            c_dsm.inc(v.tensor_scalar(pbcol[:, 1:2], pbcol[:, 0:1], -1.0, None, ALU.add))
            c_dsm.inc(v.tensor_scalar(cmpe[:, :], P.c("tilebases")[:, :], pbcol[:, 0:1], None, ALU.is_ge))
            v.wait_ge(c_tmm.h, 4)
            v.wait_ge(c_dsm.h, c_dsm.n)
            c_dsm.inc(v.tensor_scalar(erowf[0:1, :], pser, -1.0, None, ALU.add))
            v.wait_ge(c_dsm.h, c_dsm.n)
            c_dsm.inc(v.tensor_copy(eidi[0:1, :], erowf[0:1, :]))
            v.wait_ge(c_tmm.h, 5)
            bci = bass.AP(io128f, 0, [[1, 128], [0, 16]])
            v.wait_ge(c_dsm.h, c_dsm.n)
            c_dsm.inc(v.scalar_tensor_tensor(idxf[:, 0:16], psix, 1024.0, bci,
                                             ALU.mult, ALU.add))
            v.wait_ge(c_dsm.h, c_dsm.n)
            c_dsm.inc(v.scalar_tensor_tensor(idxf[:, 16:32], psix, 4096.0, bci,
                                             ALU.mult, ALU.add))
            v.wait_ge(c_dsm.h, c_dsm.n)
            c_dsm.inc(v.scalar_tensor_tensor(idxf[:, 32:48], psix, 128.0, bci,
                                             ALU.mult, ALU.add))
            v.wait_ge(c_dsm.h, c_dsm.n)
            c_dsm.inc(v.tensor_copy(idxw[:, :], idxf[:, :]))
            for pt in range(32):
                v.wait_ge(c_slm.h, pt + 1)
                v.wait_ge(c_sl.h, c_sl.n)
                if pt >= 1:
                    v.wait_ge(c_scg.h, 16 * pt)
                    v.wait_ge(c_scs.h, 16 * pt)
                c_sl.inc(v.tensor_mul(slotf[:, :], mskT[:, pt * 8:(pt + 1) * 8], pssl[pt % 2][:, :]))
                v.wait_ge(c_sl.h, c_sl.n)
                c_sl.inc(v.reduce_sum(slotc[:, :], slotf[:, :], axis=mybir.AxisListType.X))
                v.wait_ge(c_sl.h, c_sl.n)
                c_sl.inc(v.tensor_copy(sloti[:, :], slotc[:, :]))

        @blk.gpsimd
        def _(gp):
            gp.wait_ge(cinit.h, 2)
            c_dmab.inc(gp.dma_start(out=gidx[:, :].rearrange("(a b) o -> a (b o)", a=128),
                                    in_=init0[:, :]), 16)
            c_dmab.inc(gp.dma_start(out=sidx[:, :].rearrange("(a b) o -> a (b o)", a=128),
                                    in_=initb[:, :]), 16)
            gp.wait_ge(c_bse.h, 32)
            c_dmab.inc(gp.dma_start(out=sc_bs[0:1, :], in_=bsrow[0:1, :]), 16)
            gp.wait_ge(c_dmab.h, 48)
            c_dmab.inc(gp.dma_start(out=bsT[:, :],
                                    in_=sc_bs[0:1, :].rearrange("o (a b) -> (o a) b", a=32)), 16)
            gp.wait_ge(c_dsm.h, 1)
            c_dmab.inc(gp.dma_start(out=sc_bas[0:1, :].rearrange("o (a b) -> (o a) b", a=32),
                                    in_=basT[:, :]), 16)
            gp.wait_ge(c_dmab.h, 80)
            c_dmab.inc(gp.dma_start(out=basrow[0:1, :], in_=sc_bas[0:1, :]), 16)
            gp.wait_ge(c_dsm.h, 7)
            c_dmab.inc(gp.dma_start(out=sc_pb[0:1, 0:16].rearrange("o (b a) -> (o a) b", a=8),
                                    in_=pbcol[:, :], allow_slow_non_contiguous=True), 16)
            gp.wait_ge(c_dmab.h, 112)
            c_dmab.inc(gp.dma_start(out=pbrow[0:1, :], in_=sc_pb[0:1, 0:16]), 16)
            with gp.register(f"r2bc{u}") as rb:
                gp.reg_mov(rb, NTILE * 512 - 1)
                for pt in range(32):
                    gp.wait_ge(c_sl.h, 3 * (pt + 1))
                    c_scg.inc(gp.indirect_dma_start(
                        gidx[:, :], bass.IndirectOffsetOnAxis(sloti[:, 0:1], 0),
                        iota[:, pt:pt + 1], None,
                        bounds_check=rb, oob_is_err=False), 16)
                    c_scs.inc(gp.indirect_dma_start(
                        sidx[:, :], bass.IndirectOffsetOnAxis(sloti[:, 0:1], 0),
                        iota[:, pt:pt + 1], None,
                        bounds_check=rb, oob_is_err=False), 16)
            gp.wait_ge(c_scg.h, c_scg.n)
            gp.wait_ge(c_scs.h, c_scs.n)
    P.barrier()
    P.sem_release(_smark)
    pes.close()


def emit_moe_routed2(P, sc_yT, sc_w1b, sc_w2b, sc_b1b, sc_b2b, gidx, sidx, idxw, eidi, sc_moeT):
    I32 = mybir.dt.int32
    IOOA = bass.IndirectOffsetOnAxis
    id_bf = P.c("id_bf")
    NG = 4
    u = P.uid()
    _smark = P.sem_mark()
    pes = contextlib.ExitStack()
    w1sb = P.sbuf(f"m2{u}w1", [128, 8 * 4096], BF16, es=pes)
    w2sb = P.sbuf(f"m2{u}w2", [128, 32 * 1024], BF16, es=pes)
    b1c = [P.sbuf(f"m2{u}c1{i}", [128, 32], BF16, es=pes) for i in range(2)]
    b2cb = [P.sbuf(f"m2{u}cb{i}", [128, 8], BF16, es=pes) for i in range(2)]
    b2c = [P.sbuf(f"m2{u}c2{i}", [128, 8], F32, es=pes) for i in range(2)]
    gcol = [P.sbuf(f"m2{u}g{i}", [128, 1], I32, es=pes) for i in range(2)]
    scol = [P.sbuf(f"m2{u}s{i}", [128, 1], I32, es=pes) for i in range(2)]
    gth = [P.sbuf(f"m2{u}t{i}", [128, D], BF16, es=pes) for i in range(2)]
    oT = [P.sbuf(f"m2{u}u{i}", [128, D], BF16, es=pes) for i in range(4)]
    xe = [P.sbuf(f"m2{u}x{i}", [128, 8 * 512], BF16, es=pes) for i in range(2)]
    he = P.sbuf(f"m2{u}h", [128, 32 * 512], BF16, es=pes)
    oe = P.sbuf(f"m2{u}o", [128, 8 * 512], BF16, es=pes)
    psw = [P.psum(f"m2{u}p{i}", [128, 512], F32, es=pes) for i in range(3)]
    ptp = [P.psum(f"m2{u}q{i}", [128, 128], BF16, es=pes) for i in range(2)]

    wld = Ctr(P, f"m2{u}wl")
    wl2 = Ctr(P, f"m2{u}w2")
    bl1 = Ctr(P, f"m2{u}b1")
    bl2 = Ctr(P, f"m2{u}b2")
    cbc = Ctr(P, f"m2{u}cc")
    il = Slots(P, f"m2{u}il", 2)
    i2 = Slots(P, f"m2{u}i2", 2)
    gld = Slots(P, f"m2{u}gl", 2)
    scs = Slots(P, f"m2{u}sc", 4)
    c_tr = Ctr(P, f"m2{u}tr")
    c_ev = Ctr(P, f"m2{u}ev")
    c_h = Ctr(P, f"m2{u}hm")
    c_he = Ctr(P, f"m2{u}he")
    c_o = Ctr(P, f"m2{u}om")
    c_oe = Ctr(P, f"m2{u}oe")

    def ps_wait(te, m):
        pm = m - 3
        if pm < 0:
            return
        tp, r = divmod(pm, 40)
        if r < 32:
            te.wait_ge(c_he.h, 32 * tp + r + 1)
        else:
            te.wait_ge(c_oe.h, 8 * tp + (r - 32) + 1)

    with P.block() as blk:
        @blk.gpsimd
        def _(gp):
            def token_gather(t, g):
                GG = 4 * t + g
                if GG >= 2:
                    gp.wait_ge(gld.c(GG).h, 16 * (GG // 2))
                il.c(GG).inc(gp.dma_start(
                    out=gcol[GG % 2][:, :],
                    in_=gidx[t * 512 + g * 128: t * 512 + (g + 1) * 128, 0:1]), 16)
                gp.wait_ge(il.c(GG).h, 16 * (GG // 2 + 1))
                if GG >= 2:
                    pv, gv = divmod(GG - 2, 4)
                    gp.wait_ge(c_tr.h, 64 * pv + 8 * (gv + 1))
                gld.c(GG).inc(gp.indirect_dma_start(
                    gth[GG % 2][:, :], None,
                    sc_yT[:, :], IOOA(gcol[GG % 2][:, 0:1], 0)), 16)

            def scatters(tt):
                for g in range(NG):
                    GG = 4 * tt + g
                    if GG >= 2:
                        pt, pg = divmod(GG - 2, 4)
                        gp.wait_ge(scs.c(pg).h, 16 * (pt + 1))
                    i2.c(GG).inc(gp.dma_start(
                        out=scol[GG % 2][:, :],
                        in_=sidx[tt * 512 + g * 128: tt * 512 + (g + 1) * 128, 0:1]), 16)
                    gp.wait_ge(i2.c(GG).h, 16 * (GG // 2 + 1))
                    gp.wait_ge(c_ev.h, 64 * tt + 32 + 8 * (g + 1))
                    scs.c(g).inc(gp.indirect_dma_start(
                        sc_moeT[:, :], IOOA(scol[GG % 2][:, 0:1], 0),
                        oT[g][:, :], None,
                        bounds_check=rb, oob_is_err=False), 16)

            with gp.register(f"m2bc{u}") as rb:
                gp.reg_mov(rb, S - 1)
                for t in range(NTILE):
                    if t >= 1:
                        gp.wait_ge(c_h.h, 32 * t)
                    for k in range(NDC):
                        wld.inc(gp.indirect_dma_start(
                            w1sb[:, k * 4096:(k + 1) * 4096], None,
                            sc_w1b[:, :], IOOA(idxw[:, t:t + 1], 0),
                            element_offset=k * 128 * 4096), 16)
                    token_gather(t, 0)
                    token_gather(t, 1)
                    if t >= 1:
                        gp.wait_ge(c_o.h, 8 * t)
                    for dfi in range(32):
                        wl2.inc(gp.indirect_dma_start(
                            w2sb[:, dfi * 1024:(dfi + 1) * 1024], None,
                            sc_w2b[:, :], IOOA(idxw[:, 16 + t:17 + t], 0),
                            element_offset=dfi * 128 * 1024), 16)
                    if t >= 2:
                        gp.wait_ge(c_he.h, 32 * (t - 1))
                    bl1.inc(gp.indirect_dma_start(
                        b1c[t % 2][:, :], None,
                        sc_b1b[:, :], IOOA(idxw[:, 32 + t:33 + t], 0)), 16)
                    if t >= 2:
                        gp.wait_ge(cbc.h, t - 1)
                    bl2.inc(gp.indirect_dma_start(
                        b2cb[t % 2][:, :], None,
                        sc_b2b[:, :], IOOA(idxw[:, 32 + t:33 + t], 0)), 16)
                    token_gather(t, 2)
                    token_gather(t, 3)
                    if t >= 1:
                        scatters(t - 1)
                scatters(NTILE - 1)
                for i in range(4):
                    gp.wait_ge(scs.c(i).h, scs.c(i).n)

        @blk.tensor
        def _(te):
            for t in range(NTILE):
                for g in range(NG):
                    GG = 4 * t + g
                    te.wait_ge(gld.c(GG).h, 16 * (GG // 2 + 1))
                    for j in range(NDC):
                        k = 64 * t + 8 * g + j
                        if k >= 2:
                            te.wait_ge(c_ev.h, k - 1)
                        c_tr.inc(te.transpose(ptp[k % 2][:, :],
                                              gth[g % 2][:, j * 128:(j + 1) * 128],
                                              id_bf[:, :]))
                te.wait_ge(wld.h, 16 * 8 * (t + 1))
                te.wait_ge(c_ev.h, 64 * t + 32)
                for df in range(32):
                    m = 40 * t + df
                    ps_wait(te, m)
                    for k in range(NDC):
                        r = te.matmul(psw[m % 3][:, :],
                                      w1sb[:, k * 4096 + df * 128: k * 4096 + df * 128 + 128],
                                      xe[t % 2][:, k * 512:(k + 1) * 512],
                                      start=(k == 0), stop=(k == NDC - 1))
                    c_h.inc(r)
                te.wait_ge(wl2.h, 16 * 32 * (t + 1))
                te.wait_ge(c_he.h, 32 * (t + 1))
                for do in range(8):
                    m = 40 * t + 32 + do
                    ps_wait(te, m)
                    for df in range(32):
                        r = te.matmul(psw[m % 3][:, :],
                                      w2sb[:, df * 1024 + do * 128: df * 1024 + do * 128 + 128],
                                      he[:, df * 512:(df + 1) * 512],
                                      start=(df == 0), stop=(df == 31))
                    c_o.inc(r)
                te.wait_ge(c_oe.h, 8 * (t + 1))
                for g in range(NG):
                    for j in range(NDC):
                        k = 64 * t + 32 + 8 * g + j
                        if k >= 2:
                            te.wait_ge(c_ev.h, k - 1)
                        c_tr.inc(te.transpose(ptp[k % 2][:, :],
                                              oe[:, j * 512 + g * 128: j * 512 + (g + 1) * 128],
                                              id_bf[:, :]))

        @blk.scalar
        def _(sc):
            for t in range(NTILE):
                if t >= 2:
                    sc.wait_ge(c_h.h, 32 * (t - 1))
                for g in range(NG):
                    for j in range(NDC):
                        k = 64 * t + 8 * g + j
                        sc.wait_ge(c_tr.h, k + 1)
                        c_ev.inc(sc.copy(xe[t % 2][:, j * 512 + g * 128: j * 512 + (g + 1) * 128],
                                         ptp[k % 2][:, :]))
                sc.wait_ge(bl1.h, 16 * (t + 1))
                for df in range(32):
                    sc.wait_ge(c_h.h, 32 * t + df + 1)
                    if t >= 1 and df == 0:
                        sc.wait_ge(c_o.h, 8 * t)
                    c_he.inc(sc.activation(he[:, df * 512:(df + 1) * 512],
                                           psw[(40 * t + df) % 3][:, :],
                                           AF.Relu, bias=b1c[t % 2][:, df:df + 1]))
                for g in range(NG):
                    GG = 4 * t + g
                    for j in range(NDC):
                        k = 64 * t + 32 + 8 * g + j
                        sc.wait_ge(c_tr.h, k + 1)
                        if j == 0 and t >= 1:
                            sc.wait_ge(scs.c(g).h, 16 * t)
                        c_ev.inc(sc.copy(oT[g][:, j * 128:(j + 1) * 128],
                                         ptp[k % 2][:, :]))

        @blk.vector
        def _(v):
            for t in range(NTILE):
                v.wait_ge(bl2.h, 16 * (t + 1))
                if t >= 2:
                    v.wait_ge(c_oe.h, 8 * (t - 1))
                cbc.inc(v.tensor_copy(b2c[t % 2][:, :], b2cb[t % 2][:, :]))
                for do in range(8):
                    v.wait_ge(c_o.h, 8 * t + do + 1)
                    if do == 0:
                        v.wait_ge(cbc.h, t + 1)
                        if t >= 1:
                            v.wait_ge(c_tr.h, 64 * t)
                    c_oe.inc(v.tensor_scalar(oe[:, do * 512:(do + 1) * 512],
                                             psw[(40 * t + 32 + do) % 3][:, :],
                                             b2c[t % 2][:, do:do + 1], None, ALU.add))
    P.barrier()
    P.sem_release(_smark)
    pes.close()


# version 12
# speedup vs baseline: 1.1518x; 1.0386x over previous
import contextlib
import numpy as np
import ml_dtypes
import concourse.bass as bass
import concourse.mybir as mybir
from concourse.bass_utils import run_bass_kernel_spmd

F32 = mybir.dt.float32
BF16 = mybir.dt.bfloat16
U8 = mybir.dt.uint8
AF = mybir.ActivationFunctionType
ALU = mybir.AluOpType

S = 4096
D = 1024
NB = 8
NQ = 512
EPS = 1e-5
GAMMA = 0.96875
NPT = S // 128
NDC = D // 128
BF = ml_dtypes.bfloat16


def host_constants():
    c = {}
    c["id_bf"] = np.eye(128, dtype=BF)
    c["id_f32"] = np.eye(128, dtype=np.float32)
    c["ones_1x128_bf"] = np.ones((1, 128), dtype=BF)
    c["ones_1x128_f32"] = np.ones((1, 128), dtype=np.float32)
    c["ones_64x1_bf"] = np.ones((64, 1), dtype=BF)
    c["ones_128x1_bf"] = np.ones((128, 1), dtype=BF)
    c["ones_1x512_bf"] = np.ones((1, 512), dtype=BF)
    c["ones_128x1_f32"] = np.ones((128, 1), dtype=np.float32)
    kk = np.arange(128)
    c["tri128"] = (kk[:, None] <= kk[None, :]).astype(np.float32)
    k32 = np.arange(32)
    c["lt32"] = (k32[:, None] < k32[None, :]).astype(np.float32)
    c["erow"] = (np.arange(8, dtype=np.float32) * 768.0 - 1.0).reshape(1, 8)
    c["lt8"] = (np.arange(8)[:, None] < np.arange(8)[None, :]).astype(np.float32)
    c["ones_32x1_f32"] = np.ones((32, 1), dtype=np.float32)
    c["ones_8x1_f32"] = np.ones((8, 1), dtype=np.float32)
    c["tilebases"] = np.tile((np.arange(16, dtype=np.float32) * 512.0)[None, :], (8, 1))
    c["iota128f"] = np.arange(128, dtype=np.float32).reshape(128, 1)
    c["iota_col"] = np.arange(4096, dtype=np.int32).reshape(32, 128).T.copy()
    k = np.arange(128)[:, None].astype(np.float64)
    u = np.arange(896)[None, :].astype(np.float64)
    c["m01"] = ((u - 384.0) >= k).astype(BF)
    e = u - 384.0 - k
    c["mrm"] = np.where(e >= 0, GAMMA ** np.maximum(e, 0), 0.0).astype(BF)
    u2 = np.arange(512)[None, :].astype(np.float64)
    c["mrp"] = (GAMMA ** (u2 - k + 127.0)).astype(BF)
    sel = np.zeros((128, 8, 4), dtype=np.float32)
    for j in range(8):
        sel[:, j, j // 2] = 1.0 / 16.0
    c["sel"] = sel.reshape(128, 32).astype(BF)
    return c


INPUT_SHAPES = {
    "x_mine": ([S, D], F32), "x_full": ([NB * S, D], F32),
    "id_bf": ([128, 128], BF16), "id_f32": ([128, 128], F32),
    "ones_1x128_bf": ([1, 128], BF16), "ones_1x128_f32": ([1, 128], F32),
    "ones_64x1_bf": ([64, 1], BF16), "ones_128x1_bf": ([128, 1], BF16),
    "ones_1x512_bf": ([1, 512], BF16),
    "m01": ([128, 896], BF16), "mrm": ([128, 896], BF16), "mrp": ([128, 512], BF16),
    "sel": ([128, 32], BF16), "bqkv_vrow": ([1, 2 * D], BF16),
    "ones_128x1_f32": ([128, 1], F32), "tri128": ([128, 128], F32),
    "lt32": ([32, 32], F32), "erow": ([1, 8], F32), "iota_col": ([128, 32], mybir.dt.int32),
    "lt8": ([8, 8], F32), "ones_32x1_f32": ([32, 1], F32), "ones_8x1_f32": ([8, 1], F32),
    "tilebases": ([8, 16], F32), "iota128f": ([128, 1], F32),
    "WqkvT": ([D, 3 * D], F32), "bqkv": ([128, 24], F32),
    "WoT": ([D, D], F32), "bo": ([128, 8], F32),
    "g1": ([128, 8], F32), "be1": ([128, 8], F32),
    "g2": ([128, 8], F32), "be2": ([128, 8], F32),
    "WgT": ([D, 8], F32),
    "W1": ([8, D, 4 * D], F32), "b1": ([128, 256], F32), "b1t": ([8, 128, 32], F32), "b2t": ([8, 128, 8], F32),
    "W2": ([8, 4 * D, D], F32), "b2": ([128, 64], F32),
    "Wq1": ([D, D], F32), "Wk1": ([D, D], F32), "Wv1": ([D, D], F32),
    "theta": ([128, 8], F32),
    "k0u": ([32, 128, 2 * D], U8), "k1u": ([32, 128, 4 * D], U8), "k2u": ([8, 128, 4 * D], U8),
    "s012": ([1, 3], F32),
}

CONST_SBUF = {
    "id_bf": ([128, 128], BF16),
    "ones_1x128_bf": ([1, 128], BF16), "ones_1x128_f32": ([1, 128], F32),
    "ones_128x1_bf": ([128, 1], BF16),
    "ones_1x512_bf": ([1, 512], BF16),
    "m01": ([128, 896], BF16), "mrm": ([128, 896], BF16), "mrp": ([128, 512], BF16),
    "sel": ([128, 32], BF16),
    "ones_128x1_f32": ([128, 1], F32), "tri128": ([128, 128], F32),
    "lt32": ([32, 32], F32), "erow": ([1, 8], F32), "iota_col": ([128, 32], mybir.dt.int32),
    "lt8": ([8, 8], F32), "ones_32x1_f32": ([32, 1], F32), "ones_8x1_f32": ([8, 1], F32),
    "tilebases": ([8, 16], F32), "iota128f": ([128, 1], F32),
    "bqkv": ([128, 24], F32), "bo": ([128, 8], F32),
    "g1": ([128, 8], F32), "be1": ([128, 8], F32),
    "g2": ([128, 8], F32), "be2": ([128, 8], F32),
    "theta": ([128, 8], F32), "s012": ([1, 3], F32),
}


def host_prep(inputs):
    c = host_constants()
    x = np.ascontiguousarray(np.asarray(inputs["L0_attn_x"], np.float32))
    sh = dict(c)
    sh["x_full"] = x.reshape(NB * S, D)
    sh["WqkvT"] = np.ascontiguousarray(np.asarray(inputs["L0_ffn_Wqkv"], np.float32).T)
    sh["bqkv"] = np.ascontiguousarray(np.asarray(inputs["L0_ffn_bqkv"], np.float32).reshape(24, 128).T)
    sh["WoT"] = np.ascontiguousarray(np.asarray(inputs["L0_ffn_Wo"], np.float32).T)
    sh["bqkv_vrow"] = np.ascontiguousarray(np.asarray(inputs["L0_ffn_bqkv"], np.float32)[1024:3072].reshape(1, 2 * D).astype(BF))
    sh["bo"] = np.ascontiguousarray(np.asarray(inputs["L0_ffn_bo"], np.float32).reshape(8, 128).T)
    for n, src in (("g1", "L0_ffn_g1"), ("be1", "L0_ffn_be1"), ("g2", "L0_ffn_g2"), ("be2", "L0_ffn_be2")):
        sh[n] = np.ascontiguousarray(np.asarray(inputs[src], np.float32).reshape(8, 128).T)
    sh["WgT"] = np.ascontiguousarray(np.asarray(inputs["L0_ffn_Wg"], np.float32).T)
    sh["W1"] = np.ascontiguousarray(np.asarray(inputs["L0_ffn_W1"], np.float32))
    sh["b1"] = np.ascontiguousarray(np.asarray(inputs["L0_ffn_b1"], np.float32).reshape(256, 128).T)
    sh["b1t"] = np.ascontiguousarray(np.asarray(inputs["L0_ffn_b1"], np.float32).reshape(8, 32, 128).transpose(0, 2, 1))
    sh["b2t"] = np.ascontiguousarray(np.asarray(inputs["L0_ffn_b2"], np.float32).reshape(8, 8, 128).transpose(0, 2, 1))
    sh["W2"] = np.ascontiguousarray(np.asarray(inputs["L0_ffn_W2"], np.float32))
    sh["b2"] = np.ascontiguousarray(np.asarray(inputs["L0_ffn_b2"], np.float32).reshape(64, 128).T)
    sh["Wq1"] = np.ascontiguousarray(np.asarray(inputs["L1_attn_weight_q"], np.float32))
    sh["Wk1"] = np.ascontiguousarray(np.asarray(inputs["L1_attn_weight_k"], np.float32))
    sh["Wv1"] = np.ascontiguousarray(np.asarray(inputs["L1_attn_weight_v"], np.float32))
    sh["theta"] = np.ascontiguousarray(np.asarray(inputs["L1_attn_theta"], np.float32).reshape(8, 128).T)
    k0 = np.asarray(inputs["L1_ffn_k0"]).astype(np.uint8)
    k1 = np.asarray(inputs["L1_ffn_k1"]).astype(np.uint8)
    k2 = np.asarray(inputs["L1_ffn_k2"]).astype(np.uint8)
    sh["k0u"] = np.ascontiguousarray(k0.reshape(16, 128, 32, 128).transpose(2, 1, 0, 3).reshape(32, 128, 2048))
    sh["k1u"] = np.ascontiguousarray(k1.reshape(32, 128, 32, 128).transpose(2, 1, 0, 3).reshape(32, 128, 4096))
    sh["k2u"] = np.ascontiguousarray(k2.reshape(32, 128, 8, 128).transpose(2, 1, 0, 3).reshape(8, 128, 4096))
    sh["s012"] = np.ascontiguousarray(np.stack([
        np.asarray(inputs["L1_ffn_s0"], np.float32),
        np.asarray(inputs["L1_ffn_s1"], np.float32),
        np.asarray(inputs["L1_ffn_s2"], np.float32)]).reshape(1, 3))
    in_maps = []
    for core in range(NB):
        m = dict(sh)
        m["x_mine"] = np.ascontiguousarray(x[core])
        in_maps.append(m)
    return in_maps


class Ctr:

    def __init__(self, P, name, persistent=False):
        if persistent:
            self.h = P.es.enter_context(P.nc.semaphore(name))
        else:
            self.h = P.nc.alloc_semaphore(name=name)
            P._allsems.append(self.h)
        self.n = 0

    def inc(self, instr, k=1):
        instr.then_inc(self.h, k)
        self.n += k
        return self.n


class Slots:

    def __init__(self, P, name, n):
        self.cs = [Ctr(P, f"{name}_{i}") for i in range(n)]
        self.n = n

    def c(self, i):
        return self.cs[i % self.n]


class Prog:
    def __init__(self):
        self.nc = bass.Bass("TRN2", target_bir_lowering=False)
        self.es = contextlib.ExitStack()
        self.t = {}
        self._bar = None
        self._barn = 0
        self._uid = 0
        self._allsems = []

    def sem_mark(self):
        return len(self._allsems)

    def sem_release(self, mark):
        hs = self._allsems[mark:]
        if not hs:
            return
        del self._allsems[mark:]
        with self.block() as blk:
            @blk.gpsimd
            def _(gp):
                self.nc.clear_and_free_semaphores(hs)
        self.barrier()

    def uid(self):
        self._uid += 1
        return self._uid

    def dram(self, name, shape, dtype, kind=None):
        t = self.nc.dram_tensor(name, list(shape), dtype, kind=kind) if kind \
            else self.nc.dram_tensor(name, list(shape), dtype)
        self.t[name] = t
        return t

    def sbuf(self, name, shape, dtype, es=None):
        t = (es or self.es).enter_context(self.nc.sbuf_tensor(name, list(shape), dtype))
        self.t[name] = t
        return t

    def psum(self, name, shape, dtype=F32, es=None):
        t = (es or self.es).enter_context(self.nc.psum_tensor(name, list(shape), dtype))
        self.t[name] = t
        return t

    def block(self):
        return self.nc.Block()

    def barrier(self, engines=("tensor", "vector", "scalar", "gpsimd", "sync")):
        if self._bar is None:
            self._bar = self.es.enter_context(self.nc.semaphore("bar"))
        tot = self._barn + len(engines)
        b = self._bar
        with self.block() as blk:
            for e in engines:
                def mk():
                    def f(eng):
                        eng.sem_inc(b, 1)
                        eng.wait_ge(b, tot)
                    return f
                getattr(blk, e)(mk())
        self._barn = tot

    def declare_inputs(self, names):
        for n in names:
            shp, dt = INPUT_SHAPES[n]
            self.dram(n, shp, dt, kind="ExternalInput")

    def load_consts(self, names):
        ld = Ctr(self, "constld")
        with self.block() as blk:
            @blk.sync
            def _(sy):
                for n in names:
                    shp, dt = CONST_SBUF[n]
                    sb = self.sbuf("c_" + n, shp, dt)
                    ld.inc(sy.dma_start(out=sb[:, :], in_=self.t[n][:, :]), 16)
        with self.block() as blk:
            @blk.sync
            def _(sy):
                sy.wait_ge(ld.h, ld.n)
        self.barrier()

    def c(self, name):
        return self.t["c_" + name]


def emit_ln0(P, src, xnT=None, xnF=None, tag=""):
    nc = P.nc
    u = P.uid()
    _smark = P.sem_mark()
    pes = contextlib.ExitStack()
    xb = [P.sbuf(f"lnx{u}_{i}", [128, D], F32, es=pes) for i in range(2)]
    sqb = [P.sbuf(f"lnq{u}_{i}", [128, D], F32, es=pes) for i in range(2)]
    stat = [P.sbuf(f"lns{u}_{i}", [128, 8], F32, es=pes) for i in range(2)]
    xnt_tmp = None
    if xnT is None:
        xnt_tmp = [P.sbuf(f"lnt{u}_{i}", [128, D], BF16, es=pes) for i in range(2)]
    ptp = [P.psum(f"lnp{u}_{i}", [128, 128], BF16, es=pes) for i in range(2)]
    ld = Slots(P, f"ln_ld{u}", 2)
    asq = Ctr(P, f"ln_asq{u}")
    dst = Ctr(P, f"ln_dst{u}")
    art = Ctr(P, f"ln_art{u}")
    nrm = Ctr(P, f"ln_nrm{u}")
    ttp = Ctr(P, f"ln_ttp{u}")
    aev = Ctr(P, f"ln_aev{u}")
    id_bf = P.c("id_bf")

    with P.block() as blk:
        @blk.sync
        def _(sy):
            for t in range(NPT):
                if t >= 2:
                    sy.wait_ge(nrm.h, t - 1)
                ld.c(t).inc(sy.dma_start(out=xb[t % 2][:, :], in_=src(t)), 16)

        @blk.scalar
        def _(sc):
            def evacs(t):
                if xnF is None:
                    return
                for j in range(NDC):
                    n = 8 * t + j
                    sc.wait_ge(ttp.h, n + 1)
                    aev.inc(sc.copy(xnF[:, j * S + t * 128: j * S + (t + 1) * 128], ptp[n % 2][:, :]))
            for t in range(NPT):
                sc.wait_ge(ld.c(t).h, 16 * (t // 2 + 1))
                if t >= 2:
                    sc.wait_ge(dst.h, t - 1)
                asq.inc(sc.activation(sqb[t % 2][:, :], xb[t % 2][:, :], AF.Square))
                sc.wait_ge(dst.h, t + 1)
                art.inc(sc.activation(stat[t % 2][:, 7:8], stat[t % 2][:, 6:7], AF.Sqrt))
                if t >= 1:
                    evacs(t - 1)
            evacs(NPT - 1)

        @blk.vector
        def _(v):
            vc = Ctr(P, f"ln_vc{u}")
            for t in range(NPT):
                v.wait_ge(ld.c(t).h, 16 * (t // 2 + 1))
                if t >= 2:
                    v.wait_ge(nrm.h, t - 1)
                vc.inc(v.reduce_sum(stat[t % 2][:, 0:1], xb[t % 2][:, :], axis=mybir.AxisListType.X))
                v.wait_ge(asq.h, t + 1)
                vc.inc(v.reduce_sum(stat[t % 2][:, 1:2], sqb[t % 2][:, :], axis=mybir.AxisListType.X))
                v.wait_ge(vc.h, vc.n)
                vc.inc(v.tensor_scalar(stat[t % 2][:, 2:3], stat[t % 2][:, 0:1], 1.0 / D, None, ALU.mult))
                vc.inc(v.tensor_scalar(stat[t % 2][:, 3:4], stat[t % 2][:, 1:2], 1.0 / D, None, ALU.mult))
                v.wait_ge(vc.h, vc.n)
                vc.inc(v.tensor_mul(stat[t % 2][:, 4:5], stat[t % 2][:, 2:3], stat[t % 2][:, 2:3]))
                v.wait_ge(vc.h, vc.n)
                vc.inc(v.tensor_sub(stat[t % 2][:, 5:6], stat[t % 2][:, 3:4], stat[t % 2][:, 4:5]))
                v.wait_ge(vc.h, vc.n)
                dst.inc(v.tensor_scalar(stat[t % 2][:, 6:7], stat[t % 2][:, 5:6], EPS, None, ALU.add))
                v.wait_ge(art.h, t + 1)
                vc.inc(v.reciprocal(stat[t % 2][:, 6:7], stat[t % 2][:, 7:8]))
                v.wait_ge(vc.h, vc.n)
                dstt = xnT[:, t * D:(t + 1) * D] if xnT is not None else xnt_tmp[t % 2][:, :]
                if xnT is None and t >= 2:
                    v.wait_ge(ttp.h, 8 * (t - 1))
                nrm.inc(v.tensor_scalar(dstt, xb[t % 2][:, :], stat[t % 2][:, 2:3],
                                        stat[t % 2][:, 6:7], ALU.subtract, ALU.mult))

        if xnF is not None:
            @blk.tensor
            def _(te):
                for t in range(NPT):
                    te.wait_ge(nrm.h, t + 1)
                    srct = xnT[:, t * D:(t + 1) * D] if xnT is not None else xnt_tmp[t % 2][:, :]
                    for j in range(NDC):
                        n = 8 * t + j
                        if n >= 2:
                            te.wait_ge(aev.h, n - 1)
                        ttp.inc(te.transpose(ptp[n % 2][:, :], srct[:, j * 128:(j + 1) * 128], id_bf[:, :]))
    P.barrier()
    P.sem_release(_smark)
    pes.close()


def emit_l0_attn(P, xnF, xnT, sc_attn, extra=None):
    nc = P.nc
    u = P.uid()
    _smark = P.sem_mark()
    pes = contextlib.ExitStack()
    vstrip = [P.sbuf(f"at_vs{u}_{i}", [128, 65 * NPT], BF16, es=pes) for i in range(2)]
    negc = P.sbuf(f"at_nc{u}", [128, 1], F32, es=pes)
    ksl = [P.sbuf(f"at_ks{u}_{i}", [128, S], BF16, es=pes) for i in range(2)]
    pbuf = [P.sbuf(f"at_pb{u}_{i}", [128, NQ], BF16, es=pes) for i in range(6)]
    pdtmp = [P.sbuf(f"at_pd{u}_{i}", [128, NQ], BF16, es=pes) for i in range(2)]
    srec = [P.sbuf(f"at_rc{u}_{i}", [1, NQ], F32, es=pes) for i in range(2)]
    srb = [P.sbuf(f"at_rb{u}_{i}", [64, NQ], F32, es=pes) for i in range(2)]
    aout = [P.sbuf(f"at_ao{u}_{i}", [64, NQ], BF16, es=pes) for i in range(2)]
    ps_s = [P.psum(f"at_ps{u}_{i}", [128, NQ], F32, es=pes) for i in range(4)]
    ps_pv = [P.psum(f"at_pv{u}_{i}", [128, NQ], F32, es=pes) for i in range(2)]
    ps_rb = P.psum(f"at_prb{u}", [64, NQ], F32, es=pes)

    c_ic = Ctr(P, f"at_ic{u}")
    c_ksl = Ctr(P, f"at_kl{u}")
    c_mm = Ctr(P, f"at_mm{u}")
    c_exp = Ctr(P, f"at_exp{u}")
    c_msk = Ctr(P, f"at_msk{u}")
    c_pv = Ctr(P, f"at_pv{u}")
    c_vst = Ctr(P, f"at_vst{u}")
    c_rc = Ctr(P, f"at_rc{u}")
    c_rb = Ctr(P, f"at_rb{u}")
    c_arb = Ctr(P, f"at_arb{u}")
    c_do = Ctr(P, f"at_do{u}")
    c_st = Slots(P, f"at_st{u}", 2)

    ones1f = P.c("ones_1x128_f32")
    m01 = P.c("m01")

    groups = [(h, qc) for h in range(16) for qc in range(8)]
    ntile_g = {qc: 4 * (qc + 1) for qc in range(8)}

    def tiles_of(qc):
        return list(range(4 * (qc + 1)))

    def schedule():
        mm = exp = msk = pv = 0
        nd = 0
        sched = []
        for gi, (h, qc) in enumerate(groups):
            tl = []
            for kc in tiles_of(qc):
                diag = kc >= 4 * qc
                mm += 1
                exp += 1
                d = dict(kc=kc, diag=diag, mm=mm, exp=exp, msk=msk, pv=pv)
                if diag:
                    msk += 1
                    d["msk"] = msk
                    d["nd"] = nd
                    nd += 1
                pv += 1
                d["pv"] = pv
                tl.append(d)
            sched.append(dict(h=h, qc=qc, gi=gi, tiles=tl))
        return sched

    SCH = schedule()

    def xnF_head(h, c0, cn):
        j = h // 2
        r = (h % 2) * 64
        return xnF[r:r + 64, j * S + c0: j * S + c0 + cn]

    with P.block() as blk:
        @blk.vector
        def _(v):
            for i in range(2):
                c_vst.inc(v.memset(vstrip[i][:, :].rearrange("p (a b) -> p a b", b=65)[:, :, 64:65], 1.0))
            c_ic.inc(v.memset(negc[:, :], -20.0))
            c_ksl.inc(v.memset(ksl[0][64:128, :], 0.0))
            c_ksl.inc(v.memset(ksl[1][0:64, :], 0.0))
            mme = {hh: SCH[hh * 8 + 7]["tiles"][-1]["mm"] for hh in range(16)}
            pvend = {hh: sum(len(gg["tiles"]) for gg in SCH[: (hh + 1) * 8]) for hh in range(16)}

            def head_copies(h):
                if h >= 2:
                    v.wait_ge(c_mm.h, mme[h - 2])
                    v.wait_ge(c_pv.h, pvend[h - 2])
                r0 = (h % 2) * 64
                c_ksl.inc(v.tensor_copy(ksl[h % 2][r0:r0 + 64, :],
                                        xnF[r0:r0 + 64, (h // 2) * S:(h // 2 + 1) * S]))
                for kc in range(NPT):
                    c_vst.inc(v.tensor_copy(
                        vstrip[h % 2][:, 65 * kc: 65 * kc + 64],
                        xnT[:, kc * D + 64 * h: kc * D + 64 * h + 64]))
            head_copies(0)
            head_copies(1)
            for g in SCH:
                h, qc, gi = g["h"], g["qc"], g["gi"]
                if qc == 6 and h < 15:
                    head_copies(h + 1)
                for tl in g["tiles"]:
                    if tl["diag"]:
                        cofs = qc * NQ - tl["kc"] * 128 + 384
                        v.wait_ge(c_exp.h, tl["exp"])
                        if tl["pv"] > 6:
                            v.wait_ge(c_pv.h, tl["pv"] - 6)
                        c_msk.inc(v.tensor_mul(pbuf[(tl["pv"] - 1) % 6][:, :],
                                               pdtmp[tl["nd"] % 2][:, :],
                                               m01[:, cofs:cofs + NQ]))
                v.wait_ge(c_pv.h, g["tiles"][-1]["pv"])
                c_rc.inc(v.reciprocal(srec[gi % 2][:, :], ps_pv[gi % 2][64:65, :]))
                v.wait_ge(c_arb.h, gi + 1)
                if gi >= 2:
                    v.wait_ge(c_st.c(gi).h, 16 * ((gi - 2) // 2 + 1))
                c_do.inc(v.tensor_mul(aout[gi % 2][:, :], ps_pv[gi % 2][0:64, :], srb[gi % 2][:, :]))

        @blk.scalar
        def _(sc):
            sc.wait_ge(c_ic.h, 1)

            def store_group(gp_):
                hp, qp = gp_["h"], gp_["qc"]
                sc.wait_ge(c_do.h, gp_["gi"] + 1)
                c_st.c(gp_["gi"]).inc(sc.dma_start(
                    out=sc_attn[64 * hp:64 * hp + 64, qp * NQ:(qp + 1) * NQ],
                    in_=aout[gp_["gi"] % 2][:, :]), 16)

            for g in SCH:
                h, qc, gi = g["h"], g["qc"], g["gi"]
                if gi >= 1:
                    store_group(SCH[gi - 1])
                for tl in g["tiles"]:
                    sc.wait_ge(c_mm.h, tl["mm"])
                    if tl["diag"]:
                        if tl["nd"] > 1:
                            sc.wait_ge(c_msk.h, tl["nd"] - 1)
                        dst = pdtmp[tl["nd"] % 2][:, :]
                    else:
                        dst = pbuf[(tl["pv"] - 1) % 6][:, :]
                        if tl["pv"] > 6:
                            sc.wait_ge(c_pv.h, tl["pv"] - 6)
                    c_exp.inc(sc.activation(dst, ps_s[tl["mm"] % 4][:, :], AF.Exp,
                                            scale=0.125, bias=negc[:, 0:1]))
                sc.wait_ge(c_rb.h, gi + 1)
                c_arb.inc(sc.copy(srb[gi % 2][:, :], ps_rb[0:64, :]))
            store_group(SCH[-1])
            sc.wait_ge(c_st.c(0).h, c_st.c(0).n)
            sc.wait_ge(c_st.c(1).h, c_st.c(1).n)

        @blk.tensor
        def _(te):
            RUN = 3
            vst_seen = 0
            rb_pend = None

            def emit_rb(g_):
                te.wait_ge(c_rc.h, g_["gi"] + 1)
                c_rb.inc(te.matmul(ps_rb[0:64, :], ones1f[0:1, 0:64],
                                   srec[g_["gi"] % 2][:, :], start=True, stop=True))
            for g in SCH:
                h, qc, gi = g["h"], g["qc"], g["gi"]
                if qc == 0:
                    te.wait_ge(c_ksl.h, h + 3)
                def emit_s(tl):
                    if tl["mm"] > 4:
                        te.wait_ge(c_exp.h, tl["mm"] - 4)
                    c_mm.inc(te.matmul(ps_s[tl["mm"] % 4][:, :],
                                       ksl[h % 2][:, tl["kc"] * 128:tl["kc"] * 128 + 128],
                                       xnF[0:128, (h // 2) * S + qc * NQ: (h // 2) * S + (qc + 1) * NQ],
                                       start=True, stop=True))

                def emit_pv(tl):
                    nonlocal vst_seen
                    need = 2 + 32 * h + min(tl["kc"] + 1, NPT)
                    if need > vst_seen:
                        te.wait_ge(c_vst.h, need)
                        vst_seen = need
                    if tl["diag"]:
                        te.wait_ge(c_msk.h, tl["msk"])
                    else:
                        te.wait_ge(c_exp.h, tl["exp"])
                    if gi >= 2:
                        te.wait_ge(c_do.h, gi - 1)
                    c_pv.inc(te.matmul(ps_pv[gi % 2][0:65, :],
                                       vstrip[h % 2][:, 65 * tl["kc"]:65 * tl["kc"] + 65],
                                       pbuf[(tl["pv"] - 1) % 6][:, :],
                                       start=(tl["kc"] == 0), stop=(tl["kc"] == g["tiles"][-1]["kc"])))
                pending = []
                tl_all = g["tiles"]
                first = True
                for j0 in range(0, len(tl_all), RUN):
                    for tl in tl_all[j0:j0 + RUN]:
                        emit_s(tl)
                    if first and rb_pend is not None:
                        emit_rb(rb_pend)
                        rb_pend = None
                    first = False
                    for tl in pending:
                        emit_pv(tl)
                    pending = tl_all[j0:j0 + RUN]
                for tl in pending:
                    emit_pv(tl)
                rb_pend = g
            emit_rb(SCH[-1])

        if extra is not None:
            extra(blk)
    P.barrier()
    P.sem_release(_smark)
    pes.close()


if __name__ == "__main__":
    print("library")


def proj_phase(P, tag, tiles, nk, lhs_ap, rhs_ap, evac_engine, evac_fn,
               psum_shape=(128, 512), extra_mm=None, store_fn=None):
    u = P.uid()
    _smark = P.sem_mark()
    pes = contextlib.ExitStack()
    NPS = 4
    ps = [P.psum(f"pj{u}_{i}", list(psum_shape), F32, es=pes) for i in range(NPS)]
    c_mm = Ctr(P, f"pj_mm{u}")
    c_ev = Ctr(P, f"pj_ev{u}")
    with P.block() as blk:
        @blk.tensor
        def _(te):
            for ti, tile in enumerate(tiles):
                if ti >= NPS:
                    te.wait_ge(c_ev.h, ti - NPS + 1)
                for k in range(nk):
                    last = (k == nk - 1) and extra_mm is None
                    r = te.matmul(ps[ti % NPS][:, :], lhs_ap(tile, k), rhs_ap(tile, k),
                                  start=(k == 0), stop=last)
                if extra_mm is not None:
                    r = extra_mm(te, tile, ps[ti % NPS])
                c_mm.inc(r)

        def mkevac(eng):
            for ti, tile in enumerate(tiles):
                eng.wait_ge(c_mm.h, ti + 1)
                r = evac_fn(eng, ti, tile, ps[ti % NPS])
                c_ev.inc(r)
        getattr(blk, evac_engine)(mkevac)

        if store_fn is not None:
            @blk.gpsimd
            def _(gp):
                for ti, tile in enumerate(tiles):
                    gp.wait_ge(c_ev.h, ti + 1)
                    store_fn(gp, ti, tile)
    P.barrier()
    P.sem_release(_smark)
    pes.close()
    return c_ev


def emit_tiny_qproj(P, xnF, wqkv_q, sc_q):
    bq = P.c("bqkv")
    tiles = [dict(do=do, tc=tc) for do in range(8) for tc in range(8)]
    u = P.uid()
    _smark = P.sem_mark()
    pes = contextlib.ExitStack()
    ob = [P.sbuf(f"qp{u}_{i}", [128, NQ], BF16, es=pes) for i in range(2)]
    stc = Slots(P, f"qpst{u}", 2)

    def lhs(tile, k):
        return wqkv_q[:, k * D + tile["do"] * 128: k * D + tile["do"] * 128 + 128]

    def rhs(tile, k):
        return xnF[:, k * S + tile["tc"] * NQ: k * S + (tile["tc"] + 1) * NQ]

    def evac(v, ti, tile, psum):
        if ti >= 2:
            v.wait_ge(stc.c(ti).h, 16 * ((ti - 2) // 2 + 1))
        return v.tensor_scalar(ob[ti % 2][:, :], psum[:, :],
                               bq[:, tile["do"]:tile["do"] + 1], None, ALU.add)

    def store(gp, ti, tile):
        stc.c(ti).inc(gp.dma_start(
            out=sc_q[tile["do"] * 128:(tile["do"] + 1) * 128, tile["tc"] * NQ:(tile["tc"] + 1) * NQ],
            in_=ob[ti % 2][:, :]), 16)

    proj_phase(P, "qproj", tiles, 8, lhs, rhs, "vector", evac, store_fn=store)
    with P.block() as blk:
        @blk.gpsimd
        def _(gp):
            gp.wait_ge(stc.c(0).h, stc.c(0).n)
            gp.wait_ge(stc.c(1).h, stc.c(1).n)
    P.barrier()
    P.sem_release(_smark)
    pes.close()


def emit_tiny_scores(P, xnF, x_full, wqkv_kv, vrow, sc_q, sc_vall, scores):
    sel = P.c("sel")
    bq = P.c("bqkv")
    ones1 = P.c("ones_1x128_bf")
    for bp in range(NB):
        u = P.uid()
        _smark = P.sem_mark()
        pes = contextlib.ExitStack()
        xnbF = P.sbuf(f"tsx{u}", [128, 8 * S], BF16, es=pes)
        emit_ln0(P, lambda t: x_full[bp * S + t * 128: bp * S + (t + 1) * 128, :],
                 xnT=None, xnF=xnbF)
        kb = [P.sbuf(f"tsk{u}_{i}", [128, NQ], BF16, es=pes) for i in range(2)]
        qb = [P.sbuf(f"tsq{u}_{i}", [128, NQ], BF16, es=pes) for i in range(2)]
        eb = P.sbuf(f"tse{u}", [128, 8 * NQ], BF16, es=pes)
        vb = [P.sbuf(f"tsv{u}_{i}", [128, NQ], BF16, es=pes) for i in range(2)]
        psk = [P.psum(f"tspk{u}_{i}", [128, NQ], F32, es=pes) for i in range(3)]
        pss = [P.psum(f"tsps{u}_{i}", [128, 4], F32, es=pes) for i in range(2)]
        psv = [P.psum(f"tspv{u}_{i}", [128, NQ], F32, es=pes) for i in range(2)]
        qld = Slots(P, f"tsqld{u}", 2)
        vst = Slots(P, f"tsvs{u}", 2)
        c_kmm = Ctr(P, f"tskm{u}")
        c_kev = Ctr(P, f"tske{u}")
        c_e = Ctr(P, f"tsee{u}")
        c_smm = Ctr(P, f"tssm{u}")
        c_sev = Ctr(P, f"tssv{u}")
        c_vmm = Ctr(P, f"tsvm{u}")
        c_vev = Ctr(P, f"tsvv{u}")

        with P.block() as blk:
            @blk.sync
            def _(sy):
                for tc in range(8):
                    for j in range(NDC):
                        n = tc * 8 + j
                        if n >= 2:
                            sy.wait_ge(c_e.h, n - 1)
                        qld.c(n).inc(sy.dma_start(
                            out=qb[n % 2][:, :],
                            in_=sc_q[j * 128:(j + 1) * 128, tc * NQ:(tc + 1) * NQ]), 16)

            @blk.tensor
            def _(te):
                for tc in range(8):
                    for j in range(NDC):
                        n = tc * 8 + j
                        if n >= 3:
                            te.wait_ge(c_kev.h, n - 2)
                        for k in range(NDC):
                            te.matmul(psk[n % 3][:, :],
                                      wqkv_kv[:, k * 2048 + j * 128: k * 2048 + j * 128 + 128],
                                      xnbF[:, k * S + tc * NQ: k * S + (tc + 1) * NQ],
                                      start=(k == 0), stop=False)
                        c_kmm.inc(te.matmul(psk[n % 3][:, :],
                                            vrow[0:1, j * 128:(j + 1) * 128],
                                            P.c("ones_1x512_bf")[0:1, :],
                                            start=False, stop=True))
                    for sc_ in range(4):
                        g = tc * 4 + sc_
                        if g >= 2:
                            te.wait_ge(c_sev.h, g - 1)
                        te.wait_ge(c_e.h, tc * 8 + 8)
                        for j in range(NDC):
                            r = te.matmul(pss[g % 2][:, :],
                                          eb[:, j * NQ + sc_ * 128: j * NQ + sc_ * 128 + 128],
                                          sel[:, j * 4:(j + 1) * 4],
                                          start=(j == 0), stop=(j == NDC - 1))
                        c_smm.inc(r)
                    for pt in range(4):
                        for dh in range(2):
                            m = tc * 8 + pt * 2 + dh
                            if m >= 2:
                                te.wait_ge(c_vev.h, m - 1)
                            for k in range(NDC):
                                te.matmul(psv[m % 2][:, :],
                                          xnbF[:, k * S + tc * NQ + pt * 128: k * S + tc * NQ + pt * 128 + 128],
                                          wqkv_kv[:, k * 2048 + 1024 + dh * NQ: k * 2048 + 1024 + (dh + 1) * NQ],
                                          start=(k == 0), stop=False)
                            c_vmm.inc(te.matmul(psv[m % 2][:, :], ones1[:, :],
                                                vrow[0:1, 1024 + dh * NQ: 1024 + (dh + 1) * NQ],
                                                start=False, stop=True))

            @blk.scalar
            def _(sc):
                for tc in range(8):
                    for j in range(NDC):
                        n = tc * 8 + j
                        sc.wait_ge(c_kmm.h, n + 1)
                        if n >= 2:
                            sc.wait_ge(c_e.h, n - 1)
                        c_kev.inc(sc.copy(kb[n % 2][:, :], psk[n % 3][:, :]))
                    for pt in range(4):
                        for dh in range(2):
                            m = tc * 8 + pt * 2 + dh
                            sc.wait_ge(c_vmm.h, m + 1)
                            if m >= 2:
                                sc.wait_ge(vst.c(m).h, 16 * ((m - 2) // 2 + 1))
                            c_vev.inc(sc.copy(vb[m % 2][:, :], psv[m % 2][:, :]))
                            sc.wait_ge(c_vev.h, m + 1)
                            vst.c(m).inc(sc.dma_start(
                                out=sc_vall[bp * S + tc * NQ + pt * 128: bp * S + tc * NQ + pt * 128 + 128,
                                            dh * NQ:(dh + 1) * NQ],
                                in_=vb[m % 2][:, :]), 16)

            @blk.vector
            def _(v):
                for tc in range(8):
                    for j in range(NDC):
                        n = tc * 8 + j
                        v.wait_ge(c_kev.h, n + 1)
                        v.wait_ge(qld.c(n).h, 16 * (n // 2 + 1))
                        if tc >= 1 and j == 0:
                            v.wait_ge(c_smm.h, tc * 4)
                        c_e.inc(v.tensor_mul(eb[:, j * NQ:(j + 1) * NQ],
                                             qb[n % 2][:, :], kb[n % 2][:, :]))
                    for sc_ in range(4):
                        g = tc * 4 + sc_
                        v.wait_ge(c_smm.h, g + 1)
                        sg = g
                        dst = scores.rearrange("p (s h b) -> p s h b", h=4, b=8)[:, sg, :, bp:bp + 1]
                        srcp = pss[g % 2].rearrange("p (h o) -> p h o", o=1)
                        c_sev.inc(v.tensor_copy(dst, srcp))

        with P.block() as blk:
            @blk.scalar
            def _(sc):
                sc.wait_ge(vst.c(0).h, vst.c(0).n)
                sc.wait_ge(vst.c(1).h, vst.c(1).n)
        P.barrier()
        P.sem_release(_smark)
        pes.close()


def emit_tiny_softmax_pv(P, scores, sc_vall, attoutF):
    u = P.uid()
    _smark = P.sem_mark()
    pes = contextlib.ExitStack()
    att = P.sbuf(f"tpa{u}", [128, 32 * 32], F32, es=pes)
    mx = P.sbuf(f"tpm{u}", [128, 4], F32, es=pes)
    ex = [P.sbuf(f"tpe{u}_{i}", [128, 32], F32, es=pes) for i in range(2)]
    exe = [P.sbuf(f"tpx{u}_{i}", [128, 32], F32, es=pes) for i in range(2)]
    sm = P.sbuf(f"tpsm{u}", [128, 4], F32, es=pes)
    rc = P.sbuf(f"tprc{u}", [128, 4], F32, es=pes)
    vtl = [P.sbuf(f"tpv{u}_{i}", [128, D], BF16, es=pes) for i in range(2)]
    acc = [P.sbuf(f"tpc{u}_{i}", [128, D], BF16, es=pes) for i in range(2)]
    ptp = [P.psum(f"tpp{u}_{i}", [128, 128], BF16, es=pes) for i in range(2)]
    vld = Slots(P, f"tpl{u}", 2)
    c_sub = Ctr(P, f"tpsb{u}")
    c_exp = Ctr(P, f"tpxp{u}")
    c_att = Ctr(P, f"tpat{u}")
    c_vdn = Ctr(P, f"tpvd{u}")
    c_pvd = Ctr(P, f"tppd{u}")
    c_tp = Ctr(P, f"tptp{u}")
    c_ev = Ctr(P, f"tpev{u}")
    id_bf = P.c("id_bf")

    vc1 = Ctr(P, f"tpvc{u}")
    with P.block() as blk:
        @blk.vector
        def _(v):
            def tail(sg):
                v.wait_ge(c_exp.h, sg + 1)
                e3 = exe[sg % 2].rearrange("p (h b) -> p h b", b=8)
                v.wait_ge(vc1.h, vc1.n)
                vc1.inc(v.reduce_sum(sm[:, :], e3, axis=mybir.AxisListType.X))
                v.wait_ge(vc1.h, vc1.n)
                vc1.inc(v.reciprocal(rc[:, :], sm[:, :]))
                v.wait_ge(vc1.h, vc1.n)
                rcb = bass.AP(rc, 0, [[4, 128], [1, 4], [0, 8]])
                vc1.inc(v.tensor_tensor(att[:, sg * 32:(sg + 1) * 32].rearrange("p (h b) -> p h b", b=8),
                                        e3, rcb, ALU.mult))
            for sg in range(32):
                b3 = scores[:, sg * 32:(sg + 1) * 32].rearrange("p (h b) -> p h b", b=8)
                v.wait_ge(vc1.h, vc1.n)
                vc1.inc(v.reduce_max(mx[:, :], b3, axis=mybir.AxisListType.X))
                v.wait_ge(vc1.h, vc1.n)
                vc1.inc(v.tensor_tensor(ex[sg % 2].rearrange("p (h b) -> p h b", b=8), b3,
                                        bass.AP(mx, 0, [[4, 128], [1, 4], [0, 8]]), ALU.subtract))
                if sg >= 1:
                    tail(sg - 1)
            tail(31)

        @blk.scalar
        def _(sc):
            for sg in range(32):
                cnt = 5 * sg + 2
                sc.wait_ge(vc1.h, cnt)
                c_exp.inc(sc.activation(exe[sg % 2][:, :], ex[sg % 2][:, :], AF.Exp))

    P.barrier()
    with P.block() as blk:
        vc3 = Ctr(P, f"tpvc3{u}")

        @blk.sync
        def _(sy):
            for sg in range(32):
                for bp in range(NB):
                    n = sg * 8 + bp
                    if n >= 2:
                        sy.wait_ge(vc3.h, (n - 1) * 4)
                    vld.c(n).inc(sy.dma_start(
                        out=vtl[n % 2][:, :],
                        in_=sc_vall[bp * S + sg * 128: bp * S + (sg + 1) * 128, :]), 16)

        @blk.vector
        def _(v):
            for sg in range(32):
                for bp in range(NB):
                    n = sg * 8 + bp
                    v.wait_ge(vld.c(n).h, 16 * (n // 2 + 1))
                    if bp == 1 and sg >= 1:
                        v.wait_ge(c_tp.h, 8 * sg)
                    for h in range(4):
                        cidx = sg * 32 + h * 8 + bp
                        if vc3.n >= 4:
                            v.wait_ge(vc3.h, vc3.n - 3)
                        if bp == 0:
                            r = v.tensor_scalar(acc[0][:, 256 * h:256 * (h + 1)],
                                                vtl[n % 2][:, 256 * h:256 * (h + 1)],
                                                att[:, cidx:cidx + 1], None, ALU.mult)
                        else:
                            r = v.scalar_tensor_tensor(acc[bp % 2][:, 256 * h:256 * (h + 1)],
                                                       vtl[n % 2][:, 256 * h:256 * (h + 1)],
                                                       att[:, cidx:cidx + 1],
                                                       acc[(bp + 1) % 2][:, 256 * h:256 * (h + 1)],
                                                       ALU.mult, ALU.add)
                        vc3.inc(r)

        @blk.tensor
        def _(te):
            for sg in range(32):
                te.wait_ge(vc3.h, (sg + 1) * 32)
                for j in range(NDC):
                    n = sg * 8 + j
                    if n >= 2:
                        te.wait_ge(c_ev.h, n - 1)
                    c_tp.inc(te.transpose(ptp[n % 2][:, :], acc[1][:, j * 128:(j + 1) * 128], id_bf[:, :]))

        @blk.scalar
        def _(sc):
            for sg in range(32):
                for j in range(NDC):
                    n = sg * 8 + j
                    sc.wait_ge(c_tp.h, n + 1)
                    c_ev.inc(sc.copy(attoutF[:, j * S + sg * 128: j * S + (sg + 1) * 128], ptp[n % 2][:, :]))
    P.barrier()
    P.sem_release(_smark)
    pes.close()


def emit_wo_s(P, attoutF, xnF, woT_sb, sc_s):
    bo = P.c("bo")
    u = P.uid()
    _smark = P.sem_mark()
    pes = contextlib.ExitStack()
    ob = [P.sbuf(f"wo{u}_{i}", [128, NQ], BF16, es=pes) for i in range(2)]
    stc = Slots(P, f"wos{u}", 2)
    tiles = [dict(do=do, tc=tc) for tc in range(8) for do in range(8)]

    def lhs(tile, k):
        return woT_sb[:, k * D + tile["do"] * 128: k * D + tile["do"] * 128 + 128]

    def rhs(tile, k):
        return attoutF[:, k * S + tile["tc"] * NQ: k * S + (tile["tc"] + 1) * NQ]

    def evac(v, ti, tile, psum):
        if ti >= 2:
            v.wait_ge(stc.c(ti).h, 16 * ((ti - 2) // 2 + 1))
        return v.scalar_tensor_tensor(
            ob[ti % 2][:, :], psum[:, :], bo[:, tile["do"]:tile["do"] + 1],
            xnF[:, tile["do"] * S + tile["tc"] * NQ: tile["do"] * S + (tile["tc"] + 1) * NQ],
            ALU.add, ALU.add)

    def store(gp, ti, tile):
        stc.c(ti).inc(gp.dma_start(
            out=sc_s[tile["do"] * 128:(tile["do"] + 1) * 128, tile["tc"] * NQ:(tile["tc"] + 1) * NQ],
            in_=ob[ti % 2][:, :]), 16)

    proj_phase(P, "wos", tiles, 8, lhs, rhs, "vector", evac, store_fn=store)
    with P.block() as blk:
        @blk.gpsimd
        def _(gp):
            gp.wait_ge(stc.c(0).h, stc.c(0).n)
            gp.wait_ge(stc.c(1).h, stc.c(1).n)
    P.barrier()
    P.sem_release(_smark)
    pes.close()


def emit_fln(P, src_dram, gname, bname, dst_dram, dst32=None):
    g_ = P.c(gname)
    be_ = P.c(bname)
    ones128 = P.c("ones_128x1_bf")
    ones1f = P.c("ones_1x128_f32")
    u = P.uid()
    _smark = P.sem_mark()
    pes = contextlib.ExitStack()
    sb = [P.sbuf(f"fl{u}_{i}", [128, NQ], BF16, es=pes) for i in range(3)]
    sb2 = [P.sbuf(f"flb{u}_{i}", [128, NQ], BF16, es=pes) for i in range(2)]
    sqb = [P.sbuf(f"flq{u}_{i}", [128, NQ], BF16, es=pes) for i in range(2)]
    murow = P.sbuf(f"flmu{u}", [1, S], F32, es=pes)
    rsrow = P.sbuf(f"flrs{u}", [1, S], F32, es=pes)
    tmpr = P.sbuf(f"fltm{u}", [1, NQ], F32, es=pes)
    ob = [P.sbuf(f"flo{u}_{i}", [128, NQ], BF16, es=pes) for i in range(2)]
    ob32 = [P.sbuf(f"flo32{u}_{i}", [128, NQ], F32, es=pes) for i in range(2)] if dst32 is not None else None
    t1b = [P.sbuf(f"flt{u}_{i}", [128, NQ], F32, es=pes) for i in range(2)]
    ps1 = P.psum(f"flp1{u}", [1, NQ], F32, es=pes)
    ps2 = P.psum(f"flp2{u}", [1, NQ], F32, es=pes)
    psb = [P.psum(f"flpb{u}_{i}", [128, NQ], F32, es=pes) for i in range(4)]
    ld = Slots(P, f"flld{u}", 3)
    ld2 = Slots(P, f"flld2{u}", 2)
    stc = Slots(P, f"flst2{u}", 2)
    stc32 = Slots(P, f"flst3{u}", 2) if dst32 is not None else None
    c_sq = Ctr(P, f"flsq{u}")
    c_s1m = Ctr(P, f"fls1{u}")
    c_sqm = Ctr(P, f"flsqm{u}")
    c_srt = Ctr(P, f"flsr{u}")
    c_bc = Ctr(P, f"flbc{u}")
    vcs = Ctr(P, f"flvc{u}")

    NOPS = 4 if dst32 is not None else 3

    def vcs_smalls_done(tc):
        return 6 * (tc + 1) + 8 * NOPS * tc

    def vcs_norm_done(tc, j):
        return 6 * (tc + 1) + 8 * NOPS * tc + NOPS * (j + 1)

    with P.block() as blk:
        @blk.sync
        def _(sy):
            def load2(tc):
                for j in range(NDC):
                    n = tc * 8 + j
                    if n >= 2:
                        sy.wait_ge(vcs.h, vcs_norm_done((n - 2) // 8, (n - 2) % 8))
                    ld2.c(n).inc(sy.dma_start(
                        out=sb2[n % 2][:, :],
                        in_=src_dram[j * 128:(j + 1) * 128, tc * NQ:(tc + 1) * NQ]), 16)
            for tc in range(8):
                for j in range(NDC):
                    n = tc * 8 + j
                    if n >= 3:
                        sy.wait_ge(c_sqm.h, n - 2)
                    ld.c(n).inc(sy.dma_start(
                        out=sb[n % 3][:, :],
                        in_=src_dram[j * 128:(j + 1) * 128, tc * NQ:(tc + 1) * NQ]), 16)
                if tc >= 1:
                    load2(tc - 1)
            load2(7)

        @blk.scalar
        def _(sc):
            for tc in range(8):
                for j in range(NDC):
                    n = tc * 8 + j
                    sc.wait_ge(ld.c(n).h, 16 * (n // 3 + 1))
                    if n >= 2:
                        sc.wait_ge(c_sqm.h, n - 1)
                    c_sq.inc(sc.activation(sqb[n % 2][:, :], sb[n % 3][:, :], AF.Square))
                sc.wait_ge(vcs.h, vcs_smalls_done(tc) - 1)
                c_srt.inc(sc.activation(tmpr[0:1, :], rsrow[0:1, tc * NQ:(tc + 1) * NQ], AF.Sqrt))

        @blk.tensor
        def _(te):
            for tc in range(8):
                if tc >= 1:
                    te.wait_ge(vcs.h, vcs_smalls_done(tc - 1) - 4)
                for j in range(NDC):
                    n = tc * 8 + j
                    te.wait_ge(ld.c(n).h, 16 * (n // 3 + 1))
                    c_s1m.inc(te.matmul(ps1[0:1, :], ones128[:, :], sb[n % 3][:, :],
                                        start=(j == 0), stop=(j == NDC - 1)))
                    te.wait_ge(c_sq.h, n + 1)
                    c_sqm.inc(te.matmul(ps2[0:1, :], ones128[:, :], sqb[n % 2][:, :],
                                        start=(j == 0), stop=(j == NDC - 1)))
                te.wait_ge(vcs.h, vcs_smalls_done(tc))
                if tc >= 2:
                    te.wait_ge(vcs.h, vcs_norm_done(tc - 2, 7))
                te.matmul(psb[2 * (tc % 2)][:, :], ones1f[:, :], murow[0:1, tc * NQ:(tc + 1) * NQ],
                          start=True, stop=True)
                c_bc.inc(te.matmul(psb[2 * (tc % 2) + 1][:, :], ones1f[:, :], rsrow[0:1, tc * NQ:(tc + 1) * NQ],
                                   start=True, stop=True), 2)

        @blk.vector
        def _(v):
            def normalize(tc):
                v.wait_ge(c_bc.h, 2 * (tc + 1))
                for j in range(NDC):
                    n = tc * 8 + j
                    v.wait_ge(ld2.c(n).h, 16 * (n // 2 + 1))
                    v.wait_ge(vcs.h, vcs.n)
                    vcs.inc(v.tensor_sub(t1b[n % 2][:, :], sb2[n % 2][:, :], psb[2 * (tc % 2)][:, :]))
                    v.wait_ge(vcs.h, vcs.n)
                    vcs.inc(v.tensor_mul(t1b[n % 2][:, :], t1b[n % 2][:, :], psb[2 * (tc % 2) + 1][:, :]))
                    v.wait_ge(vcs.h, vcs.n)
                    if n >= 2:
                        v.wait_ge(stc.c(n).h, 16 * ((n - 2) // 2 + 1))
                    vcs.inc(v.tensor_scalar(ob[n % 2][:, :], t1b[n % 2][:, :],
                                            g_[:, j:j + 1], be_[:, j:j + 1], ALU.mult, ALU.add))
                    if dst32 is not None:
                        v.wait_ge(vcs.h, vcs.n)
                        if n >= 2:
                            v.wait_ge(stc32.c(n).h, 16 * ((n - 2) // 2 + 1))
                        vcs.inc(v.tensor_scalar(ob32[n % 2][:, :], t1b[n % 2][:, :],
                                                g_[:, j:j + 1], be_[:, j:j + 1], ALU.mult, ALU.add))
            for tc in range(8):
                v.wait_ge(c_s1m.h, tc * 8 + 8)
                v.wait_ge(c_sqm.h, tc * 8 + 8)
                v.wait_ge(vcs.h, vcs.n)
                vcs.inc(v.tensor_scalar(murow[0:1, tc * NQ:(tc + 1) * NQ], ps1[0:1, :], 1.0 / D, None, ALU.mult))
                vcs.inc(v.tensor_scalar(rsrow[0:1, tc * NQ:(tc + 1) * NQ], ps2[0:1, :], 1.0 / D, None, ALU.mult))
                v.wait_ge(vcs.h, vcs.n)
                vcs.inc(v.tensor_tensor(tmpr[0:1, :], murow[0:1, tc * NQ:(tc + 1) * NQ],
                                        murow[0:1, tc * NQ:(tc + 1) * NQ], ALU.mult))
                v.wait_ge(vcs.h, vcs.n)
                vcs.inc(v.tensor_sub(rsrow[0:1, tc * NQ:(tc + 1) * NQ], rsrow[0:1, tc * NQ:(tc + 1) * NQ], tmpr[0:1, :]))
                v.wait_ge(vcs.h, vcs.n)
                vcs.inc(v.tensor_scalar(rsrow[0:1, tc * NQ:(tc + 1) * NQ], rsrow[0:1, tc * NQ:(tc + 1) * NQ], EPS, None, ALU.add))
                v.wait_ge(c_srt.h, tc + 1)
                vcs.inc(v.reciprocal(rsrow[0:1, tc * NQ:(tc + 1) * NQ], tmpr[0:1, :]))
                assert vcs.n == vcs_smalls_done(tc), (vcs.n, tc)
                normalize(tc)
                assert vcs.n == vcs_norm_done(tc, 7)

        @blk.gpsimd
        def _(gp):
            for tc in range(8):
                for j in range(NDC):
                    n = tc * 8 + j
                    gp.wait_ge(vcs.h, vcs_norm_done(tc, j))
                    stc.c(n).inc(gp.dma_start(
                        out=dst_dram[j * 128:(j + 1) * 128, tc * NQ:(tc + 1) * NQ],
                        in_=ob[n % 2][:, :]), 16)
                    if dst32 is not None:
                        stc32.c(n).inc(gp.dma_start(
                            out=dst32[j * 128:(j + 1) * 128, tc * NQ:(tc + 1) * NQ],
                            in_=ob32[n % 2][:, :]), 16)
            gp.wait_ge(stc.c(0).h, stc.c(0).n)
            gp.wait_ge(stc.c(1).h, stc.c(1).n)
            if dst32 is not None:
                gp.wait_ge(stc32.c(0).h, stc32.c(0).n)
                gp.wait_ge(stc32.c(1).h, stc32.c(1).n)
    P.barrier()
    P.sem_release(_smark)
    pes.close()


def emit_load_wslab(P, dst_sb, srcs, tag):
    u = P.uid()
    _smark = P.sem_mark()
    c = Ctr(P, f"wl{u}")
    with P.block() as blk:
        @blk.gpsimd
        def _(gp):
            for col0, ap in srcs:
                ncols = ap.shape[-1]
                c.inc(gp.dma_start(out=dst_sb[:, col0:col0 + ncols], in_=ap), 16)
            gp.wait_ge(c.h, c.n)
    P.barrier()
    P.sem_release(_smark)


def emit_moe_masks(P, sc_y, wg_sb, mrow):
    u = P.uid()
    _smark = P.sem_mark()
    pes = contextlib.ExitStack()
    yb = [P.sbuf(f"mk{u}_{i}", [128, 8 * NQ], F32, es=pes) for i in range(2)]
    lgT = [P.sbuf(f"mkl{u}_{i}", [128, 8], F32, es=pes) for i in range(2)]
    mxc = P.sbuf(f"mkm{u}", [128, 1], F32, es=pes)
    mskT = P.sbuf(f"mks{u}", [128, 8], BF16, es=pes)
    maskF = P.sbuf(f"mkf{u}", [8, S], BF16, es=pes)
    pslg = [P.psum(f"mkp{u}_{i}", [128, 8], F32, es=pes) for i in range(2)]
    pstp = P.psum(f"mkt{u}", [8, 128], BF16, es=pes)
    ld = Slots(P, f"mkld{u}", 2)
    c_mm = Ctr(P, f"mkmm{u}")
    c_ev = Ctr(P, f"mkev{u}")
    c_tp = Ctr(P, f"mktp{u}")
    c_ae = Ctr(P, f"mkae{u}")
    id_bf = P.c("id_bf")

    with P.block() as blk:
        @blk.sync
        def _(sy):
            for tc in range(8):
                if tc >= 2:
                    sy.wait_ge(c_mm.h, (tc - 1) * 4)
                ld.c(tc).inc(sy.dma_start(
                    out=yb[tc % 2][:, :].rearrange("p (a c) -> p a c", a=8),
                    in_=sc_y[:, tc * NQ:(tc + 1) * NQ].rearrange("(a p) c -> p a c", p=128)), 16)

        @blk.tensor
        def _(te):
            def tpose(pt):
                te.wait_ge(c_ev.h, 3 * (pt + 1))
                if pt >= 1:
                    te.wait_ge(c_ae.h, pt)
                c_tp.inc(te.transpose(pstp[:, :], mskT[:, :], id_bf[:, :]))
            for tc in range(8):
                te.wait_ge(ld.c(tc).h, 16 * (tc // 2 + 1))
                for ps_ in range(4):
                    pt = tc * 4 + ps_
                    if pt >= 2:
                        te.wait_ge(c_ev.h, 3 * (pt - 1))
                    for j in range(NDC):
                        r = te.matmul(pslg[pt % 2][:, :],
                                      yb[tc % 2][:, j * NQ + ps_ * 128: j * NQ + ps_ * 128 + 128],
                                      wg_sb[:, j * 8:(j + 1) * 8],
                                      start=(j == 0), stop=(j == NDC - 1))
                    c_mm.inc(r)
                    if pt >= 1:
                        tpose(pt - 1)
            tpose(31)

        @blk.vector
        def _(v):
            for pt in range(32):
                v.wait_ge(c_mm.h, pt + 1)
                if pt >= 1:
                    v.wait_ge(c_tp.h, pt)
                v.wait_ge(c_ev.h, c_ev.n)
                c_ev.inc(v.tensor_copy(lgT[pt % 2][:, :], pslg[pt % 2][:, :]))
                v.wait_ge(c_ev.h, c_ev.n)
                c_ev.inc(v.reduce_max(mxc[:, :], lgT[pt % 2][:, :], axis=mybir.AxisListType.X))
                v.wait_ge(c_ev.h, c_ev.n)
                c_ev.inc(v.tensor_scalar(mskT[:, :], lgT[pt % 2][:, :], mxc[:, :], None, ALU.is_ge))

        @blk.scalar
        def _(sc):
            for pt in range(32):
                sc.wait_ge(c_tp.h, pt + 1)
                c_ae.inc(sc.copy(maskF[:, pt * 128:(pt + 1) * 128], pstp[:, :]))

        @blk.gpsimd
        def _(gp):
            gp.wait_ge(c_ae.h, 32)
            cst = Ctr(P, f"mkst{u}")
            cst.inc(gp.dma_start(out=mrow[:, :], in_=maskF[:, :]), 16)
            gp.wait_ge(cst.h, cst.n)
    P.barrier()
    P.sem_release(_smark)
    pes.close()


def emit_moe_experts(P, sc_y, mrow, sc_moe):
    W1 = P.t["W1"]
    W2 = P.t["W2"]
    b1 = P.c("b1")
    b2 = P.c("b2")
    ones1 = P.c("ones_1x128_bf")
    u0 = P.uid()
    pes0 = contextlib.ExitStack()
    w1sb = P.sbuf(f"mew1{u0}", [128, 8 * 4096], BF16, es=pes0)
    w2sb = P.sbuf(f"mew2{u0}", [128, 32 * 1024], BF16, es=pes0)
    for e in range(NB):
        u = P.uid()
        _smark = P.sem_mark()
        pes = contextlib.ExitStack()
        yb = [P.sbuf(f"mey{u}_{i}", [128, 8 * NQ], BF16, es=pes) for i in range(2)]
        mrow_sb = P.sbuf(f"memr{u}", [1, S], BF16, es=pes)
        hsb = P.sbuf(f"meh{u}", [128, 32 * NQ], BF16, es=pes)
        mbb = P.sbuf(f"memb{u}", [128, 8 * NQ], BF16, es=pes)
        oeb = [P.sbuf(f"meo{u}_{i}", [128, NQ], BF16, es=pes) for i in range(2)]
        psh = [P.psum(f"meph{u}_{i}", [128, NQ], F32, es=pes) for i in range(3)]
        pso = [P.psum(f"mepo{u}_{i}", [128, NQ], F32, es=pes) for i in range(4)]
        psm = P.psum(f"mepm{u}", [128, NQ], F32, es=pes)
        wld = Ctr(P, f"mewl{u}")
        yld = Slots(P, f"meyl{u}", 2)
        ost = Slots(P, f"meos{u}", 2)
        c_mbm = Ctr(P, f"memb_m{u}")
        c_mbe = Ctr(P, f"memb_e{u}")
        c_hm = Ctr(P, f"mehm{u}")
        c_he = Ctr(P, f"mehe{u}")
        c_om = Ctr(P, f"meom{u}")
        c_oe = Ctr(P, f"meoe{u}")

        with P.block() as blk:
            @blk.gpsimd
            def _(gp):
                wld.inc(gp.dma_start(out=mrow_sb[0:1, :], in_=mrow[e:e + 1, :]), 16)
                for k in range(NDC):
                    wld.inc(gp.dma_start(out=w1sb[:, k * 4096:(k + 1) * 4096],
                                         in_=W1[e, k * 128:(k + 1) * 128, :]), 16)
                for dfi in range(32):
                    wld.inc(gp.dma_start(out=w2sb[:, dfi * 1024:(dfi + 1) * 1024],
                                         in_=W2[e, dfi * 128:(dfi + 1) * 128, :]), 16)
                for tc in range(8):
                    for do in range(8):
                        n = tc * 8 + do
                        gp.wait_ge(c_oe.h, n + 1)
                        ost.c(n).inc(gp.dma_start(
                            out=sc_moe[do * 128:(do + 1) * 128, tc * NQ:(tc + 1) * NQ],
                            in_=oeb[n % 2][:, :],
                            accum_op=(ALU.bypass if e == 0 else ALU.add)), 16)
                gp.wait_ge(ost.c(0).h, ost.c(0).n)
                gp.wait_ge(ost.c(1).h, ost.c(1).n)

            @blk.sync
            def _(sy):
                for tc in range(8):
                    if tc >= 2:
                        sy.wait_ge(c_hm.h, 32 * (tc - 1))
                    yld.c(tc).inc(sy.dma_start(
                        out=yb[tc % 2][:, :].rearrange("p (a c) -> p a c", a=8),
                        in_=sc_y[:, tc * NQ:(tc + 1) * NQ].rearrange("(a p) c -> p a c", p=128)), 16)

            @blk.tensor
            def _(te):
                te.wait_ge(wld.h, wld.n)
                for tc in range(8):
                    if tc >= 1:
                        te.wait_ge(c_mbe.h, tc)
                    c_mbm.inc(te.matmul(psm[:, :], ones1[:, :], mrow_sb[0:1, tc * NQ:(tc + 1) * NQ],
                                        start=True, stop=True))
                for tc in range(8):
                    te.wait_ge(yld.c(tc).h, 16 * (tc // 2 + 1))
                    for df in range(32):
                        n3 = tc * 32 + df
                        if n3 >= 3:
                            te.wait_ge(c_he.h, n3 - 2)
                        for k in range(NDC):
                            r = te.matmul(psh[n3 % 3][:, :],
                                          wsb[:, k * 4096 + df * 128: k * 4096 + df * 128 + 128],
                                          yb[tc % 2][:, k * NQ:(k + 1) * NQ],
                                          start=(k == 0), stop=(k == NDC - 1))
                        c_hm.inc(r)
                    te.wait_ge(c_he.h, 32 * (tc + 1))
                    for dog in range(2):
                        for do4 in range(4):
                            do = dog * 4 + do4
                            n = tc * 8 + do
                            if n >= 4:
                                te.wait_ge(c_oe.h, n - 3)
                            for df in range(32):
                                r = te.matmul(pso[do4][:, :],
                                              w2sb[:, df * 1024 + do * 128: df * 1024 + do * 128 + 128],
                                              hsb[:, df * NQ:(df + 1) * NQ],
                                              start=(df == 0), stop=(df == 31))
                            c_om.inc(r)

            @blk.scalar
            def _(sc):
                for tc in range(8):
                    sc.wait_ge(c_mbm.h, tc + 1)
                    c_mbe.inc(sc.copy(mbb[:, tc * NQ:(tc + 1) * NQ], psm[:, :]))
                for tc in range(8):
                    for df in range(32):
                        n3 = tc * 32 + df
                        sc.wait_ge(c_hm.h, n3 + 1)
                        if df == 0 and tc >= 1:
                            sc.wait_ge(c_om.h, 8 * tc)
                        c_he.inc(sc.activation(hsb[:, df * NQ:(df + 1) * NQ], psh[n3 % 3][:, :],
                                               AF.Relu, bias=b1[:, e * 32 + df: e * 32 + df + 1]))

            @blk.vector
            def _(v):
                for tc in range(8):
                    for do in range(8):
                        n = tc * 8 + do
                        v.wait_ge(c_om.h, n + 1)
                        v.wait_ge(c_mbe.h, min(tc + 1, 8))
                        if n >= 2:
                            v.wait_ge(ost.c(n).h, 16 * ((n - 2) // 2 + 1))
                        v.wait_ge(c_oe.h, c_oe.n)
                        c_oe.inc(v.scalar_tensor_tensor(oeb[n % 2][:, :], pso[do % 4][:, :],
                                                        b2[:, e * 8 + do: e * 8 + do + 1],
                                                        mbb[:, tc * NQ:(tc + 1) * NQ],
                                                        ALU.add, ALU.mult))
        P.barrier()
        P.sem_release(_smark)
        pes.close()
    pes0.close()


def emit_binop(P, a_dram, b_dram, dst_dram, op=None, out_dt=BF16):
    op = op if op is not None else ALU.add
    u = P.uid()
    _smark = P.sem_mark()
    pes = contextlib.ExitStack()
    ab = [P.sbuf(f"bo{u}a{i}", [128, NQ], BF16, es=pes) for i in range(2)]
    bb = [P.sbuf(f"bo{u}b{i}", [128, NQ], BF16, es=pes) for i in range(2)]
    obf = [P.sbuf(f"bo{u}o{i}", [128, NQ], out_dt, es=pes) for i in range(2)]
    lda = Slots(P, f"bo{u}la", 2)
    ldb = Slots(P, f"bo{u}lb", 2)
    stc = Slots(P, f"bo{u}st", 2)
    c_op = Ctr(P, f"bo{u}op")
    tiles = [(j, tc) for tc in range(8) for j in range(NDC)]
    with P.block() as blk:
        @blk.sync
        def _(sy):
            for n, (j, tc) in enumerate(tiles):
                if n >= 2:
                    sy.wait_ge(c_op.h, n - 1)
                lda.c(n).inc(sy.dma_start(out=ab[n % 2][:, :],
                                          in_=a_dram[j * 128:(j + 1) * 128, tc * NQ:(tc + 1) * NQ]), 16)
                ldb.c(n).inc(sy.dma_start(out=bb[n % 2][:, :],
                                          in_=b_dram[j * 128:(j + 1) * 128, tc * NQ:(tc + 1) * NQ]), 16)

        @blk.vector
        def _(v):
            for n, (j, tc) in enumerate(tiles):
                v.wait_ge(lda.c(n).h, 16 * (n // 2 + 1))
                v.wait_ge(ldb.c(n).h, 16 * (n // 2 + 1))
                if n >= 2:
                    v.wait_ge(stc.c(n).h, 16 * ((n - 2) // 2 + 1))
                c_op.inc(v.tensor_tensor(obf[n % 2][:, :], ab[n % 2][:, :], bb[n % 2][:, :], op))

        @blk.gpsimd
        def _(gp):
            for n, (j, tc) in enumerate(tiles):
                gp.wait_ge(c_op.h, n + 1)
                stc.c(n).inc(gp.dma_start(
                    out=dst_dram[j * 128:(j + 1) * 128, tc * NQ:(tc + 1) * NQ],
                    in_=obf[n % 2][:, :]), 16)
            gp.wait_ge(stc.c(0).h, stc.c(0).n)
            gp.wait_ge(stc.c(1).h, stc.c(1).n)
    P.barrier()
    P.sem_release(_smark)
    pes.close()


def emit_x1(P, sc_s3, x_dram, sc_x1T):
    u = P.uid()
    _smark = P.sem_mark()
    pes = contextlib.ExitStack()
    s3b = [P.sbuf(f"xo{u}s{i}", [128, 128], BF16, es=pes) for i in range(3)]
    xb = [P.sbuf(f"xo{u}x{i}", [128, D], F32, es=pes) for i in range(2)]
    s3t = [P.sbuf(f"xo{u}t{i}", [128, D], BF16, es=pes) for i in range(2)]
    ob = [P.sbuf(f"xo{u}o{i}", [128, D], F32, es=pes) for i in range(2)]
    ptp = [P.psum(f"xo{u}p{i}", [128, 128], BF16, es=pes) for i in range(2)]
    lds = Slots(P, f"xo{u}l", 3)
    ldx = Slots(P, f"xo{u}m", 2)
    stc = Slots(P, f"xo{u}st", 2)
    c_tp = Ctr(P, f"xo{u}tp")
    c_ae = Ctr(P, f"xo{u}ae")
    c_ad = Ctr(P, f"xo{u}ad")
    id_bf = P.c("id_bf")
    with P.block() as blk:
        @blk.sync
        def _(sy):
            for t in range(NPT):
                if t >= 2:
                    sy.wait_ge(c_ad.h, t - 1)
                ldx.c(t).inc(sy.dma_start(out=xb[t % 2][:, :],
                                          in_=x_dram[t * 128:(t + 1) * 128, :]), 16)
                for j in range(NDC):
                    n = t * 8 + j
                    if n >= 3:
                        sy.wait_ge(c_tp.h, n - 2)
                    lds.c(n).inc(sy.dma_start(
                        out=s3b[n % 3][:, :],
                        in_=sc_s3[j * 128:(j + 1) * 128, t * 128:(t + 1) * 128]), 16)

        @blk.tensor
        def _(te):
            for t in range(NPT):
                for j in range(NDC):
                    n = t * 8 + j
                    te.wait_ge(lds.c(n).h, 16 * (n // 3 + 1))
                    if n >= 2:
                        te.wait_ge(c_ae.h, n - 1)
                    c_tp.inc(te.transpose(ptp[n % 2][:, :], s3b[n % 3][:, :], id_bf[:, :]))

        @blk.scalar
        def _(sc):
            for t in range(NPT):
                for j in range(NDC):
                    n = t * 8 + j
                    sc.wait_ge(c_tp.h, n + 1)
                    if j == 0 and t >= 2:
                        sc.wait_ge(c_ad.h, t - 1)
                    c_ae.inc(sc.copy(s3t[t % 2][:, j * 128:(j + 1) * 128], ptp[n % 2][:, :]))

        @blk.vector
        def _(v):
            for t in range(NPT):
                v.wait_ge(c_ae.h, 8 * (t + 1))
                v.wait_ge(ldx.c(t).h, 16 * (t // 2 + 1))
                if t >= 2:
                    v.wait_ge(stc.c(t).h, 16 * ((t - 2) // 2 + 1))
                c_ad.inc(v.tensor_add(ob[t % 2][:, :], xb[t % 2][:, :], s3t[t % 2][:, :]))

        @blk.gpsimd
        def _(gp):
            for t in range(NPT):
                gp.wait_ge(c_ad.h, t + 1)
                stc.c(t).inc(gp.dma_start(out=sc_x1T[t * 128:(t + 1) * 128, :],
                                          in_=ob[t % 2][:, :]), 16)
            gp.wait_ge(stc.c(0).h, stc.c(0).n)
            gp.wait_ge(stc.c(1).h, stc.c(1).n)
    P.barrier()
    P.sem_release(_smark)
    pes.close()


def emit_sfactor_cols(P, scols):
    u = P.uid()
    _smark = P.sem_mark()
    pes = contextlib.ExitStack()
    psc = P.psum(f"sf{u}", [128, 3], F32, es=pes)
    c_m = Ctr(P, f"sfm{u}")
    c_v = Ctr(P, f"sfv{u}")
    with P.block() as blk:
        @blk.tensor
        def _(te):
            c_m.inc(te.matmul(psc[:, :], P.c("ones_1x128_f32")[0:1, :], P.c("s012")[0:1, :],
                              start=True, stop=True))

        @blk.vector
        def _(v):
            v.wait_ge(c_m.h, 1)
            c_v.inc(v.tensor_scalar(scols[:, :], psc[:, :], 2.0, None, ALU.mult))
    P.barrier()
    P.sem_release(_smark)
    pes.close()


def emit_ret_proj(P, xn1F, sc_q1, sc_k1, sc_v1):
    u0 = P.uid()
    pes0 = contextlib.ExitStack()
    tcol = P.sbuf(f"rpt{u0}", [128, 8], F32, es=pes0)
    wsl = P.sbuf(f"rpw{u0}", [128, 8 * 1024], BF16, es=pes0)
    _smark = P.sem_mark()
    c_t = Ctr(P, f"rptc{u0}")
    with P.block() as blk:
        @blk.scalar
        def _(sc):
            c_t.inc(sc.activation(tcol[:, :], P.c("theta")[:, :], AF.Exp, scale=0.0))
    P.barrier()
    P.sem_release(_smark)

    for wname, dst, theta_mult, tlay in (("Wq1", sc_q1, True, False),
                                         ("Wk1", sc_k1, True, False),
                                         ("Wv1", sc_v1, False, True)):
        W = P.t[wname]
        emit_load_wslab(P, wsl, [(k * 1024, W[k * 128:(k + 1) * 128, :]) for k in range(8)], wname)
        u = P.uid()
        _smark = P.sem_mark()
        pes = contextlib.ExitStack()
        ob = [P.sbuf(f"rp{u}_{i}", [128, NQ], BF16, es=pes) for i in range(2)]
        stc = Slots(P, f"rps{u}", 2)
        if not tlay:
            tiles = [dict(do=do, tc=tc) for tc in range(8) for do in range(8)]

            def lhs(tile, k):
                return wsl[:, k * 1024 + tile["do"] * 128: k * 1024 + tile["do"] * 128 + 128]

            def rhs(tile, k):
                return xn1F[:, k * S + tile["tc"] * NQ: k * S + (tile["tc"] + 1) * NQ]

            def evac(v, ti, tile, psum):
                if ti >= 2:
                    v.wait_ge(stc.c(ti).h, 16 * ((ti - 2) // 2 + 1))
                if theta_mult:
                    return v.tensor_scalar(ob[ti % 2][:, :], psum[:, :],
                                           tcol[:, tile["do"]:tile["do"] + 1], None, ALU.mult)
                return v.tensor_copy(ob[ti % 2][:, :], psum[:, :])

            def store(gp, ti, tile):
                stc.c(ti).inc(gp.dma_start(
                    out=dst[tile["do"] * 128:(tile["do"] + 1) * 128, tile["tc"] * NQ:(tile["tc"] + 1) * NQ],
                    in_=ob[ti % 2][:, :]), 16)
        else:
            tiles = [dict(pt=pt, dh=dh) for pt in range(NPT) for dh in range(2)]

            def lhs(tile, k):
                return xn1F[:, k * S + tile["pt"] * 128: k * S + (tile["pt"] + 1) * 128]

            def rhs(tile, k):
                return wsl[:, k * 1024 + tile["dh"] * NQ: k * 1024 + (tile["dh"] + 1) * NQ]

            def evac(v, ti, tile, psum):
                if ti >= 2:
                    v.wait_ge(stc.c(ti).h, 16 * ((ti - 2) // 2 + 1))
                return v.tensor_copy(ob[ti % 2][:, :], psum[:, :])

            def store(gp, ti, tile):
                stc.c(ti).inc(gp.dma_start(
                    out=dst[tile["pt"] * 128:(tile["pt"] + 1) * 128, tile["dh"] * NQ:(tile["dh"] + 1) * NQ],
                    in_=ob[ti % 2][:, :]), 16)

        proj_phase(P, "rp" + wname, tiles, 8, lhs, rhs, "vector", evac, store_fn=store)
        with P.block() as blk:
            @blk.gpsimd
            def _(gp):
                gp.wait_ge(stc.c(0).h, stc.c(0).n)
                gp.wait_ge(stc.c(1).h, stc.c(1).n)
        P.barrier()
        P.sem_release(_smark)
        pes.close()
    pes0.close()


def emit_retention(P, sc_q1, sc_k1, sc_v1, sc_ret):
    mrm = P.c("mrm")
    mrp = P.c("mrp")
    u = P.uid()
    _smark = P.sem_mark()
    pes = contextlib.ExitStack()
    qsb = P.sbuf(f"rt{u}q", [128, 8 * NQ], BF16, es=pes)
    kbuf = [P.sbuf(f"rt{u}k{i}", [128, 8 * 128], BF16, es=pes) for i in range(4)]
    vbuf = [P.sbuf(f"rt{u}v{i}", [128, D], BF16, es=pes) for i in range(4)]
    pT = P.sbuf(f"rt{u}p", [128, 32 * NQ], BF16, es=pes)
    ob = [P.sbuf(f"rt{u}o{i}", [128, NQ], BF16, es=pes) for i in range(2)]
    pss = [P.psum(f"rt{u}s{i}", [128, NQ], F32, es=pes) for i in range(3)]
    psr = [P.psum(f"rt{u}r{i}", [128, NQ], F32, es=pes) for i in range(4)]
    qld = Ctr(P, f"rt{u}ql")
    kld = Slots(P, f"rt{u}kl", 4)
    vld = Slots(P, f"rt{u}vl", 4)
    stc = Slots(P, f"rt{u}st", 2)
    c_smm = Ctr(P, f"rt{u}sm")
    c_dec = Ctr(P, f"rt{u}dc")
    c_pv = Ctr(P, f"rt{u}pv")
    c_ev = Ctr(P, f"rt{u}ev")
    nkc = {qc: 4 * qc + 4 for qc in range(8)}
    ksum = [0]
    for qc in range(8):
        ksum.append(ksum[-1] + nkc[qc])

    with P.block() as blk:
        @blk.sync
        def _(sy):
            nk_ = 0
            nv_ = 0
            for qc in range(8):
                if qc >= 1:
                    sy.wait_ge(c_smm.h, ksum[qc])
                qld.inc(sy.dma_start(
                    out=qsb[:, :].rearrange("p (a c) -> p a c", a=8),
                    in_=sc_q1[:, qc * NQ:(qc + 1) * NQ].rearrange("(a p) c -> p a c", p=128)), 16)
                for kc in range(nkc[qc]):
                    n = nk_ + kc
                    if n >= 4:
                        sy.wait_ge(c_smm.h, n - 3)
                    kld.c(n).inc(sy.dma_start(
                        out=kbuf[n % 4][:, :].rearrange("p (a c) -> p a c", a=8),
                        in_=sc_k1[:, kc * 128:(kc + 1) * 128].rearrange("(a p) c -> p a c", p=128)), 16)
                nk_ += nkc[qc]
                for dh in range(2):
                    for kc in range(nkc[qc]):
                        m = nv_ + dh * nkc[qc] + kc
                        if m >= 4:
                            sy.wait_ge(c_pv.h, 4 * (m - 3))
                        vld.c(m).inc(sy.dma_start(
                            out=vbuf[m % 4][:, :],
                            in_=sc_v1[kc * 128:(kc + 1) * 128, :]), 16)
                nv_ += 2 * nkc[qc]

        @blk.tensor
        def _(te):
            nk_ = 0
            nv_ = 0
            for qc in range(8):
                te.wait_ge(qld.h, 16 * (qc + 1))
                for kc in range(nkc[qc]):
                    n = nk_ + kc
                    te.wait_ge(kld.c(n).h, 16 * (n // 4 + 1))
                    if n >= 3:
                        te.wait_ge(c_dec.h, n - 2)
                    for j in range(NDC):
                        r = te.matmul(pss[n % 3][:, :],
                                      kbuf[n % 4][:, j * 128:(j + 1) * 128],
                                      qsb[:, j * NQ:(j + 1) * NQ],
                                      start=(j == 0), stop=(j == NDC - 1))
                    c_smm.inc(r)
                nk_ += nkc[qc]
                te.wait_ge(c_dec.h, nk_)
                for dh in range(2):
                    for kc in range(nkc[qc]):
                        m = nv_ + dh * nkc[qc] + kc
                        te.wait_ge(vld.c(m).h, 16 * (m // 4 + 1))
                        for dj in range(4):
                            if kc == 0 and (qc * 8 + dh * 4 + dj) >= 4:
                                te.wait_ge(c_ev.h, qc * 8 + dh * 4 + dj - 3)
                            r = te.matmul(psr[dj][:, :],
                                          vbuf[m % 4][:, dh * NQ + dj * 128: dh * NQ + (dj + 1) * 128],
                                          pT[:, kc * NQ:(kc + 1) * NQ],
                                          start=(kc == 0), stop=(kc == nkc[qc] - 1))
                            if dj == 3:
                                c_pv.inc(r, 4)
                nv_ += 2 * nkc[qc]

        @blk.vector
        def _(v):
            nk_ = 0
            for qc in range(8):
                for kc in range(nkc[qc]):
                    n = nk_ + kc
                    v.wait_ge(c_smm.h, n + 1)
                    if qc >= 1 and kc == 0:
                        v.wait_ge(c_pv.h, 8 * ksum[qc])
                    delta = qc * NQ - kc * 128
                    if delta >= 128:
                        scl = float(GAMMA ** (delta - 127))
                        c_dec.inc(v.scalar_tensor_tensor(pT[:, kc * NQ:(kc + 1) * NQ],
                                                         pss[n % 3][:, :], scl, mrp[:, :],
                                                         ALU.mult, ALU.mult))
                    else:
                        cofs = delta + 384
                        c_dec.inc(v.scalar_tensor_tensor(pT[:, kc * NQ:(kc + 1) * NQ],
                                                         pss[n % 3][:, :], 1.0, mrm[:, cofs:cofs + NQ],
                                                         ALU.mult, ALU.mult))
                nk_ += nkc[qc]

        @blk.scalar
        def _(sc):
            for qc in range(8):
                for dh in range(2):
                    for dj in range(4):
                        g = qc * 8 + dh * 4 + dj
                        sc.wait_ge(c_pv.h, 4 * (2 * ksum[qc] + (dh + 1) * nkc[qc]))
                        if g >= 2:
                            sc.wait_ge(stc.c(g).h, 16 * ((g - 2) // 2 + 1))
                        c_ev.inc(sc.copy(ob[g % 2][:, :], psr[dj][:, :]))

        @blk.gpsimd
        def _(gp):
            for qc in range(8):
                for dh in range(2):
                    for dj in range(4):
                        g = qc * 8 + dh * 4 + dj
                        gp.wait_ge(c_ev.h, g + 1)
                        stc.c(g).inc(gp.dma_start(
                            out=sc_ret[(dh * 4 + dj) * 128:(dh * 4 + dj + 1) * 128, qc * NQ:(qc + 1) * NQ],
                            in_=ob[g % 2][:, :]), 16)
            gp.wait_ge(stc.c(0).h, stc.c(0).n)
            gp.wait_ge(stc.c(1).h, stc.c(1).n)
    P.barrier()
    P.sem_release(_smark)
    pes.close()


def emit_smlp_l0(P, xn1F, scols, sc_h1):
    k0 = P.t["k0u"]
    u = P.uid()
    _smark = P.sem_mark()
    pes = contextlib.ExitStack()
    wu = [P.sbuf(f"m0{u}u{i}", [128, 2048], BF16, es=pes) for i in range(2)]
    wd = [P.sbuf(f"m0{u}d{i}", [128, 1024], BF16, es=pes) for i in range(2)]
    ob = [P.sbuf(f"m0{u}o{i}", [128, NQ], BF16, es=pes) for i in range(2)]
    ps = [P.psum(f"m0{u}p{i}", [128, NQ], F32, es=pes) for i in range(4)]
    wld = Slots(P, f"m0{u}wl", 2)
    stc = Slots(P, f"m0{u}st", 2)
    c_sub = Ctr(P, f"m0{u}sb")
    c_mm = Ctr(P, f"m0{u}mm")
    c_ev = Ctr(P, f"m0{u}ev")
    with P.block() as blk:
        @blk.gpsimd
        def _(gp):
            for df in range(32):
                if df >= 2:
                    gp.wait_ge(c_sub.h, 8 * (df - 1))
                wld.c(df).inc(gp.dma_start(out=wu[df % 2][:, :], in_=k0[df, :, :]), 16)

        @blk.vector
        def _(v):
            for df in range(32):
                v.wait_ge(wld.c(df).h, 16 * (df // 2 + 1))
                if df >= 2:
                    v.wait_ge(c_mm.h, 8 * (df - 1))
                for j in range(NDC):
                    v.wait_ge(c_sub.h, c_sub.n)
                    c_sub.inc(v.tensor_sub(wd[df % 2][:, j * 128:(j + 1) * 128],
                                           wu[df % 2][:, j * 128:(j + 1) * 128],
                                           wu[df % 2][:, 1024 + j * 128: 1024 + (j + 1) * 128]))

        @blk.tensor
        def _(te):
            for df in range(32):
                te.wait_ge(c_sub.h, 8 * (df + 1))
                for tc in range(8):
                    n = df * 8 + tc
                    if n >= 4:
                        te.wait_ge(c_ev.h, n - 3)
                    for j in range(NDC):
                        r = te.matmul(ps[n % 4][:, :],
                                      wd[df % 2][:, j * 128:(j + 1) * 128],
                                      xn1F[:, j * S + tc * NQ: j * S + (tc + 1) * NQ],
                                      start=(j == 0), stop=(j == NDC - 1))
                    c_mm.inc(r)

        @blk.scalar
        def _(sc):
            for df in range(32):
                for tc in range(8):
                    n = df * 8 + tc
                    sc.wait_ge(c_mm.h, n + 1)
                    if n >= 2:
                        sc.wait_ge(stc.c(n).h, 16 * ((n - 2) // 2 + 1))
                    c_ev.inc(sc.activation(ob[n % 2][:, :], ps[n % 4][:, :], AF.Relu,
                                           scale=scols[:, 0:1]))
                    sc.wait_ge(c_ev.h, n + 1)
                    stc.c(n).inc(sc.dma_start(
                        out=sc_h1[df * 128:(df + 1) * 128, tc * NQ:(tc + 1) * NQ],
                        in_=ob[n % 2][:, :]), 16)
            sc.wait_ge(stc.c(0).h, stc.c(0).n)
            sc.wait_ge(stc.c(1).h, stc.c(1).n)
    P.barrier()
    P.sem_release(_smark)
    pes.close()


def emit_smlp_mid(P, kt, src_h, scol_ap, dst, nin_chunks, nout_blk, relu):
    u = P.uid()
    _smark = P.sem_mark()
    pes = contextlib.ExitStack()
    hsb = P.sbuf(f"mm{u}h", [128, nin_chunks * 1024], BF16, es=pes)
    wu = [P.sbuf(f"mm{u}u{i}", [128, nin_chunks * 128], BF16, es=pes) for i in range(2)]
    wpm = [P.sbuf(f"mm{u}w{i}", [128, nin_chunks * 128], BF16, es=pes) for i in range(2)]
    ob = [P.sbuf(f"mm{u}o{i}", [128, NQ], BF16, es=pes) for i in range(2)]
    ps = [P.psum(f"mm{u}p{i}", [128, NQ], F32, es=pes) for i in range(4)]
    hld = Ctr(P, f"mm{u}hl")
    wld = Slots(P, f"mm{u}wl", 2)
    stc = Slots(P, f"mm{u}st", 2)
    c_pm = Ctr(P, f"mm{u}pm")
    c_mm = Ctr(P, f"mm{u}mm")
    c_ev = Ctr(P, f"mm{u}ev")
    with P.block() as blk:
        @blk.sync
        def _(sy):
            for tcp in range(4):
                if tcp >= 1:
                    sy.wait_ge(c_mm.h, 2 * nout_blk * tcp)
                hld.inc(sy.dma_start(
                    out=hsb[:, :].rearrange("p (a c) -> p a c", c=1024),
                    in_=src_h[:, tcp * 1024:(tcp + 1) * 1024].rearrange("(a p) c -> p a c", p=128)), 16)

        @blk.gpsimd
        def _(gp):
            for tcp in range(4):
                for do in range(nout_blk):
                    n = tcp * nout_blk + do
                    if n >= 2:
                        gp.wait_ge(c_pm.h, n - 1)
                    wld.c(n).inc(gp.dma_start(out=wu[n % 2][:, :], in_=kt[do, :, :]), 16)

        @blk.vector
        def _(v):
            for tcp in range(4):
                for do in range(nout_blk):
                    n = tcp * nout_blk + do
                    v.wait_ge(wld.c(n).h, 16 * (n // 2 + 1))
                    if n >= 2:
                        v.wait_ge(c_mm.h, 2 * (n - 1))
                    v.wait_ge(c_pm.h, c_pm.n)
                    c_pm.inc(v.tensor_scalar(wpm[n % 2][:, :], wu[n % 2][:, :],
                                             2.0, -1.0, ALU.mult, ALU.add))

        @blk.tensor
        def _(te):
            for tcp in range(4):
                te.wait_ge(hld.h, 16 * (tcp + 1))
                for do in range(nout_blk):
                    n = tcp * nout_blk + do
                    te.wait_ge(c_pm.h, n + 1)
                    for half in range(2):
                        g = n * 2 + half
                        if g >= 4:
                            te.wait_ge(c_ev.h, g - 3)
                        for dfi in range(nin_chunks):
                            r = te.matmul(ps[g % 4][:, :],
                                          wpm[n % 2][:, dfi * 128:(dfi + 1) * 128],
                                          hsb[:, dfi * 1024 + half * NQ: dfi * 1024 + half * NQ + NQ],
                                          start=(dfi == 0), stop=(dfi == nin_chunks - 1))
                        c_mm.inc(r)

        @blk.scalar
        def _(sc):
            for tcp in range(4):
                for do in range(nout_blk):
                    for half in range(2):
                        g = (tcp * nout_blk + do) * 2 + half
                        sc.wait_ge(c_mm.h, g + 1)
                        if g >= 2:
                            sc.wait_ge(stc.c(g).h, 16 * ((g - 2) // 2 + 1))
                        if relu:
                            c_ev.inc(sc.activation(ob[g % 2][:, :], ps[g % 4][:, :], AF.Relu,
                                                   scale=scol_ap))
                        else:
                            c_ev.inc(sc.activation(ob[g % 2][:, :], ps[g % 4][:, :], AF.Copy,
                                                   scale=scol_ap))
                        sc.wait_ge(c_ev.h, g + 1)
                        stc.c(g).inc(sc.dma_start(
                            out=dst[do * 128:(do + 1) * 128,
                                    tcp * 1024 + half * NQ: tcp * 1024 + half * NQ + NQ],
                            in_=ob[g % 2][:, :]), 16)
            sc.wait_ge(stc.c(0).h, stc.c(0).n)
            sc.wait_ge(stc.c(1).h, stc.c(1).n)
    P.barrier()
    P.sem_release(_smark)
    pes.close()


def build_full():
    P = Prog()
    cn = list(CONST_SBUF.keys())
    P.declare_inputs(list(INPUT_SHAPES.keys()))
    out = P.dram("out", [S, D], F32, kind="ExternalOutput")
    sc_attn = P.dram("sc_attn", [D, S], BF16)
    sc_q = P.dram("sc_q", [D, S], BF16)
    sc_vall = P.dram("sc_vall", [NB * S, D], BF16)
    sc_s = P.dram("sc_s", [D, S], BF16)
    sc_y = P.dram("sc_y", [D, S], BF16)
    sc_y32 = P.dram("sc_y32", [D, S], F32)
    mrow = P.dram("mrow", [8, S], BF16)
    sc_moe = P.dram("sc_moe", [D, S], BF16)
    sc_yT = P.dram("sc_yT", [S, D], BF16)
    sc_moeT = P.dram("sc_moeT", [S, D], BF16)
    gidx = P.dram("gidx", [NTILE * 512, 1], mybir.dt.int32)
    sidx = P.dram("sidx", [NTILE * 512, 1], mybir.dt.int32)
    sc_bs = P.dram("sc_bs", [1, 256], F32)
    sc_bas = P.dram("sc_bas", [1, 256], F32)
    sc_pb = P.dram("sc_pb", [1, 256], F32)
    sc_w1b = P.dram("sc_w1b", [8 * D, 4 * D], BF16)
    sc_w2b = P.dram("sc_w2b", [8 * 4 * D, D], BF16)
    sc_b1b = P.dram("sc_b1b", [8 * 128, 1024], BF16)
    sc_b2b = P.dram("sc_b2b", [8 * 128, 1024], BF16)
    sc_s2 = P.dram("sc_s2", [D, S], BF16)
    sc_z = P.dram("sc_z", [D, S], BF16)
    sc_s3 = P.dram("sc_s3", [D, S], BF16)
    sc_x1T = P.dram("sc_x1T", [S, D], F32)
    sc_q1 = P.dram("sc_q1", [D, S], BF16)
    sc_k1 = P.dram("sc_k1", [D, S], BF16)
    sc_v1 = P.dram("sc_v1", [S, D], BF16)
    sc_ret = P.dram("sc_ret", [D, S], BF16)
    sc_h1 = P.dram("sc_h1", [4 * D, S], BF16)
    sc_h2 = P.dram("sc_h2", [4 * D, S], BF16)
    sc_mlp = P.dram("sc_mlp", [D, S], BF16)
    sc_s4 = P.dram("sc_s4", [D, S], BF16)

    P.load_consts(cn)
    idxw = P.sbuf("idxw", [128, 48], mybir.dt.int32)
    eidi = P.sbuf("eidi", [1, 16], mybir.dt.int32)
    wc = make_wconv(P)
    xm = P.t["x_mine"]
    Wt = P.t["WqkvT"]
    Wo = P.t["WoT"]
    Wg = P.t["WgT"]

    with contextlib.ExitStack() as es0:
        xnF = P.sbuf("xnF", [128, 8 * S], BF16, es=es0)
        with contextlib.ExitStack() as es1:
            xnT = P.sbuf("xnT", [128, NPT * D], BF16, es=es1)
            emit_ln0(P, lambda t: xm[t * 128:(t + 1) * 128, :], xnT=xnT, xnF=xnF)
            emit_l0_attn(P, xnF, xnT, sc_attn,
                         extra=wconv_sections(P, wc, sc_w1b, sc_w2b, sc_b1b, sc_b2b))
        with contextlib.ExitStack() as es1:
            wq = P.sbuf("wq", [128, 8 * 1024], BF16, es=es1)
            emit_load_wslab(P, wq, [(k * 1024, Wt[k * 128:(k + 1) * 128, 0:1024]) for k in range(8)], "wq")
            emit_tiny_qproj(P, xnF, wq, sc_q)
        with contextlib.ExitStack() as es1:
            wkv = P.sbuf("wkv", [128, 16 * 1024], BF16, es=es1)
            scores = P.sbuf("scores", [128, 1024], F32, es=es1)
            vrowb = P.sbuf("vrowb", [1, 2 * D], BF16, es=es1)
            emit_load_wslab(P, wkv,
                            [(k * 2048, Wt[k * 128:(k + 1) * 128, 1024:2048]) for k in range(8)] +
                            [(k * 2048 + 1024, Wt[k * 128:(k + 1) * 128, 2048:3072]) for k in range(8)], "wkv")
            emit_load_wslab(P, vrowb, [(0, P.t["bqkv_vrow"][:, :])], "vrow")
            emit_tiny_scores(P, xnF, P.t["x_full"], wkv, vrowb, sc_q, sc_vall, scores)
            with contextlib.ExitStack() as es2:
                attoutF = P.sbuf("attoutF", [128, 8 * S], BF16, es=es2)
                emit_tiny_softmax_pv(P, scores, sc_vall, attoutF)
                with contextlib.ExitStack() as es3:
                    wo = P.sbuf("wo", [128, 8 * 1024], BF16, es=es3)
                    emit_load_wslab(P, wo, [(k * 1024, Wo[k * 128:(k + 1) * 128, :]) for k in range(8)], "wo")
                    emit_wo_s(P, attoutF, xnF, wo, sc_s)
    emit_fln(P, sc_s, "g1", "be1", sc_y, dst32=sc_y32)
    with contextlib.ExitStack() as es1:
        wg = P.sbuf("wg", [128, 64], F32, es=es1)
        emit_load_wslab(P, wg, [(j * 8, Wg[j * 128:(j + 1) * 128, :]) for j in range(8)], "wg")
        emit_moe_route2(P, sc_y32, wg, gidx, sidx, sc_bs, sc_bas, sc_pb, idxw, eidi)
    emit_transpose(P, sc_y, sc_yT, 8, 32)
    emit_wconv_drain(P, wc)
    emit_moe_routed2(P, sc_yT, sc_w1b, sc_w2b, sc_b1b, sc_b2b, gidx, sidx, idxw, eidi, sc_moeT)
    emit_transpose(P, sc_moeT, sc_moe, 32, 8)
    emit_binop(P, sc_y, sc_moe, sc_s2)
    emit_fln(P, sc_s2, "g2", "be2", sc_z)
    emit_binop(P, sc_attn, sc_z, sc_s3)
    emit_x1(P, sc_s3, xm, sc_x1T)
    scols = P.sbuf("scols", [128, 3], F32)
    with contextlib.ExitStack() as es0:
        xn1F = P.sbuf("xn1F", [128, 8 * S], BF16, es=es0)
        emit_ln0(P, lambda t: sc_x1T[t * 128:(t + 1) * 128, :], xnT=None, xnF=xn1F)
        emit_sfactor_cols(P, scols)
        emit_ret_proj(P, xn1F, sc_q1, sc_k1, sc_v1)
        emit_smlp_l0(P, xn1F, scols, sc_h1)
    emit_retention(P, sc_q1, sc_k1, sc_v1, sc_ret)
    emit_smlp_mid(P, P.t["k1u"], sc_h1, scols[:, 1:2], sc_h2, 32, 32, relu=True)
    emit_smlp_mid(P, P.t["k2u"], sc_h2, scols[:, 2:3], sc_mlp, 32, 8, relu=False)
    emit_binop(P, sc_ret, sc_mlp, sc_s4)
    emit_x1(P, sc_s4, sc_x1T, out)
    P.nc.finalize()
    return P


_BUILT = None


def kernel(**inputs):
    global _BUILT
    in_maps = host_prep(inputs)
    if _BUILT is None:
        _BUILT = build_full()
    P = _BUILT
    res = run_bass_kernel_spmd(P.nc, in_maps, core_ids=list(range(NB)))
    outs = [np.asarray(res.results[c]["out"], dtype=np.float32) for c in range(NB)]
    return np.stack(outs, axis=0)


NSLOT = 768


def emit_transpose(P, src_dram, dst_dram, nrow_t, ncol_t, dt=BF16):
    u = P.uid()
    _smark = P.sem_mark()
    pes = contextlib.ExitStack()
    sb = [P.sbuf(f"tp{u}s{i}", [128, 128], dt, es=pes) for i in range(3)]
    rowb = [P.sbuf(f"tp{u}r{i}", [128, nrow_t * 128], dt, es=pes) for i in range(2)]
    ptp = [P.psum(f"tp{u}p{i}", [128, 128], dt, es=pes) for i in range(2)]
    lds = Slots(P, f"tp{u}l", 3)
    stc = Slots(P, f"tp{u}st", 2)
    c_tp = Ctr(P, f"tp{u}tp")
    c_ae = Ctr(P, f"tp{u}ae")
    assert dt == BF16
    id_ = P.c("id_bf")
    with P.block() as blk:
        @blk.sync
        def _(sy):
            for j in range(ncol_t):
                for i in range(nrow_t):
                    n = j * nrow_t + i
                    if n >= 3:
                        sy.wait_ge(c_tp.h, n - 2)
                    lds.c(n).inc(sy.dma_start(
                        out=sb[n % 3][:, :],
                        in_=src_dram[i * 128:(i + 1) * 128, j * 128:(j + 1) * 128]), 16)

        @blk.tensor
        def _(te):
            for j in range(ncol_t):
                for i in range(nrow_t):
                    n = j * nrow_t + i
                    te.wait_ge(lds.c(n).h, 16 * (n // 3 + 1))
                    if n >= 2:
                        te.wait_ge(c_ae.h, n - 1)
                    c_tp.inc(te.transpose(ptp[n % 2][:, :], sb[n % 3][:, :], id_[:, :]))

        @blk.scalar
        def _(sc):
            for j in range(ncol_t):
                for i in range(nrow_t):
                    n = j * nrow_t + i
                    sc.wait_ge(c_tp.h, n + 1)
                    if i == 0 and j >= 2:
                        sc.wait_ge(stc.c(j).h, 16 * ((j - 2) // 2 + 1))
                    c_ae.inc(sc.copy(rowb[j % 2][:, i * 128:(i + 1) * 128], ptp[n % 2][:, :]))

        @blk.gpsimd
        def _(gp):
            for j in range(ncol_t):
                gp.wait_ge(c_ae.h, (j + 1) * nrow_t)
                stc.c(j).inc(gp.dma_start(
                    out=dst_dram[j * 128:(j + 1) * 128, :],
                    in_=rowb[j % 2][:, :]), 16)
            gp.wait_ge(stc.c(0).h, stc.c(0).n)
            gp.wait_ge(stc.c(1).h, stc.c(1).n)
    P.barrier()
    P.sem_release(_smark)
    pes.close()


def emit_moe_route(P, sc_y32, wg_sb, gidx, sidx, sc_bs, sc_bas):
    u = P.uid()
    _smark = P.sem_mark()
    pes = contextlib.ExitStack()
    I32 = mybir.dt.int32
    yb = [P.sbuf(f"rr{u}y{i}", [128, 8 * NQ], F32, es=pes) for i in range(2)]
    lgT = [P.sbuf(f"rr{u}l{i}", [128, 8], F32, es=pes) for i in range(2)]
    mxc = P.sbuf(f"rr{u}m", [128, 1], F32, es=pes)
    mskT = P.sbuf(f"rr{u}s", [128, 8 * 32], F32, es=pes)
    bsrow = P.sbuf(f"rr{u}b", [1, 256], F32, es=pes)
    bsT = P.sbuf(f"rr{u}bt", [32, 8], F32, es=pes)
    basT = P.sbuf(f"rr{u}bb", [32, 8], F32, es=pes)
    basrow = P.sbuf(f"rr{u}br", [1, 256], F32, es=pes)
    slotf = P.sbuf(f"rr{u}sf", [128, 8], F32, es=pes)
    slotc = P.sbuf(f"rr{u}sc", [128, 1], F32, es=pes)
    sloti = P.sbuf(f"rr{u}si", [128, 1], I32, es=pes)
    init0 = P.sbuf(f"rr{u}i0", [128, 48], I32, es=pes)
    initb = P.sbuf(f"rr{u}ib", [128, 48], I32, es=pes)
    pslg = [P.psum(f"rr{u}p{i}", [128, 8], F32, es=pes) for i in range(2)]
    psbs = P.psum(f"rr{u}pb", [1, 8], F32, es=pes)
    psba = P.psum(f"rr{u}pa", [32, 8], F32, es=pes)
    pssl = [P.psum(f"rr{u}ps{i}", [128, 8], F32, es=pes) for i in range(2)]
    ld = Slots(P, f"rr{u}ld", 2)
    c_mm = Ctr(P, f"rr{u}mm")
    c_ev = Ctr(P, f"rr{u}ev")
    c_bs = Ctr(P, f"rr{u}bs")
    c_bse = Ctr(P, f"rr{u}be")
    c_dmab = Ctr(P, f"rr{u}db")
    c_ltm = Ctr(P, f"rr{u}lt")
    c_bae = Ctr(P, f"rr{u}ba")
    c_slm = Ctr(P, f"rr{u}sl")
    c_sl = Ctr(P, f"rr{u}s2")
    c_scg = Ctr(P, f"rr{u}scg")
    c_scs = Ctr(P, f"rr{u}scs")
    ones1f = P.c("ones_1x128_f32")
    tri = P.c("tri128")
    lt = P.c("lt32")
    erow = P.c("erow")
    iota = P.c("iota_col")
    o128f = P.c("ones_128x1_f32")

    with P.block() as blk:
        @blk.sync
        def _(sy):
            for tc in range(8):
                if tc >= 2:
                    sy.wait_ge(c_mm.h, (tc - 1) * 4)
                ld.c(tc).inc(sy.dma_start(
                    out=yb[tc % 2][:, :].rearrange("p (a c) -> p a c", a=8),
                    in_=sc_y32[:, tc * NQ:(tc + 1) * NQ].rearrange("(a p) c -> p a c", p=128)), 16)

        @blk.tensor
        def _(te):
            for tc in range(8):
                te.wait_ge(ld.c(tc).h, 16 * (tc // 2 + 1))
                for ps_ in range(4):
                    pt = tc * 4 + ps_
                    if pt >= 2:
                        te.wait_ge(c_ev.h, 3 * (pt - 1))
                    for j in range(NDC):
                        r = te.matmul(pslg[pt % 2][:, :],
                                      yb[tc % 2][:, j * NQ + ps_ * 128: j * NQ + ps_ * 128 + 128],
                                      wg_sb[:, j * 8:(j + 1) * 8],
                                      start=(j == 0), stop=(j == NDC - 1))
                    c_mm.inc(r)
                    te.wait_ge(c_ev.h, 3 * (pt + 1))
                    if pt >= 1:
                        te.wait_ge(c_bse.h, pt)
                    c_bs.inc(te.matmul(psbs, o128f[:, :],
                                       mskT[:, pt * 8:(pt + 1) * 8], start=True, stop=True))
            te.wait_ge(c_dmab.h, 64)
            c_ltm.inc(te.matmul(psba[:, :], lt[:, :], bsT[:, :], start=True, stop=True))
            te.wait_ge(c_dmab.h, 96)
            for pt in range(32):
                if pt >= 2:
                    te.wait_ge(c_sl.h, 3 * (pt - 1))
                te.matmul(pssl[pt % 2][:, :], tri[:, :], mskT[:, pt * 8:(pt + 1) * 8],
                          start=True, stop=False)
                te.matmul(pssl[pt % 2][:, :], ones1f[:, :], basrow[0:1, pt * 8:(pt + 1) * 8],
                          start=False, stop=False)
                c_slm.inc(te.matmul(pssl[pt % 2][:, :], ones1f[:, :], erow[0:1, :],
                                    start=False, stop=True))

        cinit = Ctr(P, f"rr{u}in")

        @blk.vector
        def _(v):
            cinit.inc(v.memset(init0[:, :], 0))
            cinit.inc(v.memset(initb[:, :], 100000))
            for pt in range(32):
                v.wait_ge(c_mm.h, pt + 1)
                v.wait_ge(c_ev.h, c_ev.n)
                c_ev.inc(v.tensor_copy(lgT[pt % 2][:, :], pslg[pt % 2][:, :]))
                v.wait_ge(c_ev.h, c_ev.n)
                c_ev.inc(v.reduce_max(mxc[:, :], lgT[pt % 2][:, :], axis=mybir.AxisListType.X))
                v.wait_ge(c_ev.h, c_ev.n)
                c_ev.inc(v.tensor_scalar(mskT[:, pt * 8:(pt + 1) * 8], lgT[pt % 2][:, :],
                                         mxc[:, :], None, ALU.is_ge))
                v.wait_ge(c_bs.h, pt + 1)
                c_bse.inc(v.tensor_copy(bsrow[0:1, pt * 8:(pt + 1) * 8], psbs[0:1, :]))
            v.wait_ge(c_ltm.h, 1)
            c_bae.inc(v.tensor_copy(basT[:, :], psba[:, :]))
            for pt in range(32):
                v.wait_ge(c_slm.h, pt + 1)
                v.wait_ge(c_sl.h, c_sl.n)
                if pt >= 1:
                    v.wait_ge(c_scg.h, 16 * pt)
                    v.wait_ge(c_scs.h, 16 * pt)
                c_sl.inc(v.tensor_mul(slotf[:, :], mskT[:, pt * 8:(pt + 1) * 8], pssl[pt % 2][:, :]))
                v.wait_ge(c_sl.h, c_sl.n)
                c_sl.inc(v.reduce_sum(slotc[:, :], slotf[:, :], axis=mybir.AxisListType.X))
                v.wait_ge(c_sl.h, c_sl.n)
                c_sl.inc(v.tensor_copy(sloti[:, :], slotc[:, :]))

        @blk.gpsimd
        def _(gp):
            gp.wait_ge(cinit.h, 2)
            c_dmab.inc(gp.dma_start(out=gidx[:, :].rearrange("(a b) o -> a (b o)", a=128),
                                    in_=init0[:, :]), 16)
            c_dmab.inc(gp.dma_start(out=sidx[:, :].rearrange("(a b) o -> a (b o)", a=128),
                                    in_=initb[:, :]), 16)
            gp.wait_ge(c_bse.h, 32)
            c_dmab.inc(gp.dma_start(out=sc_bs[0:1, :], in_=bsrow[0:1, :]), 16)
            gp.wait_ge(c_dmab.h, 48)
            c_dmab.inc(gp.dma_start(out=bsT[:, :],
                                    in_=sc_bs[0:1, :].rearrange("o (a b) -> (o a) b", a=32)), 16)
            gp.wait_ge(c_bae.h, 1)
            c_dmab.inc(gp.dma_start(out=sc_bas[0:1, :].rearrange("o (a b) -> (o a) b", a=32),
                                    in_=basT[:, :]), 16)
            gp.wait_ge(c_dmab.h, 80)
            c_dmab.inc(gp.dma_start(out=basrow[0:1, :], in_=sc_bas[0:1, :]), 16)
            with gp.register(f"rrbc{u}") as rb:
                gp.reg_mov(rb, 8 * NSLOT - 1)
                for pt in range(32):
                    gp.wait_ge(c_sl.h, 3 * (pt + 1))
                    c_scg.inc(gp.indirect_dma_start(
                        gidx[:, :], bass.IndirectOffsetOnAxis(sloti[:, 0:1], 0),
                        iota[:, pt:pt + 1], None,
                        bounds_check=rb, oob_is_err=False), 16)
                    c_scs.inc(gp.indirect_dma_start(
                        sidx[:, :], bass.IndirectOffsetOnAxis(sloti[:, 0:1], 0),
                        iota[:, pt:pt + 1], None,
                        bounds_check=rb, oob_is_err=False), 16)
            gp.wait_ge(c_scg.h, c_scg.n)
            gp.wait_ge(c_scs.h, c_scs.n)
    P.barrier()
    P.sem_release(_smark)
    pes.close()


def emit_moe_routed(P, sc_yT, gidx, sidx, sc_moeT):
    W1 = P.t["W1"]
    W2 = P.t["W2"]
    b1 = P.c("b1")
    b2 = P.c("b2")
    I32 = mybir.dt.int32
    NG = NSLOT // 128
    HN = NSLOT // 2
    u0 = P.uid()
    pes0 = contextlib.ExitStack()
    wsb = P.sbuf(f"rx{u0}w", [128, 8 * 4096], BF16, es=pes0)
    id_bf = P.c("id_bf")
    for e in range(NB):
        u = P.uid()
        _smark = P.sem_mark()
        pes = contextlib.ExitStack()
        gcol = [P.sbuf(f"rx{u}g{i}", [128, 1], I32, es=pes) for i in range(2)]
        scol = [P.sbuf(f"rx{u}s{i}", [128, 1], I32, es=pes) for i in range(2)]
        gth = [P.sbuf(f"rx{u}t{i}", [128, D], BF16, es=pes) for i in range(2)]
        xe = P.sbuf(f"rx{u}x", [128, 8 * NSLOT], BF16, es=pes)
        he = P.sbuf(f"rx{u}h", [128, 32 * NSLOT], BF16, es=pes)
        oe = P.sbuf(f"rx{u}o", [128, 8 * NSLOT], BF16, es=pes)
        oT = [P.sbuf(f"rx{u}ot{i}", [128, D], BF16, es=pes) for i in range(2)]
        psw = [P.psum(f"rx{u}p{i}", [128, HN], F32, es=pes) for i in range(3)]
        ptp = [P.psum(f"rx{u}q{i}", [128, 128], BF16, es=pes) for i in range(2)]
        wld = Ctr(P, f"rx{u}wl")
        ild = Slots(P, f"rx{u}il", 2)
        gld = Slots(P, f"rx{u}gl", 2)
        c_ti = Ctr(P, f"rx{u}ti")
        c_te = Ctr(P, f"rx{u}te")
        c_h = Ctr(P, f"rx{u}hm")
        c_he = Ctr(P, f"rx{u}he")
        c_o = Ctr(P, f"rx{u}om")
        c_oe = Ctr(P, f"rx{u}oe")
        c_to = Ctr(P, f"rx{u}to")
        c_toe = Ctr(P, f"rx{u}tv")
        scsl = Slots(P, f"rx{u}sx", 2)

        with P.block() as blk:
            @blk.gpsimd
            def _(gp):
                for k in range(NDC):
                    wld.inc(gp.dma_start(out=wsb[:, k * 4096:(k + 1) * 4096],
                                         in_=W1[e, k * 128:(k + 1) * 128, :]), 16)
                for g in range(NG):
                    if g >= 2:
                        gp.wait_ge(gld.c(g).h, 16 * ((g - 2) // 2 + 1))
                    ild.c(g).inc(gp.dma_start(
                        out=gcol[g % 2][:, :],
                        in_=gidx[e * NSLOT + g * 128: e * NSLOT + (g + 1) * 128, 0:1]), 16)
                    gp.wait_ge(ild.c(g).h, 16 * (g // 2 + 1))
                    if g >= 2:
                        gp.wait_ge(c_ti.h, 8 * (g - 1))
                    gld.c(g).inc(gp.indirect_dma_start(
                        gth[g % 2][:, :], None,
                        sc_yT[:, :], bass.IndirectOffsetOnAxis(gcol[g % 2][:, 0:1], 0)), 16)
                gp.wait_ge(c_h.h, 64)
                for dfi in range(32):
                    wld.inc(gp.dma_start(out=wsb[:, dfi * 1024:(dfi + 1) * 1024],
                                         in_=W2[e, dfi * 128:(dfi + 1) * 128, :]), 16)
                with gp.register(f"rxbc{u}") as rb:
                    gp.reg_mov(rb, S - 1)
                    for g in range(NG):
                        n = NG + g
                        if g >= 2:
                            gp.wait_ge(scsl.c(g).h, 16 * ((g - 2) // 2 + 1))
                        ild.c(n).inc(gp.dma_start(
                            out=scol[g % 2][:, :],
                            in_=sidx[e * NSLOT + g * 128: e * NSLOT + (g + 1) * 128, 0:1]), 16)
                        gp.wait_ge(ild.c(n).h, 16 * (n // 2 + 1))
                        gp.wait_ge(c_toe.h, 8 * (g + 1))
                        scsl.c(g).inc(gp.indirect_dma_start(
                            sc_moeT[:, :], bass.IndirectOffsetOnAxis(scol[g % 2][:, 0:1], 0),
                            oT[g % 2][:, :], None,
                            bounds_check=rb, oob_is_err=False), 16)
                gp.wait_ge(scsl.c(0).h, scsl.c(0).n)
                gp.wait_ge(scsl.c(1).h, scsl.c(1).n)

            @blk.tensor
            def _(te):
                for g in range(NG):
                    te.wait_ge(gld.c(g).h, 16 * (g // 2 + 1))
                    for j in range(NDC):
                        n = g * 8 + j
                        if n >= 2:
                            te.wait_ge(c_te.h, n - 1)
                        c_ti.inc(te.transpose(ptp[n % 2][:, :], gth[g % 2][:, j * 128:(j + 1) * 128],
                                              id_bf[:, :]))
                te.wait_ge(wld.h, 16 * NDC)
                for df in range(32):
                    for hf in range(2):
                        n = df * 2 + hf
                        if n >= 3:
                            te.wait_ge(c_he.h, n - 2)
                        if n == 0:
                            te.wait_ge(c_te.h, 8 * NG)
                        for k in range(NDC):
                            r = te.matmul(psw[n % 3][:, :],
                                          wsb[:, k * 4096 + df * 128: k * 4096 + df * 128 + 128],
                                          xe[:, k * NSLOT + hf * HN: k * NSLOT + hf * HN + HN],
                                          start=(k == 0), stop=(k == NDC - 1))
                        c_h.inc(r)
                te.wait_ge(wld.h, 16 * (NDC + 32))
                for do in range(8):
                    for hf in range(2):
                        n = do * 2 + hf
                        if n >= 3:
                            te.wait_ge(c_oe.h, n - 2)
                        if n == 0:
                            te.wait_ge(c_he.h, 64)
                        for df in range(32):
                            r = te.matmul(psw[n % 3][:, :],
                                          wsb[:, df * 1024 + do * 128: df * 1024 + do * 128 + 128],
                                          he[:, df * NSLOT + hf * HN: df * NSLOT + hf * HN + HN],
                                          start=(df == 0), stop=(df == 31))
                        c_o.inc(r)
                for g in range(NG):
                    te.wait_ge(c_oe.h, 16)
                    for j in range(NDC):
                        n = g * 8 + j
                        if g >= 2 and j == 0:
                            te.wait_ge(scsl.c(g).h, 16 * ((g - 2) // 2 + 1))
                        if n >= 2:
                            te.wait_ge(c_toe.h, n - 1)
                        c_to.inc(te.transpose(ptp[n % 2][:, :], oe[:, j * NSLOT + g * 128: j * NSLOT + (g + 1) * 128],
                                              id_bf[:, :]))

            @blk.scalar
            def _(sc):
                for g in range(NG):
                    for j in range(NDC):
                        n = g * 8 + j
                        sc.wait_ge(c_ti.h, n + 1)
                        c_te.inc(sc.copy(xe[:, j * NSLOT + g * 128: j * NSLOT + (g + 1) * 128],
                                         ptp[n % 2][:, :]))
                for df in range(32):
                    for hf in range(2):
                        n = df * 2 + hf
                        sc.wait_ge(c_h.h, n + 1)
                        c_he.inc(sc.activation(he[:, df * NSLOT + hf * HN: df * NSLOT + hf * HN + HN],
                                               psw[n % 3][:, :], AF.Relu,
                                               bias=b1[:, e * 32 + df: e * 32 + df + 1]))
                for g in range(NG):
                    for j in range(NDC):
                        n = g * 8 + j
                        sc.wait_ge(c_to.h, n + 1)
                        c_toe.inc(sc.copy(oT[g % 2][:, j * 128:(j + 1) * 128], ptp[n % 2][:, :]))

            @blk.vector
            def _(v):
                for do in range(8):
                    for hf in range(2):
                        n = do * 2 + hf
                        v.wait_ge(c_o.h, n + 1)
                        v.wait_ge(c_oe.h, c_oe.n)
                        c_oe.inc(v.tensor_scalar(oe[:, do * NSLOT + hf * HN: do * NSLOT + hf * HN + HN],
                                                 psw[n % 3][:, :],
                                                 b2[:, e * 8 + do: e * 8 + do + 1], None, ALU.add))
        P.barrier()
        P.sem_release(_smark)
        pes.close()
    pes0.close()


NTILE = 16


def make_wconv(P):
    u = P.uid()
    wc = {"mark": P.sem_mark(), "pes": contextlib.ExitStack()}
    wc["buf"] = [P.sbuf(f"wc{u}_{i}", [128, 2048], BF16, es=wc["pes"]) for i in range(2)]
    wc["ld"] = Slots(P, f"wc{u}l", 2)
    wc["st"] = Slots(P, f"wc{u}s", 2)
    return wc


def wconv_sections(P, wc, sc_w1b, sc_w2b, sc_b1b, sc_b2b):
    W1f = P.t["W1"][:, :, :].rearrange("e r c -> (e r c)").rearrange("(j p c) -> j p c", p=128, c=2048)
    W2f = P.t["W2"][:, :, :].rearrange("e r c -> (e r c)").rearrange("(j p c) -> j p c", p=128, c=2048)
    o1 = sc_w1b[:, :].rearrange("r c -> (r c)").rearrange("(j p c) -> j p c", p=128, c=2048)
    o2 = sc_w2b[:, :].rearrange("r c -> (r c)").rearrange("(j p c) -> j p c", p=128, c=2048)
    buf, ld, st = wc["buf"], wc["ld"], wc["st"]
    jobs = [(W1f[j, :, :], o1[j, :, :], 2048, 2048) for j in range(128)] + \
           [(W2f[j, :, :], o2[j, :, :], 2048, 2048) for j in range(128)]
    for e in range(8):
        jobs.append((P.t["b1t"][e, :, :], sc_b1b[e * 128:(e + 1) * 128, :], 32, 1024))
        jobs.append((P.t["b2t"][e, :, :], sc_b2b[e * 128:(e + 1) * 128, :], 8, 1024))

    def attach(blk):
        @blk.gpsimd
        def _(gp):
            for n, (iap, oap, wl, ws) in enumerate(jobs):
                if n >= 2:
                    gp.wait_ge(st.c(n).h, 16 * ((n - 2) // 2 + 1))
                ld.c(n).inc(gp.dma_start(out=buf[n % 2][:, 0:wl], in_=iap), 16)

        @blk.sync
        def _(sy):
            for n, (iap, oap, wl, ws) in enumerate(jobs):
                sy.wait_ge(ld.c(n).h, 16 * (n // 2 + 1))
                st.c(n).inc(sy.dma_start(out=oap, in_=buf[n % 2][:, 0:ws]), 16)
    return attach


def emit_wconv_drain(P, wc):
    st = wc["st"]
    with P.block() as blk:
        @blk.gpsimd
        def _(gp):
            gp.wait_ge(st.c(0).h, st.c(0).n)
            gp.wait_ge(st.c(1).h, st.c(1).n)
    P.barrier()
    P.sem_release(wc["mark"])
    wc["pes"].close()


def emit_moe_route2(P, sc_y32, wg_sb, gidx, sidx, sc_bs, sc_bas, sc_pb, idxw, eidi):
    u = P.uid()
    _smark = P.sem_mark()
    pes = contextlib.ExitStack()
    I32 = mybir.dt.int32
    yb = [P.sbuf(f"r2{u}y{i}", [128, 8 * NQ], F32, es=pes) for i in range(2)]
    lgT = [P.sbuf(f"r2{u}l{i}", [128, 8], F32, es=pes) for i in range(2)]
    mxc = P.sbuf(f"r2{u}m", [128, 1], F32, es=pes)
    mskT = P.sbuf(f"r2{u}s", [128, 8 * 32], F32, es=pes)
    bsrow = P.sbuf(f"r2{u}b", [1, 256], F32, es=pes)
    bsT = P.sbuf(f"r2{u}bt", [32, 8], F32, es=pes)
    basT = P.sbuf(f"r2{u}bb", [32, 8], F32, es=pes)
    basrow = P.sbuf(f"r2{u}br", [1, 256], F32, es=pes)
    ccol = P.sbuf(f"r2{u}cc", [8, 4], F32, es=pes)
    ccoli = P.sbuf(f"r2{u}ci", [8, 1], I32, es=pes)
    pbcol = P.sbuf(f"r2{u}pb", [8, 2], F32, es=pes)
    pbrow = P.sbuf(f"r2{u}pr", [1, 16], F32, es=pes)
    cmpe = P.sbuf(f"r2{u}ce", [8, 16], F32, es=pes)
    erowf = P.sbuf(f"r2{u}er", [1, 16], F32, es=pes)
    eidf = P.sbuf(f"r2{u}ef", [1, 16], F32, es=pes)
    idxf = P.sbuf(f"r2{u}if", [128, 48], F32, es=pes)
    slotf = P.sbuf(f"r2{u}sf", [128, 8], F32, es=pes)
    slotc = P.sbuf(f"r2{u}sc", [128, 1], F32, es=pes)
    sloti = P.sbuf(f"r2{u}si", [128, 1], I32, es=pes)
    init0 = P.sbuf(f"r2{u}i0", [128, NTILE * 4], I32, es=pes)
    initb = P.sbuf(f"r2{u}ib", [128, NTILE * 4], I32, es=pes)
    pslg = [P.psum(f"r2{u}p{i}", [128, 8], F32, es=pes) for i in range(2)]
    psm_ = P.psum(f"r2{u}pm", [128, 512], F32, es=pes)
    psbs = psm_[0:1, 0:8]
    psba = psm_[0:32, 8:16]
    psct = psm_[0:8, 16:17]
    pspb = psm_[0:8, 17:18]
    pser = psm_[0:1, 32:48]
    psix = psm_[:, 64:80]
    pssl = [P.psum(f"r2{u}ps{i}", [128, 8], F32, es=pes) for i in range(2)]
    ld = Slots(P, f"r2{u}ld", 2)
    c_mm = Ctr(P, f"r2{u}mm")
    c_ev = Ctr(P, f"r2{u}ev")
    c_bs = Ctr(P, f"r2{u}bs")
    c_bse = Ctr(P, f"r2{u}be")
    c_dmab = Ctr(P, f"r2{u}db")
    c_tmm = Ctr(P, f"r2{u}tm")
    c_dsm = Ctr(P, f"r2{u}ds")
    c_slm = Ctr(P, f"r2{u}sl")
    c_sl = Ctr(P, f"r2{u}s2")
    c_scg = Ctr(P, f"r2{u}sg")
    c_scs = Ctr(P, f"r2{u}ss")
    cinit = Ctr(P, f"r2{u}in")
    ones1f = P.c("ones_1x128_f32")
    tri = P.c("tri128")
    lt = P.c("lt32")
    lt8 = P.c("lt8")
    iota = P.c("iota_col")
    o128f = P.c("ones_128x1_f32")
    o32f = P.c("ones_32x1_f32")
    o8f = P.c("ones_8x1_f32")
    tbase = P.c("tilebases")
    io128f = P.c("iota128f")

    with P.block() as blk:
        @blk.sync
        def _(sy):
            for tc in range(8):
                if tc >= 2:
                    sy.wait_ge(c_mm.h, (tc - 1) * 4)
                ld.c(tc).inc(sy.dma_start(
                    out=yb[tc % 2][:, :].rearrange("p (a c) -> p a c", a=8),
                    in_=sc_y32[:, tc * NQ:(tc + 1) * NQ].rearrange("(a p) c -> p a c", p=128)), 16)

        @blk.tensor
        def _(te):
            for tc in range(8):
                te.wait_ge(ld.c(tc).h, 16 * (tc // 2 + 1))
                for ps_ in range(4):
                    pt = tc * 4 + ps_
                    if pt >= 2:
                        te.wait_ge(c_ev.h, 3 * (pt - 1))
                    for j in range(NDC):
                        r = te.matmul(pslg[pt % 2][:, :],
                                      yb[tc % 2][:, j * NQ + ps_ * 128: j * NQ + ps_ * 128 + 128],
                                      wg_sb[:, j * 8:(j + 1) * 8],
                                      start=(j == 0), stop=(j == NDC - 1))
                    c_mm.inc(r)
                    te.wait_ge(c_ev.h, 3 * (pt + 1))
                    if pt >= 1:
                        te.wait_ge(c_bse.h, pt)
                    c_bs.inc(te.matmul(psbs, o128f[:, :],
                                       mskT[:, pt * 8:(pt + 1) * 8], start=True, stop=True))
            te.wait_ge(c_dmab.h, 64)
            c_tmm.inc(te.matmul(psba, lt[:, :], bsT[:, :], start=True, stop=True))
            c_tmm.inc(te.matmul(psct, bsT[:, :], o32f[:, :], start=True, stop=True))
            te.wait_ge(c_dsm.h, 5)
            c_tmm.inc(te.matmul(pspb, lt8[:, :], ccol[:, 1:2], start=True, stop=True))
            te.wait_ge(c_dsm.h, 8)
            c_tmm.inc(te.matmul(pser, o8f[:, :], cmpe[:, :], start=True, stop=True))
            te.wait_ge(c_dsm.h, 10)
            c_tmm.inc(te.matmul(psix, ones1f[:, :], erowf[0:1, :], start=True, stop=True))
            te.wait_ge(c_dmab.h, 128)
            for pt in range(32):
                if pt >= 2:
                    te.wait_ge(c_sl.h, 3 * (pt - 1))
                te.matmul(pssl[pt % 2][:, :], tri[:, :], mskT[:, pt * 8:(pt + 1) * 8],
                          start=True, stop=False)
                te.matmul(pssl[pt % 2][:, :], ones1f[:, :], basrow[0:1, pt * 8:(pt + 1) * 8],
                          start=False, stop=False)
                c_slm.inc(te.matmul(pssl[pt % 2][:, :], ones1f[:, :], pbrow[0:1, 8:16],
                                    start=False, stop=True))

        @blk.vector
        def _(v):
            cinit.inc(v.memset(init0[:, :], 0))
            cinit.inc(v.memset(initb[:, :], 100000))
            for pt in range(32):
                v.wait_ge(c_mm.h, pt + 1)
                v.wait_ge(c_ev.h, c_ev.n)
                c_ev.inc(v.tensor_copy(lgT[pt % 2][:, :], pslg[pt % 2][:, :]))
                v.wait_ge(c_ev.h, c_ev.n)
                c_ev.inc(v.reduce_max(mxc[:, :], lgT[pt % 2][:, :], axis=mybir.AxisListType.X))
                v.wait_ge(c_ev.h, c_ev.n)
                c_ev.inc(v.tensor_scalar(mskT[:, pt * 8:(pt + 1) * 8], lgT[pt % 2][:, :],
                                         mxc[:, :], None, ALU.is_ge))
                v.wait_ge(c_bs.h, pt + 1)
                c_bse.inc(v.tensor_copy(bsrow[0:1, pt * 8:(pt + 1) * 8], psbs))
            v.wait_ge(c_tmm.h, 2)
            c_dsm.inc(v.tensor_copy(basT[:, :], psba))
            c_dsm.inc(v.tensor_copy(ccol[:, 0:1], psct))
            v.wait_ge(c_dsm.h, c_dsm.n)
            c_dsm.inc(v.tensor_scalar(ccol[:, 2:3], ccol[:, 0:1], 511.0, 1.0 / 512.0,
                                      ALU.add, ALU.mult))
            v.wait_ge(c_dsm.h, c_dsm.n)
            c_dsm.inc(v.tensor_copy(ccoli[:, 0:1], ccol[:, 2:3]))
            v.wait_ge(c_dsm.h, c_dsm.n)
            c_dsm.inc(v.tensor_scalar(ccol[:, 1:2], ccoli[:, 0:1], 512.0, None, ALU.mult))
            v.wait_ge(c_tmm.h, 3)
            c_dsm.inc(v.tensor_copy(pbcol[:, 0:1], pspb))
            v.wait_ge(c_dsm.h, c_dsm.n)
            c_dsm.inc(v.tensor_scalar(pbcol[:, 1:2], pbcol[:, 0:1], -1.0, None, ALU.add))
            c_dsm.inc(v.tensor_scalar(cmpe[:, :], P.c("tilebases")[:, :], pbcol[:, 0:1], None, ALU.is_ge))
            v.wait_ge(c_tmm.h, 4)
            v.wait_ge(c_dsm.h, c_dsm.n)
            c_dsm.inc(v.tensor_scalar(erowf[0:1, :], pser, -1.0, None, ALU.add))
            v.wait_ge(c_dsm.h, c_dsm.n)
            c_dsm.inc(v.tensor_copy(eidi[0:1, :], erowf[0:1, :]))
            v.wait_ge(c_tmm.h, 5)
            bci = bass.AP(io128f, 0, [[1, 128], [0, 16]])
            v.wait_ge(c_dsm.h, c_dsm.n)
            c_dsm.inc(v.scalar_tensor_tensor(idxf[:, 0:16], psix, 1024.0, bci,
                                             ALU.mult, ALU.add))
            v.wait_ge(c_dsm.h, c_dsm.n)
            c_dsm.inc(v.scalar_tensor_tensor(idxf[:, 16:32], psix, 4096.0, bci,
                                             ALU.mult, ALU.add))
            v.wait_ge(c_dsm.h, c_dsm.n)
            c_dsm.inc(v.scalar_tensor_tensor(idxf[:, 32:48], psix, 128.0, bci,
                                             ALU.mult, ALU.add))
            v.wait_ge(c_dsm.h, c_dsm.n)
            c_dsm.inc(v.tensor_copy(idxw[:, :], idxf[:, :]))
            for pt in range(32):
                v.wait_ge(c_slm.h, pt + 1)
                v.wait_ge(c_sl.h, c_sl.n)
                if pt >= 1:
                    v.wait_ge(c_scg.h, 16 * pt)
                    v.wait_ge(c_scs.h, 16 * pt)
                c_sl.inc(v.tensor_mul(slotf[:, :], mskT[:, pt * 8:(pt + 1) * 8], pssl[pt % 2][:, :]))
                v.wait_ge(c_sl.h, c_sl.n)
                c_sl.inc(v.reduce_sum(slotc[:, :], slotf[:, :], axis=mybir.AxisListType.X))
                v.wait_ge(c_sl.h, c_sl.n)
                c_sl.inc(v.tensor_copy(sloti[:, :], slotc[:, :]))

        @blk.gpsimd
        def _(gp):
            gp.wait_ge(cinit.h, 2)
            c_dmab.inc(gp.dma_start(out=gidx[:, :].rearrange("(a b) o -> a (b o)", a=128),
                                    in_=init0[:, :]), 16)
            c_dmab.inc(gp.dma_start(out=sidx[:, :].rearrange("(a b) o -> a (b o)", a=128),
                                    in_=initb[:, :]), 16)
            gp.wait_ge(c_bse.h, 32)
            c_dmab.inc(gp.dma_start(out=sc_bs[0:1, :], in_=bsrow[0:1, :]), 16)
            gp.wait_ge(c_dmab.h, 48)
            c_dmab.inc(gp.dma_start(out=bsT[:, :],
                                    in_=sc_bs[0:1, :].rearrange("o (a b) -> (o a) b", a=32)), 16)
            gp.wait_ge(c_dsm.h, 1)
            c_dmab.inc(gp.dma_start(out=sc_bas[0:1, :].rearrange("o (a b) -> (o a) b", a=32),
                                    in_=basT[:, :]), 16)
            gp.wait_ge(c_dmab.h, 80)
            c_dmab.inc(gp.dma_start(out=basrow[0:1, :], in_=sc_bas[0:1, :]), 16)
            gp.wait_ge(c_dsm.h, 7)
            c_dmab.inc(gp.dma_start(out=sc_pb[0:1, 0:16].rearrange("o (b a) -> (o a) b", a=8),
                                    in_=pbcol[:, :], allow_slow_non_contiguous=True), 16)
            gp.wait_ge(c_dmab.h, 112)
            c_dmab.inc(gp.dma_start(out=pbrow[0:1, :], in_=sc_pb[0:1, 0:16]), 16)
            with gp.register(f"r2bc{u}") as rb:
                gp.reg_mov(rb, NTILE * 512 - 1)
                for pt in range(32):
                    gp.wait_ge(c_sl.h, 3 * (pt + 1))
                    c_scg.inc(gp.indirect_dma_start(
                        gidx[:, :], bass.IndirectOffsetOnAxis(sloti[:, 0:1], 0),
                        iota[:, pt:pt + 1], None,
                        bounds_check=rb, oob_is_err=False), 16)
                    c_scs.inc(gp.indirect_dma_start(
                        sidx[:, :], bass.IndirectOffsetOnAxis(sloti[:, 0:1], 0),
                        iota[:, pt:pt + 1], None,
                        bounds_check=rb, oob_is_err=False), 16)
            gp.wait_ge(c_scg.h, c_scg.n)
            gp.wait_ge(c_scs.h, c_scs.n)
    P.barrier()
    P.sem_release(_smark)
    pes.close()


def emit_moe_routed2(P, sc_yT, sc_w1b, sc_w2b, sc_b1b, sc_b2b, gidx, sidx, idxw, eidi, sc_moeT):
    I32 = mybir.dt.int32
    IOOA = bass.IndirectOffsetOnAxis
    id_bf = P.c("id_bf")
    NG = 4
    u = P.uid()
    _smark = P.sem_mark()
    pes = contextlib.ExitStack()
    w1sb = P.sbuf(f"m2{u}w1", [128, 8 * 4096], BF16, es=pes)
    w2sb = P.sbuf(f"m2{u}w2", [128, 32 * 1024], BF16, es=pes)
    b1c = [P.sbuf(f"m2{u}c1{i}", [128, 32], BF16, es=pes) for i in range(2)]
    b2cb = [P.sbuf(f"m2{u}cb{i}", [128, 8], BF16, es=pes) for i in range(2)]
    b2c = [P.sbuf(f"m2{u}c2{i}", [128, 8], F32, es=pes) for i in range(2)]
    gcol = [P.sbuf(f"m2{u}g{i}", [128, 1], I32, es=pes) for i in range(2)]
    scol = [P.sbuf(f"m2{u}s{i}", [128, 1], I32, es=pes) for i in range(2)]
    gth = [P.sbuf(f"m2{u}t{i}", [128, D], BF16, es=pes) for i in range(2)]
    oT = [P.sbuf(f"m2{u}u{i}", [128, D], BF16, es=pes) for i in range(4)]
    xe = [P.sbuf(f"m2{u}x{i}", [128, 8 * 512], BF16, es=pes) for i in range(2)]
    he = P.sbuf(f"m2{u}h", [128, 32 * 512], BF16, es=pes)
    oe = P.sbuf(f"m2{u}o", [128, 8 * 512], BF16, es=pes)
    psw = [P.psum(f"m2{u}p{i}", [128, 512], F32, es=pes) for i in range(3)]
    ptp = [P.psum(f"m2{u}q{i}", [128, 128], BF16, es=pes) for i in range(2)]

    wld = Ctr(P, f"m2{u}wl")
    wl2 = Ctr(P, f"m2{u}w2")
    bl1 = Ctr(P, f"m2{u}b1")
    bl2 = Ctr(P, f"m2{u}b2")
    cbc = Ctr(P, f"m2{u}cc")
    il = Slots(P, f"m2{u}il", 2)
    i2 = Slots(P, f"m2{u}i2", 2)
    gld = Slots(P, f"m2{u}gl", 2)
    scs = Slots(P, f"m2{u}sc", 4)
    c_tr = Ctr(P, f"m2{u}tr")
    c_ev = Ctr(P, f"m2{u}ev")
    c_h = Ctr(P, f"m2{u}hm")
    c_he = Ctr(P, f"m2{u}he")
    c_o = Ctr(P, f"m2{u}om")
    c_oe = Ctr(P, f"m2{u}oe")

    def ps_wait(te, m):
        pm = m - 3
        if pm < 0:
            return
        tp, r = divmod(pm, 40)
        if r < 32:
            te.wait_ge(c_he.h, 32 * tp + r + 1)
        else:
            te.wait_ge(c_oe.h, 8 * tp + (r - 32) + 1)

    with P.block() as blk:
        @blk.gpsimd
        def _(gp):
            def token_gather(t, g):
                GG = 4 * t + g
                if GG >= 2:
                    gp.wait_ge(gld.c(GG).h, 16 * (GG // 2))
                il.c(GG).inc(gp.dma_start(
                    out=gcol[GG % 2][:, :],
                    in_=gidx[t * 512 + g * 128: t * 512 + (g + 1) * 128, 0:1]), 16)
                gp.wait_ge(il.c(GG).h, 16 * (GG // 2 + 1))
                if GG >= 2:
                    pv, gv = divmod(GG - 2, 4)
                    gp.wait_ge(c_tr.h, 64 * pv + 8 * (gv + 1))
                gld.c(GG).inc(gp.indirect_dma_start(
                    gth[GG % 2][:, :], None,
                    sc_yT[:, :], IOOA(gcol[GG % 2][:, 0:1], 0)), 16)

            def scatters(tt):
                for g in range(NG):
                    GG = 4 * tt + g
                    if GG >= 2:
                        pt, pg = divmod(GG - 2, 4)
                        gp.wait_ge(scs.c(pg).h, 16 * (pt + 1))
                    i2.c(GG).inc(gp.dma_start(
                        out=scol[GG % 2][:, :],
                        in_=sidx[tt * 512 + g * 128: tt * 512 + (g + 1) * 128, 0:1]), 16)
                    gp.wait_ge(i2.c(GG).h, 16 * (GG // 2 + 1))
                    gp.wait_ge(c_ev.h, 64 * tt + 32 + 8 * (g + 1))
                    scs.c(g).inc(gp.indirect_dma_start(
                        sc_moeT[:, :], IOOA(scol[GG % 2][:, 0:1], 0),
                        oT[g][:, :], None,
                        bounds_check=rb, oob_is_err=False), 16)

            with gp.register(f"m2bc{u}") as rb:
                gp.reg_mov(rb, S - 1)
                for t in range(NTILE):
                    if t >= 1:
                        gp.wait_ge(c_h.h, 32 * t)
                    for k in range(NDC):
                        wld.inc(gp.indirect_dma_start(
                            w1sb[:, k * 4096:(k + 1) * 4096], None,
                            sc_w1b[:, :], IOOA(idxw[:, t:t + 1], 0),
                            element_offset=k * 128 * 4096), 16)
                    token_gather(t, 0)
                    token_gather(t, 1)
                    if t >= 1:
                        gp.wait_ge(c_o.h, 8 * t)
                    for dfi in range(32):
                        wl2.inc(gp.indirect_dma_start(
                            w2sb[:, dfi * 1024:(dfi + 1) * 1024], None,
                            sc_w2b[:, :], IOOA(idxw[:, 16 + t:17 + t], 0),
                            element_offset=dfi * 128 * 1024), 16)
                    if t >= 2:
                        gp.wait_ge(c_he.h, 32 * (t - 1))
                    bl1.inc(gp.indirect_dma_start(
                        b1c[t % 2][:, :], None,
                        sc_b1b[:, :], IOOA(idxw[:, 32 + t:33 + t], 0)), 16)
                    if t >= 2:
                        gp.wait_ge(cbc.h, t - 1)
                    bl2.inc(gp.indirect_dma_start(
                        b2cb[t % 2][:, :], None,
                        sc_b2b[:, :], IOOA(idxw[:, 32 + t:33 + t], 0)), 16)
                    token_gather(t, 2)
                    token_gather(t, 3)
                    if t >= 1:
                        scatters(t - 1)
                scatters(NTILE - 1)
                for i in range(4):
                    gp.wait_ge(scs.c(i).h, scs.c(i).n)

        @blk.tensor
        def _(te):
            for t in range(NTILE):
                for g in range(NG):
                    GG = 4 * t + g
                    te.wait_ge(gld.c(GG).h, 16 * (GG // 2 + 1))
                    for j in range(NDC):
                        k = 64 * t + 8 * g + j
                        if k >= 2:
                            te.wait_ge(c_ev.h, k - 1)
                        c_tr.inc(te.transpose(ptp[k % 2][:, :],
                                              gth[g % 2][:, j * 128:(j + 1) * 128],
                                              id_bf[:, :]))
                te.wait_ge(wld.h, 16 * 8 * (t + 1))
                te.wait_ge(c_ev.h, 64 * t + 32)
                for df in range(32):
                    m = 40 * t + df
                    ps_wait(te, m)
                    for k in range(NDC):
                        r = te.matmul(psw[m % 3][:, :],
                                      w1sb[:, k * 4096 + df * 128: k * 4096 + df * 128 + 128],
                                      xe[t % 2][:, k * 512:(k + 1) * 512],
                                      start=(k == 0), stop=(k == NDC - 1))
                    c_h.inc(r)
                te.wait_ge(wl2.h, 16 * 32 * (t + 1))
                te.wait_ge(c_he.h, 32 * (t + 1))
                for do in range(8):
                    m = 40 * t + 32 + do
                    ps_wait(te, m)
                    for df in range(32):
                        r = te.matmul(psw[m % 3][:, :],
                                      w2sb[:, df * 1024 + do * 128: df * 1024 + do * 128 + 128],
                                      he[:, df * 512:(df + 1) * 512],
                                      start=(df == 0), stop=(df == 31))
                    c_o.inc(r)
                te.wait_ge(c_oe.h, 8 * (t + 1))
                for g in range(NG):
                    for j in range(NDC):
                        k = 64 * t + 32 + 8 * g + j
                        if k >= 2:
                            te.wait_ge(c_ev.h, k - 1)
                        c_tr.inc(te.transpose(ptp[k % 2][:, :],
                                              oe[:, j * 512 + g * 128: j * 512 + (g + 1) * 128],
                                              id_bf[:, :]))

        @blk.scalar
        def _(sc):
            for t in range(NTILE):
                if t >= 2:
                    sc.wait_ge(c_h.h, 32 * (t - 1))
                for g in range(NG):
                    for j in range(NDC):
                        k = 64 * t + 8 * g + j
                        sc.wait_ge(c_tr.h, k + 1)
                        c_ev.inc(sc.copy(xe[t % 2][:, j * 512 + g * 128: j * 512 + (g + 1) * 128],
                                         ptp[k % 2][:, :]))
                sc.wait_ge(bl1.h, 16 * (t + 1))
                for df in range(32):
                    sc.wait_ge(c_h.h, 32 * t + df + 1)
                    if t >= 1 and df == 0:
                        sc.wait_ge(c_o.h, 8 * t)
                    c_he.inc(sc.activation(he[:, df * 512:(df + 1) * 512],
                                           psw[(40 * t + df) % 3][:, :],
                                           AF.Relu, bias=b1c[t % 2][:, df:df + 1]))
                for g in range(NG):
                    GG = 4 * t + g
                    for j in range(NDC):
                        k = 64 * t + 32 + 8 * g + j
                        sc.wait_ge(c_tr.h, k + 1)
                        if j == 0 and t >= 1:
                            sc.wait_ge(scs.c(g).h, 16 * t)
                        c_ev.inc(sc.copy(oT[g][:, j * 128:(j + 1) * 128],
                                         ptp[k % 2][:, :]))

        @blk.vector
        def _(v):
            for t in range(NTILE):
                v.wait_ge(bl2.h, 16 * (t + 1))
                if t >= 2:
                    v.wait_ge(c_oe.h, 8 * (t - 1))
                cbc.inc(v.tensor_copy(b2c[t % 2][:, :], b2cb[t % 2][:, :]))
                for do in range(8):
                    v.wait_ge(c_o.h, 8 * t + do + 1)
                    if do == 0:
                        v.wait_ge(cbc.h, t + 1)
                        if t >= 1:
                            v.wait_ge(c_tr.h, 64 * t)
                    c_oe.inc(v.tensor_scalar(oe[:, do * 512:(do + 1) * 512],
                                             psw[(40 * t + 32 + do) % 3][:, :],
                                             b2c[t % 2][:, do:do + 1], None, ALU.add))
    P.barrier()
    P.sem_release(_smark)
    pes.close()
